# version 15
# baseline (speedup 1.0000x reference)
import numpy as np
import ml_dtypes
import concourse.bass as bass
import concourse.mybir as mybir
from concourse.bass_utils import run_bass_kernel_spmd

F32 = mybir.dt.float32
BF16 = mybir.dt.bfloat16
AF = mybir.ActivationFunctionType
ALU = mybir.AluOpType
AX = mybir.AxisListType

MAGIC = 12582912.0
UPTO = 6
FFNW = 10
W1MODE = 0
NB = 2
NT = 16
ND = 8
NHC = 32

ENGS = ("sync", "gpsimd", "tensor", "scalar", "vector")


class Plan:
    def __init__(self):
        from collections import defaultdict
        self.ops = {e: [] for e in ENGS}
        self.cnt = defaultdict(int)

    def inc(self, eng, fn, sem, by=1):
        self.ops[eng].append(("inc", fn, sem, by))
        self.cnt[sem] += by
        return self.cnt[sem]

    def op(self, eng, fn):
        self.ops[eng].append(("op", fn))

    def wait(self, eng, sem, val):
        if val > 0:
            self.ops[eng].append(("wait", sem, val))

    def replay(self, eng_name, obj, sems):
        for it in self.ops[eng_name]:
            if it[0] == "op":
                it[1](obj)
            elif it[0] == "inc":
                it[1](obj).then_inc(sems[it[2]], it[3])
            else:
                obj.wait_ge(sems[it[1]], it[2])


def build(debug=False):
    nc = bass.Bass("TRN2", target_bir_lowering=False)
    P = Plan()

    def T(fn):
        return P.inc("tensor", fn, "t")

    def S(fn):
        return P.inc("scalar", fn, "s")

    def V(fn):
        return P.inc("vector", fn, "v")

    def Vd(fn):
        n = P.inc("vector", fn, "v")
        P.wait("vector", "v", n)
        return n

    def DW(fn, sem):
        return P.inc("sync", fn, sem, 16)

    def DG(fn, sem):
        return P.inc("gpsimd", fn, sem, 16)

    SYNC_SEMS = ("pw0", "pw1", "qw0", "qw1", "gv", "wr", "sg0", "sg1", "nb0", "nb1")
    GP_SEMS = ("gi", "gx0", "gx1", "ga0", "ga1", "go0", "go1", "gr", "gd", "mb", "p1a", "p1b", "p2a", "p2b")
    SYNC_SEMS = SYNC_SEMS + ("tr0", "tr1")

    def fence():
        from collections import defaultdict
        c = defaultdict(int, P.cnt)
        dma = [s for s in list(SYNC_SEMS) + list(GP_SEMS) if c.get(s, 0) > 0]
        P.wait("tensor", "s", c["s"]); P.wait("tensor", "v", c["v"])
        P.wait("scalar", "t", c["t"]); P.wait("scalar", "v", c["v"])
        P.wait("vector", "t", c["t"]); P.wait("vector", "s", c["s"])
        for q in ("sync", "gpsimd"):
            P.wait(q, "t", c["t"]); P.wait(q, "s", c["s"]); P.wait(q, "v", c["v"])
            for s in dma:
                P.wait(q, s, c[s])
        for e in ("tensor", "scalar", "vector"):
            for s in dma:
                P.wait(e, s, c[s])

    x_d = nc.dram_tensor("x", [NB, 2048, 1024], F32, kind="ExternalInput").ap()
    wattn_d = nc.dram_tensor("W_attn", [1024, 3072], F32, kind="ExternalInput").ap()
    battn_d = nc.dram_tensor("b_attn", [3072], F32, kind="ExternalInput").ap()
    wout_d = nc.dram_tensor("W_out", [1024, 1024], F32, kind="ExternalInput").ap()
    bout_d = nc.dram_tensor("b_out", [1024], F32, kind="ExternalInput").ap()
    w1_d = nc.dram_tensor("w1", [4096, 1024], F32, kind="ExternalInput").ap()
    cw_d = nc.dram_tensor("cw", [4096, 3], F32, kind="ExternalInput").ap()
    cb_d = nc.dram_tensor("cb", [4096], F32, kind="ExternalInput").ap()
    w2_d = nc.dram_tensor("w2", [1024, 4096], F32, kind="ExternalInput").ap()
    identbf_d = nc.dram_tensor("ident_bf", [128, 128], BF16, kind="ExternalInput").ap()
    identf_d = nc.dram_tensor("ident_f", [128, 128], F32, kind="ExternalInput").ap()
    tribf_d = nc.dram_tensor("tri_bf", [128, 128], BF16, kind="ExternalInput").ap()
    out_d = nc.dram_tensor("out", [NB, 2048, 1024], F32, kind="ExternalOutput").ap()

    wqk_sc = nc.dram_tensor("wqk_sc", [16, 128, 1024], BF16, kind="Internal").ap()
    wv_sc = nc.dram_tensor("wv_sc", [2, 8, 128, 640], BF16, kind="Internal").ap()
    wo_sc = nc.dram_tensor("wo_sc", [8, 128, 1024], BF16, kind="Internal").ap()
    w1q_sc = nc.dram_tensor("w1q_sc", [32, 128, 1024], BF16, kind="Internal").ap()
    w2q_sc = nc.dram_tensor("w2q_sc", [32, 128, 1024], BF16, kind="Internal").ap()
    acc_sc = nc.dram_tensor("acc_sc", [NB, 16, 128, 1024], F32, kind="Internal").ap()
    row_sc = nc.dram_tensor("row_sc", [4096], F32, kind="Internal").ap()
    den_sc = nc.dram_tensor("den_sc", [2, 2, 512], F32, kind="Internal").ap()
    trit_sc = nc.dram_tensor("trit_sc", [64, 128, 1024], BF16, kind="Internal").ap()
    mean_sc = nc.dram_tensor("mean_sc", [256], F32, kind="Internal").ap()

    dbg = {}
    if debug:
        for nm, shp, dt in [("d_xnT", [128, 16384], BF16), ("d_xiT", [128, 16384], BF16),
                            ("d_qkT", [128, 32768], BF16), ("d_vsb", [128, 20480], BF16),
                            ("d_ctx", [128, 16384], BF16), ("d_g", [128, 2048], F32),
                            ("d_acc", [16, 128, 1024], F32),
                            ("d_w1q", [32, 128, 1024], BF16), ("d_w2q", [32, 128, 1024], BF16),
                            ("d_ccol", [128, 16], F32)]:
            dbg[nm] = nc.dram_tensor(nm, shp, dt, kind="ExternalOutput").ap()

    def sbt(name, shape, dtype, off):
        return nc.alloc_sbuf_tensor_at(name, shape, dtype, offset=off).ap()

    cur = [16512]

    def nxt(sz, c=cur):
        r = c[0]
        c[0] = (c[0] + sz + 31) & ~31
        return r

    ident_bf = sbt("identbf", [128, 128], BF16, nxt(256))
    ones_bf = sbt("onesbf", [128, 128], BF16, nxt(256))
    tri_bf = sbt("tribf", [128, 128], BF16, nxt(256))
    ident_f = sbt("identf", [128, 128], F32, nxt(512))
    ones_f = sbt("onesf", [128, 128], F32, nxt(512))
    ccol = sbt("ccol", [128, 16], F32, nxt(64))
    C_MAG, C_NMAG, C_EPS, C_C1, C_C2, C_SW1, C_SW2 = 0, 1, 2, 3, 4, 5, 6
    bcol = sbt("bcol", [128, 16], F32, nxt(64))
    cwsb = sbt("cwsb", [128, 96], F32, nxt(384))
    cbsb = sbt("cbsb", [128, 32], F32, nxt(128))
    stat = sbt("stat", [128, 16], F32, nxt(64))
    gst = sbt("gst", [128, 16], F32, nxt(64))
    bvrow = sbt("bvrow", [1, 1280], BF16, nxt(2560))
    borow = sbt("borow", [1, 1024], BF16, nxt(2048))
    nrm = sbt("nrm", [128, 1024], F32, nxt(4096))
    rows_sb = sbt("rowssb", [128, 1024], F32, nxt(4096))

    R0 = (cur[0] + 63) & ~63
    xiT = sbt("xiT", [128, 16384], BF16, R0)
    XNC = R0 + 32768
    xnT = sbt("xnT", [128, 16384], BF16, XNC)
    ctx_sb = sbt("ctxsb", [128, 16384], BF16, XNC)
    h2w = sbt("h2w", [128, 8192], F32, XNC)
    QK = XNC + 32768
    qkT = sbt("qkT", [128, 32768], BF16, QK)
    VSB = QK + 65536
    v_sb = sbt("vsb", [128, 20480], BF16, VSB)
    AQ = VSB + 40960
    xbuf = sbt("xbuf", [128, 2048], F32, AQ)
    xc = sbt("xc", [128, 1024], F32, AQ + 8192)
    xt1 = sbt("xt1", [128, 1024], F32, AQ + 12288)
    xn_t = sbt("xnt", [128, 1024], BF16, AQ + 16384)
    xi_t = sbt("xit", [128, 1024], BF16, AQ + 18432)
    wqkb = sbt("wqkb", [128, 2048], BF16, AQ)
    wvb = sbt("wvb", [128, 5120], BF16, AQ + 4096)
    p_sb = sbt("psb", [128, 2048], BF16, AQ)
    aqld = sbt("aqld", [128, 2048], F32, AQ + 4096)
    aqtr = sbt("aqtr", [128, 2048], BF16, AQ + 12288)
    aqsg = sbt("aqsg", [128, 2048], BF16, AQ + 16384)
    SB_END = AQ + 20480

    c0 = [QK]
    ws0 = sbt("ws0", [128, 4096], F32, nxt(16384, c0))
    ws1 = sbt("ws1", [128, 4096], F32, nxt(16384, c0))
    wcast = sbt("wcast", [128, 4096], BF16, nxt(8192, c0))
    tqtmp = sbt("tqtmp", [128, 4096], F32, nxt(16384, c0))
    tritb = sbt("tritb", [128, 4096], BF16, nxt(8192, c0))
    stg = [sbt("stg0", [128, 1024], BF16, nxt(2048, c0)),
           sbt("stg1", [128, 1024], BF16, nxt(2048, c0))]
    zstage = sbt("zstage", [128, 128], BF16, nxt(256, c0))
    ws = [ws0, ws1]

    wo_res = sbt("wores", [128, 8192], BF16, QK)
    xrb = sbt("xrb", [128, 2048], F32, QK + 16384)
    acst = sbt("acst", [128, 2048], F32, QK + 24576)

    c1 = [QK]
    g_sb = sbt("gsb", [128, 2048], F32, nxt(8192, c1))
    g_row = sbt("grow", [1, 2048], F32, nxt(8192, c1))
    stash = sbt("stash", [128, 4128], F32, nxt(16512, c1))
    ubuf = [sbt("u0", [128, 388], F32, nxt(1552, c1)),
            sbt("u1", [128, 388], F32, nxt(1552, c1))]
    h2q = sbt("h2q", [128, 8192], BF16, nxt(16384, c1))
    w1b = sbt("w1b", [128, 8192], BF16, nxt(16384, c1))
    w2b = sbt("w2b", [128, 8192], BF16, nxt(16384, c1))
    accb = sbt("accb", [128, 2048], F32, nxt(8192, c1))
    outb = sbt("outb", [128, 2048], F32, nxt(8192, c1))
    mx = sbt("mx", [128, 16], F32, nxt(64, c1))
    sbrow = sbt("sbrow", [1, 256], F32, nxt(1024, c1))
    sb_sb = sbt("sbsb", [128, 256], F32, nxt(1024, c1))
    pmb = sbt("pmb", [128, 512], F32, nxt(2048, c1))
    assert c1[0] <= VSB + 40960, c1[0]

    pg0 = nc.alloc_psum_tensor("pg0", [128, 1024], F32).ap()
    pg1 = nc.alloc_psum_tensor("pg1", [128, 1024], F32).ap()
    pg2 = nc.alloc_psum_tensor("pg2", [128, 1024], F32).ap()
    pg3 = nc.alloc_psum_tensor("pg3", [128, 1024], F32).ap()

    V(lambda e: e.memset(ones_f[:], 1.0))
    V(lambda e: e.memset(ones_bf[:], 1.0))
    V(lambda e: e.memset(ccol[:, C_MAG:C_MAG + 1], MAGIC))
    V(lambda e: e.memset(ccol[:, C_NMAG:C_NMAG + 1], -MAGIC))
    V(lambda e: e.memset(ccol[:, C_EPS:C_EPS + 1], 1e-5))
    DG(lambda e: e.dma_start(out=ident_bf[:], in_=identbf_d), "gi")
    DG(lambda e: e.dma_start(out=ident_f[:], in_=identf_d), "gi")
    DG(lambda e: e.dma_start(out=tri_bf[:], in_=tribf_d), "gi")
    DG(lambda e: e.dma_start(out=cwsb[:].rearrange("p (h j) -> p h j", h=32),
                             in_=cw_d.rearrange("(h p) j -> p h j", p=128)), "gi")
    DG(lambda e: e.dma_start(out=cbsb[:], in_=cb_d.rearrange("(h p) -> p h", p=128),
                             allow_slow_non_contiguous=True), "gi")
    DG(lambda e: e.dma_start(out=bcol[:], in_=battn_d[0:2048].rearrange("(c p) -> p c", p=128),
                             allow_slow_non_contiguous=True), "gi")
    DG(lambda e: e.dma_start(out=nrm[0:1, :], in_=battn_d[2048:3072].unsqueeze(0)), "gi")
    gg = DG(lambda e: e.dma_start(out=xc[0:1, :], in_=bout_d.unsqueeze(0)), "gi")
    P.wait("vector", "gi", gg)
    P.wait("tensor", "gi", gg)
    P.wait("scalar", "gi", gg)
    V(lambda e: e.memset(bvrow[:], 0.0))
    V(lambda e: e.memset(bvrow[:].rearrange("o (q c) -> o q c", c=160)[:, :, 64:66], 1.0))
    V(lambda e: e.tensor_copy(bvrow[:].rearrange("o (q c) -> o q c", c=160)[:, :, 0:64],
                              nrm[0:1, :].rearrange("o (h c) -> o h c", c=128)[:, :, 0:64]))
    V(lambda e: e.tensor_copy(bvrow[:].rearrange("o (q c) -> o q c", c=160)[:, :, 96:160],
                              nrm[0:1, :].rearrange("o (h c) -> o h c", c=128)[:, :, 64:128]))
    V(lambda e: e.memset(zstage[:], 0.0))
    V(lambda e: e.tensor_copy(borow[:], xc[0:1, :]))
    P.op("scalar", lambda e: e.preload_act_table(AF.Sqrt))

    cast_v = {}
    wr_w = 0
    for dc in range(ND):
        if dc >= 2:
            P.wait("sync", "v", cast_v[dc - 2])
        wt = DW(lambda e, dc=dc, b=ws[dc % 2]: e.dma_start(out=b[:, 0:3072], in_=wattn_d[dc * 128:(dc + 1) * 128, :]), "pw%d" % (dc % 2))
        P.wait("vector", "pw%d" % (dc % 2), wt)
        P.wait("vector", "wr", P.cnt["wr"])
        cast_v[dc] = V(lambda e, b=ws[dc % 2]: e.tensor_copy(wcast[:, 0:3072], b[:, 0:3072]))
        P.wait("sync", "v", cast_v[dc])
        for ct in range(16):
            DW(lambda e, ct=ct, dc=dc: e.dma_start(out=wqk_sc[ct, :, dc * 128:(dc + 1) * 128],
                                                   in_=wcast[:, ct * 128:(ct + 1) * 128]), "wr")
        for hf in range(2):
            DW(lambda e, hf=hf, dc=dc: e.dma_start(
                out=wv_sc[hf, dc].rearrange("p (q c) -> p q c", c=160)[:, :, 64:96],
                in_=zstage[:].rearrange("p (q c) -> p q c", c=32),
                allow_slow_non_contiguous=True), "wr")
            DW(lambda e, hf=hf, dc=dc: e.dma_start(
                out=wv_sc[hf, dc].rearrange("p (q c) -> p q c", c=160)[:, :, 0:64],
                in_=wcast[:, 2048 + hf * 512: 2048 + hf * 512 + 512].rearrange("p (q c) -> p q c", c=128)[:, :, 0:64],
                allow_slow_non_contiguous=True), "wr")
            wr_w = DW(lambda e, hf=hf, dc=dc: e.dma_start(
                out=wv_sc[hf, dc].rearrange("p (q c) -> p q c", c=160)[:, :, 96:160],
                in_=wcast[:, 2048 + hf * 512: 2048 + hf * 512 + 512].rearrange("p (q c) -> p q c", c=128)[:, :, 64:128],
                allow_slow_non_contiguous=True), "wr")
    wrh = {0: 0, 1: 0}
    for dc in range(ND):
        pp = dc % 2
        if dc >= 2:
            P.wait("sync", "v", cast_v[8 + dc - 2])
        wt = DW(lambda e, dc=dc, b=ws[dc % 2]: e.dma_start(out=b[:, 0:1024], in_=wout_d[dc * 128:(dc + 1) * 128, :]), "pw%d" % (dc % 2))
        P.wait("vector", "pw%d" % (dc % 2), wt)
        P.wait("vector", "wr", P.cnt["wr"])
        cast_v[8 + dc] = V(lambda e, b=ws[dc % 2], pp=pp: e.tensor_copy(wcast[:, pp * 1024:pp * 1024 + 1024], b[:, 0:1024]))
        P.wait("sync", "v", cast_v[8 + dc])
        wrh[pp] = DW(lambda e, dc=dc, pp=pp: e.dma_start(out=wo_sc[dc], in_=wcast[:, pp * 1024:pp * 1024 + 1024]), "wr")

    wq_state = {}

    def wq_pass1(w_d, ntiles, tilecols, swcol, c1col, tag):
        acm = stat[:, 0:1]
        tmpc = stat[:, 1:2]
        nch = ntiles * tilecols // 1024
        sem = "p1a" if tag == 0 else "p1b"
        prevS = {0: 0, 1: 0}
        prevV = 0
        for i in range(nch):
            if i >= 2:
                P.wait("gpsimd", "s", prevS[i % 2])
            r, cc = (i * 1024) // tilecols, (i * 1024) % tilecols
            wt = DG(lambda e, r=r, cc=cc, sl=(i % 2): e.dma_start(
                out=aqld[:, sl * 1024:sl * 1024 + 1024], in_=w_d[r * 128:(r + 1) * 128, cc:cc + 1024]), sem)
            P.wait("scalar", sem, wt)
            if i == 0:
                prevS[0] = S(lambda e: e.activation(rows_sb[:], aqld[:, 0:1024], AF.Abs, accum_out=acm))
                prevV = P.cnt["v"]
            else:
                P.wait("scalar", "v", prevV)
                prevS[i % 2] = S(lambda e, sl=(i % 2): e.activation(
                    rows_sb[:], aqld[:, sl * 1024:sl * 1024 + 1024], AF.Abs, accum_out=tmpc))
                P.wait("vector", "s", prevS[i % 2])
                prevV = Vd(lambda e: e.tensor_tensor(acm, acm, tmpc, ALU.add))
        P.wait("gpsimd", "v", prevV)
        P.wait("gpsimd", "s", max(prevS[0], prevS[1]))
        mb = DG(lambda e: e.dma_start(out=mean_sc[0:128].unsqueeze(0).rearrange("o c -> c o"), in_=acm), "mb")
        P.wait("gpsimd", "mb", mb)
        mb = DG(lambda e: e.dma_start(out=nrm[0:1, 0:128], in_=mean_sc[0:128].unsqueeze(0)), "mb")
        P.wait("vector", "mb", mb)
        Vd(lambda e: e.reduce_sum(stat[0:1, 2:3], nrm[0:1, 0:128], AX.X))
        Vd(lambda e, n=float(ntiles * 128 * tilecols): e.tensor_scalar(stat[0:1, 2:3], stat[0:1, 2:3], 1.0 / n, None, ALU.mult))
        Vd(lambda e: e.tensor_scalar(stat[0:1, 2:3], stat[0:1, 2:3], 1e-5, None, ALU.max))
        vv = Vd(lambda e: e.reciprocal(stat[0:1, 3:4], stat[0:1, 2:3]))
        P.wait("gpsimd", "v", vv)
        mb = DG(lambda e: e.dma_start(out=mean_sc[128:130].unsqueeze(0), in_=stat[0:1, 2:4]), "mb")
        P.wait("gpsimd", "mb", mb)
        DG(lambda e: e.dma_start(out=stat[:, 4:5], in_=mean_sc[128:129].unsqueeze(0).broadcast_to([128, 1])), "mb")
        mb = DG(lambda e, c=swcol: e.dma_start(out=ccol[:, c:c + 1], in_=mean_sc[129:130].unsqueeze(0).broadcast_to([128, 1])), "mb")
        P.wait("vector", "mb", mb)
        vv = Vd(lambda e, c=c1col: e.tensor_scalar(ccol[:, c:c + 1], stat[:, 4:5], 1.0 / 127.0, None, ALU.mult))
        wq_state[("p1", tag)] = (vv, P.cnt[sem], P.cnt["mb"])

    def wq_pass2(w_d, ntiles, tilecols, swcol, base_tile, tag, gate_t):
        nch = ntiles * tilecols // 1024
        sem = "p2a" if tag == 0 else "p2b"
        qs = {0: 0, 1: 0}
        vts = {0: 0, 1: 0}
        stw = {0: 0, 1: 0}
        vv, _, mbtot = wq_state[("p1", tag)]
        P.wait("scalar", "v", vv)
        P.wait("scalar", "mb", mbtot)
        P.wait("gpsimd", "t", gate_t)
        for i in range(nch):
            sl = i % 2
            if i >= 1:
                P.wait("gpsimd", "s", qs[0])
            r, cc = (i * 1024) // tilecols, (i * 1024) % tilecols
            wt = DG(lambda e, r=r, cc=cc: e.dma_start(
                out=aqld[:, 0:1024], in_=w_d[r * 128:(r + 1) * 128, cc:cc + 1024]), sem)
            P.wait("scalar", sem, wt)
            if i >= 1:
                P.wait("scalar", "v", vts[1 - sl])
            S(lambda e, c=swcol: e.activation(
                aqld[:, 1024:2048], aqld[:, 0:1024], AF.Identity,
                scale=ccol[:, c:c + 1], bias=ccol[:, C_MAG:C_MAG + 1]))
            qs[0] = S(lambda e: e.activation(aqld[:, 1024:2048], aqld[:, 1024:2048], AF.Identity, bias=ccol[:, C_NMAG:C_NMAG + 1]))
            P.wait("vector", "s", qs[0])
            P.wait("vector", "sg0", stw[sl])
            vts[sl] = V(lambda e, sl=sl: e.tensor_scalar(
                aqtr[:, sl * 1024:sl * 1024 + 1024], aqld[:, 1024:2048], 1.0, -1.0, ALU.min, ALU.max))
            P.wait("gpsimd", "v", vts[sl])
            stw[sl] = DG(lambda e, i=i, sl=sl, bt=base_tile: e.dma_start(
                out=trit_sc[bt + i], in_=aqtr[:, sl * 1024:sl * 1024 + 1024]), "sg0")

    def wq_restream():
        trT = {0: 0, 1: 0}
        trev = {0: 0, 1: 0}
        trst = {0: 0, 1: 0}
        P.wait("sync", "sg0", P.cnt["sg0"])
        for i in range(64):
            pp = i % 2
            tb = aqtr[:, pp * 1024:pp * 1024 + 1024]
            sg = aqsg[:, pp * 1024:pp * 1024 + 1024]
            P.wait("sync", "t", trT[pp])
            P.wait("sync", "v", P.cnt["v"] if i < 2 else 0)
            wd = DW(lambda e, i=i, tb=tb: e.dma_start(out=tb, in_=trit_sc[i]), "tr%d" % pp)
            P.wait("tensor", "tr%d" % pp, wd)
            P.wait("tensor", "v", trev[pp])
            pgt = pg0 if pp == 0 else pg1
            for dc in range(ND):
                tv = T(lambda e, dc=dc, tb=tb, pgt=pgt: e.matmul(
                    pgt[:, dc * 128:dc * 128 + 128], tb[:, dc * 128:(dc + 1) * 128],
                    ident_bf[:], start=True, stop=True))
            trT[pp] = tv
            P.wait("vector", "t", tv)
            P.wait("vector", "sg1", trst[pp])
            trev[pp] = V(lambda e, sg=sg, pgt=pgt: e.tensor_copy(sg, pgt[:]))
            P.wait("sync", "v", trev[pp])
            if i < 32:
                trst[pp] = DW(lambda e, i=i, sg=sg: e.dma_start(out=w1q_sc[i], in_=sg), "sg1")
            else:
                dt, q = (i - 32) // 4, (i - 32) % 4
                for j in range(8):
                    trst[pp] = DW(lambda e, dt=dt, q=q, j=j, sg=sg: e.dma_start(
                        out=w2q_sc[q * 8 + j, :, dt * 128:(dt + 1) * 128],
                        in_=sg[:, j * 128:(j + 1) * 128]), "sg1")

    wout_cast_done = P.cnt["v"]
    fence()

    for b in range(NB if UPTO >= 2 else 0):
        P.op("scalar", lambda e: e.preload_act_table(AF.Sqrt))
        lastS = {0: 0, 1: 0}
        lastV = {0: 0, 1: 0}
        evT = {0: 0, 1: 0}
        for st in range(NT):
            pp = st % 2
            P.wait("gpsimd", "s", lastS[pp])
            P.wait("gpsimd", "v", lastV[pp])
            xd = DG(lambda e, b=b, st=st, pp=pp: e.dma_start(
                out=xbuf[:, pp * 1024:pp * 1024 + 1024], in_=x_d[b, st * 128:(st + 1) * 128, :]), "gx%d" % pp)
            xs = xbuf[:, pp * 1024:pp * 1024 + 1024]
            sc = stat[:, pp * 8: pp * 8 + 8]
            vprev = P.cnt["v"]
            P.wait("vector", "gx%d" % pp, xd)
            lastV[pp] = Vd(lambda e, xs=xs, sc=sc: e.reduce_sum(sc[:, 0:1], xs, AX.X))
            P.wait("scalar", "gx%d" % pp, xd)
            P.wait("scalar", "v", vprev)
            S(lambda e, xs=xs, sc=sc: e.activation(xc[:], xs, AF.Square, accum_out=sc[:, 1:2]))
            P.wait("vector", "s", P.cnt["s"])
            Vd(lambda e, sc=sc: e.tensor_scalar(sc[:, 2:3], sc[:, 0:1], -1.0 / 1024.0, None, ALU.mult))
            Vd(lambda e, sc=sc: e.tensor_tensor(sc[:, 3:4], sc[:, 2:3], sc[:, 2:3], ALU.mult))
            vv = Vd(lambda e, sc=sc: e.scalar_tensor_tensor(sc[:, 4:5], sc[:, 1:2], 1.0 / 1024.0, sc[:, 3:4], ALU.mult, ALU.subtract))
            P.wait("scalar", "v", vv)
            sv = S(lambda e, sc=sc: e.activation(sc[:, 4:5], sc[:, 4:5], AF.Sqrt, bias=ccol[:, C_EPS:C_EPS + 1]))
            P.wait("vector", "s", sv)
            Vd(lambda e, sc=sc: e.reciprocal(sc[:, 5:6], sc[:, 4:5]))
            sv = S(lambda e, xs=xs, sc=sc: e.activation(xc[:], xs, AF.Identity, bias=sc[:, 2:3]))
            lastS[pp] = sv
            P.wait("vector", "s", sv)
            Vd(lambda e, sc=sc: e.reduce_max(sc[:, 6:7], xc[:], AX.X, apply_absolute_value=True))
            Vd(lambda e, sc=sc: e.tensor_tensor(sc[:, 6:7], sc[:, 6:7], sc[:, 5:6], ALU.mult))
            Vd(lambda e, sc=sc: e.tensor_scalar(sc[:, 6:7], sc[:, 6:7], 1e-5, None, ALU.max))
            Vd(lambda e, sc=sc: e.reciprocal(sc[:, 7:8], sc[:, 6:7]))
            Vd(lambda e, sc=sc: e.tensor_scalar(sc[:, 7:8], sc[:, 7:8], 127.0, None, ALU.mult))
            Vd(lambda e, sc=sc: e.tensor_tensor(sc[:, 7:8], sc[:, 7:8], sc[:, 5:6], ALU.mult))
            vv = Vd(lambda e, sc=sc, st=st: e.tensor_copy(gst[:, st:st + 1], sc[:, 6:7]))
            P.wait("scalar", "v", vv)
            P.wait("scalar", "t", evTT if st >= 1 else 0)
            S(lambda e, sc=sc: e.activation(xn_t[:], xc[:], AF.Identity, scale=sc[:, 5:6]))
            S(lambda e, sc=sc: e.activation(xt1[:], xc[:], AF.Identity, scale=sc[:, 7:8], bias=ccol[:, C_MAG:C_MAG + 1]))
            sv = S(lambda e: e.activation(xi_t[:], xt1[:], AF.Identity, bias=ccol[:, C_NMAG:C_NMAG + 1]))
            lastS[pp] = sv
            P.wait("tensor", "s", sv)
            P.wait("tensor", "v", evT[0])
            for dc in range(ND):
                T(lambda e, dc=dc: e.matmul(pg0[:, dc * 128:dc * 128 + 128],
                                            xn_t[:, dc * 128:(dc + 1) * 128], ident_bf[:], start=True, stop=True))
            tv0 = P.cnt["t"]
            P.wait("tensor", "v", evT[1])
            for dc in range(ND):
                T(lambda e, dc=dc: e.matmul(pg1[:, dc * 128:dc * 128 + 128],
                                            xi_t[:, dc * 128:(dc + 1) * 128], ident_bf[:], start=True, stop=True))
            evTT = P.cnt["t"]
            P.wait("vector", "t", tv0)
            evT[0] = V(lambda e, st=st: e.tensor_copy(
                xnT[:].rearrange("p (dc t) -> p dc t", dc=8)[:, :, st * 128:st * 128 + 128],
                pg0[:].rearrange("p (dc c) -> p dc c", dc=8)))
            P.wait("vector", "t", evTT)
            evT[1] = V(lambda e, st=st: e.tensor_copy(
                xiT[:].rearrange("p (dc t) -> p dc t", dc=8)[:, :, st * 128:st * 128 + 128],
                pg1[:].rearrange("p (dc c) -> p dc c", dc=8)))
            lastV[pp] = evT[1]

        if debug and b == 0:
            fence()
            DG(lambda e: e.dma_start(out=dbg["d_xnT"], in_=xnT[:]), "gd")
            DG(lambda e: e.dma_start(out=dbg["d_xiT"], in_=xiT[:]), "gd")
            fence()
        if b == 0:
            P.wait("gpsimd", "v", P.cnt["v"])
            P.wait("gpsimd", "s", P.cnt["s"])
            wq_pass1(w1_d, 32, 1024, C_SW1, C_C1, 0)
            wq_pass1(w2_d, 8, 4096, C_SW2, C_C2, 1)
            wq_state["p1_s"] = P.cnt["s"]
            wq_state["p1a"] = P.cnt["p1a"]
            wq_state["p1b"] = P.cnt["p1b"]

        if UPTO < 3:
            continue
        P.wait("tensor", "v", P.cnt["v"] if False else lastV[0] if False else max(evT[0], evT[1]))
        P.wait("sync", "v", max(evT[0], evT[1]))
        P.wait("sync", "s", P.cnt["s"])
        for e_ in ("scalar", "vector", "tensor"):
            P.wait(e_, "sg0", P.cnt["sg0"])
            P.wait(e_, "sg1", P.cnt["sg1"])
            P.wait(e_, "wr", P.cnt["wr"])
        lastT_wqk = {0: 0, 1: 0}
        lastS_ps = {0: 0, 1: 0}
        for ct in range(16):
            pp = ct % 2
            P.wait("sync", "t", lastT_wqk[pp])
            wd = DW(lambda e, ct=ct, pp=pp: e.dma_start(out=wqkb[:, pp * 1024:pp * 1024 + 1024], in_=wqk_sc[ct]), "pw%d" % pp)
            P.wait("tensor", "pw%d" % pp, wd)
            for ts in range(4):
                qq = (ct * 4 + ts) % 2
                P.wait("tensor", "s", lastS_ps[qq])
                for dc in range(ND):
                    tv = T(lambda e, pp=pp, dc=dc, ts=ts, qq=qq: e.matmul(
                        pg2[:, qq * 512:qq * 512 + 512],
                        wqkb[:, pp * 1024 + dc * 128: pp * 1024 + dc * 128 + 128],
                        xnT[:, dc * 2048 + ts * 512: dc * 2048 + ts * 512 + 512],
                        start=(dc == 0), stop=(dc == ND - 1)))
                lastT_wqk[pp] = tv
                P.wait("scalar", "t", tv)
                lastS_ps[qq] = S(lambda e, ct=ct, ts=ts, qq=qq: e.activation(
                    qkT[:, ct * 2048 + ts * 512: ct * 2048 + ts * 512 + 512],
                    pg2[:, qq * 512:qq * 512 + 512], AF.Identity, bias=bcol[:, ct:ct + 1]))
        P.wait("tensor", "s", max(lastS_ps[0], lastS_ps[1]))
        P.wait("sync", "s", wq_state.get("p1_s", 0))
        P.wait("sync", "p1a", wq_state.get("p1a", 0))
        P.wait("sync", "p1b", wq_state.get("p1b", 0))
        lastV_ps = {0: 0, 1: 0}
        lastT_wv = 0
        for hf in range(2):
            P.wait("sync", "t", lastT_wv)
            for dc in range(ND):
                wd = DW(lambda e, hf=hf, dc=dc: e.dma_start(out=wvb[:, dc * 640:dc * 640 + 640], in_=wv_sc[hf, dc]), "gv")
            P.wait("tensor", "gv", wd)
            for st in range(NT):
                qq = st % 2
                pgv = pg3 if qq == 0 else pg2
                P.wait("tensor", "v", lastV_ps[qq])
                for dc in range(ND):
                    T(lambda e, dc=dc, st=st, pgv=pgv: e.matmul(
                        pgv[:, 0:512],
                        xnT[:, dc * 2048 + st * 128: dc * 2048 + st * 128 + 128],
                        wvb[:, dc * 640: dc * 640 + 512],
                        start=(dc == 0), stop=False))
                    T(lambda e, dc=dc, st=st, pgv=pgv: e.matmul(
                        pgv[:, 512:640],
                        xnT[:, dc * 2048 + st * 128: dc * 2048 + st * 128 + 128],
                        wvb[:, dc * 640 + 512: dc * 640 + 640],
                        start=(dc == 0), stop=False))
                T(lambda e, hf=hf, pgv=pgv: e.matmul(pgv[:, 0:512], ones_bf[0:1, 0:128],
                                                     bvrow[0:1, hf * 640: hf * 640 + 512], start=False, stop=True))
                tv = T(lambda e, hf=hf, pgv=pgv: e.matmul(pgv[:, 512:640], ones_bf[0:1, 0:128],
                                                          bvrow[0:1, hf * 640 + 512: hf * 640 + 640], start=False, stop=True))
                lastT_wv = tv
                P.wait("vector", "t", tv)
                base = st * 1280 + hf * 640
                lastV_ps[qq] = V(lambda e, base=base, pgv=pgv: e.tensor_copy(
                    v_sb[:, base: base + 640], pgv[:, 0:640]))
        qkv_s = P.cnt["s"]
        qkv_v = P.cnt["v"]
        if b == 0:
            wq_pass2(w1_d, 32, 1024, C_SW1, 0, 0, lastT_wv)
            wq_pass2(w2_d, 8, 4096, C_SW2, 32, 1, lastT_wv)
            P.wait("tensor", "s", qkv_s)
            P.wait("tensor", "v", qkv_v)
            for q_ in ("sync", "gpsimd"):
                P.wait(q_, "t", P.cnt["t"])
        else:
            fence()
        if debug and b == 0:
            DG(lambda e: e.dma_start(out=dbg["d_qkT"], in_=qkT[:]), "gd")
            DG(lambda e: e.dma_start(out=dbg["d_vsb"], in_=v_sb[:]), "gd")
            fence()

        if UPTO < 4:
            continue
        P.op("scalar", lambda e: e.preload_act_table(AF.Exp))
        sc_pair = [pg0, pg1]
        ctx_pair = [pg2, pg3]
        exp_s = {0: 0, 1: 0}
        avT = {0: 0, 1: 0}
        evac_v = {0: 0, 1: 0}
        nbw = {0: 0, 1: 0}
        def attn_tail(pi, cps, qc, pr, avlast):
            P.wait("vector", "t", avlast)
            P.wait("vector", "nb%d" % pi, nbw[pi])
            V(lambda e, cps=cps, pi=pi: e.reciprocal(rows_sb[64:65, pi * 512:pi * 512 + 512], cps[64:65, 0:512]))
            rv = V(lambda e, cps=cps, pi=pi: e.reciprocal(rows_sb[32:33, pi * 512:pi * 512 + 512], cps[32:33, 512:1024]))
            P.wait("sync", "v", rv)
            DW(lambda e, pi=pi: e.dma_start(out=den_sc[pi], in_=rows_sb[32:96:32, pi * 512:pi * 512 + 512]), "nb%d" % pi)
            P.wait("sync", "nb%d" % pi, P.cnt["nb%d" % pi])
            DW(lambda e, pi=pi: e.dma_start(out=nrm[0:64, pi * 512:pi * 512 + 512],
                                            in_=den_sc[pi, 1].unsqueeze(0).broadcast_to([64, 512])), "nb%d" % pi)
            nbw[pi] = DW(lambda e, pi=pi: e.dma_start(out=nrm[64:128, pi * 512:pi * 512 + 512],
                                                      in_=den_sc[pi, 0].unsqueeze(0).broadcast_to([64, 512])), "nb%d" % pi)
            P.wait("vector", "nb%d" % pi, nbw[pi])
            dst = pr * 2048 + qc * 512
            V(lambda e, cps=cps, dst=dst, pi=pi: e.tensor_tensor(
                ctx_sb[0:64, dst:dst + 512], cps[0:64, 0:512], nrm[0:64, pi * 512:pi * 512 + 512], ALU.mult))
            evac_v[pi] = V(lambda e, cps=cps, dst=dst, pi=pi: e.tensor_tensor(
                ctx_sb[64:128, dst:dst + 512], cps[64:128, 512:1024], nrm[64:128, pi * 512:pi * 512 + 512], ALU.mult))

        def attn_av(info):
            cps, kk, off, vb0, st0, sp, pi = (info["cps"], info["kk"], info["off"],
                                              info["vb0"], info["st0"], info["sp"], info["pi"])
            P.wait("tensor", info["gate"][0], info["gate"][1])
            if st0:
                P.wait("tensor", "v", evac_v[pi])
            T(lambda e, cps=cps, vb0=vb0, kk=kk, off=off, st0=st0, sp=sp: e.matmul(
                cps[:, off:512], v_sb[:, vb0:vb0 + 128], p_sb[:, kk * 1024 + off:kk * 1024 + 512],
                start=st0, stop=sp))
            avT[kk] = T(lambda e, cps=cps, vb0=vb0, kk=kk, off=off, st0=st0, sp=sp: e.matmul(
                cps[:, 512 + off:1024], v_sb[:, vb0 + 32:vb0 + 160], p_sb[:, kk * 1024 + 512 + off:kk * 1024 + 1024],
                start=st0, stop=sp))
            if sp:
                attn_tail(pi, cps, info["qc"], info["pr"], avT[kk])

        steps = [(qc, pr, kt, 4 * qc + 4) for qc in range(4) for pr in range(8) for kt in range(4 * qc + 4)]
        pending = None
        for kap, (qc, pr, kt, nkt) in enumerate(steps):
            pi = (qc * 8 + pr) % 2
            cps = ctx_pair[pi]
            kk = kap % 2
            scp = sc_pair[kk]
            diag = kt >= 4 * qc
            off = (kt - 4 * qc) * 128 if diag else 0
            kbase = (8 + pr) * 2048 + kt * 128
            qbase = pr * 2048 + qc * 512
            P.wait("tensor", "s", exp_s[kk])
            T(lambda e, scp=scp, kbase=kbase, qbase=qbase, off=off: e.matmul(
                scp[:, off:512], qkT[0:64, kbase:kbase + 128], qkT[0:64, qbase + off:qbase + 512],
                start=True, stop=True, tile_position=(0, 0)))
            tv = T(lambda e, scp=scp, kbase=kbase, qbase=qbase, off=off: e.matmul(
                scp[:, 512 + off:1024], qkT[64:128, kbase:kbase + 128], qkT[64:128, qbase + off:qbase + 512],
                start=True, stop=True, tile_position=(64, 0)))
            P.wait("scalar", "t", tv)
            P.wait("scalar", "t", avT[kk])
            exp_s[kk] = S(lambda e, scp=scp, kk=kk, off=off: e.activation(
                p_sb[:, kk * 1024:kk * 1024 + 1024].rearrange("p (h c) -> p h c", h=2)[:, :, off:512],
                scp[:].rearrange("p (h c) -> p h c", h=2)[:, :, off:512],
                AF.Exp, scale=0.125))
            if diag:
                P.wait("vector", "s", exp_s[kk])
                mv = V(lambda e, kk=kk, off=off: e.tensor_tensor(
                    p_sb[:, kk * 1024:kk * 1024 + 1024].rearrange("p (h c) -> p h c", h=2)[:, :, off:off + 128],
                    p_sb[:, kk * 1024:kk * 1024 + 1024].rearrange("p (h c) -> p h c", h=2)[:, :, off:off + 128],
                    tri_bf[:].unsqueeze(1).broadcast_to([128, 2, 128]), ALU.mult))
                gate = ("v", mv)
            else:
                gate = ("s", exp_s[kk])
            info = dict(cps=cps, kk=kk, off=off, vb0=kt * 1280 + pr * 160,
                        st0=(kt == 0), sp=(kt == nkt - 1), pi=pi, qc=qc, pr=pr, gate=gate)
            if pending is not None:
                attn_av(pending)
            pending = info
        attn_av(pending)
        fence()
        if b == 0:
            wq_restream()
        if debug and b == 0:
            DG(lambda e: e.dma_start(out=dbg["d_ctx"], in_=ctx_sb[:]), "gd")
            fence()

        if UPTO < 5:
            continue
        for dc in range(ND):
            wd = DW(lambda e, dc=dc: e.dma_start(out=wo_res[:, dc * 1024:dc * 1024 + 1024], in_=wo_sc[dc]), "gv")
        P.wait("tensor", "gv", wd)
        addv = {0: 0, 1: 0}
        accw = {0: 0, 1: 0}
        for st in range(NT):
            pp = st % 2
            pg = pg0 if pp == 0 else pg1
            P.wait("gpsimd", "v", addv[pp])
            xd = DG(lambda e, b=b, st=st, pp=pp: e.dma_start(
                out=xrb[:, pp * 1024:pp * 1024 + 1024], in_=x_d[b, st * 128:(st + 1) * 128, :]), "gx%d" % pp)
            P.wait("tensor", "v", addv[pp])
            for half in range(2):
                for dc in range(ND):
                    T(lambda e, pg=pg, half=half, dc=dc, st=st: e.matmul(
                        pg[:, half * 512:half * 512 + 512],
                        ctx_sb[:, dc * 2048 + st * 128: dc * 2048 + st * 128 + 128],
                        wo_res[:, dc * 1024 + half * 512: dc * 1024 + half * 512 + 512],
                        start=(dc == 0), stop=False))
                tv = T(lambda e, pg=pg, half=half: e.matmul(
                    pg[:, half * 512:half * 512 + 512], ones_bf[0:1, 0:128],
                    borow[0:1, half * 512:half * 512 + 512], start=False, stop=True))
            P.wait("vector", "t", tv)
            P.wait("vector", "gx%d" % pp, xd)
            P.wait("vector", "ga%d" % pp, accw[pp])
            addv[pp] = V(lambda e, pg=pg, pp=pp: e.tensor_tensor(
                acst[:, pp * 1024:pp * 1024 + 1024], pg[:], xrb[:, pp * 1024:pp * 1024 + 1024], ALU.add))
            P.wait("gpsimd", "v", addv[pp])
            accw[pp] = DG(lambda e, b=b, st=st, pp=pp: e.dma_start(
                out=acc_sc[b, st], in_=acst[:, pp * 1024:pp * 1024 + 1024]), "ga%d" % pp)
        fence()
        if debug and b == 0:
            P.wait("gpsimd", "ga0", P.cnt["ga0"])
            P.wait("gpsimd", "ga1", P.cnt["ga1"])
            for st in range(NT):
                DG(lambda e, st=st: e.dma_start(out=dbg["d_acc"][st], in_=acc_sc[0, st]), "gd")
            fence()

        if UPTO < 6:
            continue
        P.op("scalar", lambda e: e.preload_act_table(AF.Silu))
        vv = Vd(lambda e: e.tensor_scalar(stat[:], gst[:], ccol[:, C_C1:C_C1 + 1], None, ALU.mult))
        P.wait("tensor", "v", vv)
        tv = T(lambda e: e.matmul(pg1[0:16, 0:128], stat[:], ident_f[:], start=True, stop=True))
        P.wait("vector", "t", tv)
        vv = V(lambda e: e.tensor_copy(nrm[0:16, 0:128], pg1[0:16, 0:128]))
        P.wait("gpsimd", "v", vv)
        gw = DG(lambda e: e.dma_start(out=row_sc[0:2048].rearrange("(r c) -> r c", r=16), in_=nrm[0:16, 0:128]), "gr")
        P.wait("gpsimd", "gr", gw)
        gw = DG(lambda e: e.dma_start(out=g_row[0:1, :], in_=row_sc[0:2048].unsqueeze(0)), "gr")
        P.wait("tensor", "gr", gw)
        gcp = {}
        for i in range(4):
            if i >= 2:
                P.wait("tensor", "v", gcp[i - 2])
            tv = T(lambda e, i=i: e.matmul(pg1[:, (i % 2) * 512:(i % 2) * 512 + 512],
                                           ones_f[0:1, 0:128], g_row[0:1, i * 512:i * 512 + 512],
                                           start=True, stop=True))
            P.wait("vector", "t", tv)
            gcp[i] = V(lambda e, i=i: e.tensor_copy(g_sb[:, i * 512:i * 512 + 512], pg1[:, (i % 2) * 512:(i % 2) * 512 + 512]))
        stash0 = V(lambda e: e.memset(stash[:], 0.0))
        if debug and b == 0:
            P.wait("gpsimd", "v", P.cnt["v"])
            DG(lambda e: e.dma_start(out=dbg["d_g"], in_=g_sb[:]), "gd")

        pmT_t = [0]
        w1T = {0: 0, 1: 0}
        w2T = {0: 0, 1: 0}
        w1ev = {0: 0, 1: 0, 2: 0, 3: 0}
        w2ev = {0: 0, 1: 0}
        w2last = {0: 0, 1: 0}
        uV = {k: stash0 for k in range(8)}
        uSS = {k: 0 for k in range(8)}
        pg1free_s = [0]
        pg1free_v = [0]
        pg1w = [-1]
        roundV_w = {-1: 0}
        roundS_w = {-1: 0}
        silu31 = {}
        accv = {0: 0, 1: 0}
        outw = {0: 0, 1: 0}
        sb_g = 0

        ub4 = [ubuf[0], ubuf[1], xc[:, 0:388], xc[:, 388:776]]

        def w1conv(c):
            have = c <= 7
            nonlocal_first = [True]
            for grp in range(8):
                hcs = [grp * 4 + j for j in range(4)]
                if have:
                    ppw = grp % 2
                    P.wait("sync", "t", w1T[ppw])
                    wd = DW(lambda e, grp=grp, ppw=ppw: e.dma_start(
                        out=w1b[:, ppw * 4096:ppw * 4096 + 4096].rearrange("p (h q) -> p h q", h=4),
                        in_=w1q_sc[grp * 4:grp * 4 + 4].rearrange("h p q -> p h q")), "pw%d" % ppw)
                    P.wait("tensor", "pw%d" % ppw, wd)
                for hc in hcs:
                    sl = hc % 4
                    P.wait("scalar", "v", uV[sl])
                    S(lambda e, sl=sl, hc=hc: e.copy(ub4[sl][:, 0:129], stash[:, hc * 129:hc * 129 + 129]))
                evc = {}
                for hc in hcs:
                    sl = hc % 4
                    if have:
                        pw = hc % 2
                        P.wait("tensor", "v", w1ev[pw])
                        for dc in range(ND):
                            tmm = T(lambda e, j=hc % 4, dc=dc, c=c, pw=pw, ppw=(hc // 4) % 2: e.matmul(
                                pg0[:, pw * 512:pw * 512 + 256],
                                w1b[:, ppw * 4096 + j * 1024 + dc * 128: ppw * 4096 + j * 1024 + dc * 128 + 128],
                                xiT[:, dc * 2048 + c * 256: dc * 2048 + c * 256 + 256],
                                start=(dc == 0), stop=(dc == ND - 1)))
                        w1T[(hc // 4) % 2] = tmm
                        P.wait("vector", "t", tmm)
                        P.wait("vector", "s", uSS[sl])
                        w1ev[pw] = V(lambda e, sl=sl, pw=pw, c=c: e.tensor_tensor(
                            ub4[sl][:, 129:385], pg0[:, pw * 512:pw * 512 + 256],
                            g_sb[:, c * 256:c * 256 + 256], ALU.mult))
                        evc[hc] = w1ev[pw]
                    else:
                        P.wait("vector", "s", uSS[sl])
                        evc[hc] = V(lambda e, sl=sl: e.memset(ub4[sl][:, 129:388], 0.0))
                inc_ = {}
                for hc in hcs:
                    sl = hc % 4
                    P.wait("scalar", "v", evc[hc])
                    if nonlocal_first[0]:
                        P.wait("scalar", "v", roundV_w[c - 1])
                        nonlocal_first[0] = False
                    inc_[hc] = S(lambda e, sl=sl, hc=hc: e.activation(
                        h2w[:, hc * 256:hc * 256 + 256], ub4[sl][:, 0:256], AF.Identity,
                        scale=cwsb[:, hc * 3:hc * 3 + 1], bias=cbsb[:, hc:hc + 1]))
                    if have:
                        uSS[sl] = S(lambda e, sl=sl, hc=hc: e.copy(
                            stash[:, hc * 129:hc * 129 + 129], ub4[sl][:, 256:385]))
                    else:
                        uSS[sl] = inc_[hc]
                for hc in hcs:
                    sl = hc % 4
                    P.wait("vector", "s", inc_[hc])
                    V(lambda e, sl=sl, hc=hc: e.scalar_tensor_tensor(
                        h2w[:, hc * 256:hc * 256 + 256], ub4[sl][:, 1:257], cwsb[:, hc * 3 + 1:hc * 3 + 2],
                        h2w[:, hc * 256:hc * 256 + 256], ALU.mult, ALU.add))
                    uV[sl] = V(lambda e, sl=sl, hc=hc: e.scalar_tensor_tensor(
                        h2w[:, hc * 256:hc * 256 + 256], ub4[sl][:, 2:258], cwsb[:, hc * 3 + 2:hc * 3 + 3],
                        h2w[:, hc * 256:hc * 256 + 256], ALU.mult, ALU.add))
                P.wait("scalar", "v", uV[hcs[-1] % 4])
                sl_ = S(lambda e, grp=grp: e.activation(
                    h2w[:, grp * 1024:grp * 1024 + 1024], h2w[:, grp * 1024:grp * 1024 + 1024], AF.Silu))
            silu31[c] = sl_

        def quantscale(cq):
            P.wait("vector", "s", silu31[cq])
            P.wait("vector", "t", pmT_t[0])
            pmv = V(lambda e: e.tensor_reduce(
                pmb[:, 0:256], h2w[:].rearrange("p (h t) -> p t h", h=32),
                AX.X, ALU.max, apply_absolute_value=True))
            P.wait("tensor", "v", pmv)
            P.wait("tensor", "s", P.cnt["s"])
            T(lambda e: e.matmul(pg1[:, 0:128], pmb[:, 0:128], ident_f[:], start=True, stop=True))
            tv = T(lambda e: e.matmul(pg1[:, 128:256], pmb[:, 128:256], ident_f[:], start=True, stop=True))
            pmT_t[0] = tv
            P.wait("vector", "t", tv)
            Vd(lambda e: e.tensor_reduce(
                mx[:, 0:2], pg1[:, 0:256].rearrange("p (w c) -> p w c", w=2), AX.X, ALU.max,
                apply_absolute_value=False))
            Vd(lambda e: e.tensor_scalar(mx[:, 2:4], mx[:, 0:2], 1e-5, None, ALU.max))
            Vd(lambda e: e.tensor_scalar(mx[:, 4:6], mx[:, 2:4], ccol[:, C_C2:C_C2 + 1], None, ALU.mult))
            Vd(lambda e: e.reciprocal(mx[:, 6:8], mx[:, 2:4]))
            vvs = Vd(lambda e: e.tensor_scalar(mx[:, 6:8], mx[:, 6:8], 127.0, None, ALU.mult))
            P.wait("tensor", "v", vvs)
            tv = T(lambda e: e.matmul(pg1[0:2, 256:384], mx[:, 6:8], ident_f[:], start=True, stop=True))
            P.wait("vector", "t", tv)
            vv = V(lambda e: e.tensor_copy(nrm[0:2, 0:128], pg1[0:2, 256:384]))
            pg1free_v[0] = vv
            P.wait("gpsimd", "v", vv)
            P.wait("gpsimd", "gr", P.cnt["gr"])
            sg = DG(lambda e: e.dma_start(out=row_sc[2048:2304].rearrange("(r c) -> r c", r=2), in_=nrm[0:2, 0:128]), "gr")
            P.wait("gpsimd", "gr", sg)
            sg = DG(lambda e: e.dma_start(out=sbrow[0:1, :], in_=row_sc[2048:2304].unsqueeze(0)), "gr")
            P.wait("tensor", "gr", sg)
            tso = T(lambda e: e.matmul(pg1[:, 512:768], ones_f[0:1, 0:128], sbrow[0:1, 0:256], start=True, stop=True))
            P.wait("scalar", "t", tso)
            sbs = S(lambda e: e.copy(sb_sb[:], pg1[:, 512:768]))
            pg1free_s[0] = sbs
            P.wait("vector", "s", sbs)
            qm = V(lambda e: e.tensor_tensor(
                h2w[:].rearrange("p (h t) -> p h t", h=32),
                h2w[:].rearrange("p (h t) -> p h t", h=32),
                sb_sb[:].unsqueeze(1).broadcast_to([128, 32, 256]), ALU.mult))
            P.wait("vector", "t", max(w2T[0], w2T[1]))
            P.wait("scalar", "t", max(w2T[0], w2T[1]))
            P.wait("scalar", "v", qm)
            roundV_w[cq] = V(lambda e: e.tensor_scalar(h2q[:, 0:4096], h2w[:, 0:4096], MAGIC, -MAGIC, ALU.add, ALU.add))
            for k in range(4):
                S(lambda e, k=k: e.activation(xbuf[:, 0:1024], h2w[:, 4096 + k * 1024:4096 + k * 1024 + 1024],
                                              AF.Identity, bias=ccol[:, C_MAG:C_MAG + 1]))
                rS = S(lambda e, k=k: e.activation(h2q[:, 4096 + k * 1024:4096 + k * 1024 + 1024], xbuf[:, 0:1024],
                                                   AF.Identity, bias=ccol[:, C_NMAG:C_NMAG + 1]))
            roundS_w[cq] = rS

        def w2block(cq, b=b):
            stl = [t_ for t_ in (2 * cq - 1, 2 * cq) if 0 <= t_ < 16]
            accd = {}
            for st_out in stl:
                par = st_out % 2
                P.wait("gpsimd", "v", accv[par])
                accd[st_out] = DG(lambda e, b=b, st_out=st_out, par=par: e.dma_start(
                    out=accb[:, par * 1024:par * 1024 + 1024], in_=acc_sc[b, st_out]), "ga%d" % par)
            for grp in range(8):
                pp2 = grp % 2
                P.wait("sync", "t", w2T[pp2])
                wd2 = DW(lambda e, grp=grp, pp2=pp2: e.dma_start(
                    out=w2b[:, pp2 * 4096:pp2 * 4096 + 4096].rearrange("p (h q) -> p h q", h=4),
                    in_=w2q_sc[grp * 4:grp * 4 + 4].rearrange("h p q -> p h q")), "qw%d" % pp2)
                P.wait("tensor", "qw%d" % pp2, wd2)
                if grp == 0:
                    P.wait("tensor", "v", roundV_w[cq])
                    P.wait("tensor", "s", roundS_w[cq])
                for j in range(4):
                    hc = grp * 4 + j
                    for st_out in stl:
                        w = st_out - (2 * cq - 1)
                        par = st_out % 2
                        pgw = pg2 if par == 0 else pg3
                        if hc == 0:
                            P.wait("tensor", "v", w2ev[par])
                        for half in range(2):
                            tmm = T(lambda e, pgw=pgw, hc=hc, w=w, pp2=pp2, j=j, half=half: e.matmul(
                                pgw[:, half * 512:half * 512 + 512],
                                h2q[:, hc * 256 + w * 128: hc * 256 + w * 128 + 128],
                                w2b[:, pp2 * 4096 + j * 1024 + half * 512: pp2 * 4096 + j * 1024 + half * 512 + 512],
                                start=(hc == 0), stop=(hc == NHC - 1)))
                        w2T[pp2] = tmm
                        if hc == NHC - 1:
                            w2last[par] = tmm
            for st_out in stl:
                w = st_out - (2 * cq - 1)
                par = st_out % 2
                pgw = pg2 if par == 0 else pg3
                P.wait("vector", "ga%d" % par, accd[st_out])
                P.wait("vector", "t", w2last[par])
                P.wait("vector", "go%d" % par, outw[par])
                accv[par] = V(lambda e, pgw=pgw, par=par, w=w: e.scalar_tensor_tensor(
                    outb[:, par * 1024:par * 1024 + 1024], pgw[:], mx[:, 4 + w:5 + w],
                    accb[:, par * 1024:par * 1024 + 1024], ALU.mult, ALU.add))
                w2ev[par] = accv[par]
                P.wait("gpsimd", "v", accv[par])
                outw[par] = DG(lambda e, b=b, st_out=st_out, par=par: e.dma_start(
                    out=out_d[b, st_out * 128:(st_out + 1) * 128, :], in_=outb[:, par * 1024:par * 1024 + 1024]), "go%d" % par)

        for c in range(FFNW):
            if 1 <= c:
                quantscale(c - 1)
            if c <= 8:
                w1conv(c)
            if 1 <= c:
                w2block(c - 1)
        fence()

    for s_ in list(P.cnt.keys()):
        if s_ not in ("t", "s", "v") and P.cnt[s_] > 0:
            P.wait("gpsimd", s_, P.cnt[s_])

    from contextlib import ExitStack
    with ExitStack() as stack:
        sems = {}
        for s_ in list(P.cnt.keys()):
            if P.cnt[s_] > 0:
                sems[s_] = stack.enter_context(nc.semaphore("sem_" + s_))
        block = stack.enter_context(nc.Block())

        @block.sync
        def _(sync):
            P.replay("sync", sync, sems)

        @block.gpsimd
        def _(gpsimd):
            P.replay("gpsimd", gpsimd, sems)

        @block.tensor
        def _(tensor):
            P.replay("tensor", tensor, sems)

        @block.scalar
        def _(scalar):
            P.replay("scalar", scalar, sems)

        @block.vector
        def _(vector):
            P.replay("vector", vector, sems)

    return nc


_NC_CACHE = {}


def _get_nc(debug=False):
    key = (debug, UPTO, FFNW, W1MODE)
    if key not in _NC_CACHE:
        _NC_CACHE[key] = build(debug)
    return _NC_CACHE[key]


def kernel(L0_attn_hidden_states, L0_attn_W_attn, L0_attn_b_attn, L0_attn_W_out,
           L0_attn_b_out, L0_ffn_w1, L0_ffn_conv_w, L0_ffn_conv_b, L0_ffn_w2,
           _debug=False):
    x = np.ascontiguousarray(L0_attn_hidden_states, dtype=np.float32)
    shards = x.reshape(8, NB, 2048, 1024)
    nc = _get_nc(_debug)
    common = {
        "W_attn": np.ascontiguousarray(L0_attn_W_attn, dtype=np.float32),
        "b_attn": np.ascontiguousarray(L0_attn_b_attn, dtype=np.float32),
        "W_out": np.ascontiguousarray(L0_attn_W_out, dtype=np.float32),
        "b_out": np.ascontiguousarray(L0_attn_b_out, dtype=np.float32),
        "w1": np.ascontiguousarray(L0_ffn_w1, dtype=np.float32),
        "cw": np.ascontiguousarray(L0_ffn_conv_w, dtype=np.float32).reshape(4096, 3),
        "cb": np.ascontiguousarray(L0_ffn_conv_b, dtype=np.float32),
        "w2": np.ascontiguousarray(L0_ffn_w2, dtype=np.float32),
        "ident_bf": np.eye(128, dtype=ml_dtypes.bfloat16),
        "ident_f": np.eye(128, dtype=np.float32),
        "tri_bf": np.triu(np.ones((128, 128), dtype=np.float32)).astype(ml_dtypes.bfloat16),
    }
    in_maps = [dict(common, x=np.ascontiguousarray(shards[i])) for i in range(8)]
    res = run_bass_kernel_spmd(nc, in_maps, core_ids=list(range(8)))
    out = np.concatenate([r["out"].reshape(NB, 2048, 1024) for r in res.results], axis=0)
    if _debug:
        return out, res
    return out


# version 16
# speedup vs baseline: 1.0594x; 1.0594x over previous
import numpy as np
import ml_dtypes
import concourse.bass as bass
import concourse.mybir as mybir
from concourse.bass_utils import run_bass_kernel_spmd

F32 = mybir.dt.float32
BF16 = mybir.dt.bfloat16
AF = mybir.ActivationFunctionType
ALU = mybir.AluOpType
AX = mybir.AxisListType

MAGIC = 12582912.0
UPTO = 6
FFNW = 10
W1MODE = 0
NB = 2
NT = 16
ND = 8
NHC = 32

ENGS = ("sync", "gpsimd", "tensor", "scalar", "vector")


class Plan:
    def __init__(self):
        from collections import defaultdict
        self.ops = {e: [] for e in ENGS}
        self.cnt = defaultdict(int)

    def inc(self, eng, fn, sem, by=1):
        self.ops[eng].append(("inc", fn, sem, by))
        self.cnt[sem] += by
        return self.cnt[sem]

    def op(self, eng, fn):
        self.ops[eng].append(("op", fn))

    def wait(self, eng, sem, val):
        if val > 0:
            self.ops[eng].append(("wait", sem, val))

    def replay(self, eng_name, obj, sems):
        for it in self.ops[eng_name]:
            if it[0] == "op":
                it[1](obj)
            elif it[0] == "inc":
                it[1](obj).then_inc(sems[it[2]], it[3])
            else:
                obj.wait_ge(sems[it[1]], it[2])


def build(debug=False):
    nc = bass.Bass("TRN2", target_bir_lowering=False)
    P = Plan()

    def T(fn):
        return P.inc("tensor", fn, "t")

    def S(fn):
        return P.inc("scalar", fn, "s")

    def V(fn):
        return P.inc("vector", fn, "v")

    def Vd(fn):
        n = P.inc("vector", fn, "v")
        P.wait("vector", "v", n)
        return n

    def DW(fn, sem):
        return P.inc("sync", fn, sem, 16)

    def DG(fn, sem):
        return P.inc("gpsimd", fn, sem, 16)

    SYNC_SEMS = ("pw0", "pw1", "qw0", "qw1", "gv", "wr", "sg0", "sg1", "nb0", "nb1")
    GP_SEMS = ("gi", "gx0", "gx1", "ga0", "ga1", "go0", "go1", "gr", "gd", "mb", "p1a", "p1b", "p2a", "p2b")
    SYNC_SEMS = SYNC_SEMS + ("tr0", "tr1")

    def fence():
        from collections import defaultdict
        c = defaultdict(int, P.cnt)
        dma = [s for s in list(SYNC_SEMS) + list(GP_SEMS) if c.get(s, 0) > 0]
        P.wait("tensor", "s", c["s"]); P.wait("tensor", "v", c["v"])
        P.wait("scalar", "t", c["t"]); P.wait("scalar", "v", c["v"])
        P.wait("vector", "t", c["t"]); P.wait("vector", "s", c["s"])
        for q in ("sync", "gpsimd"):
            P.wait(q, "t", c["t"]); P.wait(q, "s", c["s"]); P.wait(q, "v", c["v"])
            for s in dma:
                P.wait(q, s, c[s])
        for e in ("tensor", "scalar", "vector"):
            for s in dma:
                P.wait(e, s, c[s])

    x_d = nc.dram_tensor("x", [NB, 2048, 1024], F32, kind="ExternalInput").ap()
    wattn_d = nc.dram_tensor("W_attn", [1024, 3072], F32, kind="ExternalInput").ap()
    battn_d = nc.dram_tensor("b_attn", [3072], F32, kind="ExternalInput").ap()
    wout_d = nc.dram_tensor("W_out", [1024, 1024], F32, kind="ExternalInput").ap()
    bout_d = nc.dram_tensor("b_out", [1024], F32, kind="ExternalInput").ap()
    w1_d = nc.dram_tensor("w1", [4096, 1024], F32, kind="ExternalInput").ap()
    cw_d = nc.dram_tensor("cw", [4096, 3], F32, kind="ExternalInput").ap()
    cb_d = nc.dram_tensor("cb", [4096], F32, kind="ExternalInput").ap()
    w2_d = nc.dram_tensor("w2", [1024, 4096], F32, kind="ExternalInput").ap()
    identbf_d = nc.dram_tensor("ident_bf", [128, 128], BF16, kind="ExternalInput").ap()
    identf_d = nc.dram_tensor("ident_f", [128, 128], F32, kind="ExternalInput").ap()
    tribf_d = nc.dram_tensor("tri_bf", [128, 128], BF16, kind="ExternalInput").ap()
    out_d = nc.dram_tensor("out", [NB, 2048, 1024], F32, kind="ExternalOutput").ap()

    wqk_sc = nc.dram_tensor("wqk_sc", [16, 128, 1024], BF16, kind="Internal").ap()
    wv_sc = nc.dram_tensor("wv_sc", [2, 8, 128, 640], BF16, kind="Internal").ap()
    wo_sc = nc.dram_tensor("wo_sc", [8, 128, 1024], BF16, kind="Internal").ap()
    w1q_sc = nc.dram_tensor("w1q_sc", [32, 128, 1024], BF16, kind="Internal").ap()
    w2q_sc = nc.dram_tensor("w2q_sc", [32, 128, 1024], BF16, kind="Internal").ap()
    acc_sc = nc.dram_tensor("acc_sc", [NB, 16, 128, 1024], F32, kind="Internal").ap()
    row_sc = nc.dram_tensor("row_sc", [4096], F32, kind="Internal").ap()
    den_sc = nc.dram_tensor("den_sc", [2, 2, 512], F32, kind="Internal").ap()
    trit_sc = nc.dram_tensor("trit_sc", [64, 128, 1024], BF16, kind="Internal").ap()
    mean_sc = nc.dram_tensor("mean_sc", [256], F32, kind="Internal").ap()

    dbg = {}
    if debug:
        for nm, shp, dt in [("d_xnT", [128, 16384], BF16), ("d_xiT", [128, 16384], BF16),
                            ("d_qkT", [128, 32768], BF16), ("d_vsb", [128, 20480], BF16),
                            ("d_ctx", [128, 16384], BF16), ("d_g", [128, 2048], F32),
                            ("d_acc", [16, 128, 1024], F32),
                            ("d_w1q", [32, 128, 1024], BF16), ("d_w2q", [32, 128, 1024], BF16),
                            ("d_ccol", [128, 16], F32)]:
            dbg[nm] = nc.dram_tensor(nm, shp, dt, kind="ExternalOutput").ap()

    def sbt(name, shape, dtype, off):
        return nc.alloc_sbuf_tensor_at(name, shape, dtype, offset=off).ap()

    cur = [16512]

    def nxt(sz, c=cur):
        r = c[0]
        c[0] = (c[0] + sz + 31) & ~31
        return r

    ident_bf = sbt("identbf", [128, 128], BF16, nxt(256))
    ones_bf = sbt("onesbf", [128, 128], BF16, nxt(256))
    tri_bf = sbt("tribf", [128, 128], BF16, nxt(256))
    ident_f = sbt("identf", [128, 128], F32, nxt(512))
    ones_f = sbt("onesf", [128, 128], F32, nxt(512))
    ccol = sbt("ccol", [128, 16], F32, nxt(64))
    C_MAG, C_NMAG, C_EPS, C_C1, C_C2, C_SW1, C_SW2 = 0, 1, 2, 3, 4, 5, 6
    bcol = sbt("bcol", [128, 16], F32, nxt(64))
    cwsb = sbt("cwsb", [128, 96], F32, nxt(384))
    cbsb = sbt("cbsb", [128, 32], F32, nxt(128))
    stat = sbt("stat", [128, 16], F32, nxt(64))
    gst = sbt("gst", [128, 16], F32, nxt(64))
    bvrow = sbt("bvrow", [1, 1280], BF16, nxt(2560))
    borow = sbt("borow", [1, 1024], BF16, nxt(2048))
    nrm = sbt("nrm", [128, 1024], F32, nxt(4096))
    rows_sb = sbt("rowssb", [128, 1024], F32, nxt(4096))

    R0 = (cur[0] + 63) & ~63
    xiT = sbt("xiT", [128, 16384], BF16, R0)
    XNC = R0 + 32768
    xnT = sbt("xnT", [128, 16384], BF16, XNC)
    ctx_sb = sbt("ctxsb", [128, 16384], BF16, XNC)
    h2w = sbt("h2w", [128, 8192], F32, XNC)
    QK = XNC + 32768
    qkT = sbt("qkT", [128, 32768], BF16, QK)
    VSB = QK + 65536
    v_sb = sbt("vsb", [128, 20480], BF16, VSB)
    AQ = VSB + 40960
    xbuf = sbt("xbuf", [128, 2048], F32, AQ)
    xc = sbt("xc", [128, 1024], F32, AQ + 8192)
    xt1 = sbt("xt1", [128, 1024], F32, AQ + 12288)
    xn_t = sbt("xnt", [128, 1024], BF16, AQ + 16384)
    xi_t = sbt("xit", [128, 1024], BF16, AQ + 18432)
    wqkb = sbt("wqkb", [128, 2048], BF16, AQ)
    wvb = sbt("wvb", [128, 5120], BF16, AQ + 4096)
    p_sb = sbt("psb", [128, 2048], BF16, AQ)
    aqld = sbt("aqld", [128, 2048], F32, AQ + 4096)
    aqtr = sbt("aqtr", [128, 2048], BF16, AQ + 12288)
    aqsg = sbt("aqsg", [128, 2048], BF16, AQ + 16384)
    SB_END = AQ + 20480

    c0 = [QK]
    ws0 = sbt("ws0", [128, 4096], F32, nxt(16384, c0))
    ws1 = sbt("ws1", [128, 4096], F32, nxt(16384, c0))
    wcast = sbt("wcast", [128, 4096], BF16, nxt(8192, c0))
    tqtmp = sbt("tqtmp", [128, 4096], F32, nxt(16384, c0))
    tritb = sbt("tritb", [128, 4096], BF16, nxt(8192, c0))
    stg = [sbt("stg0", [128, 1024], BF16, nxt(2048, c0)),
           sbt("stg1", [128, 1024], BF16, nxt(2048, c0))]
    zstage = sbt("zstage", [128, 128], BF16, nxt(256, c0))
    ws = [ws0, ws1]

    wo_res = sbt("wores", [128, 8192], BF16, QK)
    xrb = sbt("xrb", [128, 2048], F32, QK + 16384)
    acst = sbt("acst", [128, 2048], F32, QK + 24576)

    c1 = [QK]
    g_sb = sbt("gsb", [128, 2048], F32, nxt(8192, c1))
    g_row = sbt("grow", [1, 2048], F32, nxt(8192, c1))
    stash = sbt("stash", [128, 4128], F32, nxt(16512, c1))
    ubuf = [sbt("u0", [128, 388], F32, nxt(1552, c1)),
            sbt("u1", [128, 388], F32, nxt(1552, c1))]
    h2q = sbt("h2q", [128, 8192], BF16, nxt(16384, c1))
    w1b = sbt("w1b", [128, 8192], BF16, nxt(16384, c1))
    w2b = sbt("w2b", [128, 8192], BF16, nxt(16384, c1))
    accb = sbt("accb", [128, 2048], F32, nxt(8192, c1))
    outb = sbt("outb", [128, 2048], F32, nxt(8192, c1))
    mx = sbt("mx", [128, 16], F32, nxt(64, c1))
    sbrow = sbt("sbrow", [1, 256], F32, nxt(1024, c1))
    sb_sb = sbt("sbsb", [128, 256], F32, nxt(1024, c1))
    pmb = sbt("pmb", [128, 512], F32, nxt(2048, c1))
    assert c1[0] <= VSB + 40960, c1[0]

    pg0 = nc.alloc_psum_tensor("pg0", [128, 1024], F32).ap()
    pg1 = nc.alloc_psum_tensor("pg1", [128, 1024], F32).ap()
    pg2 = nc.alloc_psum_tensor("pg2", [128, 1024], F32).ap()
    pg3 = nc.alloc_psum_tensor("pg3", [128, 1024], F32).ap()

    V(lambda e: e.memset(ones_f[:], 1.0))
    V(lambda e: e.memset(ones_bf[:], 1.0))
    V(lambda e: e.memset(ccol[:, C_MAG:C_MAG + 1], MAGIC))
    V(lambda e: e.memset(ccol[:, C_NMAG:C_NMAG + 1], -MAGIC))
    V(lambda e: e.memset(ccol[:, C_EPS:C_EPS + 1], 1e-5))
    DG(lambda e: e.dma_start(out=ident_bf[:], in_=identbf_d), "gi")
    DG(lambda e: e.dma_start(out=ident_f[:], in_=identf_d), "gi")
    DG(lambda e: e.dma_start(out=tri_bf[:], in_=tribf_d), "gi")
    DG(lambda e: e.dma_start(out=cwsb[:].rearrange("p (h j) -> p h j", h=32),
                             in_=cw_d.rearrange("(h p) j -> p h j", p=128)), "gi")
    DG(lambda e: e.dma_start(out=cbsb[:], in_=cb_d.rearrange("(h p) -> p h", p=128),
                             allow_slow_non_contiguous=True), "gi")
    DG(lambda e: e.dma_start(out=bcol[:], in_=battn_d[0:2048].rearrange("(c p) -> p c", p=128),
                             allow_slow_non_contiguous=True), "gi")
    DG(lambda e: e.dma_start(out=nrm[0:1, :], in_=battn_d[2048:3072].unsqueeze(0)), "gi")
    gg = DG(lambda e: e.dma_start(out=xc[0:1, :], in_=bout_d.unsqueeze(0)), "gi")
    P.wait("vector", "gi", gg)
    P.wait("tensor", "gi", gg)
    P.wait("scalar", "gi", gg)
    V(lambda e: e.memset(bvrow[:], 0.0))
    V(lambda e: e.memset(bvrow[:].rearrange("o (q c) -> o q c", c=160)[:, :, 64:66], 1.0))
    V(lambda e: e.tensor_copy(bvrow[:].rearrange("o (q c) -> o q c", c=160)[:, :, 0:64],
                              nrm[0:1, :].rearrange("o (h c) -> o h c", c=128)[:, :, 0:64]))
    V(lambda e: e.tensor_copy(bvrow[:].rearrange("o (q c) -> o q c", c=160)[:, :, 96:160],
                              nrm[0:1, :].rearrange("o (h c) -> o h c", c=128)[:, :, 64:128]))
    V(lambda e: e.memset(zstage[:], 0.0))
    V(lambda e: e.tensor_copy(borow[:], xc[0:1, :]))
    P.op("scalar", lambda e: e.preload_act_table(AF.Sqrt))

    cast_v = {}
    wr_w = 0
    for dc in range(ND):
        if dc >= 2:
            P.wait("sync", "v", cast_v[dc - 2])
        wt = DW(lambda e, dc=dc, b=ws[dc % 2]: e.dma_start(out=b[:, 0:3072], in_=wattn_d[dc * 128:(dc + 1) * 128, :]), "pw%d" % (dc % 2))
        P.wait("vector", "pw%d" % (dc % 2), wt)
        P.wait("vector", "wr", P.cnt["wr"])
        cast_v[dc] = V(lambda e, b=ws[dc % 2]: e.tensor_copy(wcast[:, 0:3072], b[:, 0:3072]))
        P.wait("sync", "v", cast_v[dc])
        for ct in range(16):
            DW(lambda e, ct=ct, dc=dc: e.dma_start(out=wqk_sc[ct, :, dc * 128:(dc + 1) * 128],
                                                   in_=wcast[:, ct * 128:(ct + 1) * 128]), "wr")
        for hf in range(2):
            DW(lambda e, hf=hf, dc=dc: e.dma_start(
                out=wv_sc[hf, dc].rearrange("p (q c) -> p q c", c=160)[:, :, 64:96],
                in_=zstage[:].rearrange("p (q c) -> p q c", c=32),
                allow_slow_non_contiguous=True), "wr")
            DW(lambda e, hf=hf, dc=dc: e.dma_start(
                out=wv_sc[hf, dc].rearrange("p (q c) -> p q c", c=160)[:, :, 0:64],
                in_=wcast[:, 2048 + hf * 512: 2048 + hf * 512 + 512].rearrange("p (q c) -> p q c", c=128)[:, :, 0:64],
                allow_slow_non_contiguous=True), "wr")
            wr_w = DW(lambda e, hf=hf, dc=dc: e.dma_start(
                out=wv_sc[hf, dc].rearrange("p (q c) -> p q c", c=160)[:, :, 96:160],
                in_=wcast[:, 2048 + hf * 512: 2048 + hf * 512 + 512].rearrange("p (q c) -> p q c", c=128)[:, :, 64:128],
                allow_slow_non_contiguous=True), "wr")
    wrh = {0: 0, 1: 0}
    for dc in range(ND):
        pp = dc % 2
        if dc >= 2:
            P.wait("sync", "v", cast_v[8 + dc - 2])
        wt = DW(lambda e, dc=dc, b=ws[dc % 2]: e.dma_start(out=b[:, 0:1024], in_=wout_d[dc * 128:(dc + 1) * 128, :]), "pw%d" % (dc % 2))
        P.wait("vector", "pw%d" % (dc % 2), wt)
        P.wait("vector", "wr", P.cnt["wr"])
        cast_v[8 + dc] = V(lambda e, b=ws[dc % 2], pp=pp: e.tensor_copy(wcast[:, pp * 1024:pp * 1024 + 1024], b[:, 0:1024]))
        P.wait("sync", "v", cast_v[8 + dc])
        wrh[pp] = DW(lambda e, dc=dc, pp=pp: e.dma_start(out=wo_sc[dc], in_=wcast[:, pp * 1024:pp * 1024 + 1024]), "wr")

    wq_state = {}

    def wq_pass1(w_d, ntiles, tilecols, swcol, c1col, tag):
        acm = stat[:, 0:1]
        tmpc = stat[:, 1:2]
        nch = ntiles * tilecols // 1024
        sem = "p1a" if tag == 0 else "p1b"
        prevS = {0: 0, 1: 0}
        prevV = 0
        for i in range(nch):
            if i >= 2:
                P.wait("gpsimd", "s", prevS[i % 2])
            r, cc = (i * 1024) // tilecols, (i * 1024) % tilecols
            wt = DG(lambda e, r=r, cc=cc, sl=(i % 2): e.dma_start(
                out=aqld[:, sl * 1024:sl * 1024 + 1024], in_=w_d[r * 128:(r + 1) * 128, cc:cc + 1024]), sem)
            P.wait("scalar", sem, wt)
            if i == 0:
                prevS[0] = S(lambda e: e.activation(rows_sb[:], aqld[:, 0:1024], AF.Abs, accum_out=acm))
                prevV = P.cnt["v"]
            else:
                P.wait("scalar", "v", prevV)
                prevS[i % 2] = S(lambda e, sl=(i % 2): e.activation(
                    rows_sb[:], aqld[:, sl * 1024:sl * 1024 + 1024], AF.Abs, accum_out=tmpc))
                P.wait("vector", "s", prevS[i % 2])
                prevV = Vd(lambda e: e.tensor_tensor(acm, acm, tmpc, ALU.add))
        P.wait("gpsimd", "v", prevV)
        P.wait("gpsimd", "s", max(prevS[0], prevS[1]))
        mb = DG(lambda e: e.dma_start(out=mean_sc[0:128].unsqueeze(0).rearrange("o c -> c o"), in_=acm), "mb")
        P.wait("gpsimd", "mb", mb)
        mb = DG(lambda e: e.dma_start(out=nrm[0:1, 0:128], in_=mean_sc[0:128].unsqueeze(0)), "mb")
        P.wait("vector", "mb", mb)
        Vd(lambda e: e.reduce_sum(stat[0:1, 2:3], nrm[0:1, 0:128], AX.X))
        Vd(lambda e, n=float(ntiles * 128 * tilecols): e.tensor_scalar(stat[0:1, 2:3], stat[0:1, 2:3], 1.0 / n, None, ALU.mult))
        Vd(lambda e: e.tensor_scalar(stat[0:1, 2:3], stat[0:1, 2:3], 1e-5, None, ALU.max))
        vv = Vd(lambda e: e.reciprocal(stat[0:1, 3:4], stat[0:1, 2:3]))
        P.wait("gpsimd", "v", vv)
        mb = DG(lambda e: e.dma_start(out=mean_sc[128:130].unsqueeze(0), in_=stat[0:1, 2:4]), "mb")
        P.wait("gpsimd", "mb", mb)
        DG(lambda e: e.dma_start(out=stat[:, 4:5], in_=mean_sc[128:129].unsqueeze(0).broadcast_to([128, 1])), "mb")
        mb = DG(lambda e, c=swcol: e.dma_start(out=ccol[:, c:c + 1], in_=mean_sc[129:130].unsqueeze(0).broadcast_to([128, 1])), "mb")
        P.wait("vector", "mb", mb)
        vv = Vd(lambda e, c=c1col: e.tensor_scalar(ccol[:, c:c + 1], stat[:, 4:5], 1.0 / 127.0, None, ALU.mult))
        wq_state[("p1", tag)] = (vv, P.cnt[sem], P.cnt["mb"])

    def wq_pass2(w_d, ntiles, tilecols, swcol, base_tile, tag, gate_t, emit_q):
        nch = ntiles * tilecols // 1024
        sem = "p2a" if tag == 0 else "p2b"
        qs = {0: 0, 1: 0}
        vts = {0: 0, 1: 0}
        stw = {0: 0, 1: 0}

        def chunk0_pre():
            vv, _, mbtot = wq_state[("p1", tag)]
            P.wait("scalar", "v", vv)
            P.wait("scalar", "mb", mbtot)
            P.wait("gpsimd", "t", gate_t)
        emit_q.append(chunk0_pre)
        def mk_chunk(i):
            def emit_chunk():
                sl = i % 2
                if i >= 1:
                    P.wait("gpsimd", "s", qs[0])
                r, cc = (i * 1024) // tilecols, (i * 1024) % tilecols
                wt = DG(lambda e, r=r, cc=cc: e.dma_start(
                    out=aqld[:, 0:1024], in_=w_d[r * 128:(r + 1) * 128, cc:cc + 1024]), sem)
                P.wait("scalar", sem, wt)
                if i >= 1:
                    P.wait("scalar", "v", vts[1 - sl])
                S(lambda e, c=swcol: e.activation(
                    aqld[:, 1024:2048], aqld[:, 0:1024], AF.Identity,
                    scale=ccol[:, c:c + 1], bias=ccol[:, C_MAG:C_MAG + 1]))
                qs[0] = S(lambda e: e.activation(aqld[:, 1024:2048], aqld[:, 1024:2048], AF.Identity, bias=ccol[:, C_NMAG:C_NMAG + 1]))
                P.wait("vector", "s", qs[0])
                P.wait("vector", "sg0", stw[sl])
                vts[sl] = V(lambda e, sl=sl: e.tensor_scalar(
                    aqtr[:, sl * 1024:sl * 1024 + 1024], aqld[:, 1024:2048], 1.0, -1.0, ALU.min, ALU.max))
                P.wait("gpsimd", "v", vts[sl])
                stw[sl] = DG(lambda e, i=i, sl=sl, bt=base_tile: e.dma_start(
                    out=trit_sc[bt + i], in_=aqtr[:, sl * 1024:sl * 1024 + 1024]), "sg0")
            return emit_chunk
        for i in range(nch):
            emit_q.append(mk_chunk(i))

    def wq_restream(emit_q):
        trT = {0: 0, 1: 0}
        trev = {0: 0, 1: 0}
        trst = {0: 0, 1: 0}
        emit_q.append(lambda: P.wait("sync", "sg0", P.cnt["sg0"]))

        def mk_tile(i):
            def emit_tile():
                pp = i % 2
                tb = aqtr[:, pp * 1024:pp * 1024 + 1024]
                sg = aqsg[:, pp * 1024:pp * 1024 + 1024]
                P.wait("sync", "t", trT[pp])
                if i < 2:
                    P.wait("sync", "v", P.cnt["v"])
                wd = DW(lambda e, i=i, tb=tb: e.dma_start(out=tb, in_=trit_sc[i]), "tr%d" % pp)
                P.wait("tensor", "tr%d" % pp, wd)
                P.wait("tensor", "v", trev[pp])
                pgt = pg0 if pp == 0 else pg1
                for dc in range(ND):
                    tv = T(lambda e, dc=dc, tb=tb, pgt=pgt: e.matmul(
                        pgt[:, dc * 128:dc * 128 + 128], tb[:, dc * 128:(dc + 1) * 128],
                        ident_bf[:], start=True, stop=True))
                trT[pp] = tv
                P.wait("vector", "t", tv)
                P.wait("vector", "sg1", trst[pp])
                trev[pp] = V(lambda e, sg=sg, pgt=pgt: e.tensor_copy(sg, pgt[:]))
                P.wait("sync", "v", trev[pp])
                if i < 32:
                    trst[pp] = DW(lambda e, i=i, sg=sg: e.dma_start(out=w1q_sc[i], in_=sg), "sg1")
                else:
                    dt, q = (i - 32) // 4, (i - 32) % 4
                    for j in range(8):
                        trst[pp] = DW(lambda e, dt=dt, q=q, j=j, sg=sg: e.dma_start(
                            out=w2q_sc[q * 8 + j, :, dt * 128:(dt + 1) * 128],
                            in_=sg[:, j * 128:(j + 1) * 128]), "sg1")
            return emit_tile
        for i in range(64):
            emit_q.append(mk_tile(i))

    wout_cast_done = P.cnt["v"]
    fence()

    for b in range(NB if UPTO >= 2 else 0):
        P.op("scalar", lambda e: e.preload_act_table(AF.Sqrt))
        lastS = {0: 0, 1: 0}
        lastV = {0: 0, 1: 0}
        evT = {0: 0, 1: 0}
        for st in range(NT):
            pp = st % 2
            P.wait("gpsimd", "s", lastS[pp])
            P.wait("gpsimd", "v", lastV[pp])
            xd = DG(lambda e, b=b, st=st, pp=pp: e.dma_start(
                out=xbuf[:, pp * 1024:pp * 1024 + 1024], in_=x_d[b, st * 128:(st + 1) * 128, :]), "gx%d" % pp)
            xs = xbuf[:, pp * 1024:pp * 1024 + 1024]
            sc = stat[:, pp * 8: pp * 8 + 8]
            vprev = P.cnt["v"]
            P.wait("vector", "gx%d" % pp, xd)
            lastV[pp] = Vd(lambda e, xs=xs, sc=sc: e.reduce_sum(sc[:, 0:1], xs, AX.X))
            P.wait("scalar", "gx%d" % pp, xd)
            P.wait("scalar", "v", vprev)
            S(lambda e, xs=xs, sc=sc: e.activation(xc[:], xs, AF.Square, accum_out=sc[:, 1:2]))
            P.wait("vector", "s", P.cnt["s"])
            Vd(lambda e, sc=sc: e.tensor_scalar(sc[:, 2:3], sc[:, 0:1], -1.0 / 1024.0, None, ALU.mult))
            Vd(lambda e, sc=sc: e.tensor_tensor(sc[:, 3:4], sc[:, 2:3], sc[:, 2:3], ALU.mult))
            vv = Vd(lambda e, sc=sc: e.scalar_tensor_tensor(sc[:, 4:5], sc[:, 1:2], 1.0 / 1024.0, sc[:, 3:4], ALU.mult, ALU.subtract))
            P.wait("scalar", "v", vv)
            sv = S(lambda e, sc=sc: e.activation(sc[:, 4:5], sc[:, 4:5], AF.Sqrt, bias=ccol[:, C_EPS:C_EPS + 1]))
            P.wait("vector", "s", sv)
            Vd(lambda e, sc=sc: e.reciprocal(sc[:, 5:6], sc[:, 4:5]))
            sv = S(lambda e, xs=xs, sc=sc: e.activation(xc[:], xs, AF.Identity, bias=sc[:, 2:3]))
            lastS[pp] = sv
            P.wait("vector", "s", sv)
            Vd(lambda e, sc=sc: e.reduce_max(sc[:, 6:7], xc[:], AX.X, apply_absolute_value=True))
            Vd(lambda e, sc=sc: e.tensor_tensor(sc[:, 6:7], sc[:, 6:7], sc[:, 5:6], ALU.mult))
            Vd(lambda e, sc=sc: e.tensor_scalar(sc[:, 6:7], sc[:, 6:7], 1e-5, None, ALU.max))
            Vd(lambda e, sc=sc: e.reciprocal(sc[:, 7:8], sc[:, 6:7]))
            Vd(lambda e, sc=sc: e.tensor_scalar(sc[:, 7:8], sc[:, 7:8], 127.0, None, ALU.mult))
            Vd(lambda e, sc=sc: e.tensor_tensor(sc[:, 7:8], sc[:, 7:8], sc[:, 5:6], ALU.mult))
            vv = Vd(lambda e, sc=sc, st=st: e.tensor_copy(gst[:, st:st + 1], sc[:, 6:7]))
            P.wait("scalar", "v", vv)
            P.wait("scalar", "t", evTT if st >= 1 else 0)
            S(lambda e, sc=sc: e.activation(xn_t[:], xc[:], AF.Identity, scale=sc[:, 5:6]))
            S(lambda e, sc=sc: e.activation(xt1[:], xc[:], AF.Identity, scale=sc[:, 7:8], bias=ccol[:, C_MAG:C_MAG + 1]))
            sv = S(lambda e: e.activation(xi_t[:], xt1[:], AF.Identity, bias=ccol[:, C_NMAG:C_NMAG + 1]))
            lastS[pp] = sv
            P.wait("tensor", "s", sv)
            P.wait("tensor", "v", evT[0])
            for dc in range(ND):
                T(lambda e, dc=dc: e.matmul(pg0[:, dc * 128:dc * 128 + 128],
                                            xn_t[:, dc * 128:(dc + 1) * 128], ident_bf[:], start=True, stop=True))
            tv0 = P.cnt["t"]
            P.wait("tensor", "v", evT[1])
            for dc in range(ND):
                T(lambda e, dc=dc: e.matmul(pg1[:, dc * 128:dc * 128 + 128],
                                            xi_t[:, dc * 128:(dc + 1) * 128], ident_bf[:], start=True, stop=True))
            evTT = P.cnt["t"]
            P.wait("vector", "t", tv0)
            evT[0] = V(lambda e, st=st: e.tensor_copy(
                xnT[:].rearrange("p (dc t) -> p dc t", dc=8)[:, :, st * 128:st * 128 + 128],
                pg0[:].rearrange("p (dc c) -> p dc c", dc=8)))
            P.wait("vector", "t", evTT)
            evT[1] = V(lambda e, st=st: e.tensor_copy(
                xiT[:].rearrange("p (dc t) -> p dc t", dc=8)[:, :, st * 128:st * 128 + 128],
                pg1[:].rearrange("p (dc c) -> p dc c", dc=8)))
            lastV[pp] = evT[1]

        if debug and b == 0:
            fence()
            DG(lambda e: e.dma_start(out=dbg["d_xnT"], in_=xnT[:]), "gd")
            DG(lambda e: e.dma_start(out=dbg["d_xiT"], in_=xiT[:]), "gd")
            fence()
        if b == 0:
            P.wait("gpsimd", "v", P.cnt["v"])
            P.wait("gpsimd", "s", P.cnt["s"])
            wq_pass1(w1_d, 32, 1024, C_SW1, C_C1, 0)
            wq_pass1(w2_d, 8, 4096, C_SW2, C_C2, 1)
            wq_state["p1_s"] = P.cnt["s"]
            wq_state["p1a"] = P.cnt["p1a"]
            wq_state["p1b"] = P.cnt["p1b"]

        if UPTO < 3:
            continue
        P.wait("tensor", "v", P.cnt["v"] if False else lastV[0] if False else max(evT[0], evT[1]))
        P.wait("sync", "v", max(evT[0], evT[1]))
        P.wait("sync", "s", P.cnt["s"])
        for e_ in ("scalar", "vector", "tensor"):
            P.wait(e_, "sg0", P.cnt["sg0"])
            P.wait(e_, "sg1", P.cnt["sg1"])
            P.wait(e_, "wr", P.cnt["wr"])
        lastT_wqk = {0: 0, 1: 0}
        lastS_ps = {0: 0, 1: 0}
        for ct in range(16):
            pp = ct % 2
            P.wait("sync", "t", lastT_wqk[pp])
            wd = DW(lambda e, ct=ct, pp=pp: e.dma_start(out=wqkb[:, pp * 1024:pp * 1024 + 1024], in_=wqk_sc[ct]), "pw%d" % pp)
            P.wait("tensor", "pw%d" % pp, wd)
            for ts in range(4):
                qq = (ct * 4 + ts) % 2
                P.wait("tensor", "s", lastS_ps[qq])
                for dc in range(ND):
                    tv = T(lambda e, pp=pp, dc=dc, ts=ts, qq=qq: e.matmul(
                        pg2[:, qq * 512:qq * 512 + 512],
                        wqkb[:, pp * 1024 + dc * 128: pp * 1024 + dc * 128 + 128],
                        xnT[:, dc * 2048 + ts * 512: dc * 2048 + ts * 512 + 512],
                        start=(dc == 0), stop=(dc == ND - 1)))
                lastT_wqk[pp] = tv
                P.wait("scalar", "t", tv)
                lastS_ps[qq] = S(lambda e, ct=ct, ts=ts, qq=qq: e.activation(
                    qkT[:, ct * 2048 + ts * 512: ct * 2048 + ts * 512 + 512],
                    pg2[:, qq * 512:qq * 512 + 512], AF.Identity, bias=bcol[:, ct:ct + 1]))
        P.wait("tensor", "s", max(lastS_ps[0], lastS_ps[1]))
        P.wait("sync", "s", wq_state.get("p1_s", 0))
        P.wait("sync", "p1a", wq_state.get("p1a", 0))
        P.wait("sync", "p1b", wq_state.get("p1b", 0))
        lastV_ps = {0: 0, 1: 0}
        lastT_wv = 0
        for hf in range(2):
            P.wait("sync", "t", lastT_wv)
            for dc in range(ND):
                wd = DW(lambda e, hf=hf, dc=dc: e.dma_start(out=wvb[:, dc * 640:dc * 640 + 640], in_=wv_sc[hf, dc]), "gv")
            P.wait("tensor", "gv", wd)
            for st in range(NT):
                qq = st % 2
                pgv = pg3 if qq == 0 else pg2
                P.wait("tensor", "v", lastV_ps[qq])
                for dc in range(ND):
                    T(lambda e, dc=dc, st=st, pgv=pgv: e.matmul(
                        pgv[:, 0:512],
                        xnT[:, dc * 2048 + st * 128: dc * 2048 + st * 128 + 128],
                        wvb[:, dc * 640: dc * 640 + 512],
                        start=(dc == 0), stop=False))
                    T(lambda e, dc=dc, st=st, pgv=pgv: e.matmul(
                        pgv[:, 512:640],
                        xnT[:, dc * 2048 + st * 128: dc * 2048 + st * 128 + 128],
                        wvb[:, dc * 640 + 512: dc * 640 + 640],
                        start=(dc == 0), stop=False))
                T(lambda e, hf=hf, pgv=pgv: e.matmul(pgv[:, 0:512], ones_bf[0:1, 0:128],
                                                     bvrow[0:1, hf * 640: hf * 640 + 512], start=False, stop=True))
                tv = T(lambda e, hf=hf, pgv=pgv: e.matmul(pgv[:, 512:640], ones_bf[0:1, 0:128],
                                                          bvrow[0:1, hf * 640 + 512: hf * 640 + 640], start=False, stop=True))
                lastT_wv = tv
                P.wait("vector", "t", tv)
                base = st * 1280 + hf * 640
                lastV_ps[qq] = V(lambda e, base=base, pgv=pgv: e.tensor_copy(
                    v_sb[:, base: base + 640], pgv[:, 0:640]))
        qkv_s = P.cnt["s"]
        qkv_v = P.cnt["v"]
        p2q = []
        if b == 0:
            wq_pass2(w1_d, 32, 1024, C_SW1, 0, 0, lastT_wv, p2q)
            wq_pass2(w2_d, 8, 4096, C_SW2, 32, 1, lastT_wv, p2q)
            P.wait("tensor", "s", qkv_s)
            P.wait("tensor", "v", qkv_v)
            for q_ in ("sync", "gpsimd"):
                P.wait(q_, "t", P.cnt["t"])
        else:
            fence()
        if debug and b == 0:
            DG(lambda e: e.dma_start(out=dbg["d_qkT"], in_=qkT[:]), "gd")
            DG(lambda e: e.dma_start(out=dbg["d_vsb"], in_=v_sb[:]), "gd")
            fence()

        if UPTO < 4:
            continue
        P.op("scalar", lambda e: e.preload_act_table(AF.Exp))
        sc_pair = [pg0, pg1]
        ctx_pair = [pg2, pg3]
        exp_s = {0: 0, 1: 0}
        avT = {0: 0, 1: 0}
        evac_v = {0: 0, 1: 0}
        nbw = {0: 0, 1: 0}
        def attn_tail(pi, cps, qc, pr, avlast):
            P.wait("vector", "t", avlast)
            P.wait("vector", "nb%d" % pi, nbw[pi])
            V(lambda e, cps=cps, pi=pi: e.reciprocal(rows_sb[64:65, pi * 512:pi * 512 + 512], cps[64:65, 0:512]))
            rv = V(lambda e, cps=cps, pi=pi: e.reciprocal(rows_sb[32:33, pi * 512:pi * 512 + 512], cps[32:33, 512:1024]))
            P.wait("sync", "v", rv)
            DW(lambda e, pi=pi: e.dma_start(out=den_sc[pi], in_=rows_sb[32:96:32, pi * 512:pi * 512 + 512]), "nb%d" % pi)
            P.wait("sync", "nb%d" % pi, P.cnt["nb%d" % pi])
            DW(lambda e, pi=pi: e.dma_start(out=nrm[0:64, pi * 512:pi * 512 + 512],
                                            in_=den_sc[pi, 1].unsqueeze(0).broadcast_to([64, 512])), "nb%d" % pi)
            nbw[pi] = DW(lambda e, pi=pi: e.dma_start(out=nrm[64:128, pi * 512:pi * 512 + 512],
                                                      in_=den_sc[pi, 0].unsqueeze(0).broadcast_to([64, 512])), "nb%d" % pi)
            P.wait("vector", "nb%d" % pi, nbw[pi])
            dst = pr * 2048 + qc * 512
            V(lambda e, cps=cps, dst=dst, pi=pi: e.tensor_tensor(
                ctx_sb[0:64, dst:dst + 512], cps[0:64, 0:512], nrm[0:64, pi * 512:pi * 512 + 512], ALU.mult))
            evac_v[pi] = V(lambda e, cps=cps, dst=dst, pi=pi: e.tensor_tensor(
                ctx_sb[64:128, dst:dst + 512], cps[64:128, 512:1024], nrm[64:128, pi * 512:pi * 512 + 512], ALU.mult))

        def attn_av(info):
            cps, kk, off, vb0, st0, sp, pi = (info["cps"], info["kk"], info["off"],
                                              info["vb0"], info["st0"], info["sp"], info["pi"])
            P.wait("tensor", info["gate"][0], info["gate"][1])
            if st0:
                P.wait("tensor", "v", evac_v[pi])
            T(lambda e, cps=cps, vb0=vb0, kk=kk, off=off, st0=st0, sp=sp: e.matmul(
                cps[:, off:512], v_sb[:, vb0:vb0 + 128], p_sb[:, kk * 1024 + off:kk * 1024 + 512],
                start=st0, stop=sp))
            avT[kk] = T(lambda e, cps=cps, vb0=vb0, kk=kk, off=off, st0=st0, sp=sp: e.matmul(
                cps[:, 512 + off:1024], v_sb[:, vb0 + 32:vb0 + 160], p_sb[:, kk * 1024 + 512 + off:kk * 1024 + 1024],
                start=st0, stop=sp))
            if sp:
                attn_tail(pi, cps, info["qc"], info["pr"], avT[kk])

        steps = [(qc, pr, kt, 4 * qc + 4) for qc in range(4) for pr in range(8) for kt in range(4 * qc + 4)]
        pending = None
        for kap, (qc, pr, kt, nkt) in enumerate(steps):
            if p2q and kap % 3 == 0:
                p2q.pop(0)()
            pi = (qc * 8 + pr) % 2
            cps = ctx_pair[pi]
            kk = kap % 2
            scp = sc_pair[kk]
            diag = kt >= 4 * qc
            off = (kt - 4 * qc) * 128 if diag else 0
            kbase = (8 + pr) * 2048 + kt * 128
            qbase = pr * 2048 + qc * 512
            P.wait("tensor", "s", exp_s[kk])
            T(lambda e, scp=scp, kbase=kbase, qbase=qbase, off=off: e.matmul(
                scp[:, off:512], qkT[0:64, kbase:kbase + 128], qkT[0:64, qbase + off:qbase + 512],
                start=True, stop=True, tile_position=(0, 0)))
            tv = T(lambda e, scp=scp, kbase=kbase, qbase=qbase, off=off: e.matmul(
                scp[:, 512 + off:1024], qkT[64:128, kbase:kbase + 128], qkT[64:128, qbase + off:qbase + 512],
                start=True, stop=True, tile_position=(64, 0)))
            P.wait("scalar", "t", tv)
            P.wait("scalar", "t", avT[kk])
            exp_s[kk] = S(lambda e, scp=scp, kk=kk, off=off: e.activation(
                p_sb[:, kk * 1024:kk * 1024 + 1024].rearrange("p (h c) -> p h c", h=2)[:, :, off:512],
                scp[:].rearrange("p (h c) -> p h c", h=2)[:, :, off:512],
                AF.Exp, scale=0.125))
            if diag:
                P.wait("vector", "s", exp_s[kk])
                mv = V(lambda e, kk=kk, off=off: e.tensor_tensor(
                    p_sb[:, kk * 1024:kk * 1024 + 1024].rearrange("p (h c) -> p h c", h=2)[:, :, off:off + 128],
                    p_sb[:, kk * 1024:kk * 1024 + 1024].rearrange("p (h c) -> p h c", h=2)[:, :, off:off + 128],
                    tri_bf[:].unsqueeze(1).broadcast_to([128, 2, 128]), ALU.mult))
                gate = ("v", mv)
            else:
                gate = ("s", exp_s[kk])
            info = dict(cps=cps, kk=kk, off=off, vb0=kt * 1280 + pr * 160,
                        st0=(kt == 0), sp=(kt == nkt - 1), pi=pi, qc=qc, pr=pr, gate=gate)
            if pending is not None:
                attn_av(pending)
            pending = info
        attn_av(pending)
        while p2q:
            p2q.pop(0)()
        fence()
        rsq = []
        if b == 0:
            wq_restream(rsq)
        if debug and b == 0:
            DG(lambda e: e.dma_start(out=dbg["d_ctx"], in_=ctx_sb[:]), "gd")
            fence()

        if UPTO < 5:
            continue
        for dc in range(ND):
            wd = DW(lambda e, dc=dc: e.dma_start(out=wo_res[:, dc * 1024:dc * 1024 + 1024], in_=wo_sc[dc]), "gv")
        P.wait("tensor", "gv", wd)
        addv = {0: 0, 1: 0}
        accw = {0: 0, 1: 0}
        for st in range(NT):
            for _ in range(4):
                if rsq:
                    rsq.pop(0)()
            pp = st % 2
            pg = pg2 if pp == 0 else pg3
            P.wait("gpsimd", "v", addv[pp])
            xd = DG(lambda e, b=b, st=st, pp=pp: e.dma_start(
                out=xrb[:, pp * 1024:pp * 1024 + 1024], in_=x_d[b, st * 128:(st + 1) * 128, :]), "gx%d" % pp)
            P.wait("tensor", "v", addv[pp])
            for half in range(2):
                for dc in range(ND):
                    T(lambda e, pg=pg, half=half, dc=dc, st=st: e.matmul(
                        pg[:, half * 512:half * 512 + 512],
                        ctx_sb[:, dc * 2048 + st * 128: dc * 2048 + st * 128 + 128],
                        wo_res[:, dc * 1024 + half * 512: dc * 1024 + half * 512 + 512],
                        start=(dc == 0), stop=False))
                tv = T(lambda e, pg=pg, half=half: e.matmul(
                    pg[:, half * 512:half * 512 + 512], ones_bf[0:1, 0:128],
                    borow[0:1, half * 512:half * 512 + 512], start=False, stop=True))
            P.wait("vector", "t", tv)
            P.wait("vector", "gx%d" % pp, xd)
            P.wait("vector", "ga%d" % pp, accw[pp])
            addv[pp] = V(lambda e, pg=pg, pp=pp: e.tensor_tensor(
                acst[:, pp * 1024:pp * 1024 + 1024], pg[:], xrb[:, pp * 1024:pp * 1024 + 1024], ALU.add))
            P.wait("gpsimd", "v", addv[pp])
            accw[pp] = DG(lambda e, b=b, st=st, pp=pp: e.dma_start(
                out=acc_sc[b, st], in_=acst[:, pp * 1024:pp * 1024 + 1024]), "ga%d" % pp)
        while rsq:
            rsq.pop(0)()
        fence()
        if debug and b == 0:
            P.wait("gpsimd", "ga0", P.cnt["ga0"])
            P.wait("gpsimd", "ga1", P.cnt["ga1"])
            for st in range(NT):
                DG(lambda e, st=st: e.dma_start(out=dbg["d_acc"][st], in_=acc_sc[0, st]), "gd")
            fence()

        if UPTO < 6:
            continue
        P.op("scalar", lambda e: e.preload_act_table(AF.Silu))
        vv = Vd(lambda e: e.tensor_scalar(stat[:], gst[:], ccol[:, C_C1:C_C1 + 1], None, ALU.mult))
        P.wait("tensor", "v", vv)
        tv = T(lambda e: e.matmul(pg1[0:16, 0:128], stat[:], ident_f[:], start=True, stop=True))
        P.wait("vector", "t", tv)
        vv = V(lambda e: e.tensor_copy(nrm[0:16, 0:128], pg1[0:16, 0:128]))
        P.wait("gpsimd", "v", vv)
        gw = DG(lambda e: e.dma_start(out=row_sc[0:2048].rearrange("(r c) -> r c", r=16), in_=nrm[0:16, 0:128]), "gr")
        P.wait("gpsimd", "gr", gw)
        gw = DG(lambda e: e.dma_start(out=g_row[0:1, :], in_=row_sc[0:2048].unsqueeze(0)), "gr")
        P.wait("tensor", "gr", gw)
        gcp = {}
        for i in range(4):
            if i >= 2:
                P.wait("tensor", "v", gcp[i - 2])
            tv = T(lambda e, i=i: e.matmul(pg1[:, (i % 2) * 512:(i % 2) * 512 + 512],
                                           ones_f[0:1, 0:128], g_row[0:1, i * 512:i * 512 + 512],
                                           start=True, stop=True))
            P.wait("vector", "t", tv)
            gcp[i] = V(lambda e, i=i: e.tensor_copy(g_sb[:, i * 512:i * 512 + 512], pg1[:, (i % 2) * 512:(i % 2) * 512 + 512]))
        stash0 = V(lambda e: e.memset(stash[:], 0.0))
        if debug and b == 0:
            P.wait("gpsimd", "v", P.cnt["v"])
            DG(lambda e: e.dma_start(out=dbg["d_g"], in_=g_sb[:]), "gd")

        pmT_t = [0]
        w1T = {0: 0, 1: 0}
        w2T = {0: 0, 1: 0}
        w1ev = {0: 0, 1: 0, 2: 0, 3: 0}
        w2ev = {0: 0, 1: 0}
        w2last = {0: 0, 1: 0}
        uV = {k: stash0 for k in range(8)}
        uSS = {k: 0 for k in range(8)}
        pg1free_s = [0]
        pg1free_v = [0]
        pg1w = [-1]
        roundV_w = {-1: 0}
        roundS_w = {-1: 0}
        silu31 = {}
        accv = {0: 0, 1: 0}
        outw = {0: 0, 1: 0}
        sb_g = 0

        ub4 = [ubuf[0], ubuf[1], xc[:, 0:388], xc[:, 388:776]]

        def w1conv(c):
            have = c <= 7
            nonlocal_first = [True]
            for grp in range(8):
                hcs = [grp * 4 + j for j in range(4)]
                if have:
                    ppw = grp % 2
                    P.wait("sync", "t", w1T[ppw])
                    wd = DW(lambda e, grp=grp, ppw=ppw: e.dma_start(
                        out=w1b[:, ppw * 4096:ppw * 4096 + 4096].rearrange("p (h q) -> p h q", h=4),
                        in_=w1q_sc[grp * 4:grp * 4 + 4].rearrange("h p q -> p h q")), "pw%d" % ppw)
                    P.wait("tensor", "pw%d" % ppw, wd)
                for hc in hcs:
                    sl = hc % 4
                    P.wait("scalar", "v", uV[sl])
                    S(lambda e, sl=sl, hc=hc: e.copy(ub4[sl][:, 0:129], stash[:, hc * 129:hc * 129 + 129]))
                evc = {}
                for hc in hcs:
                    sl = hc % 4
                    if have:
                        pw = hc % 2
                        P.wait("tensor", "v", w1ev[pw])
                        for dc in range(ND):
                            tmm = T(lambda e, j=hc % 4, dc=dc, c=c, pw=pw, ppw=(hc // 4) % 2: e.matmul(
                                pg0[:, pw * 512:pw * 512 + 256],
                                w1b[:, ppw * 4096 + j * 1024 + dc * 128: ppw * 4096 + j * 1024 + dc * 128 + 128],
                                xiT[:, dc * 2048 + c * 256: dc * 2048 + c * 256 + 256],
                                start=(dc == 0), stop=(dc == ND - 1)))
                        w1T[(hc // 4) % 2] = tmm
                        P.wait("vector", "t", tmm)
                        P.wait("vector", "s", uSS[sl])
                        w1ev[pw] = V(lambda e, sl=sl, pw=pw, c=c: e.tensor_tensor(
                            ub4[sl][:, 129:385], pg0[:, pw * 512:pw * 512 + 256],
                            g_sb[:, c * 256:c * 256 + 256], ALU.mult))
                        evc[hc] = w1ev[pw]
                    else:
                        P.wait("vector", "s", uSS[sl])
                        evc[hc] = V(lambda e, sl=sl: e.memset(ub4[sl][:, 129:388], 0.0))
                inc_ = {}
                for hc in hcs:
                    sl = hc % 4
                    P.wait("scalar", "v", evc[hc])
                    if nonlocal_first[0]:
                        P.wait("scalar", "v", roundV_w[c - 1])
                        nonlocal_first[0] = False
                    inc_[hc] = S(lambda e, sl=sl, hc=hc: e.activation(
                        h2w[:, hc * 256:hc * 256 + 256], ub4[sl][:, 0:256], AF.Identity,
                        scale=cwsb[:, hc * 3:hc * 3 + 1], bias=cbsb[:, hc:hc + 1]))
                    if have:
                        uSS[sl] = S(lambda e, sl=sl, hc=hc: e.copy(
                            stash[:, hc * 129:hc * 129 + 129], ub4[sl][:, 256:385]))
                    else:
                        uSS[sl] = inc_[hc]
                for hc in hcs:
                    sl = hc % 4
                    P.wait("vector", "s", inc_[hc])
                    V(lambda e, sl=sl, hc=hc: e.scalar_tensor_tensor(
                        h2w[:, hc * 256:hc * 256 + 256], ub4[sl][:, 1:257], cwsb[:, hc * 3 + 1:hc * 3 + 2],
                        h2w[:, hc * 256:hc * 256 + 256], ALU.mult, ALU.add))
                    uV[sl] = V(lambda e, sl=sl, hc=hc: e.scalar_tensor_tensor(
                        h2w[:, hc * 256:hc * 256 + 256], ub4[sl][:, 2:258], cwsb[:, hc * 3 + 2:hc * 3 + 3],
                        h2w[:, hc * 256:hc * 256 + 256], ALU.mult, ALU.add))
                P.wait("scalar", "v", uV[hcs[-1] % 4])
                sl_ = S(lambda e, grp=grp: e.activation(
                    h2w[:, grp * 1024:grp * 1024 + 1024], h2w[:, grp * 1024:grp * 1024 + 1024], AF.Silu))
            silu31[c] = sl_

        def quantscale(cq):
            P.wait("vector", "s", silu31[cq])
            P.wait("vector", "t", pmT_t[0])
            pmv = V(lambda e: e.tensor_reduce(
                pmb[:, 0:256], h2w[:].rearrange("p (h t) -> p t h", h=32),
                AX.X, ALU.max, apply_absolute_value=True))
            P.wait("tensor", "v", pmv)
            P.wait("tensor", "s", P.cnt["s"])
            T(lambda e: e.matmul(pg1[:, 0:128], pmb[:, 0:128], ident_f[:], start=True, stop=True))
            tv = T(lambda e: e.matmul(pg1[:, 128:256], pmb[:, 128:256], ident_f[:], start=True, stop=True))
            pmT_t[0] = tv
            P.wait("vector", "t", tv)
            Vd(lambda e: e.tensor_reduce(
                mx[:, 0:2], pg1[:, 0:256].rearrange("p (w c) -> p w c", w=2), AX.X, ALU.max,
                apply_absolute_value=False))
            Vd(lambda e: e.tensor_scalar(mx[:, 2:4], mx[:, 0:2], 1e-5, None, ALU.max))
            Vd(lambda e: e.tensor_scalar(mx[:, 4:6], mx[:, 2:4], ccol[:, C_C2:C_C2 + 1], None, ALU.mult))
            Vd(lambda e: e.reciprocal(mx[:, 6:8], mx[:, 2:4]))
            vvs = Vd(lambda e: e.tensor_scalar(mx[:, 6:8], mx[:, 6:8], 127.0, None, ALU.mult))
            P.wait("tensor", "v", vvs)
            tv = T(lambda e: e.matmul(pg1[0:2, 256:384], mx[:, 6:8], ident_f[:], start=True, stop=True))
            P.wait("vector", "t", tv)
            vv = V(lambda e: e.tensor_copy(nrm[0:2, 0:128], pg1[0:2, 256:384]))
            pg1free_v[0] = vv
            P.wait("gpsimd", "v", vv)
            P.wait("gpsimd", "gr", P.cnt["gr"])
            sg = DG(lambda e: e.dma_start(out=row_sc[2048:2304].rearrange("(r c) -> r c", r=2), in_=nrm[0:2, 0:128]), "gr")
            P.wait("gpsimd", "gr", sg)
            sg = DG(lambda e: e.dma_start(out=sbrow[0:1, :], in_=row_sc[2048:2304].unsqueeze(0)), "gr")
            P.wait("tensor", "gr", sg)
            tso = T(lambda e: e.matmul(pg1[:, 512:768], ones_f[0:1, 0:128], sbrow[0:1, 0:256], start=True, stop=True))
            P.wait("scalar", "t", tso)
            sbs = S(lambda e: e.copy(sb_sb[:], pg1[:, 512:768]))
            pg1free_s[0] = sbs
            P.wait("vector", "s", sbs)
            qm = V(lambda e: e.tensor_tensor(
                h2w[:].rearrange("p (h t) -> p h t", h=32),
                h2w[:].rearrange("p (h t) -> p h t", h=32),
                sb_sb[:].unsqueeze(1).broadcast_to([128, 32, 256]), ALU.mult))
            P.wait("vector", "t", max(w2T[0], w2T[1]))
            P.wait("scalar", "t", max(w2T[0], w2T[1]))
            P.wait("scalar", "v", qm)
            roundV_w[cq] = V(lambda e: e.tensor_scalar(h2q[:, 0:4096], h2w[:, 0:4096], MAGIC, -MAGIC, ALU.add, ALU.add))
            for k in range(4):
                S(lambda e, k=k: e.activation(xbuf[:, 0:1024], h2w[:, 4096 + k * 1024:4096 + k * 1024 + 1024],
                                              AF.Identity, bias=ccol[:, C_MAG:C_MAG + 1]))
                rS = S(lambda e, k=k: e.activation(h2q[:, 4096 + k * 1024:4096 + k * 1024 + 1024], xbuf[:, 0:1024],
                                                   AF.Identity, bias=ccol[:, C_NMAG:C_NMAG + 1]))
            roundS_w[cq] = rS

        def w2block(cq, b=b):
            stl = [t_ for t_ in (2 * cq - 1, 2 * cq) if 0 <= t_ < 16]
            accd = {}
            for st_out in stl:
                par = st_out % 2
                P.wait("gpsimd", "v", accv[par])
                accd[st_out] = DG(lambda e, b=b, st_out=st_out, par=par: e.dma_start(
                    out=accb[:, par * 1024:par * 1024 + 1024], in_=acc_sc[b, st_out]), "ga%d" % par)
            for grp in range(8):
                pp2 = grp % 2
                P.wait("sync", "t", w2T[pp2])
                wd2 = DW(lambda e, grp=grp, pp2=pp2: e.dma_start(
                    out=w2b[:, pp2 * 4096:pp2 * 4096 + 4096].rearrange("p (h q) -> p h q", h=4),
                    in_=w2q_sc[grp * 4:grp * 4 + 4].rearrange("h p q -> p h q")), "qw%d" % pp2)
                P.wait("tensor", "qw%d" % pp2, wd2)
                if grp == 0:
                    P.wait("tensor", "v", roundV_w[cq])
                    P.wait("tensor", "s", roundS_w[cq])
                for j in range(4):
                    hc = grp * 4 + j
                    for st_out in stl:
                        w = st_out - (2 * cq - 1)
                        par = st_out % 2
                        pgw = pg2 if par == 0 else pg3
                        if hc == 0:
                            P.wait("tensor", "v", w2ev[par])
                        for half in range(2):
                            tmm = T(lambda e, pgw=pgw, hc=hc, w=w, pp2=pp2, j=j, half=half: e.matmul(
                                pgw[:, half * 512:half * 512 + 512],
                                h2q[:, hc * 256 + w * 128: hc * 256 + w * 128 + 128],
                                w2b[:, pp2 * 4096 + j * 1024 + half * 512: pp2 * 4096 + j * 1024 + half * 512 + 512],
                                start=(hc == 0), stop=(hc == NHC - 1)))
                        w2T[pp2] = tmm
                        if hc == NHC - 1:
                            w2last[par] = tmm
            for st_out in stl:
                w = st_out - (2 * cq - 1)
                par = st_out % 2
                pgw = pg2 if par == 0 else pg3
                P.wait("vector", "ga%d" % par, accd[st_out])
                P.wait("vector", "t", w2last[par])
                P.wait("vector", "go%d" % par, outw[par])
                accv[par] = V(lambda e, pgw=pgw, par=par, w=w: e.scalar_tensor_tensor(
                    outb[:, par * 1024:par * 1024 + 1024], pgw[:], mx[:, 4 + w:5 + w],
                    accb[:, par * 1024:par * 1024 + 1024], ALU.mult, ALU.add))
                w2ev[par] = accv[par]
                P.wait("gpsimd", "v", accv[par])
                outw[par] = DG(lambda e, b=b, st_out=st_out, par=par: e.dma_start(
                    out=out_d[b, st_out * 128:(st_out + 1) * 128, :], in_=outb[:, par * 1024:par * 1024 + 1024]), "go%d" % par)

        for c in range(FFNW):
            if 1 <= c:
                quantscale(c - 1)
            if c <= 8:
                w1conv(c)
            if 1 <= c:
                w2block(c - 1)
        fence()

    for s_ in list(P.cnt.keys()):
        if s_ not in ("t", "s", "v") and P.cnt[s_] > 0:
            P.wait("gpsimd", s_, P.cnt[s_])

    from contextlib import ExitStack
    with ExitStack() as stack:
        sems = {}
        for s_ in list(P.cnt.keys()):
            if P.cnt[s_] > 0:
                sems[s_] = stack.enter_context(nc.semaphore("sem_" + s_))
        block = stack.enter_context(nc.Block())

        @block.sync
        def _(sync):
            P.replay("sync", sync, sems)

        @block.gpsimd
        def _(gpsimd):
            P.replay("gpsimd", gpsimd, sems)

        @block.tensor
        def _(tensor):
            P.replay("tensor", tensor, sems)

        @block.scalar
        def _(scalar):
            P.replay("scalar", scalar, sems)

        @block.vector
        def _(vector):
            P.replay("vector", vector, sems)

    return nc


_NC_CACHE = {}


def _get_nc(debug=False):
    key = (debug, UPTO, FFNW, W1MODE)
    if key not in _NC_CACHE:
        _NC_CACHE[key] = build(debug)
    return _NC_CACHE[key]


def kernel(L0_attn_hidden_states, L0_attn_W_attn, L0_attn_b_attn, L0_attn_W_out,
           L0_attn_b_out, L0_ffn_w1, L0_ffn_conv_w, L0_ffn_conv_b, L0_ffn_w2,
           _debug=False):
    x = np.ascontiguousarray(L0_attn_hidden_states, dtype=np.float32)
    shards = x.reshape(8, NB, 2048, 1024)
    nc = _get_nc(_debug)
    common = {
        "W_attn": np.ascontiguousarray(L0_attn_W_attn, dtype=np.float32),
        "b_attn": np.ascontiguousarray(L0_attn_b_attn, dtype=np.float32),
        "W_out": np.ascontiguousarray(L0_attn_W_out, dtype=np.float32),
        "b_out": np.ascontiguousarray(L0_attn_b_out, dtype=np.float32),
        "w1": np.ascontiguousarray(L0_ffn_w1, dtype=np.float32),
        "cw": np.ascontiguousarray(L0_ffn_conv_w, dtype=np.float32).reshape(4096, 3),
        "cb": np.ascontiguousarray(L0_ffn_conv_b, dtype=np.float32),
        "w2": np.ascontiguousarray(L0_ffn_w2, dtype=np.float32),
        "ident_bf": np.eye(128, dtype=ml_dtypes.bfloat16),
        "ident_f": np.eye(128, dtype=np.float32),
        "tri_bf": np.triu(np.ones((128, 128), dtype=np.float32)).astype(ml_dtypes.bfloat16),
    }
    in_maps = [dict(common, x=np.ascontiguousarray(shards[i])) for i in range(8)]
    res = run_bass_kernel_spmd(nc, in_maps, core_ids=list(range(8)))
    out = np.concatenate([r["out"].reshape(NB, 2048, 1024) for r in res.results], axis=0)
    if _debug:
        return out, res
    return out


# version 17
# speedup vs baseline: 1.0642x; 1.0045x over previous
import numpy as np
import ml_dtypes
import concourse.bass as bass
import concourse.mybir as mybir
from concourse.bass_utils import run_bass_kernel_spmd

F32 = mybir.dt.float32
BF16 = mybir.dt.bfloat16
AF = mybir.ActivationFunctionType
ALU = mybir.AluOpType
AX = mybir.AxisListType

MAGIC = 12582912.0
UPTO = 6
FFNW = 10
W1MODE = 0
NB = 2
NT = 16
ND = 8
NHC = 32

ENGS = ("sync", "gpsimd", "tensor", "scalar", "vector")


class Plan:
    def __init__(self):
        from collections import defaultdict
        self.ops = {e: [] for e in ENGS}
        self.cnt = defaultdict(int)

    def inc(self, eng, fn, sem, by=1):
        self.ops[eng].append(("inc", fn, sem, by))
        self.cnt[sem] += by
        return self.cnt[sem]

    def op(self, eng, fn):
        self.ops[eng].append(("op", fn))

    def wait(self, eng, sem, val):
        if val > 0:
            self.ops[eng].append(("wait", sem, val))

    def replay(self, eng_name, obj, sems):
        for it in self.ops[eng_name]:
            if it[0] == "op":
                it[1](obj)
            elif it[0] == "inc":
                it[1](obj).then_inc(sems[it[2]], it[3])
            else:
                obj.wait_ge(sems[it[1]], it[2])


def build(debug=False):
    nc = bass.Bass("TRN2", target_bir_lowering=False)
    P = Plan()

    def T(fn):
        return P.inc("tensor", fn, "t")

    def S(fn):
        return P.inc("scalar", fn, "s")

    def V(fn):
        return P.inc("vector", fn, "v")

    def Vd(fn):
        n = P.inc("vector", fn, "v")
        P.wait("vector", "v", n)
        return n

    def DW(fn, sem):
        return P.inc("sync", fn, sem, 16)

    def DG(fn, sem):
        return P.inc("gpsimd", fn, sem, 16)

    SYNC_SEMS = ("pw0", "pw1", "qw0", "qw1", "gv", "wr", "sg0", "sg1", "nb0", "nb1")
    GP_SEMS = ("gi", "gx0", "gx1", "ga0", "ga1", "go0", "go1", "gr", "gd", "mb", "p1a", "p1b", "p2a", "p2b")
    SYNC_SEMS = SYNC_SEMS + ("tr0", "tr1")

    def fence():
        from collections import defaultdict
        c = defaultdict(int, P.cnt)
        dma = [s for s in list(SYNC_SEMS) + list(GP_SEMS) if c.get(s, 0) > 0]
        P.wait("tensor", "s", c["s"]); P.wait("tensor", "v", c["v"])
        P.wait("scalar", "t", c["t"]); P.wait("scalar", "v", c["v"])
        P.wait("vector", "t", c["t"]); P.wait("vector", "s", c["s"])
        for q in ("sync", "gpsimd"):
            P.wait(q, "t", c["t"]); P.wait(q, "s", c["s"]); P.wait(q, "v", c["v"])
            for s in dma:
                P.wait(q, s, c[s])
        for e in ("tensor", "scalar", "vector"):
            for s in dma:
                P.wait(e, s, c[s])

    x_d = nc.dram_tensor("x", [NB, 2048, 1024], F32, kind="ExternalInput").ap()
    wattn_d = nc.dram_tensor("W_attn", [1024, 3072], F32, kind="ExternalInput").ap()
    battn_d = nc.dram_tensor("b_attn", [3072], F32, kind="ExternalInput").ap()
    wout_d = nc.dram_tensor("W_out", [1024, 1024], F32, kind="ExternalInput").ap()
    bout_d = nc.dram_tensor("b_out", [1024], F32, kind="ExternalInput").ap()
    w1_d = nc.dram_tensor("w1", [4096, 1024], F32, kind="ExternalInput").ap()
    cw_d = nc.dram_tensor("cw", [4096, 3], F32, kind="ExternalInput").ap()
    cb_d = nc.dram_tensor("cb", [4096], F32, kind="ExternalInput").ap()
    w2_d = nc.dram_tensor("w2", [1024, 4096], F32, kind="ExternalInput").ap()
    identbf_d = nc.dram_tensor("ident_bf", [128, 128], BF16, kind="ExternalInput").ap()
    identf_d = nc.dram_tensor("ident_f", [128, 128], F32, kind="ExternalInput").ap()
    tribf_d = nc.dram_tensor("tri_bf", [128, 128], BF16, kind="ExternalInput").ap()
    out_d = nc.dram_tensor("out", [NB, 2048, 1024], F32, kind="ExternalOutput").ap()

    wqk_sc = nc.dram_tensor("wqk_sc", [16, 128, 1024], BF16, kind="Internal").ap()
    wv_sc = nc.dram_tensor("wv_sc", [2, 8, 128, 640], BF16, kind="Internal").ap()
    wo_sc = nc.dram_tensor("wo_sc", [8, 128, 1024], BF16, kind="Internal").ap()
    w1q_sc = nc.dram_tensor("w1q_sc", [32, 128, 1024], BF16, kind="Internal").ap()
    w2q_sc = nc.dram_tensor("w2q_sc", [32, 128, 1024], BF16, kind="Internal").ap()
    acc_sc = nc.dram_tensor("acc_sc", [NB, 16, 128, 1024], F32, kind="Internal").ap()
    row_sc = nc.dram_tensor("row_sc", [4096], F32, kind="Internal").ap()
    den_sc = nc.dram_tensor("den_sc", [2, 2, 512], F32, kind="Internal").ap()
    trit_sc = nc.dram_tensor("trit_sc", [64, 128, 1024], BF16, kind="Internal").ap()
    mean_sc = nc.dram_tensor("mean_sc", [256], F32, kind="Internal").ap()

    dbg = {}
    if debug:
        for nm, shp, dt in [("d_xnT", [128, 16384], BF16), ("d_xiT", [128, 16384], BF16),
                            ("d_qkT", [128, 32768], BF16), ("d_vsb", [128, 20480], BF16),
                            ("d_ctx", [128, 16384], BF16), ("d_g", [128, 2048], F32),
                            ("d_acc", [16, 128, 1024], F32),
                            ("d_w1q", [32, 128, 1024], BF16), ("d_w2q", [32, 128, 1024], BF16),
                            ("d_ccol", [128, 16], F32)]:
            dbg[nm] = nc.dram_tensor(nm, shp, dt, kind="ExternalOutput").ap()

    def sbt(name, shape, dtype, off):
        return nc.alloc_sbuf_tensor_at(name, shape, dtype, offset=off).ap()

    cur = [16512]

    def nxt(sz, c=cur):
        r = c[0]
        c[0] = (c[0] + sz + 31) & ~31
        return r

    ident_bf = sbt("identbf", [128, 128], BF16, nxt(256))
    ones_bf = sbt("onesbf", [128, 128], BF16, nxt(256))
    tri_bf = sbt("tribf", [128, 128], BF16, nxt(256))
    ident_f = sbt("identf", [128, 128], F32, nxt(512))
    ones_f = sbt("onesf", [128, 128], F32, nxt(512))
    ccol = sbt("ccol", [128, 16], F32, nxt(64))
    C_MAG, C_NMAG, C_EPS, C_C1, C_C2, C_SW1, C_SW2 = 0, 1, 2, 3, 4, 5, 6
    bcol = sbt("bcol", [128, 16], F32, nxt(64))
    cwsb = sbt("cwsb", [128, 96], F32, nxt(384))
    cbsb = sbt("cbsb", [128, 32], F32, nxt(128))
    stat = sbt("stat", [128, 16], F32, nxt(64))
    gst = sbt("gst", [128, 16], F32, nxt(64))
    bvrow = sbt("bvrow", [1, 1280], BF16, nxt(2560))
    borow = sbt("borow", [1, 1024], BF16, nxt(2048))
    nrm = sbt("nrm", [128, 1024], F32, nxt(4096))
    rows_sb = sbt("rowssb", [128, 1024], F32, nxt(4096))

    R0 = (cur[0] + 63) & ~63
    xiT = sbt("xiT", [128, 16384], BF16, R0)
    XNC = R0 + 32768
    xnT = sbt("xnT", [128, 16384], BF16, XNC)
    ctx_sb = sbt("ctxsb", [128, 16384], BF16, XNC)
    h2w = sbt("h2w", [128, 8192], F32, XNC)
    QK = XNC + 32768
    qkT = sbt("qkT", [128, 32768], BF16, QK)
    VSB = QK + 65536
    v_sb = sbt("vsb", [128, 20480], BF16, VSB)
    AQ = VSB + 40960
    xbuf = sbt("xbuf", [128, 2048], F32, AQ)
    xc = sbt("xc", [128, 1024], F32, AQ + 8192)
    xt1 = sbt("xt1", [128, 1024], F32, AQ + 12288)
    xn_t = sbt("xnt", [128, 1024], BF16, AQ + 16384)
    xi_t = sbt("xit", [128, 1024], BF16, AQ + 18432)
    wqkb = sbt("wqkb", [128, 2048], BF16, AQ)
    wvb = sbt("wvb", [128, 5120], BF16, AQ + 4096)
    p_sb = sbt("psb", [128, 2048], BF16, AQ)
    aqld = sbt("aqld", [128, 2048], F32, AQ + 4096)
    aqtr = sbt("aqtr", [128, 2048], BF16, AQ + 12288)
    aqsg = sbt("aqsg", [128, 2048], BF16, AQ + 16384)
    SB_END = AQ + 20480

    c0 = [QK]
    ws0 = sbt("ws0", [128, 4096], F32, nxt(16384, c0))
    ws1 = sbt("ws1", [128, 4096], F32, nxt(16384, c0))
    wcast = sbt("wcast", [128, 4096], BF16, nxt(8192, c0))
    tqtmp = sbt("tqtmp", [128, 4096], F32, nxt(16384, c0))
    tritb = sbt("tritb", [128, 4096], BF16, nxt(8192, c0))
    stg = [sbt("stg0", [128, 1024], BF16, nxt(2048, c0)),
           sbt("stg1", [128, 1024], BF16, nxt(2048, c0))]
    zstage = sbt("zstage", [128, 128], BF16, nxt(256, c0))
    ws = [ws0, ws1]

    wo_res = sbt("wores", [128, 8192], BF16, QK)
    xrb = sbt("xrb", [128, 2048], F32, QK + 16384)
    acst = sbt("acst", [128, 2048], F32, QK + 24576)

    c1 = [QK]
    g_sb = sbt("gsb", [128, 2048], F32, nxt(8192, c1))
    g_row = sbt("grow", [1, 2048], F32, nxt(8192, c1))
    stash = sbt("stash", [128, 4128], F32, nxt(16512, c1))
    ubuf = [sbt("u0", [128, 388], F32, nxt(1552, c1)),
            sbt("u1", [128, 388], F32, nxt(1552, c1))]
    h2q = sbt("h2q", [128, 8192], BF16, nxt(16384, c1))
    w1b = sbt("w1b", [128, 8192], BF16, nxt(16384, c1))
    w2b = sbt("w2b", [128, 8192], BF16, nxt(16384, c1))
    accb = sbt("accb", [128, 2048], F32, nxt(8192, c1))
    outb = sbt("outb", [128, 2048], F32, nxt(8192, c1))
    mx = sbt("mx", [128, 16], F32, nxt(64, c1))
    sbrow = sbt("sbrow", [1, 256], F32, nxt(1024, c1))
    sb_sb = sbt("sbsb", [128, 256], F32, nxt(1024, c1))
    pmb = sbt("pmb", [128, 512], F32, nxt(2048, c1))
    assert c1[0] <= VSB + 40960, c1[0]

    pg0 = nc.alloc_psum_tensor("pg0", [128, 1024], F32).ap()
    pg1 = nc.alloc_psum_tensor("pg1", [128, 1024], F32).ap()
    pg2 = nc.alloc_psum_tensor("pg2", [128, 1024], F32).ap()
    pg3 = nc.alloc_psum_tensor("pg3", [128, 1024], F32).ap()

    V(lambda e: e.memset(ones_f[:], 1.0))
    V(lambda e: e.memset(ones_bf[:], 1.0))
    V(lambda e: e.memset(ccol[:, C_MAG:C_MAG + 1], MAGIC))
    V(lambda e: e.memset(ccol[:, C_NMAG:C_NMAG + 1], -MAGIC))
    V(lambda e: e.memset(ccol[:, C_EPS:C_EPS + 1], 1e-5))
    DG(lambda e: e.dma_start(out=ident_bf[:], in_=identbf_d), "gi")
    DG(lambda e: e.dma_start(out=ident_f[:], in_=identf_d), "gi")
    DG(lambda e: e.dma_start(out=tri_bf[:], in_=tribf_d), "gi")
    DG(lambda e: e.dma_start(out=cwsb[:].rearrange("p (h j) -> p h j", h=32),
                             in_=cw_d.rearrange("(h p) j -> p h j", p=128)), "gi")
    DG(lambda e: e.dma_start(out=cbsb[:], in_=cb_d.rearrange("(h p) -> p h", p=128),
                             allow_slow_non_contiguous=True), "gi")
    DG(lambda e: e.dma_start(out=bcol[:], in_=battn_d[0:2048].rearrange("(c p) -> p c", p=128),
                             allow_slow_non_contiguous=True), "gi")
    DG(lambda e: e.dma_start(out=nrm[0:1, :], in_=battn_d[2048:3072].unsqueeze(0)), "gi")
    gg = DG(lambda e: e.dma_start(out=xc[0:1, :], in_=bout_d.unsqueeze(0)), "gi")
    P.wait("vector", "gi", gg)
    P.wait("tensor", "gi", gg)
    P.wait("scalar", "gi", gg)
    V(lambda e: e.memset(bvrow[:], 0.0))
    V(lambda e: e.memset(bvrow[:].rearrange("o (q c) -> o q c", c=160)[:, :, 64:66], 1.0))
    V(lambda e: e.tensor_copy(bvrow[:].rearrange("o (q c) -> o q c", c=160)[:, :, 0:64],
                              nrm[0:1, :].rearrange("o (h c) -> o h c", c=128)[:, :, 0:64]))
    V(lambda e: e.tensor_copy(bvrow[:].rearrange("o (q c) -> o q c", c=160)[:, :, 96:160],
                              nrm[0:1, :].rearrange("o (h c) -> o h c", c=128)[:, :, 64:128]))
    V(lambda e: e.memset(zstage[:], 0.0))
    V(lambda e: e.tensor_copy(borow[:], xc[0:1, :]))
    P.op("scalar", lambda e: e.preload_act_table(AF.Sqrt))

    cast_v = {}
    wr_w = 0
    for dc in range(ND):
        if dc >= 2:
            P.wait("sync", "v", cast_v[dc - 2])
        wt = DW(lambda e, dc=dc, b=ws[dc % 2]: e.dma_start(out=b[:, 0:3072], in_=wattn_d[dc * 128:(dc + 1) * 128, :]), "pw%d" % (dc % 2))
        P.wait("vector", "pw%d" % (dc % 2), wt)
        P.wait("vector", "wr", P.cnt["wr"])
        cast_v[dc] = V(lambda e, b=ws[dc % 2]: e.tensor_copy(wcast[:, 0:3072], b[:, 0:3072]))
        P.wait("sync", "v", cast_v[dc])
        for ct in range(16):
            DW(lambda e, ct=ct, dc=dc: e.dma_start(out=wqk_sc[ct, :, dc * 128:(dc + 1) * 128],
                                                   in_=wcast[:, ct * 128:(ct + 1) * 128]), "wr")
        for hf in range(2):
            DW(lambda e, hf=hf, dc=dc: e.dma_start(
                out=wv_sc[hf, dc].rearrange("p (q c) -> p q c", c=160)[:, :, 64:96],
                in_=zstage[:].rearrange("p (q c) -> p q c", c=32),
                allow_slow_non_contiguous=True), "wr")
            DW(lambda e, hf=hf, dc=dc: e.dma_start(
                out=wv_sc[hf, dc].rearrange("p (q c) -> p q c", c=160)[:, :, 0:64],
                in_=wcast[:, 2048 + hf * 512: 2048 + hf * 512 + 512].rearrange("p (q c) -> p q c", c=128)[:, :, 0:64],
                allow_slow_non_contiguous=True), "wr")
            wr_w = DW(lambda e, hf=hf, dc=dc: e.dma_start(
                out=wv_sc[hf, dc].rearrange("p (q c) -> p q c", c=160)[:, :, 96:160],
                in_=wcast[:, 2048 + hf * 512: 2048 + hf * 512 + 512].rearrange("p (q c) -> p q c", c=128)[:, :, 64:128],
                allow_slow_non_contiguous=True), "wr")
    wrh = {0: 0, 1: 0}
    for dc in range(ND):
        pp = dc % 2
        if dc >= 2:
            P.wait("sync", "v", cast_v[8 + dc - 2])
        wt = DW(lambda e, dc=dc, b=ws[dc % 2]: e.dma_start(out=b[:, 0:1024], in_=wout_d[dc * 128:(dc + 1) * 128, :]), "pw%d" % (dc % 2))
        P.wait("vector", "pw%d" % (dc % 2), wt)
        P.wait("vector", "wr", P.cnt["wr"])
        cast_v[8 + dc] = V(lambda e, b=ws[dc % 2], pp=pp: e.tensor_copy(wcast[:, pp * 1024:pp * 1024 + 1024], b[:, 0:1024]))
        P.wait("sync", "v", cast_v[8 + dc])
        wrh[pp] = DW(lambda e, dc=dc, pp=pp: e.dma_start(out=wo_sc[dc], in_=wcast[:, pp * 1024:pp * 1024 + 1024]), "wr")

    wq_state = {}

    def wq_pass1(w_d, ntiles, tilecols, swcol, c1col, tag):
        acm = stat[:, 0:1]
        tmpc = stat[:, 1:2]
        nch = ntiles * tilecols // 1024
        sem = "p1a" if tag == 0 else "p1b"
        prevS = {0: 0, 1: 0}
        prevV = 0
        for i in range(nch):
            if i >= 2:
                P.wait("gpsimd", "s", prevS[i % 2])
            r, cc = (i * 1024) // tilecols, (i * 1024) % tilecols
            wt = DG(lambda e, r=r, cc=cc, sl=(i % 2): e.dma_start(
                out=aqld[:, sl * 1024:sl * 1024 + 1024], in_=w_d[r * 128:(r + 1) * 128, cc:cc + 1024]), sem)
            P.wait("scalar", sem, wt)
            if i == 0:
                prevS[0] = S(lambda e: e.activation(rows_sb[:], aqld[:, 0:1024], AF.Abs, accum_out=acm))
                prevV = P.cnt["v"]
            else:
                P.wait("scalar", "v", prevV)
                prevS[i % 2] = S(lambda e, sl=(i % 2): e.activation(
                    rows_sb[:], aqld[:, sl * 1024:sl * 1024 + 1024], AF.Abs, accum_out=tmpc))
                P.wait("vector", "s", prevS[i % 2])
                prevV = Vd(lambda e: e.tensor_tensor(acm, acm, tmpc, ALU.add))
        P.wait("gpsimd", "v", prevV)
        P.wait("gpsimd", "s", max(prevS[0], prevS[1]))
        mb = DG(lambda e: e.dma_start(out=mean_sc[0:128].unsqueeze(0).rearrange("o c -> c o"), in_=acm), "mb")
        P.wait("gpsimd", "mb", mb)
        mb = DG(lambda e: e.dma_start(out=nrm[0:1, 0:128], in_=mean_sc[0:128].unsqueeze(0)), "mb")
        P.wait("vector", "mb", mb)
        Vd(lambda e: e.reduce_sum(stat[0:1, 2:3], nrm[0:1, 0:128], AX.X))
        Vd(lambda e, n=float(ntiles * 128 * tilecols): e.tensor_scalar(stat[0:1, 2:3], stat[0:1, 2:3], 1.0 / n, None, ALU.mult))
        Vd(lambda e: e.tensor_scalar(stat[0:1, 2:3], stat[0:1, 2:3], 1e-5, None, ALU.max))
        vv = Vd(lambda e: e.reciprocal(stat[0:1, 3:4], stat[0:1, 2:3]))
        P.wait("gpsimd", "v", vv)
        mb = DG(lambda e: e.dma_start(out=mean_sc[128:130].unsqueeze(0), in_=stat[0:1, 2:4]), "mb")
        P.wait("gpsimd", "mb", mb)
        DG(lambda e: e.dma_start(out=stat[:, 4:5], in_=mean_sc[128:129].unsqueeze(0).broadcast_to([128, 1])), "mb")
        mb = DG(lambda e, c=swcol: e.dma_start(out=ccol[:, c:c + 1], in_=mean_sc[129:130].unsqueeze(0).broadcast_to([128, 1])), "mb")
        P.wait("vector", "mb", mb)
        vv = Vd(lambda e, c=c1col: e.tensor_scalar(ccol[:, c:c + 1], stat[:, 4:5], 1.0 / 127.0, None, ALU.mult))
        wq_state[("p1", tag)] = (vv, P.cnt[sem], P.cnt["mb"])

    def wq_pass2(w_d, ntiles, tilecols, swcol, base_tile, tag, gate_t, emit_q):
        nch = ntiles * tilecols // 1024
        sem = "p2a" if tag == 0 else "p2b"
        qs = {0: 0, 1: 0}
        vts = {0: 0, 1: 0}
        stw = {0: 0, 1: 0}

        def chunk0_pre():
            vv, _, mbtot = wq_state[("p1", tag)]
            P.wait("scalar", "v", vv)
            P.wait("scalar", "mb", mbtot)
            P.wait("gpsimd", "t", gate_t)
        emit_q.append(chunk0_pre)
        def mk_chunk(i):
            def emit_chunk():
                sl = i % 2
                if i >= 1:
                    P.wait("gpsimd", "s", qs[0])
                r, cc = (i * 1024) // tilecols, (i * 1024) % tilecols
                wt = DG(lambda e, r=r, cc=cc: e.dma_start(
                    out=aqld[:, 0:1024], in_=w_d[r * 128:(r + 1) * 128, cc:cc + 1024]), sem)
                P.wait("scalar", sem, wt)
                if i >= 1:
                    P.wait("scalar", "v", vts[1 - sl])
                qs[0] = S(lambda e, c=swcol: e.activation(
                    aqld[:, 1024:2048], aqld[:, 0:1024], AF.Identity,
                    scale=ccol[:, c:c + 1], bias=ccol[:, C_MAG:C_MAG + 1]))
                P.wait("vector", "s", qs[0])
                P.wait("vector", "sg0", stw[sl])
                V(lambda e: e.tensor_scalar(
                    aqld[:, 1024:2048], aqld[:, 1024:2048], -MAGIC, 1.0, ALU.add, ALU.min))
                vts[sl] = V(lambda e, sl=sl: e.tensor_scalar(
                    aqtr[:, sl * 1024:sl * 1024 + 1024], aqld[:, 1024:2048], -1.0, None, ALU.max))
                P.wait("gpsimd", "v", vts[sl])
                stw[sl] = DG(lambda e, i=i, sl=sl, bt=base_tile: e.dma_start(
                    out=trit_sc[bt + i], in_=aqtr[:, sl * 1024:sl * 1024 + 1024]), "sg0")
            return emit_chunk
        for i in range(nch):
            emit_q.append(mk_chunk(i))

    def wq_restream(emit_q):
        trT4 = {0: 0, 1: 0, 2: 0, 3: 0}
        trev = {0: 0, 1: 0}
        trst = {0: 0, 1: 0}
        emit_q.append(lambda: P.wait("sync", "sg0", P.cnt["sg0"]))

        def mk_tile(i):
            def emit_tile():
                pp = i % 2
                p4 = i % 4
                tb = (aqtr[:, (p4 % 2) * 1024:(p4 % 2) * 1024 + 1024] if p4 < 2
                      else p_sb[:, (p4 % 2) * 1024:(p4 % 2) * 1024 + 1024])
                sg = aqsg[:, pp * 1024:pp * 1024 + 1024]
                P.wait("sync", "t", trT4[p4])
                if i < 4:
                    P.wait("sync", "v", P.cnt["v"])
                    P.wait("sync", "s", P.cnt["s"])
                wd = DW(lambda e, i=i, tb=tb: e.dma_start(out=tb, in_=trit_sc[i]), "tr%d" % pp)
                P.wait("tensor", "tr%d" % pp, wd)
                P.wait("tensor", "v", trev[pp])
                pgt = pg0 if pp == 0 else pg1
                for dc in range(ND):
                    tv = T(lambda e, dc=dc, tb=tb, pgt=pgt: e.matmul(
                        pgt[:, dc * 128:dc * 128 + 128], tb[:, dc * 128:(dc + 1) * 128],
                        ident_bf[:], start=True, stop=True))
                trT4[p4] = tv
                P.wait("vector", "t", tv)
                P.wait("vector", "sg1", trst[pp])
                trev[pp] = V(lambda e, sg=sg, pgt=pgt: e.tensor_copy(sg, pgt[:]))
                P.wait("sync", "v", trev[pp])
                if i < 32:
                    trst[pp] = DW(lambda e, i=i, sg=sg: e.dma_start(out=w1q_sc[i], in_=sg), "sg1")
                else:
                    dt, q = (i - 32) // 4, (i - 32) % 4
                    for j in range(8):
                        trst[pp] = DW(lambda e, dt=dt, q=q, j=j, sg=sg: e.dma_start(
                            out=w2q_sc[q * 8 + j, :, dt * 128:(dt + 1) * 128],
                            in_=sg[:, j * 128:(j + 1) * 128]), "sg1")
            return emit_tile
        for i in range(64):
            emit_q.append(mk_tile(i))

    wout_cast_done = P.cnt["v"]
    fence()

    for b in range(NB if UPTO >= 2 else 0):
        P.op("scalar", lambda e: e.preload_act_table(AF.Sqrt))
        lastS = {0: 0, 1: 0}
        lastV = {0: 0, 1: 0}
        evT = {0: 0, 1: 0}
        for st in range(NT):
            pp = st % 2
            P.wait("gpsimd", "s", lastS[pp])
            P.wait("gpsimd", "v", lastV[pp])
            xd = DG(lambda e, b=b, st=st, pp=pp: e.dma_start(
                out=xbuf[:, pp * 1024:pp * 1024 + 1024], in_=x_d[b, st * 128:(st + 1) * 128, :]), "gx%d" % pp)
            xs = xbuf[:, pp * 1024:pp * 1024 + 1024]
            sc = stat[:, pp * 8: pp * 8 + 8]
            vprev = P.cnt["v"]
            P.wait("vector", "gx%d" % pp, xd)
            lastV[pp] = Vd(lambda e, xs=xs, sc=sc: e.reduce_sum(sc[:, 0:1], xs, AX.X))
            P.wait("scalar", "gx%d" % pp, xd)
            P.wait("scalar", "v", vprev)
            S(lambda e, xs=xs, sc=sc: e.activation(xc[:], xs, AF.Square, accum_out=sc[:, 1:2]))
            P.wait("vector", "s", P.cnt["s"])
            Vd(lambda e, sc=sc: e.tensor_scalar(sc[:, 2:3], sc[:, 0:1], -1.0 / 1024.0, None, ALU.mult))
            Vd(lambda e, sc=sc: e.tensor_tensor(sc[:, 3:4], sc[:, 2:3], sc[:, 2:3], ALU.mult))
            vv = Vd(lambda e, sc=sc: e.scalar_tensor_tensor(sc[:, 4:5], sc[:, 1:2], 1.0 / 1024.0, sc[:, 3:4], ALU.mult, ALU.subtract))
            P.wait("scalar", "v", vv)
            sv = S(lambda e, sc=sc: e.activation(sc[:, 4:5], sc[:, 4:5], AF.Sqrt, bias=ccol[:, C_EPS:C_EPS + 1]))
            P.wait("vector", "s", sv)
            Vd(lambda e, sc=sc: e.reciprocal(sc[:, 5:6], sc[:, 4:5]))
            sv = S(lambda e, xs=xs, sc=sc: e.activation(xc[:], xs, AF.Identity, bias=sc[:, 2:3]))
            lastS[pp] = sv
            P.wait("vector", "s", sv)
            Vd(lambda e, sc=sc: e.reduce_max(sc[:, 6:7], xc[:], AX.X, apply_absolute_value=True))
            Vd(lambda e, sc=sc: e.tensor_tensor(sc[:, 6:7], sc[:, 6:7], sc[:, 5:6], ALU.mult))
            Vd(lambda e, sc=sc: e.tensor_scalar(sc[:, 6:7], sc[:, 6:7], 1e-5, None, ALU.max))
            Vd(lambda e, sc=sc: e.reciprocal(sc[:, 7:8], sc[:, 6:7]))
            Vd(lambda e, sc=sc: e.tensor_scalar(sc[:, 7:8], sc[:, 7:8], 127.0, None, ALU.mult))
            Vd(lambda e, sc=sc: e.tensor_tensor(sc[:, 7:8], sc[:, 7:8], sc[:, 5:6], ALU.mult))
            vv = Vd(lambda e, sc=sc, st=st: e.tensor_copy(gst[:, st:st + 1], sc[:, 6:7]))
            P.wait("scalar", "v", vv)
            P.wait("scalar", "t", evTT if st >= 1 else 0)
            S(lambda e, sc=sc: e.activation(xn_t[:], xc[:], AF.Identity, scale=sc[:, 5:6]))
            S(lambda e, sc=sc: e.activation(xt1[:], xc[:], AF.Identity, scale=sc[:, 7:8], bias=ccol[:, C_MAG:C_MAG + 1]))
            sv = S(lambda e: e.activation(xi_t[:], xt1[:], AF.Identity, bias=ccol[:, C_NMAG:C_NMAG + 1]))
            lastS[pp] = sv
            P.wait("tensor", "s", sv)
            P.wait("tensor", "v", evT[0])
            for dc in range(ND):
                T(lambda e, dc=dc: e.matmul(pg0[:, dc * 128:dc * 128 + 128],
                                            xn_t[:, dc * 128:(dc + 1) * 128], ident_bf[:], start=True, stop=True))
            tv0 = P.cnt["t"]
            P.wait("tensor", "v", evT[1])
            for dc in range(ND):
                T(lambda e, dc=dc: e.matmul(pg1[:, dc * 128:dc * 128 + 128],
                                            xi_t[:, dc * 128:(dc + 1) * 128], ident_bf[:], start=True, stop=True))
            evTT = P.cnt["t"]
            P.wait("vector", "t", tv0)
            evT[0] = V(lambda e, st=st: e.tensor_copy(
                xnT[:].rearrange("p (dc t) -> p dc t", dc=8)[:, :, st * 128:st * 128 + 128],
                pg0[:].rearrange("p (dc c) -> p dc c", dc=8)))
            P.wait("vector", "t", evTT)
            evT[1] = V(lambda e, st=st: e.tensor_copy(
                xiT[:].rearrange("p (dc t) -> p dc t", dc=8)[:, :, st * 128:st * 128 + 128],
                pg1[:].rearrange("p (dc c) -> p dc c", dc=8)))
            lastV[pp] = evT[1]

        if debug and b == 0:
            fence()
            DG(lambda e: e.dma_start(out=dbg["d_xnT"], in_=xnT[:]), "gd")
            DG(lambda e: e.dma_start(out=dbg["d_xiT"], in_=xiT[:]), "gd")
            fence()
        if b == 0:
            P.wait("gpsimd", "v", P.cnt["v"])
            P.wait("gpsimd", "s", P.cnt["s"])
            wq_pass1(w1_d, 32, 1024, C_SW1, C_C1, 0)
            wq_pass1(w2_d, 8, 4096, C_SW2, C_C2, 1)
            wq_state["p1_s"] = P.cnt["s"]
            wq_state["p1a"] = P.cnt["p1a"]
            wq_state["p1b"] = P.cnt["p1b"]

        if UPTO < 3:
            continue
        P.wait("tensor", "v", P.cnt["v"] if False else lastV[0] if False else max(evT[0], evT[1]))
        P.wait("sync", "v", max(evT[0], evT[1]))
        P.wait("sync", "s", P.cnt["s"])
        for e_ in ("scalar", "vector", "tensor"):
            P.wait(e_, "sg0", P.cnt["sg0"])
            P.wait(e_, "sg1", P.cnt["sg1"])
            P.wait(e_, "wr", P.cnt["wr"])
        lastT_wqk = {0: 0, 1: 0}
        lastS_ps = {0: 0, 1: 0}
        for ct in range(16):
            pp = ct % 2
            P.wait("sync", "t", lastT_wqk[pp])
            wd = DW(lambda e, ct=ct, pp=pp: e.dma_start(out=wqkb[:, pp * 1024:pp * 1024 + 1024], in_=wqk_sc[ct]), "pw%d" % pp)
            P.wait("tensor", "pw%d" % pp, wd)
            for ts in range(4):
                qq = (ct * 4 + ts) % 2
                P.wait("tensor", "s", lastS_ps[qq])
                for dc in range(ND):
                    tv = T(lambda e, pp=pp, dc=dc, ts=ts, qq=qq: e.matmul(
                        pg2[:, qq * 512:qq * 512 + 512],
                        wqkb[:, pp * 1024 + dc * 128: pp * 1024 + dc * 128 + 128],
                        xnT[:, dc * 2048 + ts * 512: dc * 2048 + ts * 512 + 512],
                        start=(dc == 0), stop=(dc == ND - 1)))
                lastT_wqk[pp] = tv
                P.wait("scalar", "t", tv)
                lastS_ps[qq] = S(lambda e, ct=ct, ts=ts, qq=qq: e.activation(
                    qkT[:, ct * 2048 + ts * 512: ct * 2048 + ts * 512 + 512],
                    pg2[:, qq * 512:qq * 512 + 512], AF.Identity, bias=bcol[:, ct:ct + 1]))
        P.wait("tensor", "s", max(lastS_ps[0], lastS_ps[1]))
        P.wait("sync", "s", wq_state.get("p1_s", 0))
        P.wait("sync", "p1a", wq_state.get("p1a", 0))
        P.wait("sync", "p1b", wq_state.get("p1b", 0))
        lastV_ps = {0: 0, 1: 0}
        lastT_wv = 0
        for hf in range(2):
            P.wait("sync", "t", lastT_wv)
            for dc in range(ND):
                wd = DW(lambda e, hf=hf, dc=dc: e.dma_start(out=wvb[:, dc * 640:dc * 640 + 640], in_=wv_sc[hf, dc]), "gv")
            P.wait("tensor", "gv", wd)
            for st in range(NT):
                qq = st % 2
                pgv = pg3 if qq == 0 else pg2
                P.wait("tensor", "v", lastV_ps[qq])
                for dc in range(ND):
                    T(lambda e, dc=dc, st=st, pgv=pgv: e.matmul(
                        pgv[:, 0:512],
                        xnT[:, dc * 2048 + st * 128: dc * 2048 + st * 128 + 128],
                        wvb[:, dc * 640: dc * 640 + 512],
                        start=(dc == 0), stop=False))
                    T(lambda e, dc=dc, st=st, pgv=pgv: e.matmul(
                        pgv[:, 512:640],
                        xnT[:, dc * 2048 + st * 128: dc * 2048 + st * 128 + 128],
                        wvb[:, dc * 640 + 512: dc * 640 + 640],
                        start=(dc == 0), stop=False))
                T(lambda e, hf=hf, pgv=pgv: e.matmul(pgv[:, 0:512], ones_bf[0:1, 0:128],
                                                     bvrow[0:1, hf * 640: hf * 640 + 512], start=False, stop=True))
                tv = T(lambda e, hf=hf, pgv=pgv: e.matmul(pgv[:, 512:640], ones_bf[0:1, 0:128],
                                                          bvrow[0:1, hf * 640 + 512: hf * 640 + 640], start=False, stop=True))
                lastT_wv = tv
                P.wait("vector", "t", tv)
                base = st * 1280 + hf * 640
                lastV_ps[qq] = V(lambda e, base=base, pgv=pgv: e.tensor_copy(
                    v_sb[:, base: base + 640], pgv[:, 0:640]))
        qkv_s = P.cnt["s"]
        qkv_v = P.cnt["v"]
        p2q = []
        if b == 0:
            wq_pass2(w1_d, 32, 1024, C_SW1, 0, 0, lastT_wv, p2q)
            wq_pass2(w2_d, 8, 4096, C_SW2, 32, 1, lastT_wv, p2q)
            P.wait("tensor", "s", qkv_s)
            P.wait("tensor", "v", qkv_v)
            for q_ in ("sync", "gpsimd"):
                P.wait(q_, "t", P.cnt["t"])
        else:
            fence()
        if debug and b == 0:
            DG(lambda e: e.dma_start(out=dbg["d_qkT"], in_=qkT[:]), "gd")
            DG(lambda e: e.dma_start(out=dbg["d_vsb"], in_=v_sb[:]), "gd")
            fence()

        if UPTO < 4:
            continue
        P.op("scalar", lambda e: e.preload_act_table(AF.Exp))
        sc_pair = [pg0, pg1]
        ctx_pair = [pg2, pg3]
        exp_s = {0: 0, 1: 0}
        avT = {0: 0, 1: 0}
        evac_v = {0: 0, 1: 0}
        nbw = {0: 0, 1: 0}
        def attn_tail(pi, cps, qc, pr, avlast):
            P.wait("vector", "t", avlast)
            P.wait("vector", "nb%d" % pi, nbw[pi])
            V(lambda e, cps=cps, pi=pi: e.reciprocal(rows_sb[64:65, pi * 512:pi * 512 + 512], cps[64:65, 0:512]))
            rv = V(lambda e, cps=cps, pi=pi: e.reciprocal(rows_sb[32:33, pi * 512:pi * 512 + 512], cps[32:33, 512:1024]))
            P.wait("sync", "v", rv)
            DW(lambda e, pi=pi: e.dma_start(out=den_sc[pi], in_=rows_sb[32:96:32, pi * 512:pi * 512 + 512]), "nb%d" % pi)
            P.wait("sync", "nb%d" % pi, P.cnt["nb%d" % pi])
            DW(lambda e, pi=pi: e.dma_start(out=nrm[0:64, pi * 512:pi * 512 + 512],
                                            in_=den_sc[pi, 1].unsqueeze(0).broadcast_to([64, 512])), "nb%d" % pi)
            nbw[pi] = DW(lambda e, pi=pi: e.dma_start(out=nrm[64:128, pi * 512:pi * 512 + 512],
                                                      in_=den_sc[pi, 0].unsqueeze(0).broadcast_to([64, 512])), "nb%d" % pi)
            P.wait("vector", "nb%d" % pi, nbw[pi])
            dst = pr * 2048 + qc * 512
            V(lambda e, cps=cps, dst=dst, pi=pi: e.tensor_tensor(
                ctx_sb[0:64, dst:dst + 512], cps[0:64, 0:512], nrm[0:64, pi * 512:pi * 512 + 512], ALU.mult))
            evac_v[pi] = V(lambda e, cps=cps, dst=dst, pi=pi: e.tensor_tensor(
                ctx_sb[64:128, dst:dst + 512], cps[64:128, 512:1024], nrm[64:128, pi * 512:pi * 512 + 512], ALU.mult))

        def attn_av(info):
            cps, kk, off, vb0, st0, sp, pi = (info["cps"], info["kk"], info["off"],
                                              info["vb0"], info["st0"], info["sp"], info["pi"])
            P.wait("tensor", info["gate"][0], info["gate"][1])
            if st0:
                P.wait("tensor", "v", evac_v[pi])
            T(lambda e, cps=cps, vb0=vb0, kk=kk, off=off, st0=st0, sp=sp: e.matmul(
                cps[:, off:512], v_sb[:, vb0:vb0 + 128], p_sb[:, kk * 1024 + off:kk * 1024 + 512],
                start=st0, stop=sp))
            avT[kk] = T(lambda e, cps=cps, vb0=vb0, kk=kk, off=off, st0=st0, sp=sp: e.matmul(
                cps[:, 512 + off:1024], v_sb[:, vb0 + 32:vb0 + 160], p_sb[:, kk * 1024 + 512 + off:kk * 1024 + 1024],
                start=st0, stop=sp))
            if sp:
                attn_tail(pi, cps, info["qc"], info["pr"], avT[kk])

        steps = [(qc, pr, kt, 4 * qc + 4) for qc in range(4) for pr in range(8) for kt in range(4 * qc + 4)]
        pending = None
        p2n = len(p2q)
        for kap, (qc, pr, kt, nkt) in enumerate(steps):
            while p2q and (p2n - len(p2q)) < (kap + 1) * p2n // len(steps):
                p2q.pop(0)()
            pi = (qc * 8 + pr) % 2
            cps = ctx_pair[pi]
            kk = kap % 2
            scp = sc_pair[kk]
            diag = kt >= 4 * qc
            off = (kt - 4 * qc) * 128 if diag else 0
            kbase = (8 + pr) * 2048 + kt * 128
            qbase = pr * 2048 + qc * 512
            P.wait("tensor", "s", exp_s[kk])
            T(lambda e, scp=scp, kbase=kbase, qbase=qbase, off=off: e.matmul(
                scp[:, off:512], qkT[0:64, kbase:kbase + 128], qkT[0:64, qbase + off:qbase + 512],
                start=True, stop=True, tile_position=(0, 0)))
            tv = T(lambda e, scp=scp, kbase=kbase, qbase=qbase, off=off: e.matmul(
                scp[:, 512 + off:1024], qkT[64:128, kbase:kbase + 128], qkT[64:128, qbase + off:qbase + 512],
                start=True, stop=True, tile_position=(64, 0)))
            P.wait("scalar", "t", tv)
            P.wait("scalar", "t", avT[kk])
            exp_s[kk] = S(lambda e, scp=scp, kk=kk, off=off: e.activation(
                p_sb[:, kk * 1024:kk * 1024 + 1024].rearrange("p (h c) -> p h c", h=2)[:, :, off:512],
                scp[:].rearrange("p (h c) -> p h c", h=2)[:, :, off:512],
                AF.Exp, scale=0.125))
            if diag:
                P.wait("vector", "s", exp_s[kk])
                mv = V(lambda e, kk=kk, off=off: e.tensor_tensor(
                    p_sb[:, kk * 1024:kk * 1024 + 1024].rearrange("p (h c) -> p h c", h=2)[:, :, off:off + 128],
                    p_sb[:, kk * 1024:kk * 1024 + 1024].rearrange("p (h c) -> p h c", h=2)[:, :, off:off + 128],
                    tri_bf[:].unsqueeze(1).broadcast_to([128, 2, 128]), ALU.mult))
                gate = ("v", mv)
            else:
                gate = ("s", exp_s[kk])
            info = dict(cps=cps, kk=kk, off=off, vb0=kt * 1280 + pr * 160,
                        st0=(kt == 0), sp=(kt == nkt - 1), pi=pi, qc=qc, pr=pr, gate=gate)
            if pending is not None:
                attn_av(pending)
            pending = info
        attn_av(pending)
        while p2q:
            p2q.pop(0)()
        fence()
        rsq = []
        if b == 0:
            wq_restream(rsq)
        if debug and b == 0:
            DG(lambda e: e.dma_start(out=dbg["d_ctx"], in_=ctx_sb[:]), "gd")
            fence()

        if UPTO < 5:
            continue
        for dc in range(ND):
            wd = DW(lambda e, dc=dc: e.dma_start(out=wo_res[:, dc * 1024:dc * 1024 + 1024], in_=wo_sc[dc]), "gv")
        P.wait("tensor", "gv", wd)
        addv = {0: 0, 1: 0}
        accw = {0: 0, 1: 0}
        for st in range(NT):
            for _ in range(4):
                if rsq:
                    rsq.pop(0)()
            pp = st % 2
            pg = pg2 if pp == 0 else pg3
            P.wait("gpsimd", "v", addv[pp])
            xd = DG(lambda e, b=b, st=st, pp=pp: e.dma_start(
                out=xrb[:, pp * 1024:pp * 1024 + 1024], in_=x_d[b, st * 128:(st + 1) * 128, :]), "gx%d" % pp)
            P.wait("tensor", "v", addv[pp])
            for half in range(2):
                for dc in range(ND):
                    T(lambda e, pg=pg, half=half, dc=dc, st=st: e.matmul(
                        pg[:, half * 512:half * 512 + 512],
                        ctx_sb[:, dc * 2048 + st * 128: dc * 2048 + st * 128 + 128],
                        wo_res[:, dc * 1024 + half * 512: dc * 1024 + half * 512 + 512],
                        start=(dc == 0), stop=False))
                tv = T(lambda e, pg=pg, half=half: e.matmul(
                    pg[:, half * 512:half * 512 + 512], ones_bf[0:1, 0:128],
                    borow[0:1, half * 512:half * 512 + 512], start=False, stop=True))
            P.wait("vector", "t", tv)
            P.wait("vector", "gx%d" % pp, xd)
            P.wait("vector", "ga%d" % pp, accw[pp])
            addv[pp] = V(lambda e, pg=pg, pp=pp: e.tensor_tensor(
                acst[:, pp * 1024:pp * 1024 + 1024], pg[:], xrb[:, pp * 1024:pp * 1024 + 1024], ALU.add))
            P.wait("gpsimd", "v", addv[pp])
            accw[pp] = DG(lambda e, b=b, st=st, pp=pp: e.dma_start(
                out=acc_sc[b, st], in_=acst[:, pp * 1024:pp * 1024 + 1024]), "ga%d" % pp)
        while rsq:
            rsq.pop(0)()
        fence()
        if debug and b == 0:
            P.wait("gpsimd", "ga0", P.cnt["ga0"])
            P.wait("gpsimd", "ga1", P.cnt["ga1"])
            for st in range(NT):
                DG(lambda e, st=st: e.dma_start(out=dbg["d_acc"][st], in_=acc_sc[0, st]), "gd")
            fence()

        if UPTO < 6:
            continue
        P.op("scalar", lambda e: e.preload_act_table(AF.Silu))
        vv = Vd(lambda e: e.tensor_scalar(stat[:], gst[:], ccol[:, C_C1:C_C1 + 1], None, ALU.mult))
        P.wait("tensor", "v", vv)
        tv = T(lambda e: e.matmul(pg1[0:16, 0:128], stat[:], ident_f[:], start=True, stop=True))
        P.wait("vector", "t", tv)
        vv = V(lambda e: e.tensor_copy(nrm[0:16, 0:128], pg1[0:16, 0:128]))
        P.wait("gpsimd", "v", vv)
        gw = DG(lambda e: e.dma_start(out=row_sc[0:2048].rearrange("(r c) -> r c", r=16), in_=nrm[0:16, 0:128]), "gr")
        P.wait("gpsimd", "gr", gw)
        gw = DG(lambda e: e.dma_start(out=g_row[0:1, :], in_=row_sc[0:2048].unsqueeze(0)), "gr")
        P.wait("tensor", "gr", gw)
        gcp = {}
        for i in range(4):
            if i >= 2:
                P.wait("tensor", "v", gcp[i - 2])
            tv = T(lambda e, i=i: e.matmul(pg1[:, (i % 2) * 512:(i % 2) * 512 + 512],
                                           ones_f[0:1, 0:128], g_row[0:1, i * 512:i * 512 + 512],
                                           start=True, stop=True))
            P.wait("vector", "t", tv)
            gcp[i] = V(lambda e, i=i: e.tensor_copy(g_sb[:, i * 512:i * 512 + 512], pg1[:, (i % 2) * 512:(i % 2) * 512 + 512]))
        stash0 = V(lambda e: e.memset(stash[:], 0.0))
        if debug and b == 0:
            P.wait("gpsimd", "v", P.cnt["v"])
            DG(lambda e: e.dma_start(out=dbg["d_g"], in_=g_sb[:]), "gd")

        pmT_t = [0]
        w1T = {0: 0, 1: 0}
        w2T = {0: 0, 1: 0}
        w1ev = {0: 0, 1: 0, 2: 0, 3: 0}
        w2ev = {0: 0, 1: 0}
        w2last = {0: 0, 1: 0}
        uV = {k: stash0 for k in range(8)}
        uSS = {k: 0 for k in range(8)}
        pg1free_s = [0]
        pg1free_v = [0]
        pg1w = [-1]
        roundV_w = {-1: 0}
        roundS_w = {-1: 0}
        silu31 = {}
        accv = {0: 0, 1: 0}
        outw = {0: 0, 1: 0}
        sb_g = 0

        ub4 = [ubuf[0], ubuf[1], xc[:, 0:388], xc[:, 388:776]]

        def w1conv(c):
            have = c <= 7
            nonlocal_first = [True]
            for grp in range(8):
                hcs = [grp * 4 + j for j in range(4)]
                if have:
                    ppw = grp % 2
                    P.wait("sync", "t", w1T[ppw])
                    wd = DW(lambda e, grp=grp, ppw=ppw: e.dma_start(
                        out=w1b[:, ppw * 4096:ppw * 4096 + 4096].rearrange("p (h q) -> p h q", h=4),
                        in_=w1q_sc[grp * 4:grp * 4 + 4].rearrange("h p q -> p h q")), "pw%d" % ppw)
                    P.wait("tensor", "pw%d" % ppw, wd)
                for hc in hcs:
                    sl = hc % 4
                    P.wait("scalar", "v", uV[sl])
                    S(lambda e, sl=sl, hc=hc: e.copy(ub4[sl][:, 0:129], stash[:, hc * 129:hc * 129 + 129]))
                evc = {}
                for hc in hcs:
                    sl = hc % 4
                    if have:
                        pw = hc % 2
                        P.wait("tensor", "v", w1ev[pw])
                        for dc in range(ND):
                            tmm = T(lambda e, j=hc % 4, dc=dc, c=c, pw=pw, ppw=(hc // 4) % 2: e.matmul(
                                pg0[:, pw * 512:pw * 512 + 256],
                                w1b[:, ppw * 4096 + j * 1024 + dc * 128: ppw * 4096 + j * 1024 + dc * 128 + 128],
                                xiT[:, dc * 2048 + c * 256: dc * 2048 + c * 256 + 256],
                                start=(dc == 0), stop=(dc == ND - 1)))
                        w1T[(hc // 4) % 2] = tmm
                        P.wait("vector", "t", tmm)
                        P.wait("vector", "s", uSS[sl])
                        w1ev[pw] = V(lambda e, sl=sl, pw=pw, c=c: e.tensor_tensor(
                            ub4[sl][:, 129:385], pg0[:, pw * 512:pw * 512 + 256],
                            g_sb[:, c * 256:c * 256 + 256], ALU.mult))
                        evc[hc] = w1ev[pw]
                    else:
                        P.wait("vector", "s", uSS[sl])
                        evc[hc] = V(lambda e, sl=sl: e.memset(ub4[sl][:, 129:388], 0.0))
                inc_ = {}
                for hc in hcs:
                    sl = hc % 4
                    P.wait("scalar", "v", evc[hc])
                    if nonlocal_first[0]:
                        P.wait("scalar", "v", roundV_w[c - 1])
                        nonlocal_first[0] = False
                    inc_[hc] = S(lambda e, sl=sl, hc=hc: e.activation(
                        h2w[:, hc * 256:hc * 256 + 256], ub4[sl][:, 0:256], AF.Identity,
                        scale=cwsb[:, hc * 3:hc * 3 + 1], bias=cbsb[:, hc:hc + 1]))
                    if have:
                        uSS[sl] = S(lambda e, sl=sl, hc=hc: e.copy(
                            stash[:, hc * 129:hc * 129 + 129], ub4[sl][:, 256:385]))
                    else:
                        uSS[sl] = inc_[hc]
                for hc in hcs:
                    sl = hc % 4
                    P.wait("vector", "s", inc_[hc])
                    V(lambda e, sl=sl, hc=hc: e.scalar_tensor_tensor(
                        h2w[:, hc * 256:hc * 256 + 256], ub4[sl][:, 1:257], cwsb[:, hc * 3 + 1:hc * 3 + 2],
                        h2w[:, hc * 256:hc * 256 + 256], ALU.mult, ALU.add))
                    uV[sl] = V(lambda e, sl=sl, hc=hc: e.scalar_tensor_tensor(
                        h2w[:, hc * 256:hc * 256 + 256], ub4[sl][:, 2:258], cwsb[:, hc * 3 + 2:hc * 3 + 3],
                        h2w[:, hc * 256:hc * 256 + 256], ALU.mult, ALU.add))
                P.wait("scalar", "v", uV[hcs[-1] % 4])
                sl_ = S(lambda e, grp=grp: e.activation(
                    h2w[:, grp * 1024:grp * 1024 + 1024], h2w[:, grp * 1024:grp * 1024 + 1024], AF.Silu))
            silu31[c] = sl_

        def quantscale(cq):
            P.wait("vector", "s", silu31[cq])
            P.wait("vector", "t", pmT_t[0])
            pmv = V(lambda e: e.tensor_reduce(
                pmb[:, 0:256], h2w[:].rearrange("p (h t) -> p t h", h=32),
                AX.X, ALU.max, apply_absolute_value=True))
            P.wait("tensor", "v", pmv)
            P.wait("tensor", "s", P.cnt["s"])
            T(lambda e: e.matmul(pg1[:, 0:128], pmb[:, 0:128], ident_f[:], start=True, stop=True))
            tv = T(lambda e: e.matmul(pg1[:, 128:256], pmb[:, 128:256], ident_f[:], start=True, stop=True))
            pmT_t[0] = tv
            P.wait("vector", "t", tv)
            Vd(lambda e: e.tensor_reduce(
                mx[:, 0:2], pg1[:, 0:256].rearrange("p (w c) -> p w c", w=2), AX.X, ALU.max,
                apply_absolute_value=False))
            Vd(lambda e: e.tensor_scalar(mx[:, 2:4], mx[:, 0:2], 1e-5, None, ALU.max))
            Vd(lambda e: e.tensor_scalar(mx[:, 4:6], mx[:, 2:4], ccol[:, C_C2:C_C2 + 1], None, ALU.mult))
            Vd(lambda e: e.reciprocal(mx[:, 6:8], mx[:, 2:4]))
            vvs = Vd(lambda e: e.tensor_scalar(mx[:, 6:8], mx[:, 6:8], 127.0, None, ALU.mult))
            P.wait("tensor", "v", vvs)
            tv = T(lambda e: e.matmul(pg1[0:2, 256:384], mx[:, 6:8], ident_f[:], start=True, stop=True))
            P.wait("vector", "t", tv)
            vv = V(lambda e: e.tensor_copy(nrm[0:2, 0:128], pg1[0:2, 256:384]))
            pg1free_v[0] = vv
            P.wait("gpsimd", "v", vv)
            P.wait("gpsimd", "gr", P.cnt["gr"])
            sg = DG(lambda e: e.dma_start(out=row_sc[2048:2304].rearrange("(r c) -> r c", r=2), in_=nrm[0:2, 0:128]), "gr")
            P.wait("gpsimd", "gr", sg)
            sg = DG(lambda e: e.dma_start(out=sbrow[0:1, :], in_=row_sc[2048:2304].unsqueeze(0)), "gr")
            P.wait("tensor", "gr", sg)
            tso = T(lambda e: e.matmul(pg1[:, 512:768], ones_f[0:1, 0:128], sbrow[0:1, 0:256], start=True, stop=True))
            P.wait("scalar", "t", tso)
            sbs = S(lambda e: e.copy(sb_sb[:], pg1[:, 512:768]))
            pg1free_s[0] = sbs
            P.wait("vector", "s", sbs)
            qm = V(lambda e: e.tensor_tensor(
                h2w[:].rearrange("p (h t) -> p h t", h=32),
                h2w[:].rearrange("p (h t) -> p h t", h=32),
                sb_sb[:].unsqueeze(1).broadcast_to([128, 32, 256]), ALU.mult))
            P.wait("vector", "t", max(w2T[0], w2T[1]))
            P.wait("scalar", "t", max(w2T[0], w2T[1]))
            P.wait("scalar", "v", qm)
            roundV_w[cq] = V(lambda e: e.tensor_scalar(h2q[:, 0:4096], h2w[:, 0:4096], MAGIC, -MAGIC, ALU.add, ALU.add))
            for k in range(4):
                S(lambda e, k=k: e.activation(xbuf[:, 0:1024], h2w[:, 4096 + k * 1024:4096 + k * 1024 + 1024],
                                              AF.Identity, bias=ccol[:, C_MAG:C_MAG + 1]))
                rS = S(lambda e, k=k: e.activation(h2q[:, 4096 + k * 1024:4096 + k * 1024 + 1024], xbuf[:, 0:1024],
                                                   AF.Identity, bias=ccol[:, C_NMAG:C_NMAG + 1]))
            roundS_w[cq] = rS

        def w2block(cq, b=b):
            stl = [t_ for t_ in (2 * cq - 1, 2 * cq) if 0 <= t_ < 16]
            accd = {}
            for st_out in stl:
                par = st_out % 2
                P.wait("gpsimd", "v", accv[par])
                accd[st_out] = DG(lambda e, b=b, st_out=st_out, par=par: e.dma_start(
                    out=accb[:, par * 1024:par * 1024 + 1024], in_=acc_sc[b, st_out]), "ga%d" % par)
            for grp in range(8):
                pp2 = grp % 2
                P.wait("sync", "t", w2T[pp2])
                wd2 = DW(lambda e, grp=grp, pp2=pp2: e.dma_start(
                    out=w2b[:, pp2 * 4096:pp2 * 4096 + 4096].rearrange("p (h q) -> p h q", h=4),
                    in_=w2q_sc[grp * 4:grp * 4 + 4].rearrange("h p q -> p h q")), "qw%d" % pp2)
                P.wait("tensor", "qw%d" % pp2, wd2)
                if grp == 0:
                    P.wait("tensor", "v", roundV_w[cq])
                    P.wait("tensor", "s", roundS_w[cq])
                for j in range(4):
                    hc = grp * 4 + j
                    for st_out in stl:
                        w = st_out - (2 * cq - 1)
                        par = st_out % 2
                        pgw = pg2 if par == 0 else pg3
                        if hc == 0:
                            P.wait("tensor", "v", w2ev[par])
                        for half in range(2):
                            tmm = T(lambda e, pgw=pgw, hc=hc, w=w, pp2=pp2, j=j, half=half: e.matmul(
                                pgw[:, half * 512:half * 512 + 512],
                                h2q[:, hc * 256 + w * 128: hc * 256 + w * 128 + 128],
                                w2b[:, pp2 * 4096 + j * 1024 + half * 512: pp2 * 4096 + j * 1024 + half * 512 + 512],
                                start=(hc == 0), stop=(hc == NHC - 1)))
                        w2T[pp2] = tmm
                        if hc == NHC - 1:
                            w2last[par] = tmm
            for st_out in stl:
                w = st_out - (2 * cq - 1)
                par = st_out % 2
                pgw = pg2 if par == 0 else pg3
                P.wait("vector", "ga%d" % par, accd[st_out])
                P.wait("vector", "t", w2last[par])
                P.wait("vector", "go%d" % par, outw[par])
                accv[par] = V(lambda e, pgw=pgw, par=par, w=w: e.scalar_tensor_tensor(
                    outb[:, par * 1024:par * 1024 + 1024], pgw[:], mx[:, 4 + w:5 + w],
                    accb[:, par * 1024:par * 1024 + 1024], ALU.mult, ALU.add))
                w2ev[par] = accv[par]
                P.wait("gpsimd", "v", accv[par])
                outw[par] = DG(lambda e, b=b, st_out=st_out, par=par: e.dma_start(
                    out=out_d[b, st_out * 128:(st_out + 1) * 128, :], in_=outb[:, par * 1024:par * 1024 + 1024]), "go%d" % par)

        for c in range(FFNW):
            if 1 <= c:
                quantscale(c - 1)
            if c <= 8:
                w1conv(c)
            if 1 <= c:
                w2block(c - 1)
        fence()

    for s_ in list(P.cnt.keys()):
        if s_ not in ("t", "s", "v") and P.cnt[s_] > 0:
            P.wait("gpsimd", s_, P.cnt[s_])

    from contextlib import ExitStack
    with ExitStack() as stack:
        sems = {}
        for s_ in list(P.cnt.keys()):
            if P.cnt[s_] > 0:
                sems[s_] = stack.enter_context(nc.semaphore("sem_" + s_))
        block = stack.enter_context(nc.Block())

        @block.sync
        def _(sync):
            P.replay("sync", sync, sems)

        @block.gpsimd
        def _(gpsimd):
            P.replay("gpsimd", gpsimd, sems)

        @block.tensor
        def _(tensor):
            P.replay("tensor", tensor, sems)

        @block.scalar
        def _(scalar):
            P.replay("scalar", scalar, sems)

        @block.vector
        def _(vector):
            P.replay("vector", vector, sems)

    return nc


_NC_CACHE = {}


def _get_nc(debug=False):
    key = (debug, UPTO, FFNW, W1MODE)
    if key not in _NC_CACHE:
        _NC_CACHE[key] = build(debug)
    return _NC_CACHE[key]


def kernel(L0_attn_hidden_states, L0_attn_W_attn, L0_attn_b_attn, L0_attn_W_out,
           L0_attn_b_out, L0_ffn_w1, L0_ffn_conv_w, L0_ffn_conv_b, L0_ffn_w2,
           _debug=False):
    x = np.ascontiguousarray(L0_attn_hidden_states, dtype=np.float32)
    shards = x.reshape(8, NB, 2048, 1024)
    nc = _get_nc(_debug)
    common = {
        "W_attn": np.ascontiguousarray(L0_attn_W_attn, dtype=np.float32),
        "b_attn": np.ascontiguousarray(L0_attn_b_attn, dtype=np.float32),
        "W_out": np.ascontiguousarray(L0_attn_W_out, dtype=np.float32),
        "b_out": np.ascontiguousarray(L0_attn_b_out, dtype=np.float32),
        "w1": np.ascontiguousarray(L0_ffn_w1, dtype=np.float32),
        "cw": np.ascontiguousarray(L0_ffn_conv_w, dtype=np.float32).reshape(4096, 3),
        "cb": np.ascontiguousarray(L0_ffn_conv_b, dtype=np.float32),
        "w2": np.ascontiguousarray(L0_ffn_w2, dtype=np.float32),
        "ident_bf": np.eye(128, dtype=ml_dtypes.bfloat16),
        "ident_f": np.eye(128, dtype=np.float32),
        "tri_bf": np.triu(np.ones((128, 128), dtype=np.float32)).astype(ml_dtypes.bfloat16),
    }
    in_maps = [dict(common, x=np.ascontiguousarray(shards[i])) for i in range(8)]
    res = run_bass_kernel_spmd(nc, in_maps, core_ids=list(range(8)))
    out = np.concatenate([r["out"].reshape(NB, 2048, 1024) for r in res.results], axis=0)
    if _debug:
        return out, res
    return out


# version 19
# speedup vs baseline: 1.0763x; 1.0114x over previous
import numpy as np
import ml_dtypes
import concourse.bass as bass
import concourse.mybir as mybir
from concourse.bass_utils import run_bass_kernel_spmd

F32 = mybir.dt.float32
BF16 = mybir.dt.bfloat16
AF = mybir.ActivationFunctionType
ALU = mybir.AluOpType
AX = mybir.AxisListType

MAGIC = 12582912.0
UPTO = 6
FFNW = 10
W1MODE = 0
NB = 2
NT = 16
ND = 8
NHC = 32

ENGS = ("sync", "gpsimd", "tensor", "scalar", "vector")


class Plan:
    def __init__(self):
        from collections import defaultdict
        self.ops = {e: [] for e in ENGS}
        self.cnt = defaultdict(int)

    def inc(self, eng, fn, sem, by=1):
        self.ops[eng].append(("inc", fn, sem, by))
        self.cnt[sem] += by
        return self.cnt[sem]

    def op(self, eng, fn):
        self.ops[eng].append(("op", fn))

    def wait(self, eng, sem, val):
        if val > 0:
            self.ops[eng].append(("wait", sem, val))

    def replay(self, eng_name, obj, sems):
        for it in self.ops[eng_name]:
            if it[0] == "op":
                it[1](obj)
            elif it[0] == "inc":
                it[1](obj).then_inc(sems[it[2]], it[3])
            else:
                obj.wait_ge(sems[it[1]], it[2])


def build(debug=False):
    nc = bass.Bass("TRN2", target_bir_lowering=False)
    P = Plan()

    def T(fn):
        return P.inc("tensor", fn, "t")

    def S(fn):
        return P.inc("scalar", fn, "s")

    def V(fn):
        return P.inc("vector", fn, "v")

    def Vd(fn):
        n = P.inc("vector", fn, "v")
        P.wait("vector", "v", n)
        return n

    def DW(fn, sem):
        return P.inc("sync", fn, sem, 16)

    def DG(fn, sem):
        return P.inc("gpsimd", fn, sem, 16)

    SYNC_SEMS = ("pw0", "pw1", "qw0", "qw1", "gv", "wr", "sg0", "sg1", "nb0", "nb1")
    GP_SEMS = ("gi", "gx0", "gx1", "ga0", "ga1", "go0", "go1", "gr", "gd")

    def fence():
        from collections import defaultdict
        c = defaultdict(int, P.cnt)
        dma = [s for s in list(SYNC_SEMS) + list(GP_SEMS) if c.get(s, 0) > 0]
        P.wait("tensor", "s", c["s"]); P.wait("tensor", "v", c["v"])
        P.wait("scalar", "t", c["t"]); P.wait("scalar", "v", c["v"])
        P.wait("vector", "t", c["t"]); P.wait("vector", "s", c["s"])
        for q in ("sync", "gpsimd"):
            P.wait(q, "t", c["t"]); P.wait(q, "s", c["s"]); P.wait(q, "v", c["v"])
            for s in dma:
                P.wait(q, s, c[s])
        for e in ("tensor", "scalar", "vector"):
            for s in dma:
                P.wait(e, s, c[s])

    x_d = nc.dram_tensor("x", [NB, 2048, 1024], F32, kind="ExternalInput").ap()
    wattn_d = nc.dram_tensor("W_attn", [1024, 3072], F32, kind="ExternalInput").ap()
    battn_d = nc.dram_tensor("b_attn", [3072], F32, kind="ExternalInput").ap()
    wout_d = nc.dram_tensor("W_out", [1024, 1024], F32, kind="ExternalInput").ap()
    bout_d = nc.dram_tensor("b_out", [1024], F32, kind="ExternalInput").ap()
    w1_d = nc.dram_tensor("w1", [4096, 1024], F32, kind="ExternalInput").ap()
    cw_d = nc.dram_tensor("cw", [4096, 3], F32, kind="ExternalInput").ap()
    cb_d = nc.dram_tensor("cb", [4096], F32, kind="ExternalInput").ap()
    w2_d = nc.dram_tensor("w2", [1024, 4096], F32, kind="ExternalInput").ap()
    identbf_d = nc.dram_tensor("ident_bf", [128, 128], BF16, kind="ExternalInput").ap()
    identf_d = nc.dram_tensor("ident_f", [128, 128], F32, kind="ExternalInput").ap()
    tribf_d = nc.dram_tensor("tri_bf", [128, 128], BF16, kind="ExternalInput").ap()
    out_d = nc.dram_tensor("out", [NB, 2048, 1024], F32, kind="ExternalOutput").ap()

    wqk_sc = nc.dram_tensor("wqk_sc", [16, 128, 1024], BF16, kind="Internal").ap()
    wv_sc = nc.dram_tensor("wv_sc", [2, 8, 128, 640], BF16, kind="Internal").ap()
    wo_sc = nc.dram_tensor("wo_sc", [8, 128, 1024], BF16, kind="Internal").ap()
    w1q_sc = nc.dram_tensor("w1q_sc", [32, 128, 1024], BF16, kind="Internal").ap()
    w2q_sc = nc.dram_tensor("w2q_sc", [32, 128, 1024], BF16, kind="Internal").ap()
    acc_sc = nc.dram_tensor("acc_sc", [NB, 16, 128, 1024], F32, kind="Internal").ap()
    row_sc = nc.dram_tensor("row_sc", [4096], F32, kind="Internal").ap()
    den_sc = nc.dram_tensor("den_sc", [2, 2, 512], F32, kind="Internal").ap()

    dbg = {}
    if debug:
        for nm, shp, dt in [("d_xnT", [128, 16384], BF16), ("d_xiT", [128, 16384], BF16),
                            ("d_qkT", [128, 32768], BF16), ("d_vsb", [128, 20480], BF16),
                            ("d_ctx", [128, 16384], BF16), ("d_g", [128, 2048], F32),
                            ("d_acc", [16, 128, 1024], F32),
                            ("d_w1q", [32, 128, 1024], BF16), ("d_w2q", [32, 128, 1024], BF16),
                            ("d_ccol", [128, 16], F32)]:
            dbg[nm] = nc.dram_tensor(nm, shp, dt, kind="ExternalOutput").ap()

    def sbt(name, shape, dtype, off):
        return nc.alloc_sbuf_tensor_at(name, shape, dtype, offset=off).ap()

    cur = [16512]

    def nxt(sz, c=cur):
        r = c[0]
        c[0] = (c[0] + sz + 31) & ~31
        return r

    ident_bf = sbt("identbf", [128, 128], BF16, nxt(256))
    ones_bf = sbt("onesbf", [128, 128], BF16, nxt(256))
    tri_bf = sbt("tribf", [128, 128], BF16, nxt(256))
    ident_f = sbt("identf", [128, 128], F32, nxt(512))
    ones_f = sbt("onesf", [128, 128], F32, nxt(512))
    ccol = sbt("ccol", [128, 16], F32, nxt(64))
    C_MAG, C_NMAG, C_EPS, C_C1, C_C2, C_SW1, C_SW2 = 0, 1, 2, 3, 4, 5, 6
    bcol = sbt("bcol", [128, 16], F32, nxt(64))
    cwsb = sbt("cwsb", [128, 96], F32, nxt(384))
    cbsb = sbt("cbsb", [128, 32], F32, nxt(128))
    stat = sbt("stat", [128, 16], F32, nxt(64))
    gst = sbt("gst", [128, 16], F32, nxt(64))
    bvrow = sbt("bvrow", [1, 1280], BF16, nxt(2560))
    borow = sbt("borow", [1, 1024], BF16, nxt(2048))
    nrm = sbt("nrm", [128, 1024], F32, nxt(4096))
    rows_sb = sbt("rowssb", [128, 1024], F32, nxt(4096))

    R0 = (cur[0] + 63) & ~63
    xiT = sbt("xiT", [128, 16384], BF16, R0)
    XNC = R0 + 32768
    xnT = sbt("xnT", [128, 16384], BF16, XNC)
    ctx_sb = sbt("ctxsb", [128, 16384], BF16, XNC)
    h2w = sbt("h2w", [128, 8192], F32, XNC)
    QK = XNC + 32768
    qkT = sbt("qkT", [128, 32768], BF16, QK)
    VSB = QK + 65536
    v_sb = sbt("vsb", [128, 20480], BF16, VSB)
    AQ = VSB + 40960
    xbuf = sbt("xbuf", [128, 2048], F32, AQ)
    xc = sbt("xc", [128, 1024], F32, AQ + 8192)
    xt1 = sbt("xt1", [128, 1024], F32, AQ + 12288)
    xn_t = sbt("xnt", [128, 1024], BF16, AQ + 16384)
    xi_t = sbt("xit", [128, 1024], BF16, AQ + 18432)
    wqkb = sbt("wqkb", [128, 2048], BF16, AQ)
    wvb = sbt("wvb", [128, 5120], BF16, AQ + 4096)
    p_sb = sbt("psb", [128, 2048], BF16, AQ)
    SB_END = AQ + 20480

    c0 = [QK]
    ws0 = sbt("ws0", [128, 4096], F32, nxt(16384, c0))
    ws1 = sbt("ws1", [128, 4096], F32, nxt(16384, c0))
    wcast = sbt("wcast", [128, 4096], BF16, nxt(8192, c0))
    tqtmp = sbt("tqtmp", [128, 4096], F32, nxt(16384, c0))
    tritb = sbt("tritb", [128, 4096], BF16, nxt(8192, c0))
    stg = [sbt("stg0", [128, 1024], BF16, nxt(2048, c0)),
           sbt("stg1", [128, 1024], BF16, nxt(2048, c0))]
    zstage = sbt("zstage", [128, 128], BF16, nxt(256, c0))
    ws = [ws0, ws1]

    wo_res = sbt("wores", [128, 8192], BF16, QK)
    xrb = sbt("xrb", [128, 2048], F32, QK + 16384)
    acst = sbt("acst", [128, 2048], F32, QK + 24576)

    c1 = [QK]
    g_sb = sbt("gsb", [128, 2048], F32, nxt(8192, c1))
    g_row = sbt("grow", [1, 2048], F32, nxt(8192, c1))
    stash = sbt("stash", [128, 4128], F32, nxt(16512, c1))
    ubuf = [sbt("u0", [128, 388], F32, nxt(1552, c1)),
            sbt("u1", [128, 388], F32, nxt(1552, c1))]
    h2q = sbt("h2q", [128, 8192], BF16, nxt(16384, c1))
    w1b = sbt("w1b", [128, 8192], BF16, nxt(16384, c1))
    w2b = sbt("w2b", [128, 8192], BF16, nxt(16384, c1))
    accb = sbt("accb", [128, 2048], F32, nxt(8192, c1))
    outb = sbt("outb", [128, 2048], F32, nxt(8192, c1))
    mx = sbt("mx", [128, 16], F32, nxt(64, c1))
    sbrow = sbt("sbrow", [1, 256], F32, nxt(1024, c1))
    sb_sb = sbt("sbsb", [128, 256], F32, nxt(1024, c1))
    pmb = sbt("pmb", [128, 512], F32, nxt(2048, c1))
    assert c1[0] <= VSB + 40960, c1[0]

    pg0 = nc.alloc_psum_tensor("pg0", [128, 1024], F32).ap()
    pg1 = nc.alloc_psum_tensor("pg1", [128, 1024], F32).ap()
    pg2 = nc.alloc_psum_tensor("pg2", [128, 1024], F32).ap()
    pg3 = nc.alloc_psum_tensor("pg3", [128, 1024], F32).ap()

    V(lambda e: e.memset(ones_f[:], 1.0))
    V(lambda e: e.memset(ones_bf[:], 1.0))
    V(lambda e: e.memset(ccol[:, C_MAG:C_MAG + 1], MAGIC))
    V(lambda e: e.memset(ccol[:, C_NMAG:C_NMAG + 1], -MAGIC))
    V(lambda e: e.memset(ccol[:, C_EPS:C_EPS + 1], 1e-5))
    DG(lambda e: e.dma_start(out=ident_bf[:], in_=identbf_d), "gi")
    DG(lambda e: e.dma_start(out=ident_f[:], in_=identf_d), "gi")
    DG(lambda e: e.dma_start(out=tri_bf[:], in_=tribf_d), "gi")
    DG(lambda e: e.dma_start(out=cwsb[:].rearrange("p (h j) -> p h j", h=32),
                             in_=cw_d.rearrange("(h p) j -> p h j", p=128)), "gi")
    DG(lambda e: e.dma_start(out=cbsb[:], in_=cb_d.rearrange("(h p) -> p h", p=128),
                             allow_slow_non_contiguous=True), "gi")
    DG(lambda e: e.dma_start(out=bcol[:], in_=battn_d[0:2048].rearrange("(c p) -> p c", p=128),
                             allow_slow_non_contiguous=True), "gi")
    DG(lambda e: e.dma_start(out=nrm[0:1, :], in_=battn_d[2048:3072].unsqueeze(0)), "gi")
    gg = DG(lambda e: e.dma_start(out=xc[0:1, :], in_=bout_d.unsqueeze(0)), "gi")
    P.wait("vector", "gi", gg)
    P.wait("tensor", "gi", gg)
    P.wait("scalar", "gi", gg)
    V(lambda e: e.memset(bvrow[:], 0.0))
    V(lambda e: e.memset(bvrow[:].rearrange("o (q c) -> o q c", c=160)[:, :, 64:66], 1.0))
    V(lambda e: e.tensor_copy(bvrow[:].rearrange("o (q c) -> o q c", c=160)[:, :, 0:64],
                              nrm[0:1, :].rearrange("o (h c) -> o h c", c=128)[:, :, 0:64]))
    V(lambda e: e.tensor_copy(bvrow[:].rearrange("o (q c) -> o q c", c=160)[:, :, 96:160],
                              nrm[0:1, :].rearrange("o (h c) -> o h c", c=128)[:, :, 64:128]))
    V(lambda e: e.memset(zstage[:], 0.0))
    V(lambda e: e.tensor_copy(borow[:], xc[0:1, :]))
    P.op("scalar", lambda e: e.preload_act_table(AF.Sqrt))

    cast_v = {}
    wr_w = 0
    for dc in range(ND):
        if dc >= 2:
            P.wait("sync", "v", cast_v[dc - 2])
        wt = DW(lambda e, dc=dc, b=ws[dc % 2]: e.dma_start(out=b[:, 0:3072], in_=wattn_d[dc * 128:(dc + 1) * 128, :]), "pw%d" % (dc % 2))
        P.wait("vector", "pw%d" % (dc % 2), wt)
        P.wait("vector", "wr", P.cnt["wr"])
        cast_v[dc] = V(lambda e, b=ws[dc % 2]: e.tensor_copy(wcast[:, 0:3072], b[:, 0:3072]))
        P.wait("sync", "v", cast_v[dc])
        for ct in range(16):
            DW(lambda e, ct=ct, dc=dc: e.dma_start(out=wqk_sc[ct, :, dc * 128:(dc + 1) * 128],
                                                   in_=wcast[:, ct * 128:(ct + 1) * 128]), "wr")
        for hf in range(2):
            DW(lambda e, hf=hf, dc=dc: e.dma_start(
                out=wv_sc[hf, dc].rearrange("p (q c) -> p q c", c=160)[:, :, 64:96],
                in_=zstage[:].rearrange("p (q c) -> p q c", c=32),
                allow_slow_non_contiguous=True), "wr")
            DW(lambda e, hf=hf, dc=dc: e.dma_start(
                out=wv_sc[hf, dc].rearrange("p (q c) -> p q c", c=160)[:, :, 0:64],
                in_=wcast[:, 2048 + hf * 512: 2048 + hf * 512 + 512].rearrange("p (q c) -> p q c", c=128)[:, :, 0:64],
                allow_slow_non_contiguous=True), "wr")
            wr_w = DW(lambda e, hf=hf, dc=dc: e.dma_start(
                out=wv_sc[hf, dc].rearrange("p (q c) -> p q c", c=160)[:, :, 96:160],
                in_=wcast[:, 2048 + hf * 512: 2048 + hf * 512 + 512].rearrange("p (q c) -> p q c", c=128)[:, :, 64:128],
                allow_slow_non_contiguous=True), "wr")
    wrh = {0: 0, 1: 0}
    for dc in range(ND):
        pp = dc % 2
        if dc >= 2:
            P.wait("sync", "v", cast_v[8 + dc - 2])
        wt = DW(lambda e, dc=dc, b=ws[dc % 2]: e.dma_start(out=b[:, 0:1024], in_=wout_d[dc * 128:(dc + 1) * 128, :]), "pw%d" % (dc % 2))
        P.wait("vector", "pw%d" % (dc % 2), wt)
        P.wait("vector", "wr", P.cnt["wr"])
        cast_v[8 + dc] = V(lambda e, b=ws[dc % 2], pp=pp: e.tensor_copy(wcast[:, pp * 1024:pp * 1024 + 1024], b[:, 0:1024]))
        P.wait("sync", "v", cast_v[8 + dc])
        wrh[pp] = DW(lambda e, dc=dc, pp=pp: e.dma_start(out=wo_sc[dc], in_=wcast[:, pp * 1024:pp * 1024 + 1024]), "wr")

    def wquant(w_d, ntiles, tilecols, swcol, c1col, writer):
        acm = stat[:, 0:1]
        tmpc = stat[:, 1:2]
        red_v = {}
        for i in range(ntiles):
            if i >= 2:
                P.wait("sync", "s", red_v[i - 2][0])
            wt = DW(lambda e, i=i, b=ws[i % 2], tc=tilecols: e.dma_start(out=b[:, 0:tc], in_=w_d[i * 128:(i + 1) * 128, :]), "pw%d" % (i % 2))
            P.wait("scalar", "pw%d" % (i % 2), wt)
            if i == 0:
                sa = S(lambda e, b=ws[0], tc=tilecols: e.activation(tqtmp[:, 0:tc], b[:, 0:tc], AF.Abs, accum_out=acm))
                red_v[i] = (sa, P.cnt["v"])
            else:
                P.wait("scalar", "v", red_v[i - 1][1])
                sa = S(lambda e, b=ws[i % 2], tc=tilecols: e.activation(tqtmp[:, 0:tc], b[:, 0:tc], AF.Abs, accum_out=tmpc))
                P.wait("vector", "s", sa)
                va = Vd(lambda e: e.tensor_tensor(acm, acm, tmpc, ALU.add))
                red_v[i] = (sa, va)
        P.wait("tensor", "v", P.cnt["v"])
        P.wait("tensor", "s", P.cnt["s"])
        tv = T(lambda e: e.matmul(pg1[0:1, 0:1], acm, ones_f[:, 0:1], start=True, stop=True))
        P.wait("vector", "t", tv)
        Vd(lambda e, n=float(ntiles * 128 * tilecols): e.tensor_scalar(stat[0:1, 2:3], pg1[0:1, 0:1], 1.0 / n, None, ALU.mult))
        Vd(lambda e: e.tensor_scalar(stat[0:1, 2:3], stat[0:1, 2:3], 1e-5, None, ALU.max))
        vv = Vd(lambda e: e.reciprocal(stat[0:1, 3:4], stat[0:1, 2:3]))
        P.wait("tensor", "v", vv)
        T(lambda e: e.matmul(pg1[:, 2:3], ones_f[0:1, 0:128], stat[0:1, 2:3], start=True, stop=True))
        tv = T(lambda e: e.matmul(pg1[:, 3:4], ones_f[0:1, 0:128], stat[0:1, 3:4], start=True, stop=True))
        P.wait("vector", "t", tv)
        Vd(lambda e, c=c1col: e.tensor_scalar(ccol[:, c:c + 1], pg1[:, 2:3], 1.0 / 127.0, None, ALU.mult))
        vv = Vd(lambda e, c=swcol: e.tensor_copy(ccol[:, c:c + 1], pg1[:, 3:4]))
        P.wait("scalar", "v", vv)
        P.wait("tensor", "v", vv)
        qs = {}
        vts = {}
        for i in range(ntiles):
            if i >= 2:
                P.wait("sync", "s", qs[i - 2])
            wt = DW(lambda e, i=i, b=ws[i % 2], tc=tilecols: e.dma_start(out=b[:, 0:tc], in_=w_d[i * 128:(i + 1) * 128, :]), "pw%d" % (i % 2))
            P.wait("scalar", "pw%d" % (i % 2), wt)
            if i >= 1:
                P.wait("scalar", "v", vts[i - 1])
            S(lambda e, b=ws[i % 2], tc=tilecols, c=swcol: e.activation(
                tqtmp[:, 0:tc], b[:, 0:tc], AF.Identity, scale=ccol[:, c:c + 1], bias=ccol[:, C_MAG:C_MAG + 1]))
            qs[i] = S(lambda e, tc=tilecols: e.activation(
                tqtmp[:, 0:tc], tqtmp[:, 0:tc], AF.Identity, bias=ccol[:, C_NMAG:C_NMAG + 1]))
            P.wait("vector", "s", qs[i])
            if i >= 1:
                P.wait("vector", "t", trd[i - 1])
            vts[i] = V(lambda e, tc=tilecols: e.tensor_scalar(tritb[:, 0:tc], tqtmp[:, 0:tc], 1.0, -1.0, ALU.min, ALU.max))
            trd[i] = writer(i, vts[i])

    def w1_writer(i, vtrit):
        pp = i % 2
        pg = pg0 if pp == 0 else pg1
        P.wait("tensor", "v", vtrit)
        P.wait("tensor", "v", evd[pp])
        for dc in range(ND):
            tv = T(lambda e, dc=dc, pg=pg: e.matmul(pg[:, dc * 128:dc * 128 + 128],
                                                    tritb[:, dc * 128:(dc + 1) * 128], ident_bf[:],
                                                    start=True, stop=True))
        P.wait("vector", "t", tv)
        P.wait("vector", "sg%d" % pp, stw[pp])
        evd[pp] = V(lambda e, pp=pp, pg=pg: e.tensor_copy(stg[pp][:], pg[:]))
        P.wait("sync", "v", evd[pp])
        stw[pp] = DW(lambda e, i=i, pp=pp: e.dma_start(out=w1q_sc[i], in_=stg[pp][:]), "sg%d" % pp)
        return tv

    def w2_writer(dt, vtrit):
        P.wait("tensor", "v", vtrit)
        tvl = 0
        for rho in range(4):
            pp = (dt * 4 + rho) % 2
            pg = pg0 if pp == 0 else pg1
            P.wait("tensor", "v", evd[pp])
            for j in range(8):
                tv = T(lambda e, rho=rho, j=j, pg=pg: e.matmul(
                    pg[:, j * 128:j * 128 + 128],
                    tritb[:, (rho * 8 + j) * 128:(rho * 8 + j) * 128 + 128], ident_bf[:],
                    start=True, stop=True))
            tvl = tv
            P.wait("vector", "t", tv)
            P.wait("vector", "sg%d" % pp, stw[pp])
            evd[pp] = V(lambda e, pp=pp, pg=pg: e.tensor_copy(stg[pp][:], pg[:]))
            P.wait("sync", "v", evd[pp])
            for j in range(8):
                stw[pp] = DW(lambda e, rho=rho, j=j, dt=dt, pp=pp: e.dma_start(
                    out=w2q_sc[rho * 8 + j, :, dt * 128:(dt + 1) * 128], in_=stg[pp][:, j * 128:(j + 1) * 128]), "sg%d" % pp)
        return tvl

    wout_cast_done = P.cnt["v"]

    def emit_wquant():
        global_evd = None
        P.wait("sync", "v", wout_cast_done)
        for args, writer in (((w1_d, 32, 1024, C_SW1, C_C1), w1_writer),
                             ((w2_d, 8, 4096, C_SW2, C_C2), w2_writer)):
            evd.clear(); evd.update({0: 0, 1: 0})
            stw.clear(); stw.update({0: 0, 1: 0})
            trd.clear()
            wquant(*args, writer)
        if debug:
            for s_ in ("sg0", "sg1", "wr"):
                P.wait("gpsimd", s_, P.cnt[s_])
            P.wait("gpsimd", "v", P.cnt["v"])
            for i in range(32):
                DG(lambda e, i=i: e.dma_start(out=dbg["d_w1q"][i], in_=w1q_sc[i]), "gd")
                DG(lambda e, i=i: e.dma_start(out=dbg["d_w2q"][i], in_=w2q_sc[i]), "gd")
            DG(lambda e: e.dma_start(out=dbg["d_ccol"], in_=ccol[:]), "gd")
            DG(lambda e: e.dma_start(out=dbg["d_g"][:, 0:16], in_=stat[:]), "gd")

    evd = {0: 0, 1: 0}
    stw = {0: 0, 1: 0}
    trd = {}
    fence()

    for b in range(NB if UPTO >= 2 else 0):
        P.op("scalar", lambda e: e.preload_act_table(AF.Sqrt))
        lastS = {0: 0, 1: 0}
        lastV = {0: 0, 1: 0}
        evT = {0: 0, 1: 0}
        for st in range(NT):
            pp = st % 2
            P.wait("gpsimd", "s", lastS[pp])
            P.wait("gpsimd", "v", lastV[pp])
            xd = DG(lambda e, b=b, st=st, pp=pp: e.dma_start(
                out=xbuf[:, pp * 1024:pp * 1024 + 1024], in_=x_d[b, st * 128:(st + 1) * 128, :]), "gx%d" % pp)
            xs = xbuf[:, pp * 1024:pp * 1024 + 1024]
            sc = stat[:, pp * 8: pp * 8 + 8]
            vprev = P.cnt["v"]
            P.wait("vector", "gx%d" % pp, xd)
            lastV[pp] = Vd(lambda e, xs=xs, sc=sc: e.reduce_sum(sc[:, 0:1], xs, AX.X))
            P.wait("scalar", "gx%d" % pp, xd)
            P.wait("scalar", "v", vprev)
            S(lambda e, xs=xs, sc=sc: e.activation(xc[:], xs, AF.Square, accum_out=sc[:, 1:2]))
            P.wait("vector", "s", P.cnt["s"])
            Vd(lambda e, sc=sc: e.tensor_scalar(sc[:, 2:3], sc[:, 0:1], -1.0 / 1024.0, None, ALU.mult))
            Vd(lambda e, sc=sc: e.tensor_tensor(sc[:, 3:4], sc[:, 2:3], sc[:, 2:3], ALU.mult))
            vv = Vd(lambda e, sc=sc: e.scalar_tensor_tensor(sc[:, 4:5], sc[:, 1:2], 1.0 / 1024.0, sc[:, 3:4], ALU.mult, ALU.subtract))
            P.wait("scalar", "v", vv)
            sv = S(lambda e, sc=sc: e.activation(sc[:, 4:5], sc[:, 4:5], AF.Sqrt, bias=ccol[:, C_EPS:C_EPS + 1]))
            P.wait("vector", "s", sv)
            Vd(lambda e, sc=sc: e.reciprocal(sc[:, 5:6], sc[:, 4:5]))
            sv = S(lambda e, xs=xs, sc=sc: e.activation(xc[:], xs, AF.Identity, bias=sc[:, 2:3]))
            lastS[pp] = sv
            P.wait("vector", "s", sv)
            Vd(lambda e, sc=sc: e.reduce_max(sc[:, 6:7], xc[:], AX.X, apply_absolute_value=True))
            Vd(lambda e, sc=sc: e.tensor_tensor(sc[:, 6:7], sc[:, 6:7], sc[:, 5:6], ALU.mult))
            Vd(lambda e, sc=sc: e.tensor_scalar(sc[:, 6:7], sc[:, 6:7], 1e-5, None, ALU.max))
            Vd(lambda e, sc=sc: e.reciprocal(sc[:, 7:8], sc[:, 6:7]))
            Vd(lambda e, sc=sc: e.tensor_scalar(sc[:, 7:8], sc[:, 7:8], 127.0, None, ALU.mult))
            Vd(lambda e, sc=sc: e.tensor_tensor(sc[:, 7:8], sc[:, 7:8], sc[:, 5:6], ALU.mult))
            vv = Vd(lambda e, sc=sc, st=st: e.tensor_copy(gst[:, st:st + 1], sc[:, 6:7]))
            P.wait("scalar", "v", vv)
            P.wait("scalar", "t", evTT if st >= 1 else 0)
            S(lambda e, sc=sc: e.activation(xn_t[:], xc[:], AF.Identity, scale=sc[:, 5:6]))
            S(lambda e, sc=sc: e.activation(xt1[:], xc[:], AF.Identity, scale=sc[:, 7:8], bias=ccol[:, C_MAG:C_MAG + 1]))
            sv = S(lambda e: e.activation(xi_t[:], xt1[:], AF.Identity, bias=ccol[:, C_NMAG:C_NMAG + 1]))
            lastS[pp] = sv
            P.wait("tensor", "s", sv)
            P.wait("tensor", "v", evT[0])
            for dc in range(ND):
                T(lambda e, dc=dc: e.matmul(pg0[:, dc * 128:dc * 128 + 128],
                                            xn_t[:, dc * 128:(dc + 1) * 128], ident_bf[:], start=True, stop=True))
            tv0 = P.cnt["t"]
            P.wait("tensor", "v", evT[1])
            for dc in range(ND):
                T(lambda e, dc=dc: e.matmul(pg1[:, dc * 128:dc * 128 + 128],
                                            xi_t[:, dc * 128:(dc + 1) * 128], ident_bf[:], start=True, stop=True))
            evTT = P.cnt["t"]
            P.wait("vector", "t", tv0)
            evT[0] = V(lambda e, st=st: e.tensor_copy(
                xnT[:].rearrange("p (dc t) -> p dc t", dc=8)[:, :, st * 128:st * 128 + 128],
                pg0[:].rearrange("p (dc c) -> p dc c", dc=8)))
            P.wait("vector", "t", evTT)
            evT[1] = V(lambda e, st=st: e.tensor_copy(
                xiT[:].rearrange("p (dc t) -> p dc t", dc=8)[:, :, st * 128:st * 128 + 128],
                pg1[:].rearrange("p (dc c) -> p dc c", dc=8)))
            lastV[pp] = evT[1]

        if debug and b == 0:
            fence()
            DG(lambda e: e.dma_start(out=dbg["d_xnT"], in_=xnT[:]), "gd")
            DG(lambda e: e.dma_start(out=dbg["d_xiT"], in_=xiT[:]), "gd")
            fence()
        if b == 0:
            emit_wquant()

        if UPTO < 3:
            continue
        P.wait("tensor", "v", P.cnt["v"] if False else lastV[0] if False else max(evT[0], evT[1]))
        P.wait("sync", "v", max(evT[0], evT[1]))
        P.wait("sync", "s", P.cnt["s"])
        for e_ in ("scalar", "vector", "tensor"):
            P.wait(e_, "sg0", P.cnt["sg0"])
            P.wait(e_, "sg1", P.cnt["sg1"])
            P.wait(e_, "wr", P.cnt["wr"])
        lastT_wqk = {0: 0, 1: 0}
        lastS_ps = {0: 0, 1: 0}
        for ct in range(16):
            pp = ct % 2
            P.wait("sync", "t", lastT_wqk[pp])
            wd = DW(lambda e, ct=ct, pp=pp: e.dma_start(out=wqkb[:, pp * 1024:pp * 1024 + 1024], in_=wqk_sc[ct]), "pw%d" % pp)
            P.wait("tensor", "pw%d" % pp, wd)
            for ts in range(4):
                qq = (ct * 4 + ts) % 2
                P.wait("tensor", "s", lastS_ps[qq])
                for dc in range(ND):
                    tv = T(lambda e, pp=pp, dc=dc, ts=ts, qq=qq: e.matmul(
                        pg2[:, qq * 512:qq * 512 + 512],
                        wqkb[:, pp * 1024 + dc * 128: pp * 1024 + dc * 128 + 128],
                        xnT[:, dc * 2048 + ts * 512: dc * 2048 + ts * 512 + 512],
                        start=(dc == 0), stop=(dc == ND - 1)))
                lastT_wqk[pp] = tv
                P.wait("scalar", "t", tv)
                lastS_ps[qq] = S(lambda e, ct=ct, ts=ts, qq=qq: e.activation(
                    qkT[:, ct * 2048 + ts * 512: ct * 2048 + ts * 512 + 512],
                    pg2[:, qq * 512:qq * 512 + 512], AF.Identity, bias=bcol[:, ct:ct + 1]))
        P.wait("tensor", "s", max(lastS_ps[0], lastS_ps[1]))
        lastV_ps = {0: 0, 1: 0}
        lastT_wv = 0
        for hf in range(2):
            P.wait("sync", "t", lastT_wv)
            for dc in range(ND):
                wd = DW(lambda e, hf=hf, dc=dc: e.dma_start(out=wvb[:, dc * 640:dc * 640 + 640], in_=wv_sc[hf, dc]), "gv")
            P.wait("tensor", "gv", wd)
            for st in range(NT):
                qq = st % 2
                pgv = pg3 if qq == 0 else pg2
                P.wait("tensor", "v", lastV_ps[qq])
                for dc in range(ND):
                    T(lambda e, dc=dc, st=st, pgv=pgv: e.matmul(
                        pgv[:, 0:512],
                        xnT[:, dc * 2048 + st * 128: dc * 2048 + st * 128 + 128],
                        wvb[:, dc * 640: dc * 640 + 512],
                        start=(dc == 0), stop=False))
                    T(lambda e, dc=dc, st=st, pgv=pgv: e.matmul(
                        pgv[:, 512:640],
                        xnT[:, dc * 2048 + st * 128: dc * 2048 + st * 128 + 128],
                        wvb[:, dc * 640 + 512: dc * 640 + 640],
                        start=(dc == 0), stop=False))
                T(lambda e, hf=hf, pgv=pgv: e.matmul(pgv[:, 0:512], ones_bf[0:1, 0:128],
                                                     bvrow[0:1, hf * 640: hf * 640 + 512], start=False, stop=True))
                tv = T(lambda e, hf=hf, pgv=pgv: e.matmul(pgv[:, 512:640], ones_bf[0:1, 0:128],
                                                          bvrow[0:1, hf * 640 + 512: hf * 640 + 640], start=False, stop=True))
                lastT_wv = tv
                P.wait("vector", "t", tv)
                base = st * 1280 + hf * 640
                lastV_ps[qq] = V(lambda e, base=base, pgv=pgv: e.tensor_copy(
                    v_sb[:, base: base + 640], pgv[:, 0:640]))
        fence()
        if debug and b == 0:
            DG(lambda e: e.dma_start(out=dbg["d_qkT"], in_=qkT[:]), "gd")
            DG(lambda e: e.dma_start(out=dbg["d_vsb"], in_=v_sb[:]), "gd")
            fence()

        if UPTO < 4:
            continue
        P.op("scalar", lambda e: e.preload_act_table(AF.Exp))
        sc_pair = [pg0, pg1]
        ctx_pair = [pg2, pg3]
        exp_s = {0: 0, 1: 0}
        avT = {0: 0, 1: 0}
        evac_v = {0: 0, 1: 0}
        nbw = {0: 0, 1: 0}
        def attn_tail(pi, cps, qc, pr, avlast):
            P.wait("vector", "t", avlast)
            P.wait("vector", "nb%d" % pi, nbw[pi])
            V(lambda e, cps=cps, pi=pi: e.reciprocal(rows_sb[64:65, pi * 512:pi * 512 + 512], cps[64:65, 0:512]))
            rv = V(lambda e, cps=cps, pi=pi: e.reciprocal(rows_sb[32:33, pi * 512:pi * 512 + 512], cps[32:33, 512:1024]))
            P.wait("sync", "v", rv)
            DW(lambda e, pi=pi: e.dma_start(out=den_sc[pi], in_=rows_sb[32:96:32, pi * 512:pi * 512 + 512]), "nb%d" % pi)
            P.wait("sync", "nb%d" % pi, P.cnt["nb%d" % pi])
            DW(lambda e, pi=pi: e.dma_start(out=nrm[0:64, pi * 512:pi * 512 + 512],
                                            in_=den_sc[pi, 1].unsqueeze(0).broadcast_to([64, 512])), "nb%d" % pi)
            nbw[pi] = DW(lambda e, pi=pi: e.dma_start(out=nrm[64:128, pi * 512:pi * 512 + 512],
                                                      in_=den_sc[pi, 0].unsqueeze(0).broadcast_to([64, 512])), "nb%d" % pi)
            P.wait("vector", "nb%d" % pi, nbw[pi])
            dst = pr * 2048 + qc * 512
            V(lambda e, cps=cps, dst=dst, pi=pi: e.tensor_tensor(
                ctx_sb[0:64, dst:dst + 512], cps[0:64, 0:512], nrm[0:64, pi * 512:pi * 512 + 512], ALU.mult))
            evac_v[pi] = V(lambda e, cps=cps, dst=dst, pi=pi: e.tensor_tensor(
                ctx_sb[64:128, dst:dst + 512], cps[64:128, 512:1024], nrm[64:128, pi * 512:pi * 512 + 512], ALU.mult))

        def attn_av(info):
            cps, kk, off, vb0, st0, sp, pi = (info["cps"], info["kk"], info["off"],
                                              info["vb0"], info["st0"], info["sp"], info["pi"])
            P.wait("tensor", info["gate"][0], info["gate"][1])
            if st0:
                P.wait("tensor", "v", evac_v[pi])
            T(lambda e, cps=cps, vb0=vb0, kk=kk, off=off, st0=st0, sp=sp: e.matmul(
                cps[:, off:512], v_sb[:, vb0:vb0 + 128], p_sb[:, kk * 1024 + off:kk * 1024 + 512],
                start=st0, stop=sp))
            avT[kk] = T(lambda e, cps=cps, vb0=vb0, kk=kk, off=off, st0=st0, sp=sp: e.matmul(
                cps[:, 512 + off:1024], v_sb[:, vb0 + 32:vb0 + 160], p_sb[:, kk * 1024 + 512 + off:kk * 1024 + 1024],
                start=st0, stop=sp))
            if sp:
                attn_tail(pi, cps, info["qc"], info["pr"], avT[kk])

        steps = [(qc, pr, kt, 4 * qc + 4) for qc in range(4) for pr in range(8) for kt in range(4 * qc + 4)]
        pending = None
        for kap, (qc, pr, kt, nkt) in enumerate(steps):
            pi = (qc * 8 + pr) % 2
            cps = ctx_pair[pi]
            kk = kap % 2
            scp = sc_pair[kk]
            diag = kt >= 4 * qc
            off = (kt - 4 * qc) * 128 if diag else 0
            kbase = (8 + pr) * 2048 + kt * 128
            qbase = pr * 2048 + qc * 512
            P.wait("tensor", "s", exp_s[kk])
            T(lambda e, scp=scp, kbase=kbase, qbase=qbase, off=off: e.matmul(
                scp[:, off:512], qkT[0:64, kbase:kbase + 128], qkT[0:64, qbase + off:qbase + 512],
                start=True, stop=True, tile_position=(0, 0)))
            tv = T(lambda e, scp=scp, kbase=kbase, qbase=qbase, off=off: e.matmul(
                scp[:, 512 + off:1024], qkT[64:128, kbase:kbase + 128], qkT[64:128, qbase + off:qbase + 512],
                start=True, stop=True, tile_position=(64, 0)))
            P.wait("scalar", "t", tv)
            P.wait("scalar", "t", avT[kk])
            exp_s[kk] = S(lambda e, scp=scp, kk=kk, off=off: e.activation(
                p_sb[:, kk * 1024:kk * 1024 + 1024].rearrange("p (h c) -> p h c", h=2)[:, :, off:512],
                scp[:].rearrange("p (h c) -> p h c", h=2)[:, :, off:512],
                AF.Exp, scale=0.125))
            if diag:
                P.wait("vector", "s", exp_s[kk])
                mv = V(lambda e, kk=kk, off=off: e.tensor_tensor(
                    p_sb[:, kk * 1024:kk * 1024 + 1024].rearrange("p (h c) -> p h c", h=2)[:, :, off:off + 128],
                    p_sb[:, kk * 1024:kk * 1024 + 1024].rearrange("p (h c) -> p h c", h=2)[:, :, off:off + 128],
                    tri_bf[:].unsqueeze(1).broadcast_to([128, 2, 128]), ALU.mult))
                gate = ("v", mv)
            else:
                gate = ("s", exp_s[kk])
            info = dict(cps=cps, kk=kk, off=off, vb0=kt * 1280 + pr * 160,
                        st0=(kt == 0), sp=(kt == nkt - 1), pi=pi, qc=qc, pr=pr, gate=gate)
            if pending is not None:
                attn_av(pending)
            pending = info
        attn_av(pending)
        fence()
        if debug and b == 0:
            DG(lambda e: e.dma_start(out=dbg["d_ctx"], in_=ctx_sb[:]), "gd")
            fence()

        if UPTO < 5:
            continue
        for dc in range(ND):
            wd = DW(lambda e, dc=dc: e.dma_start(out=wo_res[:, dc * 1024:dc * 1024 + 1024], in_=wo_sc[dc]), "gv")
        P.wait("tensor", "gv", wd)
        addv = {0: 0, 1: 0}
        accw = {0: 0, 1: 0}
        for st in range(NT):
            pp = st % 2
            pg = pg0 if pp == 0 else pg1
            P.wait("gpsimd", "v", addv[pp])
            xd = DG(lambda e, b=b, st=st, pp=pp: e.dma_start(
                out=xrb[:, pp * 1024:pp * 1024 + 1024], in_=x_d[b, st * 128:(st + 1) * 128, :]), "gx%d" % pp)
            P.wait("tensor", "v", addv[pp])
            for half in range(2):
                for dc in range(ND):
                    T(lambda e, pg=pg, half=half, dc=dc, st=st: e.matmul(
                        pg[:, half * 512:half * 512 + 512],
                        ctx_sb[:, dc * 2048 + st * 128: dc * 2048 + st * 128 + 128],
                        wo_res[:, dc * 1024 + half * 512: dc * 1024 + half * 512 + 512],
                        start=(dc == 0), stop=False))
                tv = T(lambda e, pg=pg, half=half: e.matmul(
                    pg[:, half * 512:half * 512 + 512], ones_bf[0:1, 0:128],
                    borow[0:1, half * 512:half * 512 + 512], start=False, stop=True))
            P.wait("vector", "t", tv)
            P.wait("vector", "gx%d" % pp, xd)
            P.wait("vector", "ga%d" % pp, accw[pp])
            addv[pp] = V(lambda e, pg=pg, pp=pp: e.tensor_tensor(
                acst[:, pp * 1024:pp * 1024 + 1024], pg[:], xrb[:, pp * 1024:pp * 1024 + 1024], ALU.add))
            P.wait("gpsimd", "v", addv[pp])
            accw[pp] = DG(lambda e, b=b, st=st, pp=pp: e.dma_start(
                out=acc_sc[b, st], in_=acst[:, pp * 1024:pp * 1024 + 1024]), "ga%d" % pp)
        fence()
        if debug and b == 0:
            P.wait("gpsimd", "ga0", P.cnt["ga0"])
            P.wait("gpsimd", "ga1", P.cnt["ga1"])
            for st in range(NT):
                DG(lambda e, st=st: e.dma_start(out=dbg["d_acc"][st], in_=acc_sc[0, st]), "gd")
            fence()

        if UPTO < 6:
            continue
        P.op("scalar", lambda e: e.preload_act_table(AF.Silu))
        vv = Vd(lambda e: e.tensor_scalar(stat[:], gst[:], ccol[:, C_C1:C_C1 + 1], None, ALU.mult))
        P.wait("tensor", "v", vv)
        tv = T(lambda e: e.matmul(pg1[0:16, 0:128], stat[:], ident_f[:], start=True, stop=True))
        P.wait("vector", "t", tv)
        vv = V(lambda e: e.tensor_copy(nrm[0:16, 0:128], pg1[0:16, 0:128]))
        P.wait("gpsimd", "v", vv)
        gw = DG(lambda e: e.dma_start(out=row_sc[0:2048].rearrange("(r c) -> r c", r=16), in_=nrm[0:16, 0:128]), "gr")
        P.wait("gpsimd", "gr", gw)
        gw = DG(lambda e: e.dma_start(out=g_sb[:], in_=row_sc[0:2048].unsqueeze(0).broadcast_to([128, 2048])), "gr")
        P.wait("vector", "gr", gw)
        stash0 = V(lambda e: e.memset(stash[:], 0.0))
        if debug and b == 0:
            P.wait("gpsimd", "v", P.cnt["v"])
            DG(lambda e: e.dma_start(out=dbg["d_g"], in_=g_sb[:]), "gd")

        pmT_t = [0]
        w1T = {0: 0, 1: 0}
        w2T = {0: 0, 1: 0}
        w1ev = {0: 0, 1: 0, 2: 0, 3: 0}
        w2ev = {0: 0, 1: 0}
        w2last = {0: 0, 1: 0}
        uV = {k: stash0 for k in range(8)}
        uSS = {k: 0 for k in range(8)}
        pg1free_s = [0]
        pg1free_v = [0]
        pg1w = [-1]
        roundV_w = {-1: 0}
        roundS_w = {-1: 0}
        silu31 = {}
        accv = {0: 0, 1: 0}
        outw = {0: 0, 1: 0}
        sb_g = 0

        ub4 = [ubuf[0], ubuf[1], xc[:, 0:388], xc[:, 388:776]]

        def w1conv(c):
            have = c <= 7
            nonlocal_first = [True]
            for grp in range(8):
                hcs = [grp * 4 + j for j in range(4)]
                if have:
                    ppw = grp % 2
                    P.wait("sync", "t", w1T[ppw])
                    wd = DW(lambda e, grp=grp, ppw=ppw: e.dma_start(
                        out=w1b[:, ppw * 4096:ppw * 4096 + 4096].rearrange("p (h q) -> p h q", h=4),
                        in_=w1q_sc[grp * 4:grp * 4 + 4].rearrange("h p q -> p h q")), "pw%d" % ppw)
                    P.wait("tensor", "pw%d" % ppw, wd)
                for hc in hcs:
                    sl = hc % 4
                    P.wait("scalar", "v", uV[sl])
                    S(lambda e, sl=sl, hc=hc: e.copy(ub4[sl][:, 0:129], stash[:, hc * 129:hc * 129 + 129]))
                evc = {}
                for hc in hcs:
                    sl = hc % 4
                    if have:
                        pw = hc % 2
                        P.wait("tensor", "v", w1ev[pw])
                        for dc in range(ND):
                            tmm = T(lambda e, j=hc % 4, dc=dc, c=c, pw=pw, ppw=(hc // 4) % 2: e.matmul(
                                pg0[:, pw * 512:pw * 512 + 256],
                                w1b[:, ppw * 4096 + j * 1024 + dc * 128: ppw * 4096 + j * 1024 + dc * 128 + 128],
                                xiT[:, dc * 2048 + c * 256: dc * 2048 + c * 256 + 256],
                                start=(dc == 0), stop=(dc == ND - 1)))
                        w1T[(hc // 4) % 2] = tmm
                        P.wait("vector", "t", tmm)
                        P.wait("vector", "s", uSS[sl])
                        w1ev[pw] = V(lambda e, sl=sl, pw=pw, c=c: e.tensor_tensor(
                            ub4[sl][:, 129:385], pg0[:, pw * 512:pw * 512 + 256],
                            g_sb[:, c * 256:c * 256 + 256], ALU.mult))
                        evc[hc] = w1ev[pw]
                    else:
                        P.wait("vector", "s", uSS[sl])
                        evc[hc] = V(lambda e, sl=sl: e.memset(ub4[sl][:, 129:388], 0.0))
                inc_ = {}
                for hc in hcs:
                    sl = hc % 4
                    P.wait("scalar", "v", evc[hc])
                    if nonlocal_first[0]:
                        P.wait("scalar", "v", roundV_w[c - 1])
                        nonlocal_first[0] = False
                    inc_[hc] = S(lambda e, sl=sl, hc=hc: e.activation(
                        h2w[:, hc * 256:hc * 256 + 256], ub4[sl][:, 0:256], AF.Identity,
                        scale=cwsb[:, hc * 3:hc * 3 + 1], bias=cbsb[:, hc:hc + 1]))
                    if have:
                        uSS[sl] = S(lambda e, sl=sl, hc=hc: e.copy(
                            stash[:, hc * 129:hc * 129 + 129], ub4[sl][:, 256:385]))
                    else:
                        uSS[sl] = inc_[hc]
                for hc in hcs:
                    sl = hc % 4
                    P.wait("vector", "s", inc_[hc])
                    V(lambda e, sl=sl, hc=hc: e.scalar_tensor_tensor(
                        h2w[:, hc * 256:hc * 256 + 256], ub4[sl][:, 1:257], cwsb[:, hc * 3 + 1:hc * 3 + 2],
                        h2w[:, hc * 256:hc * 256 + 256], ALU.mult, ALU.add))
                    uV[sl] = V(lambda e, sl=sl, hc=hc: e.scalar_tensor_tensor(
                        h2w[:, hc * 256:hc * 256 + 256], ub4[sl][:, 2:258], cwsb[:, hc * 3 + 2:hc * 3 + 3],
                        h2w[:, hc * 256:hc * 256 + 256], ALU.mult, ALU.add))
                P.wait("scalar", "v", uV[hcs[-1] % 4])
                sl_ = S(lambda e, grp=grp: e.activation(
                    h2w[:, grp * 1024:grp * 1024 + 1024], h2w[:, grp * 1024:grp * 1024 + 1024], AF.Silu))
            silu31[c] = sl_

        def quantscale(cq):
            P.wait("vector", "s", silu31[cq])
            P.wait("vector", "t", pmT_t[0])
            pmv = V(lambda e: e.tensor_reduce(
                pmb[:, 0:256], h2w[:].rearrange("p (h t) -> p t h", h=32),
                AX.X, ALU.max, apply_absolute_value=True))
            P.wait("tensor", "v", pmv)
            P.wait("tensor", "s", P.cnt["s"])
            T(lambda e: e.matmul(pg1[:, 0:128], pmb[:, 0:128], ident_f[:], start=True, stop=True))
            tv = T(lambda e: e.matmul(pg1[:, 128:256], pmb[:, 128:256], ident_f[:], start=True, stop=True))
            pmT_t[0] = tv
            P.wait("vector", "t", tv)
            Vd(lambda e: e.tensor_reduce(
                mx[:, 0:2], pg1[:, 0:256].rearrange("p (w c) -> p w c", w=2), AX.X, ALU.max,
                apply_absolute_value=False))
            Vd(lambda e: e.tensor_scalar(mx[:, 2:4], mx[:, 0:2], 1e-5, None, ALU.max))
            Vd(lambda e: e.tensor_scalar(mx[:, 4:6], mx[:, 2:4], ccol[:, C_C2:C_C2 + 1], None, ALU.mult))
            Vd(lambda e: e.reciprocal(mx[:, 6:8], mx[:, 2:4]))
            vvs = Vd(lambda e: e.tensor_scalar(mx[:, 6:8], mx[:, 6:8], 127.0, None, ALU.mult))
            P.wait("tensor", "v", vvs)
            tv = T(lambda e: e.matmul(pg1[0:2, 256:384], mx[:, 6:8], ident_f[:], start=True, stop=True))
            P.wait("vector", "t", tv)
            vv = V(lambda e: e.tensor_copy(nrm[0:2, 0:128], pg1[0:2, 256:384]))
            pg1free_v[0] = vv
            P.wait("gpsimd", "v", vv)
            P.wait("gpsimd", "gr", P.cnt["gr"])
            sg = DG(lambda e: e.dma_start(out=row_sc[2048:2304].rearrange("(r c) -> r c", r=2), in_=nrm[0:2, 0:128]), "gr")
            P.wait("gpsimd", "gr", sg)
            sg = DG(lambda e: e.dma_start(out=sb_sb[:], in_=row_sc[2048:2304].unsqueeze(0).broadcast_to([128, 256])), "gr")
            P.wait("vector", "gr", sg)
            qm = V(lambda e: e.tensor_tensor(
                h2w[:].rearrange("p (h t) -> p h t", h=32),
                h2w[:].rearrange("p (h t) -> p h t", h=32),
                sb_sb[:].unsqueeze(1).broadcast_to([128, 32, 256]), ALU.mult))
            P.wait("vector", "t", max(w2T[0], w2T[1]))
            P.wait("scalar", "t", max(w2T[0], w2T[1]))
            P.wait("scalar", "v", qm)
            roundV_w[cq] = V(lambda e: e.tensor_scalar(h2q[:, 0:4096], h2w[:, 0:4096], MAGIC, -MAGIC, ALU.add, ALU.add))
            for k in range(4):
                S(lambda e, k=k: e.activation(xbuf[:, 0:1024], h2w[:, 4096 + k * 1024:4096 + k * 1024 + 1024],
                                              AF.Identity, bias=ccol[:, C_MAG:C_MAG + 1]))
                rS = S(lambda e, k=k: e.activation(h2q[:, 4096 + k * 1024:4096 + k * 1024 + 1024], xbuf[:, 0:1024],
                                                   AF.Identity, bias=ccol[:, C_NMAG:C_NMAG + 1]))
            roundS_w[cq] = rS

        def w2block(cq, b=b):
            stl = [t_ for t_ in (2 * cq - 1, 2 * cq) if 0 <= t_ < 16]
            accd = {}
            for st_out in stl:
                par = st_out % 2
                P.wait("gpsimd", "v", accv[par])
                accd[st_out] = DG(lambda e, b=b, st_out=st_out, par=par: e.dma_start(
                    out=accb[:, par * 1024:par * 1024 + 1024], in_=acc_sc[b, st_out]), "ga%d" % par)
            for grp in range(8):
                pp2 = grp % 2
                P.wait("sync", "t", w2T[pp2])
                wd2 = DW(lambda e, grp=grp, pp2=pp2: e.dma_start(
                    out=w2b[:, pp2 * 4096:pp2 * 4096 + 4096].rearrange("p (h q) -> p h q", h=4),
                    in_=w2q_sc[grp * 4:grp * 4 + 4].rearrange("h p q -> p h q")), "qw%d" % pp2)
                P.wait("tensor", "qw%d" % pp2, wd2)
                if grp == 0:
                    P.wait("tensor", "v", roundV_w[cq])
                    P.wait("tensor", "s", roundS_w[cq])
                for j in range(4):
                    hc = grp * 4 + j
                    for st_out in stl:
                        w = st_out - (2 * cq - 1)
                        par = st_out % 2
                        pgw = pg2 if par == 0 else pg3
                        if hc == 0:
                            P.wait("tensor", "v", w2ev[par])
                        for half in range(2):
                            tmm = T(lambda e, pgw=pgw, hc=hc, w=w, pp2=pp2, j=j, half=half: e.matmul(
                                pgw[:, half * 512:half * 512 + 512],
                                h2q[:, hc * 256 + w * 128: hc * 256 + w * 128 + 128],
                                w2b[:, pp2 * 4096 + j * 1024 + half * 512: pp2 * 4096 + j * 1024 + half * 512 + 512],
                                start=(hc == 0), stop=(hc == NHC - 1)))
                        w2T[pp2] = tmm
                        if hc == NHC - 1:
                            w2last[par] = tmm
            for st_out in stl:
                w = st_out - (2 * cq - 1)
                par = st_out % 2
                pgw = pg2 if par == 0 else pg3
                P.wait("vector", "ga%d" % par, accd[st_out])
                P.wait("vector", "t", w2last[par])
                P.wait("vector", "go%d" % par, outw[par])
                accv[par] = V(lambda e, pgw=pgw, par=par, w=w: e.scalar_tensor_tensor(
                    outb[:, par * 1024:par * 1024 + 1024], pgw[:], mx[:, 4 + w:5 + w],
                    accb[:, par * 1024:par * 1024 + 1024], ALU.mult, ALU.add))
                w2ev[par] = accv[par]
                P.wait("gpsimd", "v", accv[par])
                outw[par] = DG(lambda e, b=b, st_out=st_out, par=par: e.dma_start(
                    out=out_d[b, st_out * 128:(st_out + 1) * 128, :], in_=outb[:, par * 1024:par * 1024 + 1024]), "go%d" % par)

        for c in range(FFNW):
            if 1 <= c:
                quantscale(c - 1)
            if c <= 8:
                w1conv(c)
            if 1 <= c:
                w2block(c - 1)
        fence()

    for s_ in list(P.cnt.keys()):
        if s_ not in ("t", "s", "v") and P.cnt[s_] > 0:
            P.wait("gpsimd", s_, P.cnt[s_])

    from contextlib import ExitStack
    with ExitStack() as stack:
        sems = {}
        for s_ in list(P.cnt.keys()):
            if P.cnt[s_] > 0:
                sems[s_] = stack.enter_context(nc.semaphore("sem_" + s_))
        block = stack.enter_context(nc.Block())

        @block.sync
        def _(sync):
            P.replay("sync", sync, sems)

        @block.gpsimd
        def _(gpsimd):
            P.replay("gpsimd", gpsimd, sems)

        @block.tensor
        def _(tensor):
            P.replay("tensor", tensor, sems)

        @block.scalar
        def _(scalar):
            P.replay("scalar", scalar, sems)

        @block.vector
        def _(vector):
            P.replay("vector", vector, sems)

    return nc


_NC_CACHE = {}


def _get_nc(debug=False):
    key = (debug, UPTO, FFNW, W1MODE)
    if key not in _NC_CACHE:
        _NC_CACHE[key] = build(debug)
    return _NC_CACHE[key]


def kernel(L0_attn_hidden_states, L0_attn_W_attn, L0_attn_b_attn, L0_attn_W_out,
           L0_attn_b_out, L0_ffn_w1, L0_ffn_conv_w, L0_ffn_conv_b, L0_ffn_w2,
           _debug=False):
    x = np.ascontiguousarray(L0_attn_hidden_states, dtype=np.float32)
    shards = x.reshape(8, NB, 2048, 1024)
    nc = _get_nc(_debug)
    common = {
        "W_attn": np.ascontiguousarray(L0_attn_W_attn, dtype=np.float32),
        "b_attn": np.ascontiguousarray(L0_attn_b_attn, dtype=np.float32),
        "W_out": np.ascontiguousarray(L0_attn_W_out, dtype=np.float32),
        "b_out": np.ascontiguousarray(L0_attn_b_out, dtype=np.float32),
        "w1": np.ascontiguousarray(L0_ffn_w1, dtype=np.float32),
        "cw": np.ascontiguousarray(L0_ffn_conv_w, dtype=np.float32).reshape(4096, 3),
        "cb": np.ascontiguousarray(L0_ffn_conv_b, dtype=np.float32),
        "w2": np.ascontiguousarray(L0_ffn_w2, dtype=np.float32),
        "ident_bf": np.eye(128, dtype=ml_dtypes.bfloat16),
        "ident_f": np.eye(128, dtype=np.float32),
        "tri_bf": np.triu(np.ones((128, 128), dtype=np.float32)).astype(ml_dtypes.bfloat16),
    }
    in_maps = [dict(common, x=np.ascontiguousarray(shards[i])) for i in range(8)]
    res = run_bass_kernel_spmd(nc, in_maps, core_ids=list(range(8)))
    out = np.concatenate([r["out"].reshape(NB, 2048, 1024) for r in res.results], axis=0)
    if _debug:
        return out, res
    return out


# version 20
# speedup vs baseline: 1.0917x; 1.0143x over previous
import numpy as np
import ml_dtypes
import concourse.bass as bass
import concourse.mybir as mybir
from concourse.bass_utils import run_bass_kernel_spmd

F32 = mybir.dt.float32
BF16 = mybir.dt.bfloat16
AF = mybir.ActivationFunctionType
ALU = mybir.AluOpType
AX = mybir.AxisListType

MAGIC = 12582912.0
UPTO = 6
FFNW = 10
W1MODE = 0
NB = 2
NT = 16
ND = 8
NHC = 32

ENGS = ("sync", "gpsimd", "tensor", "scalar", "vector")


class Plan:
    def __init__(self):
        from collections import defaultdict
        self.ops = {e: [] for e in ENGS}
        self.cnt = defaultdict(int)

    def inc(self, eng, fn, sem, by=1):
        self.ops[eng].append(("inc", fn, sem, by))
        self.cnt[sem] += by
        return self.cnt[sem]

    def op(self, eng, fn):
        self.ops[eng].append(("op", fn))

    def wait(self, eng, sem, val):
        if val > 0:
            self.ops[eng].append(("wait", sem, val))

    def replay(self, eng_name, obj, sems):
        for it in self.ops[eng_name]:
            if it[0] == "op":
                it[1](obj)
            elif it[0] == "inc":
                it[1](obj).then_inc(sems[it[2]], it[3])
            else:
                obj.wait_ge(sems[it[1]], it[2])


def build(debug=False):
    nc = bass.Bass("TRN2", target_bir_lowering=False)
    P = Plan()

    def T(fn):
        return P.inc("tensor", fn, "t")

    def S(fn):
        return P.inc("scalar", fn, "s")

    def V(fn):
        return P.inc("vector", fn, "v")

    def Vd(fn):
        n = P.inc("vector", fn, "v")
        P.wait("vector", "v", n)
        return n

    def DW(fn, sem):
        return P.inc("sync", fn, sem, 16)

    def DG(fn, sem):
        return P.inc("gpsimd", fn, sem, 16)

    SYNC_SEMS = ("pw0", "pw1", "qw0", "qw1", "gv", "wr", "sg0", "sg1", "nb0", "nb1")
    GP_SEMS = ("gi", "gx0", "gx1", "ga0", "ga1", "go0", "go1", "gr", "gd")

    def fence():
        from collections import defaultdict
        c = defaultdict(int, P.cnt)
        dma = [s for s in list(SYNC_SEMS) + list(GP_SEMS) if c.get(s, 0) > 0]
        P.wait("tensor", "s", c["s"]); P.wait("tensor", "v", c["v"])
        P.wait("scalar", "t", c["t"]); P.wait("scalar", "v", c["v"])
        P.wait("vector", "t", c["t"]); P.wait("vector", "s", c["s"])
        for q in ("sync", "gpsimd"):
            P.wait(q, "t", c["t"]); P.wait(q, "s", c["s"]); P.wait(q, "v", c["v"])
            for s in dma:
                P.wait(q, s, c[s])
        for e in ("tensor", "scalar", "vector"):
            for s in dma:
                P.wait(e, s, c[s])

    x_d = nc.dram_tensor("x", [NB, 2048, 1024], F32, kind="ExternalInput").ap()
    wattn_d = nc.dram_tensor("W_attn", [1024, 3072], F32, kind="ExternalInput").ap()
    battn_d = nc.dram_tensor("b_attn", [3072], F32, kind="ExternalInput").ap()
    wout_d = nc.dram_tensor("W_out", [1024, 1024], F32, kind="ExternalInput").ap()
    bout_d = nc.dram_tensor("b_out", [1024], F32, kind="ExternalInput").ap()
    w1_d = nc.dram_tensor("w1", [4096, 1024], F32, kind="ExternalInput").ap()
    cw_d = nc.dram_tensor("cw", [4096, 3], F32, kind="ExternalInput").ap()
    cb_d = nc.dram_tensor("cb", [4096], F32, kind="ExternalInput").ap()
    w2_d = nc.dram_tensor("w2", [1024, 4096], F32, kind="ExternalInput").ap()
    identbf_d = nc.dram_tensor("ident_bf", [128, 128], BF16, kind="ExternalInput").ap()
    identf_d = nc.dram_tensor("ident_f", [128, 128], F32, kind="ExternalInput").ap()
    tribf_d = nc.dram_tensor("tri_bf", [128, 128], BF16, kind="ExternalInput").ap()
    out_d = nc.dram_tensor("out", [NB, 2048, 1024], F32, kind="ExternalOutput").ap()

    wqk_sc = nc.dram_tensor("wqk_sc", [16, 128, 1024], BF16, kind="Internal").ap()
    wv_sc = nc.dram_tensor("wv_sc", [2, 8, 128, 640], BF16, kind="Internal").ap()
    wo_sc = nc.dram_tensor("wo_sc", [8, 128, 1024], BF16, kind="Internal").ap()
    w1q_sc = nc.dram_tensor("w1q_sc", [32, 128, 1024], BF16, kind="Internal").ap()
    w2q_sc = nc.dram_tensor("w2q_sc", [32, 128, 1024], BF16, kind="Internal").ap()
    acc_sc = nc.dram_tensor("acc_sc", [NB, 16, 128, 1024], F32, kind="Internal").ap()
    row_sc = nc.dram_tensor("row_sc", [4096], F32, kind="Internal").ap()
    den_sc = nc.dram_tensor("den_sc", [2, 2, 512], F32, kind="Internal").ap()

    dbg = {}
    if debug:
        for nm, shp, dt in [("d_xnT", [128, 16384], BF16), ("d_xiT", [128, 16384], BF16),
                            ("d_qkT", [128, 32768], BF16), ("d_vsb", [128, 20480], BF16),
                            ("d_ctx", [128, 16384], BF16), ("d_g", [128, 2048], F32),
                            ("d_acc", [16, 128, 1024], F32),
                            ("d_w1q", [32, 128, 1024], BF16), ("d_w2q", [32, 128, 1024], BF16),
                            ("d_ccol", [128, 16], F32)]:
            dbg[nm] = nc.dram_tensor(nm, shp, dt, kind="ExternalOutput").ap()

    def sbt(name, shape, dtype, off):
        return nc.alloc_sbuf_tensor_at(name, shape, dtype, offset=off).ap()

    cur = [16512]

    def nxt(sz, c=cur):
        r = c[0]
        c[0] = (c[0] + sz + 31) & ~31
        return r

    ident_bf = sbt("identbf", [128, 128], BF16, nxt(256))
    ones_bf = sbt("onesbf", [128, 128], BF16, nxt(256))
    tri_bf = sbt("tribf", [128, 128], BF16, nxt(256))
    ident_f = sbt("identf", [128, 128], F32, nxt(512))
    ones_f = sbt("onesf", [128, 128], F32, nxt(512))
    ccol = sbt("ccol", [128, 16], F32, nxt(64))
    C_MAG, C_NMAG, C_EPS, C_C1, C_C2, C_SW1, C_SW2 = 0, 1, 2, 3, 4, 5, 6
    bcol = sbt("bcol", [128, 16], F32, nxt(64))
    cwsb = sbt("cwsb", [128, 96], F32, nxt(384))
    cbsb = sbt("cbsb", [128, 32], F32, nxt(128))
    stat = sbt("stat", [128, 16], F32, nxt(64))
    gst = sbt("gst", [128, 16], F32, nxt(64))
    bvrow = sbt("bvrow", [1, 1280], BF16, nxt(2560))
    borow = sbt("borow", [1, 1024], BF16, nxt(2048))
    nrm = sbt("nrm", [128, 1024], F32, nxt(4096))
    rows_sb = sbt("rowssb", [128, 1024], F32, nxt(4096))

    R0 = (cur[0] + 63) & ~63
    xiT = sbt("xiT", [128, 16384], BF16, R0)
    XNC = R0 + 32768
    xnT = sbt("xnT", [128, 16384], BF16, XNC)
    ctx_sb = sbt("ctxsb", [128, 16384], BF16, XNC)
    h2w = sbt("h2w", [128, 8192], F32, XNC)
    QK = XNC + 32768
    qkT = sbt("qkT", [128, 32768], BF16, QK)
    VSB = QK + 65536
    v_sb = sbt("vsb", [128, 20480], BF16, VSB)
    AQ = VSB + 40960
    xbuf = sbt("xbuf", [128, 2048], F32, AQ)
    xc = sbt("xc", [128, 1024], F32, AQ + 8192)
    xt1 = sbt("xt1", [128, 1024], F32, AQ + 12288)
    xn_t = sbt("xnt", [128, 1024], BF16, AQ + 16384)
    xi_t = sbt("xit", [128, 1024], BF16, AQ + 18432)
    wqkb = sbt("wqkb", [128, 2048], BF16, AQ)
    wvb = sbt("wvb", [128, 5120], BF16, AQ + 4096)
    p_sb = sbt("psb", [128, 2048], BF16, AQ)
    SB_END = AQ + 20480

    c0 = [QK]
    ws0 = sbt("ws0", [128, 4096], F32, nxt(16384, c0))
    ws1 = sbt("ws1", [128, 4096], F32, nxt(16384, c0))
    wcast = sbt("wcast", [128, 4096], BF16, nxt(8192, c0))
    tqtmp = sbt("tqtmp", [128, 4096], F32, nxt(16384, c0))
    tritb = sbt("tritb", [128, 4096], BF16, nxt(8192, c0))
    stg = [sbt("stg0", [128, 1024], BF16, nxt(2048, c0)),
           sbt("stg1", [128, 1024], BF16, nxt(2048, c0))]
    zstage = sbt("zstage", [128, 128], BF16, nxt(256, c0))
    ws = [ws0, ws1]

    wo_res = sbt("wores", [128, 8192], BF16, QK)
    xrb = sbt("xrb", [128, 2048], F32, QK + 16384)
    acst = sbt("acst", [128, 2048], F32, QK + 24576)

    c1 = [QK]
    g_sb = sbt("gsb", [128, 2048], F32, nxt(8192, c1))
    g_row = sbt("grow", [1, 2048], F32, nxt(8192, c1))
    stash = sbt("stash", [128, 4128], F32, nxt(16512, c1))
    ubuf = [sbt("u0", [128, 388], F32, nxt(1552, c1)),
            sbt("u1", [128, 388], F32, nxt(1552, c1))]
    h2q = sbt("h2q", [128, 8192], BF16, nxt(16384, c1))
    w1b = sbt("w1b", [128, 8192], BF16, nxt(16384, c1))
    w2b = sbt("w2b", [128, 8192], BF16, nxt(16384, c1))
    accb = sbt("accb", [128, 2048], F32, nxt(8192, c1))
    outb = sbt("outb", [128, 2048], F32, nxt(8192, c1))
    mx = sbt("mx", [128, 16], F32, nxt(64, c1))
    sbrow = sbt("sbrow", [1, 256], F32, nxt(1024, c1))
    sb_sb = sbt("sbsb", [128, 256], F32, nxt(1024, c1))
    pmb = sbt("pmb", [128, 512], F32, nxt(2048, c1))
    assert c1[0] <= VSB + 40960, c1[0]

    pg0 = nc.alloc_psum_tensor("pg0", [128, 1024], F32).ap()
    pg1 = nc.alloc_psum_tensor("pg1", [128, 1024], F32).ap()
    pg2 = nc.alloc_psum_tensor("pg2", [128, 1024], F32).ap()
    pg3 = nc.alloc_psum_tensor("pg3", [128, 1024], F32).ap()

    V(lambda e: e.memset(ones_f[:], 1.0))
    V(lambda e: e.memset(ones_bf[:], 1.0))
    V(lambda e: e.memset(ccol[:, C_MAG:C_MAG + 1], MAGIC))
    V(lambda e: e.memset(ccol[:, C_NMAG:C_NMAG + 1], -MAGIC))
    V(lambda e: e.memset(ccol[:, C_EPS:C_EPS + 1], 1e-5))
    DG(lambda e: e.dma_start(out=ident_bf[:], in_=identbf_d), "gi")
    DG(lambda e: e.dma_start(out=ident_f[:], in_=identf_d), "gi")
    DG(lambda e: e.dma_start(out=tri_bf[:], in_=tribf_d), "gi")
    DG(lambda e: e.dma_start(out=cwsb[:].rearrange("p (h j) -> p h j", h=32),
                             in_=cw_d.rearrange("(h p) j -> p h j", p=128)), "gi")
    DG(lambda e: e.dma_start(out=cbsb[:], in_=cb_d.rearrange("(h p) -> p h", p=128),
                             allow_slow_non_contiguous=True), "gi")
    DG(lambda e: e.dma_start(out=bcol[:], in_=battn_d[0:2048].rearrange("(c p) -> p c", p=128),
                             allow_slow_non_contiguous=True), "gi")
    DG(lambda e: e.dma_start(out=nrm[0:1, :], in_=battn_d[2048:3072].unsqueeze(0)), "gi")
    gg = DG(lambda e: e.dma_start(out=xc[0:1, :], in_=bout_d.unsqueeze(0)), "gi")
    P.wait("vector", "gi", gg)
    P.wait("tensor", "gi", gg)
    P.wait("scalar", "gi", gg)
    V(lambda e: e.memset(bvrow[:], 0.0))
    V(lambda e: e.memset(bvrow[:].rearrange("o (q c) -> o q c", c=160)[:, :, 64:66], 1.0))
    V(lambda e: e.tensor_copy(bvrow[:].rearrange("o (q c) -> o q c", c=160)[:, :, 0:64],
                              nrm[0:1, :].rearrange("o (h c) -> o h c", c=128)[:, :, 0:64]))
    V(lambda e: e.tensor_copy(bvrow[:].rearrange("o (q c) -> o q c", c=160)[:, :, 96:160],
                              nrm[0:1, :].rearrange("o (h c) -> o h c", c=128)[:, :, 64:128]))
    V(lambda e: e.memset(zstage[:], 0.0))
    V(lambda e: e.tensor_copy(borow[:], xc[0:1, :]))
    P.op("scalar", lambda e: e.preload_act_table(AF.Sqrt))

    cast_v = {}
    wr_w = 0
    for dc in range(ND):
        if dc >= 2:
            P.wait("sync", "v", cast_v[dc - 2])
        wt = DW(lambda e, dc=dc, b=ws[dc % 2]: e.dma_start(out=b[:, 0:3072], in_=wattn_d[dc * 128:(dc + 1) * 128, :]), "pw%d" % (dc % 2))
        P.wait("vector", "pw%d" % (dc % 2), wt)
        P.wait("vector", "wr", P.cnt["wr"])
        cast_v[dc] = V(lambda e, b=ws[dc % 2]: e.tensor_copy(wcast[:, 0:3072], b[:, 0:3072]))
        P.wait("sync", "v", cast_v[dc])
        for ct in range(16):
            DW(lambda e, ct=ct, dc=dc: e.dma_start(out=wqk_sc[ct, :, dc * 128:(dc + 1) * 128],
                                                   in_=wcast[:, ct * 128:(ct + 1) * 128]), "wr")
        for hf in range(2):
            DW(lambda e, hf=hf, dc=dc: e.dma_start(
                out=wv_sc[hf, dc].rearrange("p (q c) -> p q c", c=160)[:, :, 64:96],
                in_=zstage[:].rearrange("p (q c) -> p q c", c=32),
                allow_slow_non_contiguous=True), "wr")
            DW(lambda e, hf=hf, dc=dc: e.dma_start(
                out=wv_sc[hf, dc].rearrange("p (q c) -> p q c", c=160)[:, :, 0:64],
                in_=wcast[:, 2048 + hf * 512: 2048 + hf * 512 + 512].rearrange("p (q c) -> p q c", c=128)[:, :, 0:64],
                allow_slow_non_contiguous=True), "wr")
            wr_w = DW(lambda e, hf=hf, dc=dc: e.dma_start(
                out=wv_sc[hf, dc].rearrange("p (q c) -> p q c", c=160)[:, :, 96:160],
                in_=wcast[:, 2048 + hf * 512: 2048 + hf * 512 + 512].rearrange("p (q c) -> p q c", c=128)[:, :, 64:128],
                allow_slow_non_contiguous=True), "wr")
    wrh = {0: 0, 1: 0}
    for dc in range(ND):
        pp = dc % 2
        if dc >= 2:
            P.wait("sync", "v", cast_v[8 + dc - 2])
        wt = DW(lambda e, dc=dc, b=ws[dc % 2]: e.dma_start(out=b[:, 0:1024], in_=wout_d[dc * 128:(dc + 1) * 128, :]), "pw%d" % (dc % 2))
        P.wait("vector", "pw%d" % (dc % 2), wt)
        P.wait("vector", "wr", P.cnt["wr"])
        cast_v[8 + dc] = V(lambda e, b=ws[dc % 2], pp=pp: e.tensor_copy(wcast[:, pp * 1024:pp * 1024 + 1024], b[:, 0:1024]))
        P.wait("sync", "v", cast_v[8 + dc])
        wrh[pp] = DW(lambda e, dc=dc, pp=pp: e.dma_start(out=wo_sc[dc], in_=wcast[:, pp * 1024:pp * 1024 + 1024]), "wr")

    def wquant(w_d, ntiles, tilecols, swcol, c1col, writer):
        acm = stat[:, 0:1]
        tmpc = stat[:, 1:2]
        red_v = {}
        for i in range(ntiles):
            if i >= 2:
                P.wait("sync", "s", red_v[i - 2][0])
            wt = DW(lambda e, i=i, b=ws[i % 2], tc=tilecols: e.dma_start(out=b[:, 0:tc], in_=w_d[i * 128:(i + 1) * 128, :]), "pw%d" % (i % 2))
            P.wait("scalar", "pw%d" % (i % 2), wt)
            if i == 0:
                sa = S(lambda e, b=ws[0], tc=tilecols: e.activation(tqtmp[:, 0:tc], b[:, 0:tc], AF.Abs, accum_out=acm))
                red_v[i] = (sa, P.cnt["v"])
            else:
                P.wait("scalar", "v", red_v[i - 1][1])
                sa = S(lambda e, b=ws[i % 2], tc=tilecols: e.activation(tqtmp[:, 0:tc], b[:, 0:tc], AF.Abs, accum_out=tmpc))
                P.wait("vector", "s", sa)
                va = Vd(lambda e: e.tensor_tensor(acm, acm, tmpc, ALU.add))
                red_v[i] = (sa, va)
        P.wait("tensor", "v", P.cnt["v"])
        P.wait("tensor", "s", P.cnt["s"])
        tv = T(lambda e: e.matmul(pg1[0:1, 0:1], acm, ones_f[:, 0:1], start=True, stop=True))
        P.wait("vector", "t", tv)
        Vd(lambda e, n=float(ntiles * 128 * tilecols): e.tensor_scalar(stat[0:1, 2:3], pg1[0:1, 0:1], 1.0 / n, None, ALU.mult))
        Vd(lambda e: e.tensor_scalar(stat[0:1, 2:3], stat[0:1, 2:3], 1e-5, None, ALU.max))
        vv = Vd(lambda e: e.reciprocal(stat[0:1, 3:4], stat[0:1, 2:3]))
        P.wait("tensor", "v", vv)
        T(lambda e: e.matmul(pg1[:, 2:3], ones_f[0:1, 0:128], stat[0:1, 2:3], start=True, stop=True))
        tv = T(lambda e: e.matmul(pg1[:, 3:4], ones_f[0:1, 0:128], stat[0:1, 3:4], start=True, stop=True))
        P.wait("vector", "t", tv)
        Vd(lambda e, c=c1col: e.tensor_scalar(ccol[:, c:c + 1], pg1[:, 2:3], 1.0 / 127.0, None, ALU.mult))
        vv = Vd(lambda e, c=swcol: e.tensor_copy(ccol[:, c:c + 1], pg1[:, 3:4]))
        P.wait("scalar", "v", vv)
        P.wait("tensor", "v", vv)
        qs = {}
        vts = {}
        dbl = 2 if tilecols <= 1024 else 1
        for i in range(ntiles):
            toff = (i % dbl) * tilecols
            if i >= 2:
                P.wait("sync", "s", qs[i - 2])
            wt = DW(lambda e, i=i, b=ws[i % 2], tc=tilecols: e.dma_start(out=b[:, 0:tc], in_=w_d[i * 128:(i + 1) * 128, :]), "pw%d" % (i % 2))
            P.wait("scalar", "pw%d" % (i % 2), wt)
            if i >= dbl:
                P.wait("scalar", "v", vts[i - dbl])
            S(lambda e, b=ws[i % 2], tc=tilecols, c=swcol, to=toff: e.activation(
                tqtmp[:, to:to + tc], b[:, 0:tc], AF.Identity, scale=ccol[:, c:c + 1], bias=ccol[:, C_MAG:C_MAG + 1]))
            qs[i] = S(lambda e, tc=tilecols, to=toff: e.activation(
                tqtmp[:, to:to + tc], tqtmp[:, to:to + tc], AF.Identity, bias=ccol[:, C_NMAG:C_NMAG + 1]))
            P.wait("vector", "s", qs[i])
            if i >= dbl:
                P.wait("vector", "t", trd[i - dbl])
            vts[i] = V(lambda e, tc=tilecols, to=toff: e.tensor_scalar(
                tritb[:, to:to + tc], tqtmp[:, to:to + tc], 1.0, -1.0, ALU.min, ALU.max))
            trd[i] = writer(i, vts[i], toff)

    def w1_writer(i, vtrit, toff=0):
        pp = i % 2
        pg = pg0 if pp == 0 else pg1
        P.wait("tensor", "v", vtrit)
        P.wait("tensor", "v", evd[pp])
        for dc in range(ND):
            tv = T(lambda e, dc=dc, pg=pg, to=toff: e.matmul(pg[:, dc * 128:dc * 128 + 128],
                                                    tritb[:, to + dc * 128:to + (dc + 1) * 128], ident_bf[:],
                                                    start=True, stop=True))
        P.wait("vector", "t", tv)
        P.wait("vector", "sg%d" % pp, stw[pp])
        evd[pp] = V(lambda e, pp=pp, pg=pg: e.tensor_copy(stg[pp][:], pg[:]))
        P.wait("sync", "v", evd[pp])
        stw[pp] = DW(lambda e, i=i, pp=pp: e.dma_start(out=w1q_sc[i], in_=stg[pp][:]), "sg%d" % pp)
        return tv

    def w2_writer(dt, vtrit, toff=0):
        P.wait("tensor", "v", vtrit)
        tvl = 0
        for rho in range(4):
            pp = (dt * 4 + rho) % 2
            pg = pg0 if pp == 0 else pg1
            P.wait("tensor", "v", evd[pp])
            for j in range(8):
                tv = T(lambda e, rho=rho, j=j, pg=pg: e.matmul(
                    pg[:, j * 128:j * 128 + 128],
                    tritb[:, (rho * 8 + j) * 128:(rho * 8 + j) * 128 + 128], ident_bf[:],
                    start=True, stop=True))
            tvl = tv
            P.wait("vector", "t", tv)
            P.wait("vector", "sg%d" % pp, stw[pp])
            evd[pp] = V(lambda e, pp=pp, pg=pg: e.tensor_copy(stg[pp][:], pg[:]))
            P.wait("sync", "v", evd[pp])
            for j in range(8):
                stw[pp] = DW(lambda e, rho=rho, j=j, dt=dt, pp=pp: e.dma_start(
                    out=w2q_sc[rho * 8 + j, :, dt * 128:(dt + 1) * 128], in_=stg[pp][:, j * 128:(j + 1) * 128]), "sg%d" % pp)
        return tvl

    wout_cast_done = P.cnt["v"]

    def emit_wquant():
        global_evd = None
        P.wait("sync", "v", wout_cast_done)
        for args, writer in (((w1_d, 32, 1024, C_SW1, C_C1), w1_writer),
                             ((w2_d, 8, 4096, C_SW2, C_C2), w2_writer)):
            evd.clear(); evd.update({0: 0, 1: 0})
            stw.clear(); stw.update({0: 0, 1: 0})
            trd.clear()
            wquant(*args, writer)
        if debug:
            for s_ in ("sg0", "sg1", "wr"):
                P.wait("gpsimd", s_, P.cnt[s_])
            P.wait("gpsimd", "v", P.cnt["v"])
            for i in range(32):
                DG(lambda e, i=i: e.dma_start(out=dbg["d_w1q"][i], in_=w1q_sc[i]), "gd")
                DG(lambda e, i=i: e.dma_start(out=dbg["d_w2q"][i], in_=w2q_sc[i]), "gd")
            DG(lambda e: e.dma_start(out=dbg["d_ccol"], in_=ccol[:]), "gd")
            DG(lambda e: e.dma_start(out=dbg["d_g"][:, 0:16], in_=stat[:]), "gd")

    evd = {0: 0, 1: 0}
    stw = {0: 0, 1: 0}
    trd = {}
    fence()

    for b in range(NB if UPTO >= 2 else 0):
        P.op("scalar", lambda e: e.preload_act_table(AF.Sqrt))
        lastS = {0: 0, 1: 0}
        lastV = {0: 0, 1: 0}
        evT = {0: 0, 1: 0}
        for st in range(NT):
            pp = st % 2
            P.wait("gpsimd", "s", lastS[pp])
            P.wait("gpsimd", "v", lastV[pp])
            xd = DG(lambda e, b=b, st=st, pp=pp: e.dma_start(
                out=xbuf[:, pp * 1024:pp * 1024 + 1024], in_=x_d[b, st * 128:(st + 1) * 128, :]), "gx%d" % pp)
            xs = xbuf[:, pp * 1024:pp * 1024 + 1024]
            sc = stat[:, pp * 8: pp * 8 + 8]
            vprev = P.cnt["v"]
            P.wait("vector", "gx%d" % pp, xd)
            lastV[pp] = Vd(lambda e, xs=xs, sc=sc: e.reduce_sum(sc[:, 0:1], xs, AX.X))
            P.wait("scalar", "gx%d" % pp, xd)
            P.wait("scalar", "v", vprev)
            S(lambda e, xs=xs, sc=sc: e.activation(xc[:], xs, AF.Square, accum_out=sc[:, 1:2]))
            P.wait("vector", "s", P.cnt["s"])
            Vd(lambda e, sc=sc: e.tensor_scalar(sc[:, 2:3], sc[:, 0:1], -1.0 / 1024.0, None, ALU.mult))
            Vd(lambda e, sc=sc: e.tensor_tensor(sc[:, 3:4], sc[:, 2:3], sc[:, 2:3], ALU.mult))
            vv = Vd(lambda e, sc=sc: e.scalar_tensor_tensor(sc[:, 4:5], sc[:, 1:2], 1.0 / 1024.0, sc[:, 3:4], ALU.mult, ALU.subtract))
            P.wait("scalar", "v", vv)
            sv = S(lambda e, sc=sc: e.activation(sc[:, 4:5], sc[:, 4:5], AF.Sqrt, bias=ccol[:, C_EPS:C_EPS + 1]))
            P.wait("vector", "s", sv)
            Vd(lambda e, sc=sc: e.reciprocal(sc[:, 5:6], sc[:, 4:5]))
            sv = S(lambda e, xs=xs, sc=sc: e.activation(xc[:], xs, AF.Identity, bias=sc[:, 2:3]))
            lastS[pp] = sv
            P.wait("vector", "s", sv)
            Vd(lambda e, sc=sc: e.reduce_max(sc[:, 6:7], xc[:], AX.X, apply_absolute_value=True))
            Vd(lambda e, sc=sc: e.tensor_tensor(sc[:, 6:7], sc[:, 6:7], sc[:, 5:6], ALU.mult))
            Vd(lambda e, sc=sc: e.tensor_scalar(sc[:, 6:7], sc[:, 6:7], 1e-5, None, ALU.max))
            Vd(lambda e, sc=sc: e.reciprocal(sc[:, 7:8], sc[:, 6:7]))
            Vd(lambda e, sc=sc: e.tensor_scalar(sc[:, 7:8], sc[:, 7:8], 127.0, None, ALU.mult))
            Vd(lambda e, sc=sc: e.tensor_tensor(sc[:, 7:8], sc[:, 7:8], sc[:, 5:6], ALU.mult))
            vv = Vd(lambda e, sc=sc, st=st: e.tensor_copy(gst[:, st:st + 1], sc[:, 6:7]))
            P.wait("scalar", "v", vv)
            P.wait("scalar", "t", evTT if st >= 1 else 0)
            S(lambda e, sc=sc: e.activation(xn_t[:], xc[:], AF.Identity, scale=sc[:, 5:6]))
            S(lambda e, sc=sc: e.activation(xt1[:], xc[:], AF.Identity, scale=sc[:, 7:8], bias=ccol[:, C_MAG:C_MAG + 1]))
            sv = S(lambda e: e.activation(xi_t[:], xt1[:], AF.Identity, bias=ccol[:, C_NMAG:C_NMAG + 1]))
            lastS[pp] = sv
            P.wait("tensor", "s", sv)
            P.wait("tensor", "v", evT[0])
            for dc in range(ND):
                T(lambda e, dc=dc: e.matmul(pg0[:, dc * 128:dc * 128 + 128],
                                            xn_t[:, dc * 128:(dc + 1) * 128], ident_bf[:], start=True, stop=True))
            tv0 = P.cnt["t"]
            P.wait("tensor", "v", evT[1])
            for dc in range(ND):
                T(lambda e, dc=dc: e.matmul(pg1[:, dc * 128:dc * 128 + 128],
                                            xi_t[:, dc * 128:(dc + 1) * 128], ident_bf[:], start=True, stop=True))
            evTT = P.cnt["t"]
            P.wait("vector", "t", tv0)
            evT[0] = V(lambda e, st=st: e.tensor_copy(
                xnT[:].rearrange("p (dc t) -> p dc t", dc=8)[:, :, st * 128:st * 128 + 128],
                pg0[:].rearrange("p (dc c) -> p dc c", dc=8)))
            P.wait("vector", "t", evTT)
            evT[1] = V(lambda e, st=st: e.tensor_copy(
                xiT[:].rearrange("p (dc t) -> p dc t", dc=8)[:, :, st * 128:st * 128 + 128],
                pg1[:].rearrange("p (dc c) -> p dc c", dc=8)))
            lastV[pp] = evT[1]

        if debug and b == 0:
            fence()
            DG(lambda e: e.dma_start(out=dbg["d_xnT"], in_=xnT[:]), "gd")
            DG(lambda e: e.dma_start(out=dbg["d_xiT"], in_=xiT[:]), "gd")
            fence()
        if b == 0:
            emit_wquant()

        if UPTO < 3:
            continue
        P.wait("tensor", "v", P.cnt["v"] if False else lastV[0] if False else max(evT[0], evT[1]))
        P.wait("sync", "v", max(evT[0], evT[1]))
        P.wait("sync", "s", P.cnt["s"])
        for e_ in ("scalar", "vector", "tensor"):
            P.wait(e_, "sg0", P.cnt["sg0"])
            P.wait(e_, "sg1", P.cnt["sg1"])
            P.wait(e_, "wr", P.cnt["wr"])
        lastT_wqk = {0: 0, 1: 0}
        lastS_ps = {0: 0, 1: 0}
        for ct in range(16):
            pp = ct % 2
            P.wait("sync", "t", lastT_wqk[pp])
            wd = DW(lambda e, ct=ct, pp=pp: e.dma_start(out=wqkb[:, pp * 1024:pp * 1024 + 1024], in_=wqk_sc[ct]), "pw%d" % pp)
            P.wait("tensor", "pw%d" % pp, wd)
            for ts in range(4):
                qq = (ct * 4 + ts) % 2
                P.wait("tensor", "s", lastS_ps[qq])
                for dc in range(ND):
                    tv = T(lambda e, pp=pp, dc=dc, ts=ts, qq=qq: e.matmul(
                        pg2[:, qq * 512:qq * 512 + 512],
                        wqkb[:, pp * 1024 + dc * 128: pp * 1024 + dc * 128 + 128],
                        xnT[:, dc * 2048 + ts * 512: dc * 2048 + ts * 512 + 512],
                        start=(dc == 0), stop=(dc == ND - 1)))
                lastT_wqk[pp] = tv
                P.wait("scalar", "t", tv)
                lastS_ps[qq] = S(lambda e, ct=ct, ts=ts, qq=qq: e.activation(
                    qkT[:, ct * 2048 + ts * 512: ct * 2048 + ts * 512 + 512],
                    pg2[:, qq * 512:qq * 512 + 512], AF.Identity, bias=bcol[:, ct:ct + 1]))
        P.wait("tensor", "s", max(lastS_ps[0], lastS_ps[1]))
        lastV_ps = {0: 0, 1: 0}
        lastT_wv = 0
        for hf in range(2):
            P.wait("sync", "t", lastT_wv)
            for dc in range(ND):
                wd = DW(lambda e, hf=hf, dc=dc: e.dma_start(out=wvb[:, dc * 640:dc * 640 + 640], in_=wv_sc[hf, dc]), "gv")
            P.wait("tensor", "gv", wd)
            for st in range(NT):
                qq = st % 2
                pgv = pg3 if qq == 0 else pg2
                P.wait("tensor", "v", lastV_ps[qq])
                for dc in range(ND):
                    T(lambda e, dc=dc, st=st, pgv=pgv: e.matmul(
                        pgv[:, 0:512],
                        xnT[:, dc * 2048 + st * 128: dc * 2048 + st * 128 + 128],
                        wvb[:, dc * 640: dc * 640 + 512],
                        start=(dc == 0), stop=False))
                    T(lambda e, dc=dc, st=st, pgv=pgv: e.matmul(
                        pgv[:, 512:640],
                        xnT[:, dc * 2048 + st * 128: dc * 2048 + st * 128 + 128],
                        wvb[:, dc * 640 + 512: dc * 640 + 640],
                        start=(dc == 0), stop=False))
                T(lambda e, hf=hf, pgv=pgv: e.matmul(pgv[:, 0:512], ones_bf[0:1, 0:128],
                                                     bvrow[0:1, hf * 640: hf * 640 + 512], start=False, stop=True))
                tv = T(lambda e, hf=hf, pgv=pgv: e.matmul(pgv[:, 512:640], ones_bf[0:1, 0:128],
                                                          bvrow[0:1, hf * 640 + 512: hf * 640 + 640], start=False, stop=True))
                lastT_wv = tv
                P.wait("vector", "t", tv)
                base = st * 1280 + hf * 640
                lastV_ps[qq] = V(lambda e, base=base, pgv=pgv: e.tensor_copy(
                    v_sb[:, base: base + 640], pgv[:, 0:640]))
        fence()
        if debug and b == 0:
            DG(lambda e: e.dma_start(out=dbg["d_qkT"], in_=qkT[:]), "gd")
            DG(lambda e: e.dma_start(out=dbg["d_vsb"], in_=v_sb[:]), "gd")
            fence()

        if UPTO < 4:
            continue
        P.op("scalar", lambda e: e.preload_act_table(AF.Exp))
        sc_pair = [pg0, pg1]
        ctx_pair = [pg2, pg3]
        exp_s = {0: 0, 1: 0}
        avT = {0: 0, 1: 0}
        evac_v = {0: 0, 1: 0}
        nbw = {0: 0, 1: 0}
        def attn_tail(pi, cps, qc, pr, avlast):
            P.wait("vector", "t", avlast)
            P.wait("vector", "nb%d" % pi, nbw[pi])
            V(lambda e, cps=cps, pi=pi: e.reciprocal(rows_sb[64:65, pi * 512:pi * 512 + 512], cps[64:65, 0:512]))
            rv = V(lambda e, cps=cps, pi=pi: e.reciprocal(rows_sb[32:33, pi * 512:pi * 512 + 512], cps[32:33, 512:1024]))
            P.wait("sync", "v", rv)
            DW(lambda e, pi=pi: e.dma_start(out=den_sc[pi], in_=rows_sb[32:96:32, pi * 512:pi * 512 + 512]), "nb%d" % pi)
            P.wait("sync", "nb%d" % pi, P.cnt["nb%d" % pi])
            DW(lambda e, pi=pi: e.dma_start(out=nrm[0:64, pi * 512:pi * 512 + 512],
                                            in_=den_sc[pi, 1].unsqueeze(0).broadcast_to([64, 512])), "nb%d" % pi)
            nbw[pi] = DW(lambda e, pi=pi: e.dma_start(out=nrm[64:128, pi * 512:pi * 512 + 512],
                                                      in_=den_sc[pi, 0].unsqueeze(0).broadcast_to([64, 512])), "nb%d" % pi)
            P.wait("vector", "nb%d" % pi, nbw[pi])
            dst = pr * 2048 + qc * 512
            V(lambda e, cps=cps, dst=dst, pi=pi: e.tensor_tensor(
                ctx_sb[0:64, dst:dst + 512], cps[0:64, 0:512], nrm[0:64, pi * 512:pi * 512 + 512], ALU.mult))
            evac_v[pi] = V(lambda e, cps=cps, dst=dst, pi=pi: e.tensor_tensor(
                ctx_sb[64:128, dst:dst + 512], cps[64:128, 512:1024], nrm[64:128, pi * 512:pi * 512 + 512], ALU.mult))

        def attn_av(info):
            cps, kk, off, vb0, st0, sp, pi = (info["cps"], info["kk"], info["off"],
                                              info["vb0"], info["st0"], info["sp"], info["pi"])
            P.wait("tensor", info["gate"][0], info["gate"][1])
            if st0:
                P.wait("tensor", "v", evac_v[pi])
            T(lambda e, cps=cps, vb0=vb0, kk=kk, off=off, st0=st0, sp=sp: e.matmul(
                cps[:, off:512], v_sb[:, vb0:vb0 + 128], p_sb[:, kk * 1024 + off:kk * 1024 + 512],
                start=st0, stop=sp))
            avT[kk] = T(lambda e, cps=cps, vb0=vb0, kk=kk, off=off, st0=st0, sp=sp: e.matmul(
                cps[:, 512 + off:1024], v_sb[:, vb0 + 32:vb0 + 160], p_sb[:, kk * 1024 + 512 + off:kk * 1024 + 1024],
                start=st0, stop=sp))
            if sp:
                attn_tail(pi, cps, info["qc"], info["pr"], avT[kk])

        steps = [(qc, pr, kt, 4 * qc + 4) for qc in range(4) for pr in range(8) for kt in range(4 * qc + 4)]
        pending = None
        for kap, (qc, pr, kt, nkt) in enumerate(steps):
            pi = (qc * 8 + pr) % 2
            cps = ctx_pair[pi]
            kk = kap % 2
            scp = sc_pair[kk]
            diag = kt >= 4 * qc
            off = (kt - 4 * qc) * 128 if diag else 0
            kbase = (8 + pr) * 2048 + kt * 128
            qbase = pr * 2048 + qc * 512
            P.wait("tensor", "s", exp_s[kk])
            T(lambda e, scp=scp, kbase=kbase, qbase=qbase, off=off: e.matmul(
                scp[:, off:512], qkT[0:64, kbase:kbase + 128], qkT[0:64, qbase + off:qbase + 512],
                start=True, stop=True, tile_position=(0, 0)))
            tv = T(lambda e, scp=scp, kbase=kbase, qbase=qbase, off=off: e.matmul(
                scp[:, 512 + off:1024], qkT[64:128, kbase:kbase + 128], qkT[64:128, qbase + off:qbase + 512],
                start=True, stop=True, tile_position=(64, 0)))
            P.wait("scalar", "t", tv)
            P.wait("scalar", "t", avT[kk])
            exp_s[kk] = S(lambda e, scp=scp, kk=kk, off=off: e.activation(
                p_sb[:, kk * 1024:kk * 1024 + 1024].rearrange("p (h c) -> p h c", h=2)[:, :, off:512],
                scp[:].rearrange("p (h c) -> p h c", h=2)[:, :, off:512],
                AF.Exp, scale=0.125))
            if diag:
                P.wait("vector", "s", exp_s[kk])
                mv = V(lambda e, kk=kk, off=off: e.tensor_tensor(
                    p_sb[:, kk * 1024:kk * 1024 + 1024].rearrange("p (h c) -> p h c", h=2)[:, :, off:off + 128],
                    p_sb[:, kk * 1024:kk * 1024 + 1024].rearrange("p (h c) -> p h c", h=2)[:, :, off:off + 128],
                    tri_bf[:].unsqueeze(1).broadcast_to([128, 2, 128]), ALU.mult))
                gate = ("v", mv)
            else:
                gate = ("s", exp_s[kk])
            info = dict(cps=cps, kk=kk, off=off, vb0=kt * 1280 + pr * 160,
                        st0=(kt == 0), sp=(kt == nkt - 1), pi=pi, qc=qc, pr=pr, gate=gate)
            if pending is not None:
                attn_av(pending)
            pending = info
        attn_av(pending)
        fence()
        if debug and b == 0:
            DG(lambda e: e.dma_start(out=dbg["d_ctx"], in_=ctx_sb[:]), "gd")
            fence()

        if UPTO < 5:
            continue
        for dc in range(ND):
            wd = DW(lambda e, dc=dc: e.dma_start(out=wo_res[:, dc * 1024:dc * 1024 + 1024], in_=wo_sc[dc]), "gv")
        P.wait("tensor", "gv", wd)
        addv = {0: 0, 1: 0}
        accw = {0: 0, 1: 0}
        for st in range(NT):
            pp = st % 2
            pg = pg0 if pp == 0 else pg1
            P.wait("gpsimd", "v", addv[pp])
            xd = DG(lambda e, b=b, st=st, pp=pp: e.dma_start(
                out=xrb[:, pp * 1024:pp * 1024 + 1024], in_=x_d[b, st * 128:(st + 1) * 128, :]), "gx%d" % pp)
            P.wait("tensor", "v", addv[pp])
            for half in range(2):
                for dc in range(ND):
                    T(lambda e, pg=pg, half=half, dc=dc, st=st: e.matmul(
                        pg[:, half * 512:half * 512 + 512],
                        ctx_sb[:, dc * 2048 + st * 128: dc * 2048 + st * 128 + 128],
                        wo_res[:, dc * 1024 + half * 512: dc * 1024 + half * 512 + 512],
                        start=(dc == 0), stop=False))
                tv = T(lambda e, pg=pg, half=half: e.matmul(
                    pg[:, half * 512:half * 512 + 512], ones_bf[0:1, 0:128],
                    borow[0:1, half * 512:half * 512 + 512], start=False, stop=True))
            P.wait("vector", "t", tv)
            P.wait("vector", "gx%d" % pp, xd)
            P.wait("vector", "ga%d" % pp, accw[pp])
            addv[pp] = V(lambda e, pg=pg, pp=pp: e.tensor_tensor(
                acst[:, pp * 1024:pp * 1024 + 1024], pg[:], xrb[:, pp * 1024:pp * 1024 + 1024], ALU.add))
            P.wait("gpsimd", "v", addv[pp])
            accw[pp] = DG(lambda e, b=b, st=st, pp=pp: e.dma_start(
                out=acc_sc[b, st], in_=acst[:, pp * 1024:pp * 1024 + 1024]), "ga%d" % pp)
        fence()
        if debug and b == 0:
            P.wait("gpsimd", "ga0", P.cnt["ga0"])
            P.wait("gpsimd", "ga1", P.cnt["ga1"])
            for st in range(NT):
                DG(lambda e, st=st: e.dma_start(out=dbg["d_acc"][st], in_=acc_sc[0, st]), "gd")
            fence()

        if UPTO < 6:
            continue
        P.op("scalar", lambda e: e.preload_act_table(AF.Silu))
        vv = Vd(lambda e: e.tensor_scalar(stat[:], gst[:], ccol[:, C_C1:C_C1 + 1], None, ALU.mult))
        P.wait("tensor", "v", vv)
        tv = T(lambda e: e.matmul(pg1[0:16, 0:128], stat[:], ident_f[:], start=True, stop=True))
        P.wait("vector", "t", tv)
        vv = V(lambda e: e.tensor_copy(nrm[0:16, 0:128], pg1[0:16, 0:128]))
        P.wait("gpsimd", "v", vv)
        gw = DG(lambda e: e.dma_start(out=row_sc[0:2048].rearrange("(r c) -> r c", r=16), in_=nrm[0:16, 0:128]), "gr")
        P.wait("gpsimd", "gr", gw)
        gw = DG(lambda e: e.dma_start(out=g_sb[:], in_=row_sc[0:2048].unsqueeze(0).broadcast_to([128, 2048])), "gr")
        P.wait("vector", "gr", gw)
        stash0 = V(lambda e: e.memset(stash[:], 0.0))
        if debug and b == 0:
            P.wait("gpsimd", "v", P.cnt["v"])
            DG(lambda e: e.dma_start(out=dbg["d_g"], in_=g_sb[:]), "gd")

        pmT_t = [0]
        w1T = {0: 0, 1: 0}
        w2T = {0: 0, 1: 0}
        w1ev = {0: 0, 1: 0, 2: 0, 3: 0}
        w2ev = {0: 0, 1: 0}
        w2last = {0: 0, 1: 0}
        uV = {k: stash0 for k in range(8)}
        uSS = {k: 0 for k in range(8)}
        pg1free_s = [0]
        pg1free_v = [0]
        pg1w = [-1]
        roundV_w = {-1: 0}
        roundS_w = {-1: 0}
        silu31 = {}
        accv = {0: 0, 1: 0}
        outw = {0: 0, 1: 0}
        sb_g = 0

        ub4 = [ubuf[0], ubuf[1], xc[:, 0:388], xc[:, 388:776]]

        def w1conv(c):
            have = c <= 7
            nonlocal_first = [True]
            for grp in range(8):
                hcs = [grp * 4 + j for j in range(4)]
                if have:
                    ppw = grp % 2
                    P.wait("sync", "t", w1T[ppw])
                    wd = DW(lambda e, grp=grp, ppw=ppw: e.dma_start(
                        out=w1b[:, ppw * 4096:ppw * 4096 + 4096].rearrange("p (h q) -> p h q", h=4),
                        in_=w1q_sc[grp * 4:grp * 4 + 4].rearrange("h p q -> p h q")), "pw%d" % ppw)
                    P.wait("tensor", "pw%d" % ppw, wd)
                for hc in hcs:
                    sl = hc % 4
                    P.wait("scalar", "v", uV[sl])
                    S(lambda e, sl=sl, hc=hc: e.copy(ub4[sl][:, 0:129], stash[:, hc * 129:hc * 129 + 129]))
                evc = {}
                for hc in hcs:
                    sl = hc % 4
                    if have:
                        pw = hc % 2
                        P.wait("tensor", "v", w1ev[pw])
                        for dc in range(ND):
                            tmm = T(lambda e, j=hc % 4, dc=dc, c=c, pw=pw, ppw=(hc // 4) % 2: e.matmul(
                                pg0[:, pw * 512:pw * 512 + 256],
                                w1b[:, ppw * 4096 + j * 1024 + dc * 128: ppw * 4096 + j * 1024 + dc * 128 + 128],
                                xiT[:, dc * 2048 + c * 256: dc * 2048 + c * 256 + 256],
                                start=(dc == 0), stop=(dc == ND - 1)))
                        w1T[(hc // 4) % 2] = tmm
                        P.wait("vector", "t", tmm)
                        P.wait("vector", "s", uSS[sl])
                        w1ev[pw] = V(lambda e, sl=sl, pw=pw, c=c: e.tensor_tensor(
                            ub4[sl][:, 129:385], pg0[:, pw * 512:pw * 512 + 256],
                            g_sb[:, c * 256:c * 256 + 256], ALU.mult))
                        evc[hc] = w1ev[pw]
                    else:
                        P.wait("vector", "s", uSS[sl])
                        evc[hc] = V(lambda e, sl=sl: e.memset(ub4[sl][:, 129:388], 0.0))
                inc_ = {}
                for hc in hcs:
                    sl = hc % 4
                    P.wait("scalar", "v", evc[hc])
                    if nonlocal_first[0]:
                        P.wait("scalar", "v", roundV_w[c - 1])
                        nonlocal_first[0] = False
                    inc_[hc] = S(lambda e, sl=sl, hc=hc: e.activation(
                        h2w[:, hc * 256:hc * 256 + 256], ub4[sl][:, 0:256], AF.Identity,
                        scale=cwsb[:, hc * 3:hc * 3 + 1], bias=cbsb[:, hc:hc + 1]))
                    if have:
                        uSS[sl] = S(lambda e, sl=sl, hc=hc: e.copy(
                            stash[:, hc * 129:hc * 129 + 129], ub4[sl][:, 256:385]))
                    else:
                        uSS[sl] = inc_[hc]
                for hc in hcs:
                    sl = hc % 4
                    P.wait("vector", "s", inc_[hc])
                    V(lambda e, sl=sl, hc=hc: e.scalar_tensor_tensor(
                        h2w[:, hc * 256:hc * 256 + 256], ub4[sl][:, 1:257], cwsb[:, hc * 3 + 1:hc * 3 + 2],
                        h2w[:, hc * 256:hc * 256 + 256], ALU.mult, ALU.add))
                    uV[sl] = V(lambda e, sl=sl, hc=hc: e.scalar_tensor_tensor(
                        h2w[:, hc * 256:hc * 256 + 256], ub4[sl][:, 2:258], cwsb[:, hc * 3 + 2:hc * 3 + 3],
                        h2w[:, hc * 256:hc * 256 + 256], ALU.mult, ALU.add))
                P.wait("scalar", "v", uV[hcs[-1] % 4])
                sl_ = S(lambda e, grp=grp: e.activation(
                    h2w[:, grp * 1024:grp * 1024 + 1024], h2w[:, grp * 1024:grp * 1024 + 1024], AF.Silu))
            silu31[c] = sl_

        def quantscale(cq):
            P.wait("vector", "s", silu31[cq])
            P.wait("vector", "t", pmT_t[0])
            pmv = V(lambda e: e.tensor_reduce(
                pmb[:, 0:256], h2w[:].rearrange("p (h t) -> p t h", h=32),
                AX.X, ALU.max, apply_absolute_value=True))
            P.wait("tensor", "v", pmv)
            P.wait("tensor", "s", P.cnt["s"])
            T(lambda e: e.matmul(pg1[:, 0:128], pmb[:, 0:128], ident_f[:], start=True, stop=True))
            tv = T(lambda e: e.matmul(pg1[:, 128:256], pmb[:, 128:256], ident_f[:], start=True, stop=True))
            pmT_t[0] = tv
            P.wait("vector", "t", tv)
            Vd(lambda e: e.tensor_reduce(
                mx[:, 0:2], pg1[:, 0:256].rearrange("p (w c) -> p w c", w=2), AX.X, ALU.max,
                apply_absolute_value=False))
            Vd(lambda e: e.tensor_scalar(mx[:, 2:4], mx[:, 0:2], 1e-5, None, ALU.max))
            Vd(lambda e: e.tensor_scalar(mx[:, 4:6], mx[:, 2:4], ccol[:, C_C2:C_C2 + 1], None, ALU.mult))
            Vd(lambda e: e.reciprocal(mx[:, 6:8], mx[:, 2:4]))
            vvs = Vd(lambda e: e.tensor_scalar(mx[:, 6:8], mx[:, 6:8], 127.0, None, ALU.mult))
            P.wait("tensor", "v", vvs)
            tv = T(lambda e: e.matmul(pg1[0:2, 256:384], mx[:, 6:8], ident_f[:], start=True, stop=True))
            P.wait("vector", "t", tv)
            vv = V(lambda e: e.tensor_copy(nrm[0:2, 0:128], pg1[0:2, 256:384]))
            pg1free_v[0] = vv
            P.wait("gpsimd", "v", vv)
            P.wait("gpsimd", "gr", P.cnt["gr"])
            sg = DG(lambda e: e.dma_start(out=row_sc[2048:2304].rearrange("(r c) -> r c", r=2), in_=nrm[0:2, 0:128]), "gr")
            P.wait("gpsimd", "gr", sg)
            sg = DG(lambda e: e.dma_start(out=sb_sb[:], in_=row_sc[2048:2304].unsqueeze(0).broadcast_to([128, 256])), "gr")
            P.wait("vector", "gr", sg)
            qm = V(lambda e: e.tensor_tensor(
                h2w[:].rearrange("p (h t) -> p h t", h=32),
                h2w[:].rearrange("p (h t) -> p h t", h=32),
                sb_sb[:].unsqueeze(1).broadcast_to([128, 32, 256]), ALU.mult))
            P.wait("vector", "t", max(w2T[0], w2T[1]))
            P.wait("scalar", "t", max(w2T[0], w2T[1]))
            P.wait("scalar", "v", qm)
            roundV_w[cq] = V(lambda e: e.tensor_scalar(h2q[:, 0:4096], h2w[:, 0:4096], MAGIC, -MAGIC, ALU.add, ALU.add))
            for k in range(4):
                S(lambda e, k=k: e.activation(xbuf[:, 0:1024], h2w[:, 4096 + k * 1024:4096 + k * 1024 + 1024],
                                              AF.Identity, bias=ccol[:, C_MAG:C_MAG + 1]))
                rS = S(lambda e, k=k: e.activation(h2q[:, 4096 + k * 1024:4096 + k * 1024 + 1024], xbuf[:, 0:1024],
                                                   AF.Identity, bias=ccol[:, C_NMAG:C_NMAG + 1]))
            roundS_w[cq] = rS

        def w2block(cq, b=b):
            stl = [t_ for t_ in (2 * cq - 1, 2 * cq) if 0 <= t_ < 16]
            accd = {}
            for st_out in stl:
                par = st_out % 2
                P.wait("gpsimd", "v", accv[par])
                accd[st_out] = DG(lambda e, b=b, st_out=st_out, par=par: e.dma_start(
                    out=accb[:, par * 1024:par * 1024 + 1024], in_=acc_sc[b, st_out]), "ga%d" % par)
            for grp in range(8):
                pp2 = grp % 2
                P.wait("sync", "t", w2T[pp2])
                wd2 = DW(lambda e, grp=grp, pp2=pp2: e.dma_start(
                    out=w2b[:, pp2 * 4096:pp2 * 4096 + 4096].rearrange("p (h q) -> p h q", h=4),
                    in_=w2q_sc[grp * 4:grp * 4 + 4].rearrange("h p q -> p h q")), "qw%d" % pp2)
                P.wait("tensor", "qw%d" % pp2, wd2)
                if grp == 0:
                    P.wait("tensor", "v", roundV_w[cq])
                    P.wait("tensor", "s", roundS_w[cq])
                for j in range(4):
                    hc = grp * 4 + j
                    for st_out in stl:
                        w = st_out - (2 * cq - 1)
                        par = st_out % 2
                        pgw = pg2 if par == 0 else pg3
                        if hc == 0:
                            P.wait("tensor", "v", w2ev[par])
                        for half in range(2):
                            tmm = T(lambda e, pgw=pgw, hc=hc, w=w, pp2=pp2, j=j, half=half: e.matmul(
                                pgw[:, half * 512:half * 512 + 512],
                                h2q[:, hc * 256 + w * 128: hc * 256 + w * 128 + 128],
                                w2b[:, pp2 * 4096 + j * 1024 + half * 512: pp2 * 4096 + j * 1024 + half * 512 + 512],
                                start=(hc == 0), stop=(hc == NHC - 1)))
                        w2T[pp2] = tmm
                        if hc == NHC - 1:
                            w2last[par] = tmm
            for st_out in stl:
                w = st_out - (2 * cq - 1)
                par = st_out % 2
                pgw = pg2 if par == 0 else pg3
                P.wait("vector", "ga%d" % par, accd[st_out])
                P.wait("vector", "t", w2last[par])
                P.wait("vector", "go%d" % par, outw[par])
                accv[par] = V(lambda e, pgw=pgw, par=par, w=w: e.scalar_tensor_tensor(
                    outb[:, par * 1024:par * 1024 + 1024], pgw[:], mx[:, 4 + w:5 + w],
                    accb[:, par * 1024:par * 1024 + 1024], ALU.mult, ALU.add))
                w2ev[par] = accv[par]
                P.wait("gpsimd", "v", accv[par])
                outw[par] = DG(lambda e, b=b, st_out=st_out, par=par: e.dma_start(
                    out=out_d[b, st_out * 128:(st_out + 1) * 128, :], in_=outb[:, par * 1024:par * 1024 + 1024]), "go%d" % par)

        for c in range(FFNW):
            if 1 <= c:
                quantscale(c - 1)
            if c <= 8:
                w1conv(c)
            if 1 <= c:
                w2block(c - 1)
        fence()

    for s_ in list(P.cnt.keys()):
        if s_ not in ("t", "s", "v") and P.cnt[s_] > 0:
            P.wait("gpsimd", s_, P.cnt[s_])

    from contextlib import ExitStack
    with ExitStack() as stack:
        sems = {}
        for s_ in list(P.cnt.keys()):
            if P.cnt[s_] > 0:
                sems[s_] = stack.enter_context(nc.semaphore("sem_" + s_))
        block = stack.enter_context(nc.Block())

        @block.sync
        def _(sync):
            P.replay("sync", sync, sems)

        @block.gpsimd
        def _(gpsimd):
            P.replay("gpsimd", gpsimd, sems)

        @block.tensor
        def _(tensor):
            P.replay("tensor", tensor, sems)

        @block.scalar
        def _(scalar):
            P.replay("scalar", scalar, sems)

        @block.vector
        def _(vector):
            P.replay("vector", vector, sems)

    return nc


_NC_CACHE = {}


def _get_nc(debug=False):
    key = (debug, UPTO, FFNW, W1MODE)
    if key not in _NC_CACHE:
        _NC_CACHE[key] = build(debug)
    return _NC_CACHE[key]


def kernel(L0_attn_hidden_states, L0_attn_W_attn, L0_attn_b_attn, L0_attn_W_out,
           L0_attn_b_out, L0_ffn_w1, L0_ffn_conv_w, L0_ffn_conv_b, L0_ffn_w2,
           _debug=False):
    x = np.ascontiguousarray(L0_attn_hidden_states, dtype=np.float32)
    shards = x.reshape(8, NB, 2048, 1024)
    nc = _get_nc(_debug)
    common = {
        "W_attn": np.ascontiguousarray(L0_attn_W_attn, dtype=np.float32),
        "b_attn": np.ascontiguousarray(L0_attn_b_attn, dtype=np.float32),
        "W_out": np.ascontiguousarray(L0_attn_W_out, dtype=np.float32),
        "b_out": np.ascontiguousarray(L0_attn_b_out, dtype=np.float32),
        "w1": np.ascontiguousarray(L0_ffn_w1, dtype=np.float32),
        "cw": np.ascontiguousarray(L0_ffn_conv_w, dtype=np.float32).reshape(4096, 3),
        "cb": np.ascontiguousarray(L0_ffn_conv_b, dtype=np.float32),
        "w2": np.ascontiguousarray(L0_ffn_w2, dtype=np.float32),
        "ident_bf": np.eye(128, dtype=ml_dtypes.bfloat16),
        "ident_f": np.eye(128, dtype=np.float32),
        "tri_bf": np.triu(np.ones((128, 128), dtype=np.float32)).astype(ml_dtypes.bfloat16),
    }
    in_maps = [dict(common, x=np.ascontiguousarray(shards[i])) for i in range(8)]
    res = run_bass_kernel_spmd(nc, in_maps, core_ids=list(range(8)))
    out = np.concatenate([r["out"].reshape(NB, 2048, 1024) for r in res.results], axis=0)
    if _debug:
        return out, res
    return out


# version 22
# speedup vs baseline: 1.0973x; 1.0051x over previous
import numpy as np
import ml_dtypes
import concourse.bass as bass
import concourse.mybir as mybir
from concourse.bass_utils import run_bass_kernel_spmd

F32 = mybir.dt.float32
BF16 = mybir.dt.bfloat16
AF = mybir.ActivationFunctionType
ALU = mybir.AluOpType
AX = mybir.AxisListType

MAGIC = 12582912.0
UPTO = 6
FFNW = 10
W1MODE = 0
NB = 2
NT = 16
ND = 8
NHC = 32

ENGS = ("sync", "gpsimd", "tensor", "scalar", "vector")


class Plan:
    def __init__(self):
        from collections import defaultdict
        self.ops = {e: [] for e in ENGS}
        self.cnt = defaultdict(int)

    def inc(self, eng, fn, sem, by=1):
        self.ops[eng].append(("inc", fn, sem, by))
        self.cnt[sem] += by
        return self.cnt[sem]

    def op(self, eng, fn):
        self.ops[eng].append(("op", fn))

    def wait(self, eng, sem, val):
        if val > 0:
            self.ops[eng].append(("wait", sem, val))

    def replay(self, eng_name, obj, sems):
        for it in self.ops[eng_name]:
            if it[0] == "op":
                it[1](obj)
            elif it[0] == "inc":
                it[1](obj).then_inc(sems[it[2]], it[3])
            else:
                obj.wait_ge(sems[it[1]], it[2])


def build(debug=False):
    nc = bass.Bass("TRN2", target_bir_lowering=False)
    P = Plan()

    def T(fn):
        return P.inc("tensor", fn, "t")

    def S(fn):
        return P.inc("scalar", fn, "s")

    def V(fn):
        return P.inc("vector", fn, "v")

    def Vd(fn):
        n = P.inc("vector", fn, "v")
        P.wait("vector", "v", n)
        return n

    def DW(fn, sem):
        return P.inc("sync", fn, sem, 16)

    def DG(fn, sem):
        return P.inc("gpsimd", fn, sem, 16)

    SYNC_SEMS = ("pw0", "pw1", "qw0", "qw1", "gv", "wr", "sg0", "sg1", "nb0", "nb1")
    GP_SEMS = ("gi", "gx0", "gx1", "ga0", "ga1", "go0", "go1", "gr", "gd")

    def fence():
        from collections import defaultdict
        c = defaultdict(int, P.cnt)
        dma = [s for s in list(SYNC_SEMS) + list(GP_SEMS) if c.get(s, 0) > 0]
        P.wait("tensor", "s", c["s"]); P.wait("tensor", "v", c["v"])
        P.wait("scalar", "t", c["t"]); P.wait("scalar", "v", c["v"])
        P.wait("vector", "t", c["t"]); P.wait("vector", "s", c["s"])
        for q in ("sync", "gpsimd"):
            P.wait(q, "t", c["t"]); P.wait(q, "s", c["s"]); P.wait(q, "v", c["v"])
            for s in dma:
                P.wait(q, s, c[s])
        for e in ("tensor", "scalar", "vector"):
            for s in dma:
                P.wait(e, s, c[s])

    x_d = nc.dram_tensor("x", [NB, 2048, 1024], F32, kind="ExternalInput").ap()
    wattn_d = nc.dram_tensor("W_attn", [1024, 3072], F32, kind="ExternalInput").ap()
    battn_d = nc.dram_tensor("b_attn", [3072], F32, kind="ExternalInput").ap()
    wout_d = nc.dram_tensor("W_out", [1024, 1024], F32, kind="ExternalInput").ap()
    bout_d = nc.dram_tensor("b_out", [1024], F32, kind="ExternalInput").ap()
    w1_d = nc.dram_tensor("w1", [4096, 1024], F32, kind="ExternalInput").ap()
    cw_d = nc.dram_tensor("cw", [4096, 3], F32, kind="ExternalInput").ap()
    cb_d = nc.dram_tensor("cb", [4096], F32, kind="ExternalInput").ap()
    w2_d = nc.dram_tensor("w2", [1024, 4096], F32, kind="ExternalInput").ap()
    identbf_d = nc.dram_tensor("ident_bf", [128, 128], BF16, kind="ExternalInput").ap()
    identf_d = nc.dram_tensor("ident_f", [128, 128], F32, kind="ExternalInput").ap()
    tribf_d = nc.dram_tensor("tri_bf", [128, 128], BF16, kind="ExternalInput").ap()
    out_d = nc.dram_tensor("out", [NB, 2048, 1024], F32, kind="ExternalOutput").ap()

    wqk_sc = nc.dram_tensor("wqk_sc", [16, 128, 1024], BF16, kind="Internal").ap()
    wv_sc = nc.dram_tensor("wv_sc", [2, 8, 128, 640], BF16, kind="Internal").ap()
    wo_sc = nc.dram_tensor("wo_sc", [8, 128, 1024], BF16, kind="Internal").ap()
    w1q_sc = nc.dram_tensor("w1q_sc", [32, 128, 1024], BF16, kind="Internal").ap()
    w2q_sc = nc.dram_tensor("w2q_sc", [32, 128, 1024], BF16, kind="Internal").ap()
    acc_sc = nc.dram_tensor("acc_sc", [NB, 16, 128, 1024], F32, kind="Internal").ap()
    row_sc = nc.dram_tensor("row_sc", [4096], F32, kind="Internal").ap()
    den_sc = nc.dram_tensor("den_sc", [2, 2, 512], F32, kind="Internal").ap()

    dbg = {}
    if debug:
        for nm, shp, dt in [("d_xnT", [128, 16384], BF16), ("d_xiT", [128, 16384], BF16),
                            ("d_qkT", [128, 32768], BF16), ("d_vsb", [128, 20480], BF16),
                            ("d_ctx", [128, 16384], BF16), ("d_g", [128, 2048], F32),
                            ("d_acc", [16, 128, 1024], F32),
                            ("d_w1q", [32, 128, 1024], BF16), ("d_w2q", [32, 128, 1024], BF16),
                            ("d_ccol", [128, 16], F32)]:
            dbg[nm] = nc.dram_tensor(nm, shp, dt, kind="ExternalOutput").ap()

    def sbt(name, shape, dtype, off):
        return nc.alloc_sbuf_tensor_at(name, shape, dtype, offset=off).ap()

    cur = [16512]

    def nxt(sz, c=cur):
        r = c[0]
        c[0] = (c[0] + sz + 31) & ~31
        return r

    ident_bf = sbt("identbf", [128, 128], BF16, nxt(256))
    ones_bf = sbt("onesbf", [128, 128], BF16, nxt(256))
    tri_bf = sbt("tribf", [128, 128], BF16, nxt(256))
    ident_f = sbt("identf", [128, 128], F32, nxt(512))
    ones_f = sbt("onesf", [128, 128], F32, nxt(512))
    ccol = sbt("ccol", [128, 16], F32, nxt(64))
    C_MAG, C_NMAG, C_EPS, C_C1, C_C2, C_SW1, C_SW2 = 0, 1, 2, 3, 4, 5, 6
    bcol = sbt("bcol", [128, 16], F32, nxt(64))
    cwsb = sbt("cwsb", [128, 96], F32, nxt(384))
    cbsb = sbt("cbsb", [128, 32], F32, nxt(128))
    stat = sbt("stat", [128, 16], F32, nxt(64))
    gst = sbt("gst", [128, 16], F32, nxt(64))
    bvrow = sbt("bvrow", [1, 1280], BF16, nxt(2560))
    borow = sbt("borow", [1, 1024], BF16, nxt(2048))
    nrm = sbt("nrm", [128, 1024], F32, nxt(4096))
    rows_sb = sbt("rowssb", [128, 1024], F32, nxt(4096))

    R0 = (cur[0] + 63) & ~63
    xiT = sbt("xiT", [128, 16384], BF16, R0)
    XNC = R0 + 32768
    xnT = sbt("xnT", [128, 16384], BF16, XNC)
    ctx_sb = sbt("ctxsb", [128, 16384], BF16, XNC)
    h2w = sbt("h2w", [128, 8192], F32, XNC)
    QK = XNC + 32768
    qkT = sbt("qkT", [128, 32768], BF16, QK)
    VSB = QK + 65536
    v_sb = sbt("vsb", [128, 20480], BF16, VSB)
    AQ = VSB + 40960
    xbuf = sbt("xbuf", [128, 2048], F32, AQ)
    xc = sbt("xc", [128, 1024], F32, AQ + 8192)
    xt1 = sbt("xt1", [128, 1024], F32, AQ + 12288)
    xn_t = sbt("xnt", [128, 1024], BF16, AQ + 16384)
    xi_t = sbt("xit", [128, 1024], BF16, AQ + 18432)
    wqkb = sbt("wqkb", [128, 2048], BF16, AQ)
    wvb = sbt("wvb", [128, 5120], BF16, AQ + 4096)
    p_sb = sbt("psb", [128, 2048], BF16, AQ)
    SB_END = AQ + 20480

    c0 = [QK]
    ws0 = sbt("ws0", [128, 4096], F32, nxt(16384, c0))
    ws1 = sbt("ws1", [128, 4096], F32, nxt(16384, c0))
    wcast = sbt("wcast", [128, 4096], BF16, nxt(8192, c0))
    tqtmp = sbt("tqtmp", [128, 4096], F32, nxt(16384, c0))
    tritb = sbt("tritb", [128, 4096], BF16, nxt(8192, c0))
    stg = [sbt("stg0", [128, 1024], BF16, nxt(2048, c0)),
           sbt("stg1", [128, 1024], BF16, nxt(2048, c0))]
    zstage = sbt("zstage", [128, 128], BF16, nxt(256, c0))
    ws = [ws0, ws1]

    wo_res = sbt("wores", [128, 8192], BF16, QK)
    xrb = sbt("xrb", [128, 2048], F32, QK + 16384)
    acst = sbt("acst", [128, 2048], F32, QK + 24576)

    c1 = [QK]
    g_sb = sbt("gsb", [128, 2048], F32, nxt(8192, c1))
    g_row = sbt("grow", [1, 2048], F32, nxt(8192, c1))
    stash = sbt("stash", [128, 4128], F32, nxt(16512, c1))
    ubuf = [sbt("u0", [128, 388], F32, nxt(1552, c1)),
            sbt("u1", [128, 388], F32, nxt(1552, c1))]
    h2q = sbt("h2q", [128, 8192], BF16, nxt(16384, c1))
    w1b = sbt("w1b", [128, 8192], BF16, nxt(16384, c1))
    w2b = sbt("w2b", [128, 8192], BF16, nxt(16384, c1))
    accb = sbt("accb", [128, 2048], F32, nxt(8192, c1))
    outb = sbt("outb", [128, 2048], F32, nxt(8192, c1))
    mx = sbt("mx", [128, 16], F32, nxt(64, c1))
    sbrow = sbt("sbrow", [1, 256], F32, nxt(1024, c1))
    sb_sb = sbt("sbsb", [128, 256], F32, nxt(1024, c1))
    pmb = sbt("pmb", [128, 512], F32, nxt(2048, c1))
    assert c1[0] <= VSB + 40960, c1[0]

    pg0 = nc.alloc_psum_tensor("pg0", [128, 1024], F32).ap()
    pg1 = nc.alloc_psum_tensor("pg1", [128, 1024], F32).ap()
    pg2 = nc.alloc_psum_tensor("pg2", [128, 1024], F32).ap()
    pg3 = nc.alloc_psum_tensor("pg3", [128, 1024], F32).ap()

    V(lambda e: e.memset(ones_f[:], 1.0))
    V(lambda e: e.memset(ones_bf[:], 1.0))
    V(lambda e: e.memset(ccol[:, C_MAG:C_MAG + 1], MAGIC))
    V(lambda e: e.memset(ccol[:, C_NMAG:C_NMAG + 1], -MAGIC))
    V(lambda e: e.memset(ccol[:, C_EPS:C_EPS + 1], 1e-5))
    DG(lambda e: e.dma_start(out=ident_bf[:], in_=identbf_d), "gi")
    DG(lambda e: e.dma_start(out=ident_f[:], in_=identf_d), "gi")
    DG(lambda e: e.dma_start(out=tri_bf[:], in_=tribf_d), "gi")
    DG(lambda e: e.dma_start(out=cwsb[:].rearrange("p (h j) -> p h j", h=32),
                             in_=cw_d.rearrange("(h p) j -> p h j", p=128)), "gi")
    DG(lambda e: e.dma_start(out=cbsb[:], in_=cb_d.rearrange("(h p) -> p h", p=128),
                             allow_slow_non_contiguous=True), "gi")
    DG(lambda e: e.dma_start(out=bcol[:], in_=battn_d[0:2048].rearrange("(c p) -> p c", p=128),
                             allow_slow_non_contiguous=True), "gi")
    DG(lambda e: e.dma_start(out=nrm[0:1, :], in_=battn_d[2048:3072].unsqueeze(0)), "gi")
    gg = DG(lambda e: e.dma_start(out=xc[0:1, :], in_=bout_d.unsqueeze(0)), "gi")
    P.wait("vector", "gi", gg)
    P.wait("tensor", "gi", gg)
    P.wait("scalar", "gi", gg)
    V(lambda e: e.memset(bvrow[:], 0.0))
    V(lambda e: e.memset(bvrow[:].rearrange("o (q c) -> o q c", c=160)[:, :, 64:66], 1.0))
    V(lambda e: e.tensor_copy(bvrow[:].rearrange("o (q c) -> o q c", c=160)[:, :, 0:64],
                              nrm[0:1, :].rearrange("o (h c) -> o h c", c=128)[:, :, 0:64]))
    V(lambda e: e.tensor_copy(bvrow[:].rearrange("o (q c) -> o q c", c=160)[:, :, 96:160],
                              nrm[0:1, :].rearrange("o (h c) -> o h c", c=128)[:, :, 64:128]))
    V(lambda e: e.memset(zstage[:], 0.0))
    V(lambda e: e.tensor_copy(borow[:], xc[0:1, :]))
    P.op("scalar", lambda e: e.preload_act_table(AF.Sqrt))

    cast_v = {}
    wr_w = 0
    for dc in range(ND):
        if dc >= 2:
            P.wait("sync", "v", cast_v[dc - 2])
        wt = DW(lambda e, dc=dc, b=ws[dc % 2]: e.dma_start(out=b[:, 0:3072], in_=wattn_d[dc * 128:(dc + 1) * 128, :]), "pw%d" % (dc % 2))
        P.wait("vector", "pw%d" % (dc % 2), wt)
        P.wait("vector", "wr", P.cnt["wr"])
        cast_v[dc] = V(lambda e, b=ws[dc % 2]: e.tensor_copy(wcast[:, 0:3072], b[:, 0:3072]))
        P.wait("sync", "v", cast_v[dc])
        for ct in range(16):
            DW(lambda e, ct=ct, dc=dc: e.dma_start(out=wqk_sc[ct, :, dc * 128:(dc + 1) * 128],
                                                   in_=wcast[:, ct * 128:(ct + 1) * 128]), "wr")
        for hf in range(2):
            DW(lambda e, hf=hf, dc=dc: e.dma_start(
                out=wv_sc[hf, dc].rearrange("p (q c) -> p q c", c=160)[:, :, 64:96],
                in_=zstage[:].rearrange("p (q c) -> p q c", c=32),
                allow_slow_non_contiguous=True), "wr")
            DW(lambda e, hf=hf, dc=dc: e.dma_start(
                out=wv_sc[hf, dc].rearrange("p (q c) -> p q c", c=160)[:, :, 0:64],
                in_=wcast[:, 2048 + hf * 512: 2048 + hf * 512 + 512].rearrange("p (q c) -> p q c", c=128)[:, :, 0:64],
                allow_slow_non_contiguous=True), "wr")
            wr_w = DW(lambda e, hf=hf, dc=dc: e.dma_start(
                out=wv_sc[hf, dc].rearrange("p (q c) -> p q c", c=160)[:, :, 96:160],
                in_=wcast[:, 2048 + hf * 512: 2048 + hf * 512 + 512].rearrange("p (q c) -> p q c", c=128)[:, :, 64:128],
                allow_slow_non_contiguous=True), "wr")
    wrh = {0: 0, 1: 0}
    for dc in range(ND):
        pp = dc % 2
        if dc >= 2:
            P.wait("sync", "v", cast_v[8 + dc - 2])
        wt = DW(lambda e, dc=dc, b=ws[dc % 2]: e.dma_start(out=b[:, 0:1024], in_=wout_d[dc * 128:(dc + 1) * 128, :]), "pw%d" % (dc % 2))
        P.wait("vector", "pw%d" % (dc % 2), wt)
        P.wait("vector", "wr", P.cnt["wr"])
        cast_v[8 + dc] = V(lambda e, b=ws[dc % 2], pp=pp: e.tensor_copy(wcast[:, pp * 1024:pp * 1024 + 1024], b[:, 0:1024]))
        P.wait("sync", "v", cast_v[8 + dc])
        wrh[pp] = DW(lambda e, dc=dc, pp=pp: e.dma_start(out=wo_sc[dc], in_=wcast[:, pp * 1024:pp * 1024 + 1024]), "wr")

    def wquant(w_d, ntiles, tilecols, swcol, c1col, writer):
        acm = stat[:, 0:1]
        tmpc = stat[:, 1:2]
        red_v = {}
        for i in range(ntiles):
            if i >= 2:
                P.wait("sync", "s", red_v[i - 2][0])
            wt = DW(lambda e, i=i, b=ws[i % 2], tc=tilecols: e.dma_start(out=b[:, 0:tc], in_=w_d[i * 128:(i + 1) * 128, :]), "pw%d" % (i % 2))
            P.wait("scalar", "pw%d" % (i % 2), wt)
            if i == 0:
                sa = S(lambda e, b=ws[0], tc=tilecols: e.activation(tqtmp[:, 0:tc], b[:, 0:tc], AF.Abs, accum_out=acm))
                red_v[i] = (sa, P.cnt["v"])
            else:
                P.wait("scalar", "v", red_v[i - 1][1])
                sa = S(lambda e, b=ws[i % 2], tc=tilecols: e.activation(tqtmp[:, 0:tc], b[:, 0:tc], AF.Abs, accum_out=tmpc))
                P.wait("vector", "s", sa)
                va = Vd(lambda e: e.tensor_tensor(acm, acm, tmpc, ALU.add))
                red_v[i] = (sa, va)
        P.wait("tensor", "v", P.cnt["v"])
        P.wait("tensor", "s", P.cnt["s"])
        tv = T(lambda e: e.matmul(pg1[0:1, 0:1], acm, ones_f[:, 0:1], start=True, stop=True))
        P.wait("vector", "t", tv)
        Vd(lambda e, n=float(ntiles * 128 * tilecols): e.tensor_scalar(stat[0:1, 2:3], pg1[0:1, 0:1], 1.0 / n, None, ALU.mult))
        Vd(lambda e: e.tensor_scalar(stat[0:1, 2:3], stat[0:1, 2:3], 1e-5, None, ALU.max))
        vv = Vd(lambda e: e.reciprocal(stat[0:1, 3:4], stat[0:1, 2:3]))
        P.wait("tensor", "v", vv)
        T(lambda e: e.matmul(pg1[:, 2:3], ones_f[0:1, 0:128], stat[0:1, 2:3], start=True, stop=True))
        tv = T(lambda e: e.matmul(pg1[:, 3:4], ones_f[0:1, 0:128], stat[0:1, 3:4], start=True, stop=True))
        P.wait("vector", "t", tv)
        Vd(lambda e, c=c1col: e.tensor_scalar(ccol[:, c:c + 1], pg1[:, 2:3], 1.0 / 127.0, None, ALU.mult))
        vv = Vd(lambda e, c=swcol: e.tensor_copy(ccol[:, c:c + 1], pg1[:, 3:4]))
        P.wait("scalar", "v", vv)
        P.wait("tensor", "v", vv)
        qs = {}
        vts = {}
        dbl = 2 if tilecols <= 1024 else 1
        for i in range(ntiles):
            toff = (i % dbl) * tilecols
            if i >= 2:
                P.wait("sync", "s", qs[i - 2])
            wt = DW(lambda e, i=i, b=ws[i % 2], tc=tilecols: e.dma_start(out=b[:, 0:tc], in_=w_d[i * 128:(i + 1) * 128, :]), "pw%d" % (i % 2))
            P.wait("scalar", "pw%d" % (i % 2), wt)
            if i >= dbl:
                P.wait("scalar", "v", vts[i - dbl])
            S(lambda e, b=ws[i % 2], tc=tilecols, c=swcol, to=toff: e.activation(
                tqtmp[:, to:to + tc], b[:, 0:tc], AF.Identity, scale=ccol[:, c:c + 1], bias=ccol[:, C_MAG:C_MAG + 1]))
            qs[i] = S(lambda e, tc=tilecols, to=toff: e.activation(
                tqtmp[:, to:to + tc], tqtmp[:, to:to + tc], AF.Identity, bias=ccol[:, C_NMAG:C_NMAG + 1]))
            P.wait("vector", "s", qs[i])
            if i >= dbl:
                P.wait("vector", "t", trd[i - dbl])
            vts[i] = V(lambda e, tc=tilecols, to=toff: e.tensor_scalar(
                tritb[:, to:to + tc], tqtmp[:, to:to + tc], 1.0, -1.0, ALU.min, ALU.max))
            trd[i] = writer(i, vts[i], toff)

    def w1_writer(i, vtrit, toff=0):
        pp = i % 2
        pg = pg0 if pp == 0 else pg1
        P.wait("tensor", "v", vtrit)
        P.wait("tensor", "v", evd[pp])
        for dc in range(ND):
            tv = T(lambda e, dc=dc, pg=pg, to=toff: e.matmul(pg[:, dc * 128:dc * 128 + 128],
                                                    tritb[:, to + dc * 128:to + (dc + 1) * 128], ident_bf[:],
                                                    start=True, stop=True))
        P.wait("vector", "t", tv)
        P.wait("vector", "sg%d" % pp, stw[pp])
        evd[pp] = V(lambda e, pp=pp, pg=pg: e.tensor_copy(stg[pp][:], pg[:]))
        P.wait("sync", "v", evd[pp])
        stw[pp] = DW(lambda e, i=i, pp=pp: e.dma_start(out=w1q_sc[i], in_=stg[pp][:]), "sg%d" % pp)
        return tv

    def w2_writer(dt, vtrit, toff=0):
        P.wait("tensor", "v", vtrit)
        tvl = 0
        for rho in range(4):
            pp = (dt * 4 + rho) % 2
            pg = pg0 if pp == 0 else pg1
            P.wait("tensor", "v", evd[pp])
            for j in range(8):
                tv = T(lambda e, rho=rho, j=j, pg=pg: e.matmul(
                    pg[:, j * 128:j * 128 + 128],
                    tritb[:, (rho * 8 + j) * 128:(rho * 8 + j) * 128 + 128], ident_bf[:],
                    start=True, stop=True))
            tvl = tv
            P.wait("vector", "t", tv)
            P.wait("vector", "sg%d" % pp, stw[pp])
            evd[pp] = V(lambda e, pp=pp, pg=pg: e.tensor_copy(stg[pp][:], pg[:]))
            P.wait("sync", "v", evd[pp])
            for j in range(8):
                stw[pp] = DW(lambda e, rho=rho, j=j, dt=dt, pp=pp: e.dma_start(
                    out=w2q_sc[rho * 8 + j, :, dt * 128:(dt + 1) * 128], in_=stg[pp][:, j * 128:(j + 1) * 128]), "sg%d" % pp)
        return tvl

    wout_cast_done = P.cnt["v"]

    def emit_wquant():
        global_evd = None
        P.wait("sync", "v", wout_cast_done)
        for args, writer in (((w1_d, 32, 1024, C_SW1, C_C1), w1_writer),
                             ((w2_d, 8, 4096, C_SW2, C_C2), w2_writer)):
            evd.clear(); evd.update({0: 0, 1: 0})
            stw.clear(); stw.update({0: 0, 1: 0})
            trd.clear()
            wquant(*args, writer)
        if debug:
            for s_ in ("sg0", "sg1", "wr"):
                P.wait("gpsimd", s_, P.cnt[s_])
            P.wait("gpsimd", "v", P.cnt["v"])
            for i in range(32):
                DG(lambda e, i=i: e.dma_start(out=dbg["d_w1q"][i], in_=w1q_sc[i]), "gd")
                DG(lambda e, i=i: e.dma_start(out=dbg["d_w2q"][i], in_=w2q_sc[i]), "gd")
            DG(lambda e: e.dma_start(out=dbg["d_ccol"], in_=ccol[:]), "gd")
            DG(lambda e: e.dma_start(out=dbg["d_g"][:, 0:16], in_=stat[:]), "gd")

    evd = {0: 0, 1: 0}
    stw = {0: 0, 1: 0}
    trd = {}
    fence()

    for b in range(NB if UPTO >= 2 else 0):
        P.op("scalar", lambda e: e.preload_act_table(AF.Sqrt))
        p1v = {0: 0, 1: 0}
        p1s = {0: 0, 1: 0}
        for st in range(NT):
            pp = st % 2
            P.wait("gpsimd", "v", p1v[pp])
            P.wait("gpsimd", "s", p1s[pp])
            xd = DG(lambda e, b=b, st=st, pp=pp: e.dma_start(
                out=xbuf[:, pp * 1024:pp * 1024 + 1024], in_=x_d[b, st * 128:(st + 1) * 128, :]), "gx%d" % pp)
            xs = xbuf[:, pp * 1024:pp * 1024 + 1024]
            P.wait("vector", "gx%d" % pp, xd)
            p1v[pp] = V(lambda e, xs=xs, st=st: e.reduce_sum(stat[:, st:st + 1], xs, AX.X))
            P.wait("scalar", "gx%d" % pp, xd)
            p1s[pp] = S(lambda e, xs=xs, st=st, pp=pp: e.activation(
                xc[:] if pp == 0 else xt1[:], xs, AF.Square, accum_out=mx[:, st:st + 1]))
        P.wait("vector", "s", P.cnt["s"])
        Vd(lambda e: e.tensor_scalar(rows_sb[:, 0:16], stat[:, 0:16], -1.0 / 1024.0, None, ALU.mult))
        Vd(lambda e: e.tensor_tensor(rows_sb[:, 16:32], rows_sb[:, 0:16], rows_sb[:, 0:16], ALU.mult))
        vv = Vd(lambda e: e.scalar_tensor_tensor(rows_sb[:, 32:48], mx[:, 0:16], 1.0 / 1024.0,
                                                 rows_sb[:, 16:32], ALU.mult, ALU.subtract))
        P.wait("scalar", "v", vv)
        sv = S(lambda e: e.activation(rows_sb[:, 32:48], rows_sb[:, 32:48], AF.Sqrt, bias=ccol[:, C_EPS:C_EPS + 1]))
        P.wait("vector", "s", sv)
        istd_done = Vd(lambda e: e.reciprocal(rows_sb[:, 48:64], rows_sb[:, 32:48]))
        p2v = {0: 0, 1: 0}
        p2s = {0: 0, 1: 0}
        for st in range(NT):
            pp = st % 2
            P.wait("gpsimd", "v", p2v[pp])
            P.wait("gpsimd", "s", p2s[pp])
            xd = DG(lambda e, b=b, st=st, pp=pp: e.dma_start(
                out=xbuf[:, pp * 1024:pp * 1024 + 1024], in_=x_d[b, st * 128:(st + 1) * 128, :]), "gx%d" % pp)
            xs = xbuf[:, pp * 1024:pp * 1024 + 1024]
            P.wait("scalar", "gx%d" % pp, xd)
            if st < 2:
                P.wait("scalar", "v", istd_done)
            cbuf = xc if pp == 0 else xt1
            p2s[pp] = S(lambda e, xs=xs, st=st, cbuf=cbuf: e.activation(
                cbuf[:], xs, AF.Identity, bias=rows_sb[:, st:st + 1]))
            P.wait("vector", "s", p2s[pp])
            p2v[pp] = V(lambda e, st=st, cbuf=cbuf: e.reduce_max(
                rows_sb[:, 64 + st:65 + st], cbuf[:], AX.X, apply_absolute_value=True))
        P.wait("vector", "s", P.cnt["s"])
        Vd(lambda e: e.tensor_tensor(rows_sb[:, 64:80], rows_sb[:, 64:80], rows_sb[:, 48:64], ALU.mult))
        Vd(lambda e: e.tensor_scalar(gst[:, 0:16], rows_sb[:, 64:80], 1e-5, None, ALU.max))
        Vd(lambda e: e.reciprocal(rows_sb[:, 80:96], gst[:, 0:16]))
        Vd(lambda e: e.tensor_scalar(rows_sb[:, 80:96], rows_sb[:, 80:96], 127.0, None, ALU.mult))
        aw = Vd(lambda e: e.tensor_tensor(rows_sb[:, 96:112], rows_sb[:, 80:96], rows_sb[:, 48:64], ALU.mult))
        P.wait("scalar", "v", aw)
        p3s = {0: 0, 1: 0}
        evT = {0: 0, 1: 0}
        evTT = 0
        for st in range(NT):
            pp = st % 2
            P.wait("gpsimd", "s", p3s[pp])
            xd = DG(lambda e, b=b, st=st, pp=pp: e.dma_start(
                out=xbuf[:, pp * 1024:pp * 1024 + 1024], in_=x_d[b, st * 128:(st + 1) * 128, :]), "gx%d" % pp)
            xs = xbuf[:, pp * 1024:pp * 1024 + 1024]
            P.wait("scalar", "gx%d" % pp, xd)
            cbuf = xc if pp == 0 else xt1
            S(lambda e, xs=xs, st=st, cbuf=cbuf: e.activation(
                cbuf[:], xs, AF.Identity, bias=rows_sb[:, st:st + 1]))
            P.wait("scalar", "t", evTT if st >= 1 else 0)
            S(lambda e, st=st, cbuf=cbuf: e.activation(xn_t[:], cbuf[:], AF.Identity, scale=rows_sb[:, 48 + st:49 + st]))
            S(lambda e, st=st, cbuf=cbuf: e.activation(
                xbuf[:, pp * 1024:pp * 1024 + 1024] if False else cbuf[:],
                cbuf[:], AF.Identity, scale=rows_sb[:, 96 + st:97 + st], bias=ccol[:, C_MAG:C_MAG + 1]))
            sv = S(lambda e, cbuf=cbuf: e.activation(xi_t[:], cbuf[:], AF.Identity, bias=ccol[:, C_NMAG:C_NMAG + 1]))
            p3s[pp] = sv
            P.wait("tensor", "s", sv)
            P.wait("tensor", "v", evT[0])
            for dc in range(ND):
                T(lambda e, dc=dc: e.matmul(pg0[:, dc * 128:dc * 128 + 128],
                                            xn_t[:, dc * 128:(dc + 1) * 128], ident_bf[:], start=True, stop=True))
            tv0 = P.cnt["t"]
            P.wait("tensor", "v", evT[1])
            for dc in range(ND):
                T(lambda e, dc=dc: e.matmul(pg1[:, dc * 128:dc * 128 + 128],
                                            xi_t[:, dc * 128:(dc + 1) * 128], ident_bf[:], start=True, stop=True))
            evTT = P.cnt["t"]
            P.wait("vector", "t", tv0)
            evT[0] = V(lambda e, st=st: e.tensor_copy(
                xnT[:].rearrange("p (dc t) -> p dc t", dc=8)[:, :, st * 128:st * 128 + 128],
                pg0[:].rearrange("p (dc c) -> p dc c", dc=8)))
            P.wait("vector", "t", evTT)
            evT[1] = V(lambda e, st=st: e.tensor_copy(
                xiT[:].rearrange("p (dc t) -> p dc t", dc=8)[:, :, st * 128:st * 128 + 128],
                pg1[:].rearrange("p (dc c) -> p dc c", dc=8)))

        if debug and b == 0:
            fence()
            DG(lambda e: e.dma_start(out=dbg["d_xnT"], in_=xnT[:]), "gd")
            DG(lambda e: e.dma_start(out=dbg["d_xiT"], in_=xiT[:]), "gd")
            fence()
        if b == 0:
            emit_wquant()

        if UPTO < 3:
            continue
        P.wait("tensor", "v", P.cnt["v"] if False else lastV[0] if False else max(evT[0], evT[1]))
        P.wait("sync", "v", max(evT[0], evT[1]))
        P.wait("sync", "s", P.cnt["s"])
        for e_ in ("scalar", "vector", "tensor"):
            P.wait(e_, "sg0", P.cnt["sg0"])
            P.wait(e_, "sg1", P.cnt["sg1"])
            P.wait(e_, "wr", P.cnt["wr"])
        lastT_wqk = {0: 0, 1: 0}
        lastS_ps = {0: 0, 1: 0}
        for ct in range(16):
            pp = ct % 2
            P.wait("sync", "t", lastT_wqk[pp])
            wd = DW(lambda e, ct=ct, pp=pp: e.dma_start(out=wqkb[:, pp * 1024:pp * 1024 + 1024], in_=wqk_sc[ct]), "pw%d" % pp)
            P.wait("tensor", "pw%d" % pp, wd)
            for ts in range(4):
                qq = (ct * 4 + ts) % 2
                P.wait("tensor", "s", lastS_ps[qq])
                for dc in range(ND):
                    tv = T(lambda e, pp=pp, dc=dc, ts=ts, qq=qq: e.matmul(
                        pg2[:, qq * 512:qq * 512 + 512],
                        wqkb[:, pp * 1024 + dc * 128: pp * 1024 + dc * 128 + 128],
                        xnT[:, dc * 2048 + ts * 512: dc * 2048 + ts * 512 + 512],
                        start=(dc == 0), stop=(dc == ND - 1)))
                lastT_wqk[pp] = tv
                P.wait("scalar", "t", tv)
                lastS_ps[qq] = S(lambda e, ct=ct, ts=ts, qq=qq: e.activation(
                    qkT[:, ct * 2048 + ts * 512: ct * 2048 + ts * 512 + 512],
                    pg2[:, qq * 512:qq * 512 + 512], AF.Identity, bias=bcol[:, ct:ct + 1]))
        P.wait("tensor", "s", max(lastS_ps[0], lastS_ps[1]))
        lastV_ps = {0: 0, 1: 0}
        lastT_wv = 0
        for hf in range(2):
            P.wait("sync", "t", lastT_wv)
            for dc in range(ND):
                wd = DW(lambda e, hf=hf, dc=dc: e.dma_start(out=wvb[:, dc * 640:dc * 640 + 640], in_=wv_sc[hf, dc]), "gv")
            P.wait("tensor", "gv", wd)
            for st in range(NT):
                qq = st % 2
                pgv = pg3 if qq == 0 else pg2
                P.wait("tensor", "v", lastV_ps[qq])
                for dc in range(ND):
                    T(lambda e, dc=dc, st=st, pgv=pgv: e.matmul(
                        pgv[:, 0:512],
                        xnT[:, dc * 2048 + st * 128: dc * 2048 + st * 128 + 128],
                        wvb[:, dc * 640: dc * 640 + 512],
                        start=(dc == 0), stop=False))
                    T(lambda e, dc=dc, st=st, pgv=pgv: e.matmul(
                        pgv[:, 512:640],
                        xnT[:, dc * 2048 + st * 128: dc * 2048 + st * 128 + 128],
                        wvb[:, dc * 640 + 512: dc * 640 + 640],
                        start=(dc == 0), stop=False))
                T(lambda e, hf=hf, pgv=pgv: e.matmul(pgv[:, 0:512], ones_bf[0:1, 0:128],
                                                     bvrow[0:1, hf * 640: hf * 640 + 512], start=False, stop=True))
                tv = T(lambda e, hf=hf, pgv=pgv: e.matmul(pgv[:, 512:640], ones_bf[0:1, 0:128],
                                                          bvrow[0:1, hf * 640 + 512: hf * 640 + 640], start=False, stop=True))
                lastT_wv = tv
                P.wait("vector", "t", tv)
                base = st * 1280 + hf * 640
                lastV_ps[qq] = V(lambda e, base=base, pgv=pgv: e.tensor_copy(
                    v_sb[:, base: base + 640], pgv[:, 0:640]))
        fence()
        if debug and b == 0:
            DG(lambda e: e.dma_start(out=dbg["d_qkT"], in_=qkT[:]), "gd")
            DG(lambda e: e.dma_start(out=dbg["d_vsb"], in_=v_sb[:]), "gd")
            fence()

        if UPTO < 4:
            continue
        P.op("scalar", lambda e: e.preload_act_table(AF.Exp))
        sc_pair = [pg0, pg1]
        ctx_pair = [pg2, pg3]
        exp_s = {0: 0, 1: 0}
        avT = {0: 0, 1: 0}
        evac_v = {0: 0, 1: 0}
        nbw = {0: 0, 1: 0}
        def attn_tail(pi, cps, qc, pr, avlast):
            P.wait("vector", "t", avlast)
            P.wait("vector", "nb%d" % pi, nbw[pi])
            V(lambda e, cps=cps, pi=pi: e.reciprocal(rows_sb[64:65, pi * 512:pi * 512 + 512], cps[64:65, 0:512]))
            rv = V(lambda e, cps=cps, pi=pi: e.reciprocal(rows_sb[32:33, pi * 512:pi * 512 + 512], cps[32:33, 512:1024]))
            P.wait("sync", "v", rv)
            DW(lambda e, pi=pi: e.dma_start(out=den_sc[pi], in_=rows_sb[32:96:32, pi * 512:pi * 512 + 512]), "nb%d" % pi)
            P.wait("sync", "nb%d" % pi, P.cnt["nb%d" % pi])
            DW(lambda e, pi=pi: e.dma_start(out=nrm[0:64, pi * 512:pi * 512 + 512],
                                            in_=den_sc[pi, 1].unsqueeze(0).broadcast_to([64, 512])), "nb%d" % pi)
            nbw[pi] = DW(lambda e, pi=pi: e.dma_start(out=nrm[64:128, pi * 512:pi * 512 + 512],
                                                      in_=den_sc[pi, 0].unsqueeze(0).broadcast_to([64, 512])), "nb%d" % pi)
            P.wait("vector", "nb%d" % pi, nbw[pi])
            dst = pr * 2048 + qc * 512
            V(lambda e, cps=cps, dst=dst, pi=pi: e.tensor_tensor(
                ctx_sb[0:64, dst:dst + 512], cps[0:64, 0:512], nrm[0:64, pi * 512:pi * 512 + 512], ALU.mult))
            evac_v[pi] = V(lambda e, cps=cps, dst=dst, pi=pi: e.tensor_tensor(
                ctx_sb[64:128, dst:dst + 512], cps[64:128, 512:1024], nrm[64:128, pi * 512:pi * 512 + 512], ALU.mult))

        def attn_av(info):
            cps, kk, off, vb0, st0, sp, pi = (info["cps"], info["kk"], info["off"],
                                              info["vb0"], info["st0"], info["sp"], info["pi"])
            P.wait("tensor", info["gate"][0], info["gate"][1])
            if st0:
                P.wait("tensor", "v", evac_v[pi])
            T(lambda e, cps=cps, vb0=vb0, kk=kk, off=off, st0=st0, sp=sp: e.matmul(
                cps[:, off:512], v_sb[:, vb0:vb0 + 128], p_sb[:, kk * 1024 + off:kk * 1024 + 512],
                start=st0, stop=sp))
            avT[kk] = T(lambda e, cps=cps, vb0=vb0, kk=kk, off=off, st0=st0, sp=sp: e.matmul(
                cps[:, 512 + off:1024], v_sb[:, vb0 + 32:vb0 + 160], p_sb[:, kk * 1024 + 512 + off:kk * 1024 + 1024],
                start=st0, stop=sp))
            if sp:
                attn_tail(pi, cps, info["qc"], info["pr"], avT[kk])

        steps = [(qc, pr, kt, 4 * qc + 4) for qc in range(4) for pr in range(8) for kt in range(4 * qc + 4)]
        pending = None
        for kap, (qc, pr, kt, nkt) in enumerate(steps):
            pi = (qc * 8 + pr) % 2
            cps = ctx_pair[pi]
            kk = kap % 2
            scp = sc_pair[kk]
            diag = kt >= 4 * qc
            off = (kt - 4 * qc) * 128 if diag else 0
            kbase = (8 + pr) * 2048 + kt * 128
            qbase = pr * 2048 + qc * 512
            P.wait("tensor", "s", exp_s[kk])
            T(lambda e, scp=scp, kbase=kbase, qbase=qbase, off=off: e.matmul(
                scp[:, off:512], qkT[0:64, kbase:kbase + 128], qkT[0:64, qbase + off:qbase + 512],
                start=True, stop=True, tile_position=(0, 0)))
            tv = T(lambda e, scp=scp, kbase=kbase, qbase=qbase, off=off: e.matmul(
                scp[:, 512 + off:1024], qkT[64:128, kbase:kbase + 128], qkT[64:128, qbase + off:qbase + 512],
                start=True, stop=True, tile_position=(64, 0)))
            P.wait("scalar", "t", tv)
            P.wait("scalar", "t", avT[kk])
            exp_s[kk] = S(lambda e, scp=scp, kk=kk, off=off: e.activation(
                p_sb[:, kk * 1024:kk * 1024 + 1024].rearrange("p (h c) -> p h c", h=2)[:, :, off:512],
                scp[:].rearrange("p (h c) -> p h c", h=2)[:, :, off:512],
                AF.Exp, scale=0.125))
            if diag:
                P.wait("vector", "s", exp_s[kk])
                mv = V(lambda e, kk=kk, off=off: e.tensor_tensor(
                    p_sb[:, kk * 1024:kk * 1024 + 1024].rearrange("p (h c) -> p h c", h=2)[:, :, off:off + 128],
                    p_sb[:, kk * 1024:kk * 1024 + 1024].rearrange("p (h c) -> p h c", h=2)[:, :, off:off + 128],
                    tri_bf[:].unsqueeze(1).broadcast_to([128, 2, 128]), ALU.mult))
                gate = ("v", mv)
            else:
                gate = ("s", exp_s[kk])
            info = dict(cps=cps, kk=kk, off=off, vb0=kt * 1280 + pr * 160,
                        st0=(kt == 0), sp=(kt == nkt - 1), pi=pi, qc=qc, pr=pr, gate=gate)
            if pending is not None:
                attn_av(pending)
            pending = info
        attn_av(pending)
        fence()
        if debug and b == 0:
            DG(lambda e: e.dma_start(out=dbg["d_ctx"], in_=ctx_sb[:]), "gd")
            fence()

        if UPTO < 5:
            continue
        for dc in range(ND):
            wd = DW(lambda e, dc=dc: e.dma_start(out=wo_res[:, dc * 1024:dc * 1024 + 1024], in_=wo_sc[dc]), "gv")
        P.wait("tensor", "gv", wd)
        addv = {0: 0, 1: 0}
        accw = {0: 0, 1: 0}
        for st in range(NT):
            pp = st % 2
            pg = pg0 if pp == 0 else pg1
            P.wait("gpsimd", "v", addv[pp])
            xd = DG(lambda e, b=b, st=st, pp=pp: e.dma_start(
                out=xrb[:, pp * 1024:pp * 1024 + 1024], in_=x_d[b, st * 128:(st + 1) * 128, :]), "gx%d" % pp)
            P.wait("tensor", "v", addv[pp])
            for half in range(2):
                for dc in range(ND):
                    T(lambda e, pg=pg, half=half, dc=dc, st=st: e.matmul(
                        pg[:, half * 512:half * 512 + 512],
                        ctx_sb[:, dc * 2048 + st * 128: dc * 2048 + st * 128 + 128],
                        wo_res[:, dc * 1024 + half * 512: dc * 1024 + half * 512 + 512],
                        start=(dc == 0), stop=False))
                tv = T(lambda e, pg=pg, half=half: e.matmul(
                    pg[:, half * 512:half * 512 + 512], ones_bf[0:1, 0:128],
                    borow[0:1, half * 512:half * 512 + 512], start=False, stop=True))
            P.wait("vector", "t", tv)
            P.wait("vector", "gx%d" % pp, xd)
            P.wait("vector", "ga%d" % pp, accw[pp])
            addv[pp] = V(lambda e, pg=pg, pp=pp: e.tensor_tensor(
                acst[:, pp * 1024:pp * 1024 + 1024], pg[:], xrb[:, pp * 1024:pp * 1024 + 1024], ALU.add))
            P.wait("gpsimd", "v", addv[pp])
            accw[pp] = DG(lambda e, b=b, st=st, pp=pp: e.dma_start(
                out=acc_sc[b, st], in_=acst[:, pp * 1024:pp * 1024 + 1024]), "ga%d" % pp)
        wout_t = P.cnt["t"]
        wout_v = P.cnt["v"]
        wout_ga0 = P.cnt["ga0"]
        wout_ga1 = P.cnt["ga1"]
        P.wait("gpsimd", "t", wout_t)
        P.wait("gpsimd", "v", wout_v)
        P.wait("gpsimd", "ga0", wout_ga0)
        P.wait("gpsimd", "ga1", wout_ga1)
        P.wait("vector", "t", wout_t)
        P.wait("vector", "ga0", wout_ga0)
        P.wait("vector", "ga1", wout_ga1)
        P.wait("scalar", "t", wout_t)
        P.wait("tensor", "v", wout_v)
        if debug and b == 0:
            P.wait("gpsimd", "ga0", P.cnt["ga0"])
            P.wait("gpsimd", "ga1", P.cnt["ga1"])
            for st in range(NT):
                DG(lambda e, st=st: e.dma_start(out=dbg["d_acc"][st], in_=acc_sc[0, st]), "gd")
            fence()

        if UPTO < 6:
            continue
        P.op("scalar", lambda e: e.preload_act_table(AF.Silu))
        vv = Vd(lambda e: e.tensor_scalar(stat[:], gst[:], ccol[:, C_C1:C_C1 + 1], None, ALU.mult))
        P.wait("tensor", "v", vv)
        tv = T(lambda e: e.matmul(pg1[0:16, 0:128], stat[:], ident_f[:], start=True, stop=True))
        P.wait("vector", "t", tv)
        vv = V(lambda e: e.tensor_copy(nrm[0:16, 0:128], pg1[0:16, 0:128]))
        P.wait("gpsimd", "v", vv)
        gw = DG(lambda e: e.dma_start(out=row_sc[0:2048].rearrange("(r c) -> r c", r=16), in_=nrm[0:16, 0:128]), "gr")
        P.wait("gpsimd", "gr", gw)
        gw = DG(lambda e: e.dma_start(out=g_sb[:], in_=row_sc[0:2048].unsqueeze(0).broadcast_to([128, 2048])), "gr")
        P.wait("vector", "gr", gw)
        stash0 = V(lambda e: e.memset(stash[:], 0.0))
        if debug and b == 0:
            P.wait("gpsimd", "v", P.cnt["v"])
            DG(lambda e: e.dma_start(out=dbg["d_g"], in_=g_sb[:]), "gd")

        pmT_t = [0]
        w1T = {0: 0, 1: 0}
        w2T = {0: 0, 1: 0}
        w1ev = {0: 0, 1: 0, 2: 0, 3: 0}
        w2ev = {0: 0, 1: 0}
        w2last = {0: 0, 1: 0}
        uV = {k: stash0 for k in range(8)}
        uSS = {k: 0 for k in range(8)}
        pg1free_s = [0]
        pg1free_v = [0]
        pg1w = [-1]
        roundV_w = {-1: 0}
        roundS_w = {-1: 0}
        silu31 = {}
        accv = {0: 0, 1: 0}
        outw = {0: 0, 1: 0}
        sb_g = 0

        ub4 = [ubuf[0], ubuf[1], xc[:, 0:388], xc[:, 388:776]]

        def w1conv(c):
            have = c <= 7
            nonlocal_first = [True]
            for grp in range(8):
                hcs = [grp * 4 + j for j in range(4)]
                if have:
                    ppw = grp % 2
                    P.wait("sync", "t", w1T[ppw])
                    wd = DW(lambda e, grp=grp, ppw=ppw: e.dma_start(
                        out=w1b[:, ppw * 4096:ppw * 4096 + 4096].rearrange("p (h q) -> p h q", h=4),
                        in_=w1q_sc[grp * 4:grp * 4 + 4].rearrange("h p q -> p h q")), "pw%d" % ppw)
                    P.wait("tensor", "pw%d" % ppw, wd)
                for hc in hcs:
                    sl = hc % 4
                    P.wait("scalar", "v", uV[sl])
                    S(lambda e, sl=sl, hc=hc: e.copy(ub4[sl][:, 0:129], stash[:, hc * 129:hc * 129 + 129]))
                evc = {}
                for hc in hcs:
                    sl = hc % 4
                    if have:
                        pw = hc % 2
                        P.wait("tensor", "v", w1ev[pw])
                        for dc in range(ND):
                            tmm = T(lambda e, j=hc % 4, dc=dc, c=c, pw=pw, ppw=(hc // 4) % 2: e.matmul(
                                pg0[:, pw * 512:pw * 512 + 256],
                                w1b[:, ppw * 4096 + j * 1024 + dc * 128: ppw * 4096 + j * 1024 + dc * 128 + 128],
                                xiT[:, dc * 2048 + c * 256: dc * 2048 + c * 256 + 256],
                                start=(dc == 0), stop=(dc == ND - 1)))
                        w1T[(hc // 4) % 2] = tmm
                        P.wait("vector", "t", tmm)
                        P.wait("vector", "s", uSS[sl])
                        w1ev[pw] = V(lambda e, sl=sl, pw=pw, c=c: e.tensor_tensor(
                            ub4[sl][:, 129:385], pg0[:, pw * 512:pw * 512 + 256],
                            g_sb[:, c * 256:c * 256 + 256], ALU.mult))
                        evc[hc] = w1ev[pw]
                    else:
                        P.wait("vector", "s", uSS[sl])
                        evc[hc] = V(lambda e, sl=sl: e.memset(ub4[sl][:, 129:388], 0.0))
                inc_ = {}
                for hc in hcs:
                    sl = hc % 4
                    P.wait("scalar", "v", evc[hc])
                    if nonlocal_first[0]:
                        P.wait("scalar", "v", roundV_w[c - 1])
                        nonlocal_first[0] = False
                    inc_[hc] = S(lambda e, sl=sl, hc=hc: e.activation(
                        h2w[:, hc * 256:hc * 256 + 256], ub4[sl][:, 0:256], AF.Identity,
                        scale=cwsb[:, hc * 3:hc * 3 + 1], bias=cbsb[:, hc:hc + 1]))
                    if have:
                        uSS[sl] = S(lambda e, sl=sl, hc=hc: e.copy(
                            stash[:, hc * 129:hc * 129 + 129], ub4[sl][:, 256:385]))
                    else:
                        uSS[sl] = inc_[hc]
                for hc in hcs:
                    sl = hc % 4
                    P.wait("vector", "s", inc_[hc])
                    V(lambda e, sl=sl, hc=hc: e.scalar_tensor_tensor(
                        h2w[:, hc * 256:hc * 256 + 256], ub4[sl][:, 1:257], cwsb[:, hc * 3 + 1:hc * 3 + 2],
                        h2w[:, hc * 256:hc * 256 + 256], ALU.mult, ALU.add))
                    uV[sl] = V(lambda e, sl=sl, hc=hc: e.scalar_tensor_tensor(
                        h2w[:, hc * 256:hc * 256 + 256], ub4[sl][:, 2:258], cwsb[:, hc * 3 + 2:hc * 3 + 3],
                        h2w[:, hc * 256:hc * 256 + 256], ALU.mult, ALU.add))
                P.wait("scalar", "v", uV[hcs[-1] % 4])
                sl_ = S(lambda e, grp=grp: e.activation(
                    h2w[:, grp * 1024:grp * 1024 + 1024], h2w[:, grp * 1024:grp * 1024 + 1024], AF.Silu))
            silu31[c] = sl_

        def quantscale(cq):
            P.wait("vector", "s", silu31[cq])
            P.wait("vector", "t", pmT_t[0])
            pmv = V(lambda e: e.tensor_reduce(
                pmb[:, 0:256], h2w[:].rearrange("p (h t) -> p t h", h=32),
                AX.X, ALU.max, apply_absolute_value=True))
            P.wait("tensor", "v", pmv)
            P.wait("tensor", "s", P.cnt["s"])
            T(lambda e: e.matmul(pg1[:, 0:128], pmb[:, 0:128], ident_f[:], start=True, stop=True))
            tv = T(lambda e: e.matmul(pg1[:, 128:256], pmb[:, 128:256], ident_f[:], start=True, stop=True))
            pmT_t[0] = tv
            P.wait("vector", "t", tv)
            Vd(lambda e: e.tensor_reduce(
                mx[:, 0:2], pg1[:, 0:256].rearrange("p (w c) -> p w c", w=2), AX.X, ALU.max,
                apply_absolute_value=False))
            Vd(lambda e: e.tensor_scalar(mx[:, 2:4], mx[:, 0:2], 1e-5, None, ALU.max))
            Vd(lambda e: e.tensor_scalar(mx[:, 4:6], mx[:, 2:4], ccol[:, C_C2:C_C2 + 1], None, ALU.mult))
            Vd(lambda e: e.reciprocal(mx[:, 6:8], mx[:, 2:4]))
            vvs = Vd(lambda e: e.tensor_scalar(mx[:, 6:8], mx[:, 6:8], 127.0, None, ALU.mult))
            P.wait("tensor", "v", vvs)
            tv = T(lambda e: e.matmul(pg1[0:2, 256:384], mx[:, 6:8], ident_f[:], start=True, stop=True))
            P.wait("vector", "t", tv)
            vv = V(lambda e: e.tensor_copy(nrm[0:2, 0:128], pg1[0:2, 256:384]))
            pg1free_v[0] = vv
            P.wait("gpsimd", "v", vv)
            P.wait("gpsimd", "gr", P.cnt["gr"])
            sg = DG(lambda e: e.dma_start(out=row_sc[2048:2304].rearrange("(r c) -> r c", r=2), in_=nrm[0:2, 0:128]), "gr")
            P.wait("gpsimd", "gr", sg)
            sg = DG(lambda e: e.dma_start(out=sb_sb[:], in_=row_sc[2048:2304].unsqueeze(0).broadcast_to([128, 256])), "gr")
            P.wait("vector", "gr", sg)
            qm = V(lambda e: e.tensor_tensor(
                h2w[:].rearrange("p (h t) -> p h t", h=32),
                h2w[:].rearrange("p (h t) -> p h t", h=32),
                sb_sb[:].unsqueeze(1).broadcast_to([128, 32, 256]), ALU.mult))
            P.wait("vector", "t", max(w2T[0], w2T[1]))
            P.wait("scalar", "t", max(w2T[0], w2T[1]))
            P.wait("scalar", "v", qm)
            roundV_w[cq] = V(lambda e: e.tensor_scalar(h2q[:, 0:4096], h2w[:, 0:4096], MAGIC, -MAGIC, ALU.add, ALU.add))
            for k in range(4):
                S(lambda e, k=k: e.activation(xbuf[:, 0:1024], h2w[:, 4096 + k * 1024:4096 + k * 1024 + 1024],
                                              AF.Identity, bias=ccol[:, C_MAG:C_MAG + 1]))
                rS = S(lambda e, k=k: e.activation(h2q[:, 4096 + k * 1024:4096 + k * 1024 + 1024], xbuf[:, 0:1024],
                                                   AF.Identity, bias=ccol[:, C_NMAG:C_NMAG + 1]))
            roundS_w[cq] = rS

        def w2block(cq, b=b):
            stl = [t_ for t_ in (2 * cq - 1, 2 * cq) if 0 <= t_ < 16]
            accd = {}
            for st_out in stl:
                par = st_out % 2
                P.wait("gpsimd", "v", accv[par])
                accd[st_out] = DG(lambda e, b=b, st_out=st_out, par=par: e.dma_start(
                    out=accb[:, par * 1024:par * 1024 + 1024], in_=acc_sc[b, st_out]), "ga%d" % par)
            for grp in range(8):
                pp2 = grp % 2
                P.wait("sync", "t", w2T[pp2])
                wd2 = DW(lambda e, grp=grp, pp2=pp2: e.dma_start(
                    out=w2b[:, pp2 * 4096:pp2 * 4096 + 4096].rearrange("p (h q) -> p h q", h=4),
                    in_=w2q_sc[grp * 4:grp * 4 + 4].rearrange("h p q -> p h q")), "qw%d" % pp2)
                P.wait("tensor", "qw%d" % pp2, wd2)
                if grp == 0:
                    P.wait("tensor", "v", roundV_w[cq])
                    P.wait("tensor", "s", roundS_w[cq])
                for j in range(4):
                    hc = grp * 4 + j
                    for st_out in stl:
                        w = st_out - (2 * cq - 1)
                        par = st_out % 2
                        pgw = pg2 if par == 0 else pg3
                        if hc == 0:
                            P.wait("tensor", "v", w2ev[par])
                        for half in range(2):
                            tmm = T(lambda e, pgw=pgw, hc=hc, w=w, pp2=pp2, j=j, half=half: e.matmul(
                                pgw[:, half * 512:half * 512 + 512],
                                h2q[:, hc * 256 + w * 128: hc * 256 + w * 128 + 128],
                                w2b[:, pp2 * 4096 + j * 1024 + half * 512: pp2 * 4096 + j * 1024 + half * 512 + 512],
                                start=(hc == 0), stop=(hc == NHC - 1)))
                        w2T[pp2] = tmm
                        if hc == NHC - 1:
                            w2last[par] = tmm
            for st_out in stl:
                w = st_out - (2 * cq - 1)
                par = st_out % 2
                pgw = pg2 if par == 0 else pg3
                P.wait("vector", "ga%d" % par, accd[st_out])
                P.wait("vector", "t", w2last[par])
                P.wait("vector", "go%d" % par, outw[par])
                accv[par] = V(lambda e, pgw=pgw, par=par, w=w: e.scalar_tensor_tensor(
                    outb[:, par * 1024:par * 1024 + 1024], pgw[:], mx[:, 4 + w:5 + w],
                    accb[:, par * 1024:par * 1024 + 1024], ALU.mult, ALU.add))
                w2ev[par] = accv[par]
                P.wait("gpsimd", "v", accv[par])
                outw[par] = DG(lambda e, b=b, st_out=st_out, par=par: e.dma_start(
                    out=out_d[b, st_out * 128:(st_out + 1) * 128, :], in_=outb[:, par * 1024:par * 1024 + 1024]), "go%d" % par)

        for c in range(FFNW):
            if 1 <= c:
                quantscale(c - 1)
            if c <= 8:
                w1conv(c)
            if 1 <= c:
                w2block(c - 1)
        fence()

    for s_ in list(P.cnt.keys()):
        if s_ not in ("t", "s", "v") and P.cnt[s_] > 0:
            P.wait("gpsimd", s_, P.cnt[s_])

    from contextlib import ExitStack
    with ExitStack() as stack:
        sems = {}
        for s_ in list(P.cnt.keys()):
            if P.cnt[s_] > 0:
                sems[s_] = stack.enter_context(nc.semaphore("sem_" + s_))
        block = stack.enter_context(nc.Block())

        @block.sync
        def _(sync):
            P.replay("sync", sync, sems)

        @block.gpsimd
        def _(gpsimd):
            P.replay("gpsimd", gpsimd, sems)

        @block.tensor
        def _(tensor):
            P.replay("tensor", tensor, sems)

        @block.scalar
        def _(scalar):
            P.replay("scalar", scalar, sems)

        @block.vector
        def _(vector):
            P.replay("vector", vector, sems)

    return nc


_NC_CACHE = {}


def _get_nc(debug=False):
    key = (debug, UPTO, FFNW, W1MODE)
    if key not in _NC_CACHE:
        _NC_CACHE[key] = build(debug)
    return _NC_CACHE[key]


def kernel(L0_attn_hidden_states, L0_attn_W_attn, L0_attn_b_attn, L0_attn_W_out,
           L0_attn_b_out, L0_ffn_w1, L0_ffn_conv_w, L0_ffn_conv_b, L0_ffn_w2,
           _debug=False):
    x = np.ascontiguousarray(L0_attn_hidden_states, dtype=np.float32)
    shards = x.reshape(8, NB, 2048, 1024)
    nc = _get_nc(_debug)
    common = {
        "W_attn": np.ascontiguousarray(L0_attn_W_attn, dtype=np.float32),
        "b_attn": np.ascontiguousarray(L0_attn_b_attn, dtype=np.float32),
        "W_out": np.ascontiguousarray(L0_attn_W_out, dtype=np.float32),
        "b_out": np.ascontiguousarray(L0_attn_b_out, dtype=np.float32),
        "w1": np.ascontiguousarray(L0_ffn_w1, dtype=np.float32),
        "cw": np.ascontiguousarray(L0_ffn_conv_w, dtype=np.float32).reshape(4096, 3),
        "cb": np.ascontiguousarray(L0_ffn_conv_b, dtype=np.float32),
        "w2": np.ascontiguousarray(L0_ffn_w2, dtype=np.float32),
        "ident_bf": np.eye(128, dtype=ml_dtypes.bfloat16),
        "ident_f": np.eye(128, dtype=np.float32),
        "tri_bf": np.triu(np.ones((128, 128), dtype=np.float32)).astype(ml_dtypes.bfloat16),
    }
    in_maps = [dict(common, x=np.ascontiguousarray(shards[i])) for i in range(8)]
    res = run_bass_kernel_spmd(nc, in_maps, core_ids=list(range(8)))
    out = np.concatenate([r["out"].reshape(NB, 2048, 1024) for r in res.results], axis=0)
    if _debug:
        return out, res
    return out


# version 27
# speedup vs baseline: 1.1690x; 1.0654x over previous
import numpy as np
import ml_dtypes
import concourse.bass as bass
import concourse.mybir as mybir
from concourse.bass_utils import run_bass_kernel_spmd

F32 = mybir.dt.float32
BF16 = mybir.dt.bfloat16
AF = mybir.ActivationFunctionType
ALU = mybir.AluOpType
AX = mybir.AxisListType

MAGIC = 12582912.0
UPTO = 6
FFNW = 10
W1MODE = 0
NB = 2
NT = 16
ND = 8
NHC = 32

ENGS = ("sync", "gpsimd", "tensor", "scalar", "vector")


class Plan:
    def __init__(self):
        from collections import defaultdict
        self.ops = {e: [] for e in ENGS}
        self.cnt = defaultdict(int)

    def inc(self, eng, fn, sem, by=1):
        self.ops[eng].append(("inc", fn, sem, by))
        self.cnt[sem] += by
        return self.cnt[sem]

    def op(self, eng, fn):
        self.ops[eng].append(("op", fn))

    def wait(self, eng, sem, val):
        if val > 0:
            self.ops[eng].append(("wait", sem, val))

    def replay(self, eng_name, obj, sems):
        for it in self.ops[eng_name]:
            if it[0] == "op":
                it[1](obj)
            elif it[0] == "inc":
                it[1](obj).then_inc(sems[it[2]], it[3])
            else:
                obj.wait_ge(sems[it[1]], it[2])


def build(debug=False):
    nc = bass.Bass("TRN2", target_bir_lowering=False)
    P = Plan()

    def T(fn):
        return P.inc("tensor", fn, "t")

    def S(fn):
        return P.inc("scalar", fn, "s")

    def V(fn):
        return P.inc("vector", fn, "v")

    def Vd(fn):
        n = P.inc("vector", fn, "v")
        P.wait("vector", "v", n)
        return n

    def DW(fn, sem):
        return P.inc("sync", fn, sem, 16)

    def DG(fn, sem):
        return P.inc("gpsimd", fn, sem, 16)

    SYNC_SEMS = ("pw0", "pw1", "qw0", "qw1", "gv", "wr", "sg0", "sg1", "nb0", "nb1")
    GP_SEMS = ("gi", "gx0", "gx1", "ga0", "ga1", "go0", "go1", "gr", "gd")

    def fence():
        from collections import defaultdict
        c = defaultdict(int, P.cnt)
        dma = [s for s in list(SYNC_SEMS) + list(GP_SEMS) if c.get(s, 0) > 0]
        P.wait("tensor", "s", c["s"]); P.wait("tensor", "v", c["v"])
        P.wait("scalar", "t", c["t"]); P.wait("scalar", "v", c["v"])
        P.wait("vector", "t", c["t"]); P.wait("vector", "s", c["s"])
        for q in ("sync", "gpsimd"):
            P.wait(q, "t", c["t"]); P.wait(q, "s", c["s"]); P.wait(q, "v", c["v"])
            for s in dma:
                P.wait(q, s, c[s])
        for e in ("tensor", "scalar", "vector"):
            for s in dma:
                P.wait(e, s, c[s])

    x_d = nc.dram_tensor("x", [NB, 2048, 1024], F32, kind="ExternalInput").ap()
    wattn_d = nc.dram_tensor("W_attn", [1024, 3072], F32, kind="ExternalInput").ap()
    battn_d = nc.dram_tensor("b_attn", [3072], F32, kind="ExternalInput").ap()
    wout_d = nc.dram_tensor("W_out", [1024, 1024], F32, kind="ExternalInput").ap()
    bout_d = nc.dram_tensor("b_out", [1024], F32, kind="ExternalInput").ap()
    w1_d = nc.dram_tensor("w1", [4096, 1024], F32, kind="ExternalInput").ap()
    cw_d = nc.dram_tensor("cw", [4096, 3], F32, kind="ExternalInput").ap()
    cb_d = nc.dram_tensor("cb", [4096], F32, kind="ExternalInput").ap()
    w2_d = nc.dram_tensor("w2", [1024, 4096], F32, kind="ExternalInput").ap()
    identbf_d = nc.dram_tensor("ident_bf", [128, 128], BF16, kind="ExternalInput").ap()
    identf_d = nc.dram_tensor("ident_f", [128, 128], F32, kind="ExternalInput").ap()
    tribf_d = nc.dram_tensor("tri_bf", [128, 128], BF16, kind="ExternalInput").ap()
    out_d = nc.dram_tensor("out", [NB, 2048, 1024], F32, kind="ExternalOutput").ap()

    wqk_sc = nc.dram_tensor("wqk_sc", [16, 128, 1024], BF16, kind="Internal").ap()
    wv_sc = nc.dram_tensor("wv_sc", [2, 8, 128, 640], BF16, kind="Internal").ap()
    wo_sc = nc.dram_tensor("wo_sc", [8, 128, 1024], BF16, kind="Internal").ap()
    w1q_sc = nc.dram_tensor("w1q_sc", [32, 128, 1024], BF16, kind="Internal").ap()
    w2q_sc = nc.dram_tensor("w2q_sc", [32, 128, 1024], BF16, kind="Internal").ap()
    acc_sc = nc.dram_tensor("acc_sc", [NB, 16, 128, 1024], F32, kind="Internal").ap()
    row_sc = nc.dram_tensor("row_sc", [4096], F32, kind="Internal").ap()
    den_sc = nc.dram_tensor("den_sc", [2, 2, 512], F32, kind="Internal").ap()

    dbg = {}
    if debug:
        for nm, shp, dt in [("d_xnT", [128, 16384], BF16), ("d_xiT", [128, 16384], BF16),
                            ("d_qkT", [128, 32768], BF16), ("d_vsb", [128, 20480], BF16),
                            ("d_ctx", [128, 16384], BF16), ("d_g", [128, 2048], F32),
                            ("d_acc", [16, 128, 1024], F32),
                            ("d_w1q", [32, 128, 1024], BF16), ("d_w2q", [32, 128, 1024], BF16),
                            ("d_ccol", [128, 16], F32)]:
            dbg[nm] = nc.dram_tensor(nm, shp, dt, kind="ExternalOutput").ap()

    def sbt(name, shape, dtype, off):
        return nc.alloc_sbuf_tensor_at(name, shape, dtype, offset=off).ap()

    cur = [16512]

    def nxt(sz, c=cur):
        r = c[0]
        c[0] = (c[0] + sz + 31) & ~31
        return r

    ident_bf = sbt("identbf", [128, 128], BF16, nxt(256))
    ones_bf = sbt("onesbf", [128, 128], BF16, nxt(256))
    tri_bf = sbt("tribf", [128, 128], BF16, nxt(256))
    ident_f = sbt("identf", [128, 128], F32, nxt(512))
    ones_f = sbt("onesf", [128, 128], F32, nxt(512))
    ccol = sbt("ccol", [128, 16], F32, nxt(64))
    C_MAG, C_NMAG, C_EPS, C_C1, C_C2, C_SW1, C_SW2 = 0, 1, 2, 3, 4, 5, 6
    bcol = sbt("bcol", [128, 16], F32, nxt(64))
    cwsb = sbt("cwsb", [128, 96], F32, nxt(384))
    cbsb = sbt("cbsb", [128, 32], F32, nxt(128))
    stat = sbt("stat", [128, 16], F32, nxt(64))
    gst = sbt("gst", [128, 16], F32, nxt(64))
    bvrow = sbt("bvrow", [1, 1280], BF16, nxt(2560))
    borow = sbt("borow", [1, 1024], BF16, nxt(2048))
    nrm = sbt("nrm", [128, 1024], F32, nxt(4096))
    rows_sb = sbt("rowssb", [128, 1024], F32, nxt(4096))

    R0 = (cur[0] + 63) & ~63
    xiT = sbt("xiT", [128, 16384], BF16, R0)
    XNC = R0 + 32768
    xnT = sbt("xnT", [128, 16384], BF16, XNC)
    ctx_sb = sbt("ctxsb", [128, 16384], BF16, XNC)
    h2w = sbt("h2w", [128, 8192], F32, XNC)
    QK = XNC + 32768
    qkT = sbt("qkT", [128, 32768], BF16, QK)
    VSB = QK + 65536
    v_sb = sbt("vsb", [128, 20480], BF16, VSB)
    AQ = VSB + 40960
    xbuf = sbt("xbuf", [128, 2048], F32, AQ)
    xc = sbt("xc", [128, 1024], F32, AQ + 8192)
    xt1 = sbt("xt1", [128, 1024], F32, AQ + 12288)
    xn_t = sbt("xnt", [128, 1024], BF16, AQ + 16384)
    xi_t = sbt("xit", [128, 1024], BF16, AQ + 18432)
    wqkb = sbt("wqkb", [128, 2048], BF16, AQ)
    wvb = sbt("wvb", [128, 5120], BF16, AQ + 4096)
    p_sb = sbt("psb", [128, 2048], BF16, AQ)
    SB_END = AQ + 20480

    c0 = [QK]
    ws0 = sbt("ws0", [128, 4096], F32, nxt(16384, c0))
    ws1 = sbt("ws1", [128, 4096], F32, nxt(16384, c0))
    wcast = sbt("wcast", [128, 4096], BF16, nxt(8192, c0))
    tqtmp = sbt("tqtmp", [128, 4096], F32, nxt(16384, c0))
    tritb = sbt("tritb", [128, 4096], BF16, nxt(8192, c0))
    stg = [sbt("stg0", [128, 1024], BF16, nxt(2048, c0)),
           sbt("stg1", [128, 1024], BF16, nxt(2048, c0))]
    zstage = sbt("zstage", [128, 128], BF16, nxt(256, c0))
    ws = [ws0, ws1]

    wo_res = sbt("wores", [128, 8192], BF16, QK)
    xrb = sbt("xrb", [128, 2048], F32, QK + 16384)
    acst = sbt("acst", [128, 2048], F32, QK + 24576)

    c1 = [QK]
    g_sb = sbt("gsb", [128, 2048], F32, nxt(8192, c1))
    g_row = sbt("grow", [1, 2048], F32, nxt(8192, c1))
    stash = sbt("stash", [128, 4128], F32, nxt(16512, c1))
    ubuf = [sbt("u0", [128, 388], F32, nxt(1552, c1)),
            sbt("u1", [128, 388], F32, nxt(1552, c1))]
    h2q = sbt("h2q", [128, 8192], BF16, nxt(16384, c1))
    w1b = sbt("w1b", [128, 8192], BF16, nxt(16384, c1))
    w2b = sbt("w2b", [128, 8192], BF16, nxt(16384, c1))
    accb = sbt("accb", [128, 2048], F32, nxt(8192, c1))
    outb = sbt("outb", [128, 2048], F32, nxt(8192, c1))
    mx = sbt("mx", [128, 16], F32, nxt(64, c1))
    sbrow = sbt("sbrow", [1, 256], F32, nxt(1024, c1))
    sb_sb = sbt("sbsb", [128, 256], F32, nxt(1024, c1))
    pmb = sbt("pmb", [128, 512], F32, nxt(2048, c1))
    assert c1[0] <= VSB + 40960, c1[0]

    pg0 = nc.alloc_psum_tensor("pg0", [128, 1024], F32).ap()
    pg1 = nc.alloc_psum_tensor("pg1", [128, 1024], F32).ap()
    pg2 = nc.alloc_psum_tensor("pg2", [128, 1024], F32).ap()
    pg3 = nc.alloc_psum_tensor("pg3", [128, 1024], F32).ap()

    V(lambda e: e.memset(ones_f[:], 1.0))
    V(lambda e: e.memset(ones_bf[:], 1.0))
    V(lambda e: e.memset(ccol[:, C_MAG:C_MAG + 1], MAGIC))
    V(lambda e: e.memset(ccol[:, C_NMAG:C_NMAG + 1], -MAGIC))
    V(lambda e: e.memset(ccol[:, C_EPS:C_EPS + 1], 1e-5))
    DG(lambda e: e.dma_start(out=ident_bf[:], in_=identbf_d), "gi")
    DG(lambda e: e.dma_start(out=ident_f[:], in_=identf_d), "gi")
    DG(lambda e: e.dma_start(out=tri_bf[:], in_=tribf_d), "gi")
    DG(lambda e: e.dma_start(out=cwsb[:].rearrange("p (h j) -> p h j", h=32),
                             in_=cw_d.rearrange("(h p) j -> p h j", p=128)), "gi")
    DG(lambda e: e.dma_start(out=cbsb[:], in_=cb_d.rearrange("(h p) -> p h", p=128),
                             allow_slow_non_contiguous=True), "gi")
    DG(lambda e: e.dma_start(out=bcol[:], in_=battn_d[0:2048].rearrange("(c p) -> p c", p=128),
                             allow_slow_non_contiguous=True), "gi")
    DG(lambda e: e.dma_start(out=nrm[0:1, :], in_=battn_d[2048:3072].unsqueeze(0)), "gi")
    gg = DG(lambda e: e.dma_start(out=xc[0:1, :], in_=bout_d.unsqueeze(0)), "gi")
    P.wait("vector", "gi", gg)
    P.wait("tensor", "gi", gg)
    P.wait("scalar", "gi", gg)
    V(lambda e: e.memset(bvrow[:], 0.0))
    V(lambda e: e.memset(bvrow[:].rearrange("o (q c) -> o q c", c=160)[:, :, 64:66], 1.0))
    V(lambda e: e.tensor_copy(bvrow[:].rearrange("o (q c) -> o q c", c=160)[:, :, 0:64],
                              nrm[0:1, :].rearrange("o (h c) -> o h c", c=128)[:, :, 0:64]))
    V(lambda e: e.tensor_copy(bvrow[:].rearrange("o (q c) -> o q c", c=160)[:, :, 96:160],
                              nrm[0:1, :].rearrange("o (h c) -> o h c", c=128)[:, :, 64:128]))
    V(lambda e: e.memset(zstage[:], 0.0))
    V(lambda e: e.tensor_copy(borow[:], xc[0:1, :]))
    P.op("scalar", lambda e: e.preload_act_table(AF.Sqrt))

    cast_v = {}
    wr_w = 0
    for dc in range(ND):
        if dc >= 2:
            P.wait("sync", "v", cast_v[dc - 2])
        wt = DW(lambda e, dc=dc, b=ws[dc % 2]: e.dma_start(out=b[:, 0:3072], in_=wattn_d[dc * 128:(dc + 1) * 128, :]), "pw%d" % (dc % 2))
        P.wait("vector", "pw%d" % (dc % 2), wt)
        P.wait("vector", "wr", P.cnt["wr"])
        cast_v[dc] = V(lambda e, b=ws[dc % 2]: e.tensor_copy(wcast[:, 0:3072], b[:, 0:3072]))
        P.wait("sync", "v", cast_v[dc])
        for ct in range(16):
            DW(lambda e, ct=ct, dc=dc: e.dma_start(out=wqk_sc[ct, :, dc * 128:(dc + 1) * 128],
                                                   in_=wcast[:, ct * 128:(ct + 1) * 128]), "wr")
        for hf in range(2):
            DW(lambda e, hf=hf, dc=dc: e.dma_start(
                out=wv_sc[hf, dc].rearrange("p (q c) -> p q c", c=160)[:, :, 64:96],
                in_=zstage[:].rearrange("p (q c) -> p q c", c=32),
                allow_slow_non_contiguous=True), "wr")
            DW(lambda e, hf=hf, dc=dc: e.dma_start(
                out=wv_sc[hf, dc].rearrange("p (q c) -> p q c", c=160)[:, :, 0:64],
                in_=wcast[:, 2048 + hf * 512: 2048 + hf * 512 + 512].rearrange("p (q c) -> p q c", c=128)[:, :, 0:64],
                allow_slow_non_contiguous=True), "wr")
            wr_w = DW(lambda e, hf=hf, dc=dc: e.dma_start(
                out=wv_sc[hf, dc].rearrange("p (q c) -> p q c", c=160)[:, :, 96:160],
                in_=wcast[:, 2048 + hf * 512: 2048 + hf * 512 + 512].rearrange("p (q c) -> p q c", c=128)[:, :, 64:128],
                allow_slow_non_contiguous=True), "wr")
    wrh = {0: 0, 1: 0}
    for dc in range(ND):
        pp = dc % 2
        if dc >= 2:
            P.wait("sync", "v", cast_v[8 + dc - 2])
        wt = DW(lambda e, dc=dc, b=ws[dc % 2]: e.dma_start(out=b[:, 0:1024], in_=wout_d[dc * 128:(dc + 1) * 128, :]), "pw%d" % (dc % 2))
        P.wait("vector", "pw%d" % (dc % 2), wt)
        P.wait("vector", "wr", P.cnt["wr"])
        cast_v[8 + dc] = V(lambda e, b=ws[dc % 2], pp=pp: e.tensor_copy(wcast[:, pp * 1024:pp * 1024 + 1024], b[:, 0:1024]))
        P.wait("sync", "v", cast_v[8 + dc])
        wrh[pp] = DW(lambda e, dc=dc, pp=pp: e.dma_start(out=wo_sc[dc], in_=wcast[:, pp * 1024:pp * 1024 + 1024]), "wr")

    def wquant(w_d, ntiles, tilecols, swcol, c1col, writer):
        acm = stat[:, 0:1]
        tmpc = stat[:, 1:2]
        red_v = {}
        for i in range(ntiles):
            if i >= 2:
                P.wait("sync", "s", red_v[i - 2][0])
            wt = DW(lambda e, i=i, b=ws[i % 2], tc=tilecols: e.dma_start(out=b[:, 0:tc], in_=w_d[i * 128:(i + 1) * 128, :]), "pw%d" % (i % 2))
            P.wait("scalar", "pw%d" % (i % 2), wt)
            if i == 0:
                sa = S(lambda e, b=ws[0], tc=tilecols: e.activation(tqtmp[:, 0:tc], b[:, 0:tc], AF.Abs, accum_out=acm))
                red_v[i] = (sa, P.cnt["v"])
            else:
                P.wait("scalar", "v", red_v[i - 1][1])
                sa = S(lambda e, b=ws[i % 2], tc=tilecols: e.activation(tqtmp[:, 0:tc], b[:, 0:tc], AF.Abs, accum_out=tmpc))
                P.wait("vector", "s", sa)
                va = Vd(lambda e: e.tensor_tensor(acm, acm, tmpc, ALU.add))
                red_v[i] = (sa, va)
        P.wait("tensor", "v", P.cnt["v"])
        P.wait("tensor", "s", P.cnt["s"])
        tv = T(lambda e: e.matmul(pg1[0:1, 0:1], acm, ones_f[:, 0:1], start=True, stop=True))
        P.wait("vector", "t", tv)
        Vd(lambda e, n=float(ntiles * 128 * tilecols): e.tensor_scalar(stat[0:1, 2:3], pg1[0:1, 0:1], 1.0 / n, None, ALU.mult))
        Vd(lambda e: e.tensor_scalar(stat[0:1, 2:3], stat[0:1, 2:3], 1e-5, None, ALU.max))
        vv = Vd(lambda e: e.reciprocal(stat[0:1, 3:4], stat[0:1, 2:3]))
        P.wait("tensor", "v", vv)
        T(lambda e: e.matmul(pg1[:, 2:3], ones_f[0:1, 0:128], stat[0:1, 2:3], start=True, stop=True))
        tv = T(lambda e: e.matmul(pg1[:, 3:4], ones_f[0:1, 0:128], stat[0:1, 3:4], start=True, stop=True))
        P.wait("vector", "t", tv)
        Vd(lambda e, c=c1col: e.tensor_scalar(ccol[:, c:c + 1], pg1[:, 2:3], 1.0 / 127.0, None, ALU.mult))
        vv = Vd(lambda e, c=swcol: e.tensor_copy(ccol[:, c:c + 1], pg1[:, 3:4]))
        P.wait("scalar", "v", vv)
        P.wait("tensor", "v", vv)
        qs = {}
        vts = {}
        dbl = 2 if tilecols <= 1024 else 1
        for i in range(ntiles):
            toff = (i % dbl) * tilecols
            if i >= 2:
                P.wait("sync", "s", qs[i - 2])
            wt = DW(lambda e, i=i, b=ws[i % 2], tc=tilecols: e.dma_start(out=b[:, 0:tc], in_=w_d[i * 128:(i + 1) * 128, :]), "pw%d" % (i % 2))
            P.wait("scalar", "pw%d" % (i % 2), wt)
            if i >= dbl:
                P.wait("scalar", "v", vts[i - dbl])
            S(lambda e, b=ws[i % 2], tc=tilecols, c=swcol, to=toff: e.activation(
                tqtmp[:, to:to + tc], b[:, 0:tc], AF.Identity, scale=ccol[:, c:c + 1], bias=ccol[:, C_MAG:C_MAG + 1]))
            qs[i] = S(lambda e, tc=tilecols, to=toff: e.activation(
                tqtmp[:, to:to + tc], tqtmp[:, to:to + tc], AF.Identity, bias=ccol[:, C_NMAG:C_NMAG + 1]))
            P.wait("vector", "s", qs[i])
            if i >= dbl:
                P.wait("vector", "t", trd[i - dbl])
            vts[i] = V(lambda e, tc=tilecols, to=toff: e.tensor_scalar(
                tritb[:, to:to + tc], tqtmp[:, to:to + tc], 1.0, -1.0, ALU.min, ALU.max))
            trd[i] = writer(i, vts[i], toff)

    def w1_writer(i, vtrit, toff=0):
        pp = i % 2
        pg = pg0 if pp == 0 else pg1
        P.wait("tensor", "v", vtrit)
        P.wait("tensor", "v", evd[pp])
        for dc in range(ND):
            tv = T(lambda e, dc=dc, pg=pg, to=toff: e.matmul(pg[:, dc * 128:dc * 128 + 128],
                                                    tritb[:, to + dc * 128:to + (dc + 1) * 128], ident_bf[:],
                                                    start=True, stop=True))
        P.wait("vector", "t", tv)
        P.wait("vector", "sg%d" % pp, stw[pp])
        evd[pp] = V(lambda e, pp=pp, pg=pg: e.tensor_copy(stg[pp][:], pg[:]))
        P.wait("sync", "v", evd[pp])
        stw[pp] = DW(lambda e, i=i, pp=pp: e.dma_start(out=w1q_sc[i], in_=stg[pp][:]), "sg%d" % pp)
        return tv

    def w2_writer(dt, vtrit, toff=0):
        P.wait("tensor", "v", vtrit)
        tvl = 0
        for rho in range(4):
            pp = (dt * 4 + rho) % 2
            pg = pg0 if pp == 0 else pg1
            P.wait("tensor", "v", evd[pp])
            for j in range(8):
                tv = T(lambda e, rho=rho, j=j, pg=pg: e.matmul(
                    pg[:, j * 128:j * 128 + 128],
                    tritb[:, (rho * 8 + j) * 128:(rho * 8 + j) * 128 + 128], ident_bf[:],
                    start=True, stop=True))
            tvl = tv
            P.wait("vector", "t", tv)
            P.wait("vector", "sg%d" % pp, stw[pp])
            evd[pp] = V(lambda e, pp=pp, pg=pg: e.tensor_copy(stg[pp][:], pg[:]))
            P.wait("sync", "v", evd[pp])
            for j in range(8):
                stw[pp] = DW(lambda e, rho=rho, j=j, dt=dt, pp=pp: e.dma_start(
                    out=w2q_sc[rho * 8 + j, :, dt * 128:(dt + 1) * 128], in_=stg[pp][:, j * 128:(j + 1) * 128]), "sg%d" % pp)
        return tvl

    wout_cast_done = P.cnt["v"]

    def emit_wquant():
        global_evd = None
        P.wait("sync", "v", wout_cast_done)
        for args, writer in (((w1_d, 32, 1024, C_SW1, C_C1), w1_writer),
                             ((w2_d, 8, 4096, C_SW2, C_C2), w2_writer)):
            evd.clear(); evd.update({0: 0, 1: 0})
            stw.clear(); stw.update({0: 0, 1: 0})
            trd.clear()
            wquant(*args, writer)
        if debug:
            for s_ in ("sg0", "sg1", "wr"):
                P.wait("gpsimd", s_, P.cnt[s_])
            P.wait("gpsimd", "v", P.cnt["v"])
            for i in range(32):
                DG(lambda e, i=i: e.dma_start(out=dbg["d_w1q"][i], in_=w1q_sc[i]), "gd")
                DG(lambda e, i=i: e.dma_start(out=dbg["d_w2q"][i], in_=w2q_sc[i]), "gd")
            DG(lambda e: e.dma_start(out=dbg["d_ccol"], in_=ccol[:]), "gd")
            DG(lambda e: e.dma_start(out=dbg["d_g"][:, 0:16], in_=stat[:]), "gd")

    evd = {0: 0, 1: 0}
    stw = {0: 0, 1: 0}
    trd = {}
    fence()

    for b in range(NB if UPTO >= 2 else 0):
        P.op("scalar", lambda e: e.preload_act_table(AF.Sqrt))
        p1v = {0: 0, 1: 0}
        p1s = {0: 0, 1: 0}
        for st in range(NT):
            pp = st % 2
            P.wait("gpsimd", "v", p1v[pp])
            P.wait("gpsimd", "s", p1s[pp])
            xd = DG(lambda e, b=b, st=st, pp=pp: e.dma_start(
                out=xbuf[:, pp * 1024:pp * 1024 + 1024], in_=x_d[b, st * 128:(st + 1) * 128, :]), "gx%d" % pp)
            xs = xbuf[:, pp * 1024:pp * 1024 + 1024]
            P.wait("vector", "gx%d" % pp, xd)
            p1v[pp] = V(lambda e, xs=xs, st=st: e.reduce_sum(stat[:, st:st + 1], xs, AX.X))
            P.wait("scalar", "gx%d" % pp, xd)
            p1s[pp] = S(lambda e, xs=xs, st=st, pp=pp: e.activation(
                xc[:] if pp == 0 else xt1[:], xs, AF.Square, accum_out=mx[:, st:st + 1]))
        P.wait("vector", "s", P.cnt["s"])
        Vd(lambda e: e.tensor_scalar(rows_sb[:, 0:16], stat[:, 0:16], -1.0 / 1024.0, None, ALU.mult))
        Vd(lambda e: e.tensor_tensor(rows_sb[:, 16:32], rows_sb[:, 0:16], rows_sb[:, 0:16], ALU.mult))
        vv = Vd(lambda e: e.scalar_tensor_tensor(rows_sb[:, 32:48], mx[:, 0:16], 1.0 / 1024.0,
                                                 rows_sb[:, 16:32], ALU.mult, ALU.subtract))
        P.wait("scalar", "v", vv)
        sv = S(lambda e: e.activation(rows_sb[:, 32:48], rows_sb[:, 32:48], AF.Sqrt, bias=ccol[:, C_EPS:C_EPS + 1]))
        P.wait("vector", "s", sv)
        istd_done = Vd(lambda e: e.reciprocal(rows_sb[:, 48:64], rows_sb[:, 32:48]))
        p2v = {0: 0, 1: 0}
        p2s = {0: 0, 1: 0}
        for st in range(NT):
            pp = st % 2
            P.wait("gpsimd", "v", p2v[pp])
            P.wait("gpsimd", "s", p2s[pp])
            xd = DG(lambda e, b=b, st=st, pp=pp: e.dma_start(
                out=xbuf[:, pp * 1024:pp * 1024 + 1024], in_=x_d[b, st * 128:(st + 1) * 128, :]), "gx%d" % pp)
            xs = xbuf[:, pp * 1024:pp * 1024 + 1024]
            P.wait("scalar", "gx%d" % pp, xd)
            if st < 2:
                P.wait("scalar", "v", istd_done)
            cbuf = xc if pp == 0 else xt1
            p2s[pp] = S(lambda e, xs=xs, st=st, cbuf=cbuf: e.activation(
                cbuf[:], xs, AF.Identity, bias=rows_sb[:, st:st + 1]))
            P.wait("vector", "s", p2s[pp])
            p2v[pp] = V(lambda e, st=st, cbuf=cbuf: e.reduce_max(
                rows_sb[:, 64 + st:65 + st], cbuf[:], AX.X, apply_absolute_value=True))
        P.wait("vector", "s", P.cnt["s"])
        Vd(lambda e: e.tensor_tensor(rows_sb[:, 64:80], rows_sb[:, 64:80], rows_sb[:, 48:64], ALU.mult))
        Vd(lambda e: e.tensor_scalar(gst[:, 0:16], rows_sb[:, 64:80], 1e-5, None, ALU.max))
        Vd(lambda e: e.reciprocal(rows_sb[:, 80:96], gst[:, 0:16]))
        Vd(lambda e: e.tensor_scalar(rows_sb[:, 80:96], rows_sb[:, 80:96], 127.0, None, ALU.mult))
        aw = Vd(lambda e: e.tensor_tensor(rows_sb[:, 96:112], rows_sb[:, 80:96], rows_sb[:, 48:64], ALU.mult))
        P.wait("scalar", "v", aw)
        p3s = {0: 0, 1: 0}
        evT = {0: 0, 1: 0}
        evTT = 0
        for st in range(NT):
            pp = st % 2
            P.wait("gpsimd", "s", p3s[pp])
            xd = DG(lambda e, b=b, st=st, pp=pp: e.dma_start(
                out=xbuf[:, pp * 1024:pp * 1024 + 1024], in_=x_d[b, st * 128:(st + 1) * 128, :]), "gx%d" % pp)
            xs = xbuf[:, pp * 1024:pp * 1024 + 1024]
            P.wait("scalar", "gx%d" % pp, xd)
            cbuf = xc if pp == 0 else xt1
            S(lambda e, xs=xs, st=st, cbuf=cbuf: e.activation(
                cbuf[:], xs, AF.Identity, bias=rows_sb[:, st:st + 1]))
            P.wait("scalar", "t", evTT if st >= 1 else 0)
            S(lambda e, st=st, cbuf=cbuf: e.activation(xn_t[:], cbuf[:], AF.Identity, scale=rows_sb[:, 48 + st:49 + st]))
            S(lambda e, st=st, cbuf=cbuf: e.activation(
                xbuf[:, pp * 1024:pp * 1024 + 1024] if False else cbuf[:],
                cbuf[:], AF.Identity, scale=rows_sb[:, 96 + st:97 + st], bias=ccol[:, C_MAG:C_MAG + 1]))
            sv = S(lambda e, cbuf=cbuf: e.activation(xi_t[:], cbuf[:], AF.Identity, bias=ccol[:, C_NMAG:C_NMAG + 1]))
            p3s[pp] = sv
            P.wait("tensor", "s", sv)
            P.wait("tensor", "v", evT[0])
            for dc in range(ND):
                T(lambda e, dc=dc: e.matmul(pg0[:, dc * 128:dc * 128 + 128],
                                            xn_t[:, dc * 128:(dc + 1) * 128], ident_bf[:], start=True, stop=True))
            tv0 = P.cnt["t"]
            P.wait("tensor", "v", evT[1])
            for dc in range(ND):
                T(lambda e, dc=dc: e.matmul(pg1[:, dc * 128:dc * 128 + 128],
                                            xi_t[:, dc * 128:(dc + 1) * 128], ident_bf[:], start=True, stop=True))
            evTT = P.cnt["t"]
            P.wait("vector", "t", tv0)
            evT[0] = V(lambda e, st=st: e.tensor_copy(
                xnT[:].rearrange("p (dc t) -> p dc t", dc=8)[:, :, st * 128:st * 128 + 128],
                pg0[:].rearrange("p (dc c) -> p dc c", dc=8)))
            P.wait("vector", "t", evTT)
            evT[1] = V(lambda e, st=st: e.tensor_copy(
                xiT[:].rearrange("p (dc t) -> p dc t", dc=8)[:, :, st * 128:st * 128 + 128],
                pg1[:].rearrange("p (dc c) -> p dc c", dc=8)))

        if debug and b == 0:
            fence()
            DG(lambda e: e.dma_start(out=dbg["d_xnT"], in_=xnT[:]), "gd")
            DG(lambda e: e.dma_start(out=dbg["d_xiT"], in_=xiT[:]), "gd")
            fence()
        if b == 0:
            emit_wquant()

        if UPTO < 3:
            continue
        P.wait("tensor", "v", P.cnt["v"] if False else lastV[0] if False else max(evT[0], evT[1]))
        P.wait("sync", "v", max(evT[0], evT[1]))
        P.wait("sync", "s", P.cnt["s"])
        for e_ in ("scalar", "vector", "tensor"):
            P.wait(e_, "sg0", P.cnt["sg0"])
            P.wait(e_, "sg1", P.cnt["sg1"])
            P.wait(e_, "wr", P.cnt["wr"])
        lastT_wqk = {0: 0, 1: 0}
        lastS_ps = {0: 0, 1: 0}
        for ct in range(16):
            pp = ct % 2
            P.wait("sync", "t", lastT_wqk[pp])
            wd = DW(lambda e, ct=ct, pp=pp: e.dma_start(out=wqkb[:, pp * 1024:pp * 1024 + 1024], in_=wqk_sc[ct]), "pw%d" % pp)
            P.wait("tensor", "pw%d" % pp, wd)
            for ts in range(4):
                qq = (ct * 4 + ts) % 2
                P.wait("tensor", "s", lastS_ps[qq])
                for dc in range(ND):
                    tv = T(lambda e, pp=pp, dc=dc, ts=ts, qq=qq: e.matmul(
                        pg2[:, qq * 512:qq * 512 + 512],
                        wqkb[:, pp * 1024 + dc * 128: pp * 1024 + dc * 128 + 128],
                        xnT[:, dc * 2048 + ts * 512: dc * 2048 + ts * 512 + 512],
                        start=(dc == 0), stop=(dc == ND - 1)))
                lastT_wqk[pp] = tv
                P.wait("scalar", "t", tv)
                lastS_ps[qq] = S(lambda e, ct=ct, ts=ts, qq=qq: e.activation(
                    qkT[:, ct * 2048 + ts * 512: ct * 2048 + ts * 512 + 512],
                    pg2[:, qq * 512:qq * 512 + 512], AF.Identity, bias=bcol[:, ct:ct + 1]))
        P.wait("tensor", "s", max(lastS_ps[0], lastS_ps[1]))
        lastV_ps = {0: 0, 1: 0}
        lastT_wv = 0
        for hf in range(2):
            P.wait("sync", "t", lastT_wv)
            for dc in range(ND):
                wd = DW(lambda e, hf=hf, dc=dc: e.dma_start(out=wvb[:, dc * 640:dc * 640 + 640], in_=wv_sc[hf, dc]), "gv")
            P.wait("tensor", "gv", wd)
            for st in range(NT):
                qq = st % 2
                pgv = pg3 if qq == 0 else pg2
                P.wait("tensor", "v", lastV_ps[qq])
                for dc in range(ND):
                    T(lambda e, dc=dc, st=st, pgv=pgv: e.matmul(
                        pgv[:, 0:512],
                        xnT[:, dc * 2048 + st * 128: dc * 2048 + st * 128 + 128],
                        wvb[:, dc * 640: dc * 640 + 512],
                        start=(dc == 0), stop=False))
                    T(lambda e, dc=dc, st=st, pgv=pgv: e.matmul(
                        pgv[:, 512:640],
                        xnT[:, dc * 2048 + st * 128: dc * 2048 + st * 128 + 128],
                        wvb[:, dc * 640 + 512: dc * 640 + 640],
                        start=(dc == 0), stop=False))
                T(lambda e, hf=hf, pgv=pgv: e.matmul(pgv[:, 0:512], ones_bf[0:1, 0:128],
                                                     bvrow[0:1, hf * 640: hf * 640 + 512], start=False, stop=True))
                tv = T(lambda e, hf=hf, pgv=pgv: e.matmul(pgv[:, 512:640], ones_bf[0:1, 0:128],
                                                          bvrow[0:1, hf * 640 + 512: hf * 640 + 640], start=False, stop=True))
                lastT_wv = tv
                P.wait("vector", "t", tv)
                base = st * 1280 + hf * 640
                lastV_ps[qq] = V(lambda e, base=base, pgv=pgv: e.tensor_copy(
                    v_sb[:, base: base + 640], pgv[:, 0:640]))
        fence()
        if debug and b == 0:
            DG(lambda e: e.dma_start(out=dbg["d_qkT"], in_=qkT[:]), "gd")
            DG(lambda e: e.dma_start(out=dbg["d_vsb"], in_=v_sb[:]), "gd")
            fence()

        if UPTO < 4:
            continue
        P.op("scalar", lambda e: e.preload_act_table(AF.Exp))
        sc_pair = [pg0, pg1]
        ctx_pair = [pg2, pg3]
        exp_s = {0: 0, 1: 0}
        avT = {0: 0, 1: 0}
        evac_v = {0: 0, 1: 0}
        nbw = {0: 0, 1: 0}
        def attn_tail(pi, cps, qc, pr, avlast):
            P.wait("vector", "t", avlast)
            P.wait("vector", "nb%d" % pi, nbw[pi])
            V(lambda e, cps=cps, pi=pi: e.reciprocal(rows_sb[64:65, pi * 512:pi * 512 + 512], cps[64:65, 0:512]))
            rv = V(lambda e, cps=cps, pi=pi: e.reciprocal(rows_sb[32:33, pi * 512:pi * 512 + 512], cps[32:33, 512:1024]))
            P.wait("sync", "v", rv)
            DW(lambda e, pi=pi: e.dma_start(out=den_sc[pi], in_=rows_sb[32:96:32, pi * 512:pi * 512 + 512]), "nb%d" % pi)
            P.wait("sync", "nb%d" % pi, P.cnt["nb%d" % pi])
            DW(lambda e, pi=pi: e.dma_start(out=nrm[0:64, pi * 512:pi * 512 + 512],
                                            in_=den_sc[pi, 1].unsqueeze(0).broadcast_to([64, 512])), "nb%d" % pi)
            nbw[pi] = DW(lambda e, pi=pi: e.dma_start(out=nrm[64:128, pi * 512:pi * 512 + 512],
                                                      in_=den_sc[pi, 0].unsqueeze(0).broadcast_to([64, 512])), "nb%d" % pi)
            P.wait("vector", "nb%d" % pi, nbw[pi])
            dst = pr * 2048 + qc * 512
            V(lambda e, cps=cps, dst=dst, pi=pi: e.tensor_tensor(
                ctx_sb[0:64, dst:dst + 512], cps[0:64, 0:512], nrm[0:64, pi * 512:pi * 512 + 512], ALU.mult))
            evac_v[pi] = V(lambda e, cps=cps, dst=dst, pi=pi: e.tensor_tensor(
                ctx_sb[64:128, dst:dst + 512], cps[64:128, 512:1024], nrm[64:128, pi * 512:pi * 512 + 512], ALU.mult))

        def attn_av(info):
            cps, kk, off, vb0, st0, sp, pi = (info["cps"], info["kk"], info["off"],
                                              info["vb0"], info["st0"], info["sp"], info["pi"])
            P.wait("tensor", info["gate"][0], info["gate"][1])
            if st0:
                P.wait("tensor", "v", evac_v[pi])
            T(lambda e, cps=cps, vb0=vb0, kk=kk, off=off, st0=st0, sp=sp: e.matmul(
                cps[:, off:512], v_sb[:, vb0:vb0 + 128], p_sb[:, kk * 1024 + off:kk * 1024 + 512],
                start=st0, stop=sp))
            avT[kk] = T(lambda e, cps=cps, vb0=vb0, kk=kk, off=off, st0=st0, sp=sp: e.matmul(
                cps[:, 512 + off:1024], v_sb[:, vb0 + 32:vb0 + 160], p_sb[:, kk * 1024 + 512 + off:kk * 1024 + 1024],
                start=st0, stop=sp))
            if sp:
                attn_tail(pi, cps, info["qc"], info["pr"], avT[kk])

        steps = [(qc, pr, kt, 4 * qc + 4) for qc in range(4) for pr in range(8) for kt in range(4 * qc + 4)]
        pending = None
        for kap, (qc, pr, kt, nkt) in enumerate(steps):
            pi = (qc * 8 + pr) % 2
            cps = ctx_pair[pi]
            kk = kap % 2
            scp = sc_pair[kk]
            diag = kt >= 4 * qc
            off = (kt - 4 * qc) * 128 if diag else 0
            kbase = (8 + pr) * 2048 + kt * 128
            qbase = pr * 2048 + qc * 512
            P.wait("tensor", "s", exp_s[kk])
            T(lambda e, scp=scp, kbase=kbase, qbase=qbase, off=off: e.matmul(
                scp[:, off:512], qkT[0:64, kbase:kbase + 128], qkT[0:64, qbase + off:qbase + 512],
                start=True, stop=True, tile_position=(0, 0)))
            tv = T(lambda e, scp=scp, kbase=kbase, qbase=qbase, off=off: e.matmul(
                scp[:, 512 + off:1024], qkT[64:128, kbase:kbase + 128], qkT[64:128, qbase + off:qbase + 512],
                start=True, stop=True, tile_position=(64, 0)))
            P.wait("scalar", "t", tv)
            P.wait("scalar", "t", avT[kk])
            exp_s[kk] = S(lambda e, scp=scp, kk=kk, off=off: e.activation(
                p_sb[:, kk * 1024:kk * 1024 + 1024].rearrange("p (h c) -> p h c", h=2)[:, :, off:512],
                scp[:].rearrange("p (h c) -> p h c", h=2)[:, :, off:512],
                AF.Exp, scale=0.125))
            if diag:
                P.wait("vector", "s", exp_s[kk])
                mv = V(lambda e, kk=kk, off=off: e.tensor_tensor(
                    p_sb[:, kk * 1024:kk * 1024 + 1024].rearrange("p (h c) -> p h c", h=2)[:, :, off:off + 128],
                    p_sb[:, kk * 1024:kk * 1024 + 1024].rearrange("p (h c) -> p h c", h=2)[:, :, off:off + 128],
                    tri_bf[:].unsqueeze(1).broadcast_to([128, 2, 128]), ALU.mult))
                gate = ("v", mv)
            else:
                gate = ("s", exp_s[kk])
            info = dict(cps=cps, kk=kk, off=off, vb0=kt * 1280 + pr * 160,
                        st0=(kt == 0), sp=(kt == nkt - 1), pi=pi, qc=qc, pr=pr, gate=gate)
            if pending is not None:
                attn_av(pending)
            pending = info
        attn_av(pending)
        fence()
        if debug and b == 0:
            DG(lambda e: e.dma_start(out=dbg["d_ctx"], in_=ctx_sb[:]), "gd")
            fence()

        if UPTO < 5:
            continue
        for dc in range(ND):
            wd = DW(lambda e, dc=dc: e.dma_start(out=wo_res[:, dc * 1024:dc * 1024 + 1024], in_=wo_sc[dc]), "gv")
        P.wait("tensor", "gv", wd)
        addv = {0: 0, 1: 0}
        accw = {0: 0, 1: 0}
        for st in range(NT):
            pp = st % 2
            pg = pg0 if pp == 0 else pg1
            P.wait("gpsimd", "v", addv[pp])
            xd = DG(lambda e, b=b, st=st, pp=pp: e.dma_start(
                out=xrb[:, pp * 1024:pp * 1024 + 1024], in_=x_d[b, st * 128:(st + 1) * 128, :]), "gx%d" % pp)
            P.wait("tensor", "v", addv[pp])
            for half in range(2):
                for dc in range(ND):
                    T(lambda e, pg=pg, half=half, dc=dc, st=st: e.matmul(
                        pg[:, half * 512:half * 512 + 512],
                        ctx_sb[:, dc * 2048 + st * 128: dc * 2048 + st * 128 + 128],
                        wo_res[:, dc * 1024 + half * 512: dc * 1024 + half * 512 + 512],
                        start=(dc == 0), stop=False))
                tv = T(lambda e, pg=pg, half=half: e.matmul(
                    pg[:, half * 512:half * 512 + 512], ones_bf[0:1, 0:128],
                    borow[0:1, half * 512:half * 512 + 512], start=False, stop=True))
            P.wait("vector", "t", tv)
            P.wait("vector", "gx%d" % pp, xd)
            P.wait("vector", "ga%d" % pp, accw[pp])
            addv[pp] = V(lambda e, pg=pg, pp=pp: e.tensor_tensor(
                acst[:, pp * 1024:pp * 1024 + 1024], pg[:], xrb[:, pp * 1024:pp * 1024 + 1024], ALU.add))
            P.wait("gpsimd", "v", addv[pp])
            accw[pp] = DG(lambda e, b=b, st=st, pp=pp: e.dma_start(
                out=acc_sc[b, st], in_=acst[:, pp * 1024:pp * 1024 + 1024]), "ga%d" % pp)
        wout_t = P.cnt["t"]
        wout_v = P.cnt["v"]
        wout_ga0 = P.cnt["ga0"]
        wout_ga1 = P.cnt["ga1"]
        P.wait("gpsimd", "t", wout_t)
        P.wait("gpsimd", "v", wout_v)
        P.wait("gpsimd", "ga0", wout_ga0)
        P.wait("gpsimd", "ga1", wout_ga1)
        P.wait("vector", "t", wout_t)
        P.wait("vector", "ga0", wout_ga0)
        P.wait("vector", "ga1", wout_ga1)
        P.wait("scalar", "t", wout_t)
        P.wait("tensor", "v", wout_v)
        if debug and b == 0:
            P.wait("gpsimd", "ga0", P.cnt["ga0"])
            P.wait("gpsimd", "ga1", P.cnt["ga1"])
            for st in range(NT):
                DG(lambda e, st=st: e.dma_start(out=dbg["d_acc"][st], in_=acc_sc[0, st]), "gd")
            fence()

        if UPTO < 6:
            continue
        P.op("scalar", lambda e: e.preload_act_table(AF.Silu))
        vv = Vd(lambda e: e.tensor_scalar(stat[:], gst[:], ccol[:, C_C1:C_C1 + 1], None, ALU.mult))
        P.wait("tensor", "v", vv)
        tv = T(lambda e: e.matmul(pg1[0:16, 0:128], stat[:], ident_f[:], start=True, stop=True))
        P.wait("vector", "t", tv)
        vv = V(lambda e: e.tensor_copy(nrm[0:16, 0:128], pg1[0:16, 0:128]))
        P.wait("gpsimd", "v", vv)
        gw = DG(lambda e: e.dma_start(out=row_sc[0:2048].rearrange("(r c) -> r c", r=16), in_=nrm[0:16, 0:128]), "gr")
        P.wait("gpsimd", "gr", gw)
        gw = DG(lambda e: e.dma_start(out=g_sb[:], in_=row_sc[0:2048].unsqueeze(0).broadcast_to([128, 2048])), "gr")
        P.wait("vector", "gr", gw)
        stash0 = V(lambda e: e.memset(stash[:], 0.0))
        if debug and b == 0:
            P.wait("gpsimd", "v", P.cnt["v"])
            DG(lambda e: e.dma_start(out=dbg["d_g"], in_=g_sb[:]), "gd")

        pmT_t = [0]
        w1T = {0: 0, 1: 0}
        w2T = {0: 0, 1: 0}
        w1ev = {0: 0, 1: 0, 2: 0, 3: 0}
        w2ev = {0: 0, 1: 0}
        w2last = {0: 0, 1: 0}
        uV = {k: stash0 for k in range(8)}
        uSS = {k: 0 for k in range(8)}
        pg1free_s = [0]
        pg1free_v = [0]
        pg1w = [-1]
        roundV_w = {-1: 0}
        roundS_w = {-1: 0}
        silu31 = {}
        accv = {0: 0, 1: 0}
        outw = {0: 0, 1: 0}
        sb_g = 0

        ub4 = [ubuf[0], ubuf[1], xc[:, 0:388], xc[:, 388:776]]

        def w1conv(c):
            have = c <= 7
            nonlocal_first = [True]
            for grp in range(8):
                hcs = [grp * 4 + j for j in range(4)]
                if have:
                    ppw = grp % 2
                    P.wait("sync", "t", w1T[ppw])
                    wd = DW(lambda e, grp=grp, ppw=ppw: e.dma_start(
                        out=w1b[:, ppw * 4096:ppw * 4096 + 4096].rearrange("p (h q) -> p h q", h=4),
                        in_=w1q_sc[grp * 4:grp * 4 + 4].rearrange("h p q -> p h q")), "pw%d" % ppw)
                    P.wait("tensor", "pw%d" % ppw, wd)
                for hc in hcs:
                    sl = hc % 4
                    P.wait("scalar", "v", uV[sl])
                    S(lambda e, sl=sl, hc=hc: e.copy(ub4[sl][:, 0:129], stash[:, hc * 129:hc * 129 + 129]))
                evc = {}
                for hc in hcs:
                    sl = hc % 4
                    if have:
                        pw = hc % 3
                        spg = (pg0, pg0, pg1)[pw]
                        sof = (0, 512, 512)[pw]
                        P.wait("tensor", "v", w1ev[pw])
                        for dc in range(ND):
                            tmm = T(lambda e, j=hc % 4, dc=dc, c=c, spg=spg, sof=sof, ppw=(hc // 4) % 2: e.matmul(
                                spg[:, sof:sof + 256],
                                w1b[:, ppw * 4096 + j * 1024 + dc * 128: ppw * 4096 + j * 1024 + dc * 128 + 128],
                                xiT[:, dc * 2048 + c * 256: dc * 2048 + c * 256 + 256],
                                start=(dc == 0), stop=(dc == ND - 1)))
                        w1T[(hc // 4) % 2] = tmm
                        P.wait("vector", "t", tmm)
                        P.wait("vector", "s", uSS[sl])
                        w1ev[pw] = V(lambda e, sl=sl, spg=spg, sof=sof, c=c: e.tensor_tensor(
                            ub4[sl][:, 129:385], spg[:, sof:sof + 256],
                            g_sb[:, c * 256:c * 256 + 256], ALU.mult))
                        evc[hc] = w1ev[pw]
                    else:
                        P.wait("vector", "s", uSS[sl])
                        evc[hc] = V(lambda e, sl=sl: e.memset(ub4[sl][:, 129:388], 0.0))
                inc_ = {}
                for hc in hcs:
                    sl = hc % 4
                    P.wait("scalar", "v", evc[hc])
                    if nonlocal_first[0]:
                        P.wait("scalar", "v", roundV_w[c - 1])
                        nonlocal_first[0] = False
                    inc_[hc] = S(lambda e, sl=sl, hc=hc: e.activation(
                        h2w[:, hc * 256:hc * 256 + 256], ub4[sl][:, 0:256], AF.Identity,
                        scale=cwsb[:, hc * 3:hc * 3 + 1], bias=cbsb[:, hc:hc + 1]))
                    if have:
                        uSS[sl] = S(lambda e, sl=sl, hc=hc: e.copy(
                            stash[:, hc * 129:hc * 129 + 129], ub4[sl][:, 256:385]))
                    else:
                        uSS[sl] = inc_[hc]
                for hc in hcs:
                    sl = hc % 4
                    P.wait("vector", "s", inc_[hc])
                    V(lambda e, sl=sl, hc=hc: e.scalar_tensor_tensor(
                        h2w[:, hc * 256:hc * 256 + 256], ub4[sl][:, 1:257], cwsb[:, hc * 3 + 1:hc * 3 + 2],
                        h2w[:, hc * 256:hc * 256 + 256], ALU.mult, ALU.add))
                    uV[sl] = V(lambda e, sl=sl, hc=hc: e.scalar_tensor_tensor(
                        h2w[:, hc * 256:hc * 256 + 256], ub4[sl][:, 2:258], cwsb[:, hc * 3 + 2:hc * 3 + 3],
                        h2w[:, hc * 256:hc * 256 + 256], ALU.mult, ALU.add))
                P.wait("scalar", "v", uV[hcs[-1] % 4])
                sl_ = S(lambda e, grp=grp: e.activation(
                    h2w[:, grp * 1024:grp * 1024 + 1024], h2w[:, grp * 1024:grp * 1024 + 1024], AF.Silu))
            silu31[c] = sl_

        def quantscale(cq):
            pq = 8 * (cq % 2)
            P.wait("vector", "s", silu31[cq])
            P.wait("vector", "t", pmT_t[0])
            pmv = V(lambda e: e.tensor_reduce(
                pmb[:, 0:256], h2w[:].rearrange("p (h t) -> p t h", h=32),
                AX.X, ALU.max, apply_absolute_value=True))
            P.wait("tensor", "v", pmv)
            P.wait("tensor", "s", P.cnt["s"])
            T(lambda e: e.matmul(pg1[:, 0:128], pmb[:, 0:128], ident_f[:], start=True, stop=True))
            tv = T(lambda e: e.matmul(pg1[:, 128:256], pmb[:, 128:256], ident_f[:], start=True, stop=True))
            pmT_t[0] = tv
            P.wait("vector", "t", tv)
            Vd(lambda e, pq=pq: e.tensor_reduce(
                mx[:, pq + 0:pq + 2], pg1[:, 0:256].rearrange("p (w c) -> p w c", w=2), AX.X, ALU.max,
                apply_absolute_value=False))
            Vd(lambda e, pq=pq: e.tensor_scalar(mx[:, pq + 2:pq + 4], mx[:, pq + 0:pq + 2], 1e-5, None, ALU.max))
            Vd(lambda e, pq=pq: e.tensor_scalar(mx[:, pq + 4:pq + 6], mx[:, pq + 2:pq + 4], ccol[:, C_C2:C_C2 + 1], None, ALU.mult))
            Vd(lambda e, pq=pq: e.reciprocal(mx[:, pq + 6:pq + 8], mx[:, pq + 2:pq + 4]))
            vvs = Vd(lambda e, pq=pq: e.tensor_scalar(mx[:, pq + 6:pq + 8], mx[:, pq + 6:pq + 8], 127.0, None, ALU.mult))
            P.wait("tensor", "v", vvs)
            tv = T(lambda e, pq=pq: e.matmul(pg1[0:2, 256:384], mx[:, pq + 6:pq + 8], ident_f[:], start=True, stop=True))
            P.wait("vector", "t", tv)
            vv = V(lambda e: e.tensor_copy(nrm[0:2, 0:128], pg1[0:2, 256:384]))
            pg1free_v[0] = vv
            P.wait("gpsimd", "v", vv)
            P.wait("gpsimd", "gr", P.cnt["gr"])
            sg = DG(lambda e: e.dma_start(out=row_sc[2048:2304].rearrange("(r c) -> r c", r=2), in_=nrm[0:2, 0:128]), "gr")
            P.wait("gpsimd", "gr", sg)
            sg = DG(lambda e: e.dma_start(out=sb_sb[:], in_=row_sc[2048:2304].unsqueeze(0).broadcast_to([128, 256])), "gr")
            P.wait("vector", "gr", sg)
            qm = V(lambda e: e.tensor_tensor(
                h2w[:].rearrange("p (h t) -> p h t", h=32),
                h2w[:].rearrange("p (h t) -> p h t", h=32),
                sb_sb[:].unsqueeze(1).broadcast_to([128, 32, 256]), ALU.mult))
            P.wait("vector", "t", max(w2T[0], w2T[1]))
            P.wait("scalar", "t", max(w2T[0], w2T[1]))
            P.wait("scalar", "v", qm)
            roundV_w[cq] = V(lambda e: e.tensor_scalar(h2q[:, 0:4096], h2w[:, 0:4096], MAGIC, -MAGIC, ALU.add, ALU.add))
            for k in range(4):
                S(lambda e, k=k: e.activation(xbuf[:, 0:1024], h2w[:, 4096 + k * 1024:4096 + k * 1024 + 1024],
                                              AF.Identity, bias=ccol[:, C_MAG:C_MAG + 1]))
                rS = S(lambda e, k=k: e.activation(h2q[:, 4096 + k * 1024:4096 + k * 1024 + 1024], xbuf[:, 0:1024],
                                                   AF.Identity, bias=ccol[:, C_NMAG:C_NMAG + 1]))
            roundS_w[cq] = rS

        accd_all = {}

        def w2mms(cq, b=b):
            stl = [t_ for t_ in (2 * cq - 1, 2 * cq) if 0 <= t_ < 16]
            for st_out in stl:
                par = st_out % 2
                P.wait("gpsimd", "v", accv[par])
                accd_all[st_out] = DG(lambda e, b=b, st_out=st_out, par=par: e.dma_start(
                    out=accb[:, par * 1024:par * 1024 + 1024], in_=acc_sc[b, st_out]), "ga%d" % par)
            for grp in range(8):
                pp2 = grp % 2
                P.wait("sync", "t", w2T[pp2])
                wd2 = DW(lambda e, grp=grp, pp2=pp2: e.dma_start(
                    out=w2b[:, pp2 * 4096:pp2 * 4096 + 4096].rearrange("p (h q) -> p h q", h=4),
                    in_=w2q_sc[grp * 4:grp * 4 + 4].rearrange("h p q -> p h q")), "qw%d" % pp2)
                P.wait("tensor", "qw%d" % pp2, wd2)
                if grp == 0:
                    P.wait("tensor", "v", roundV_w[cq])
                    P.wait("tensor", "s", roundS_w[cq])
                for j in range(4):
                    hc = grp * 4 + j
                    for st_out in stl:
                        w = st_out - (2 * cq - 1)
                        par = st_out % 2
                        pgw = pg2 if par == 0 else pg3
                        if hc == 0:
                            P.wait("tensor", "v", w2ev[par])
                        for half in range(2):
                            tmm = T(lambda e, pgw=pgw, hc=hc, w=w, pp2=pp2, j=j, half=half: e.matmul(
                                pgw[:, half * 512:half * 512 + 512],
                                h2q[:, hc * 256 + w * 128: hc * 256 + w * 128 + 128],
                                w2b[:, pp2 * 4096 + j * 1024 + half * 512: pp2 * 4096 + j * 1024 + half * 512 + 512],
                                start=(hc == 0), stop=(hc == NHC - 1)))
                        w2T[pp2] = tmm
                        if hc == NHC - 1:
                            w2last[par] = tmm

        def w2tail(cq, b=b):
            stl = [t_ for t_ in (2 * cq - 1, 2 * cq) if 0 <= t_ < 16]
            pq = 8 * (cq % 2)
            for st_out in stl:
                w = st_out - (2 * cq - 1)
                par = st_out % 2
                pgw = pg2 if par == 0 else pg3
                P.wait("vector", "ga%d" % par, accd_all[st_out])
                P.wait("vector", "t", w2last[par])
                P.wait("vector", "go%d" % par, outw[par])
                accv[par] = V(lambda e, pgw=pgw, par=par, w=w, pq=pq: e.scalar_tensor_tensor(
                    outb[:, par * 1024:par * 1024 + 1024], pgw[:], mx[:, pq + 4 + w:pq + 5 + w],
                    accb[:, par * 1024:par * 1024 + 1024], ALU.mult, ALU.add))
                w2ev[par] = accv[par]
                P.wait("gpsimd", "v", accv[par])
                outw[par] = DG(lambda e, b=b, st_out=st_out, par=par: e.dma_start(
                    out=out_d[b, st_out * 128:(st_out + 1) * 128, :], in_=outb[:, par * 1024:par * 1024 + 1024]), "go%d" % par)

        for c in range(FFNW):
            if 1 <= c:
                quantscale(c - 1)
            if 2 <= c:
                w2tail(c - 2)
            if c <= 8:
                w1conv(c)
            if 1 <= c:
                w2mms(c - 1)
        if FFNW >= 2:
            w2tail(FFNW - 2)
        fence()

    for s_ in list(P.cnt.keys()):
        if s_ not in ("t", "s", "v") and P.cnt[s_] > 0:
            P.wait("gpsimd", s_, P.cnt[s_])

    from contextlib import ExitStack
    with ExitStack() as stack:
        sems = {}
        for s_ in list(P.cnt.keys()):
            if P.cnt[s_] > 0:
                sems[s_] = stack.enter_context(nc.semaphore("sem_" + s_))
        block = stack.enter_context(nc.Block())

        @block.sync
        def _(sync):
            P.replay("sync", sync, sems)

        @block.gpsimd
        def _(gpsimd):
            P.replay("gpsimd", gpsimd, sems)

        @block.tensor
        def _(tensor):
            P.replay("tensor", tensor, sems)

        @block.scalar
        def _(scalar):
            P.replay("scalar", scalar, sems)

        @block.vector
        def _(vector):
            P.replay("vector", vector, sems)

    return nc


_NC_CACHE = {}


def _get_nc(debug=False):
    key = (debug, UPTO, FFNW, W1MODE)
    if key not in _NC_CACHE:
        _NC_CACHE[key] = build(debug)
    return _NC_CACHE[key]


def kernel(L0_attn_hidden_states, L0_attn_W_attn, L0_attn_b_attn, L0_attn_W_out,
           L0_attn_b_out, L0_ffn_w1, L0_ffn_conv_w, L0_ffn_conv_b, L0_ffn_w2,
           _debug=False):
    x = np.ascontiguousarray(L0_attn_hidden_states, dtype=np.float32)
    shards = x.reshape(8, NB, 2048, 1024)
    nc = _get_nc(_debug)
    common = {
        "W_attn": np.ascontiguousarray(L0_attn_W_attn, dtype=np.float32),
        "b_attn": np.ascontiguousarray(L0_attn_b_attn, dtype=np.float32),
        "W_out": np.ascontiguousarray(L0_attn_W_out, dtype=np.float32),
        "b_out": np.ascontiguousarray(L0_attn_b_out, dtype=np.float32),
        "w1": np.ascontiguousarray(L0_ffn_w1, dtype=np.float32),
        "cw": np.ascontiguousarray(L0_ffn_conv_w, dtype=np.float32).reshape(4096, 3),
        "cb": np.ascontiguousarray(L0_ffn_conv_b, dtype=np.float32),
        "w2": np.ascontiguousarray(L0_ffn_w2, dtype=np.float32),
        "ident_bf": np.eye(128, dtype=ml_dtypes.bfloat16),
        "ident_f": np.eye(128, dtype=np.float32),
        "tri_bf": np.triu(np.ones((128, 128), dtype=np.float32)).astype(ml_dtypes.bfloat16),
    }
    in_maps = [dict(common, x=np.ascontiguousarray(shards[i])) for i in range(8)]
    res = run_bass_kernel_spmd(nc, in_maps, core_ids=list(range(8)))
    out = np.concatenate([r["out"].reshape(NB, 2048, 1024) for r in res.results], axis=0)
    if _debug:
        return out, res
    return out
